# version 15
# baseline (speedup 1.0000x reference)
import numpy as np

import concourse.bass as bass
import concourse.mybir as mybir
from concourse.bass_utils import run_bass_kernel_spmd

F32 = mybir.dt.float32
F16 = mybir.dt.float16
I8 = mybir.dt.int8
AF = mybir.ActivationFunctionType
ALU = mybir.AluOpType
AX = mybir.AxisListType

T = 2048
D = 1024
NCH = D // 128
B = 8

ENGINES = ("tensor", "vector", "scalar", "gpsimd", "sync")

CONST_SCALARS = [1e-5, 1.0, 0.5, 1e-8, 1e-30, 1e-6]
CW32 = 128 + 128 + 1 + len(CONST_SCALARS)
OFF_ID = 0
OFF_ONES = 128
OFF_IOTA = 256
OFF_SC = 257
CW16 = 128 + 128 + 4 * 512
OFF_ID16 = 0
OFF_ONES16 = 128
OFF_MASK16 = 256

P32_COLS = 22016
P16_COLS = 41984


def _consts32():
    ident = np.eye(128, dtype=np.float32)
    ones = np.ones((128, 128), dtype=np.float32)
    iota8 = np.zeros((128, 1), dtype=np.float32)
    iota8[:8, 0] = np.arange(8)
    sc = np.tile(np.asarray(CONST_SCALARS, dtype=np.float32)[None, :], (128, 1))
    return np.ascontiguousarray(np.concatenate([ident, ones, iota8, sc], axis=1))


def _consts16():
    ident = np.eye(128, dtype=np.float32)
    ones = np.ones((128, 128), dtype=np.float32)
    masks = []
    for v in range(4):
        m = np.zeros((128, 512), dtype=np.float32)
        for r in range(128):
            c0 = v * 128 + r + 1
            if c0 < 512:
                m[r, c0:] = -30000.0
        masks.append(m)
    return np.ascontiguousarray(np.concatenate([ident, ones] + masks, axis=1).astype(np.float16))


class Alloc:
    def __init__(self, tile, cols, name):
        self.tile, self.cols, self.name = tile, cols, name
        self.freel = [(0, cols)]
        self.live = {}

    def take(self, n, tag=""):
        n = int(n)
        for i, (s, ln) in enumerate(self.freel):
            if ln >= n:
                if ln == n:
                    self.freel.pop(i)
                else:
                    self.freel[i] = (s + n, ln - n)
                self.live[s] = (n, tag)
                ap = self.tile[:, s:s + n]
                if not hasattr(self, "_reg"):
                    self._reg = {}
                self._reg[id(ap)] = (s, ap)
                return ap
        raise RuntimeError(f"pool {self.name}: need {n}, free={self.freel} live={self.live}")

    def drop(self, ap):
        s, _ = self._reg.pop(id(ap))
        n, _ = self.live.pop(s)
        self.freel.append((s, n))
        self.freel.sort()
        m = []
        for st, ln in self.freel:
            if m and m[-1][0] + m[-1][1] == st:
                m[-1] = (m[-1][0], m[-1][1] + ln)
            else:
                m.append((st, ln))
        self.freel = m


class Ctx:
    pass


class Prog:
    def __init__(self, nc, es):
        self.nc, self.es = nc, es
        self.ops = []
        self.cnt = {}
        self._bar_idx = 0
        self.eng_done = {e: self.sem(f"done_{e}") for e in ENGINES}

    def sem(self, name):
        s = self.es.enter_context(self.nc.semaphore(name))
        self.cnt[id(s)] = 0
        self._nsem = getattr(self, "_nsem", 0) + 1
        return s

    def nsem(self):
        return getattr(self, "_nsem", 0)

    def val(self, s):
        return self.cnt[id(s)]

    def bump(self, s, k=1):
        self.cnt[id(s)] += k
        return self.cnt[id(s)]

    def emit(self, engine, fn):
        self.ops.append((engine, fn))

    def barrier(self):
        active = sorted(set(e for e, _ in self.ops[self._bar_idx:]) - {"sync"})
        for e in active:
            s = self.eng_done[e]
            self.bump(s, 1)
            self.emit(e, lambda eng, s=s: eng.nop().then_inc(s, 1))
        snap = {e: self.val(self.eng_done[e]) for e in ENGINES}
        for e in ENGINES:
            def wait(eng, e=e, snap=snap):
                for oe in ENGINES:
                    if oe != e and snap[oe] > 0:
                        eng.wait_ge(self.eng_done[oe], snap[oe])
            self.emit(e, wait)
        self._bar_idx = len(self.ops)

    def build(self, block):
        for e in ENGINES:
            def run(eng, e=e):
                ev = _EngView(eng)
                for oe, fn in self.ops:
                    if oe == e:
                        object.__setattr__(ev, "_tick", ev._tick + 1)
                        fn(ev)
            getattr(block, e)(run)


def _apr(o):
    try:
        t = o.tensor
        nm = t.name
        off = o.offset
        pat = [list(p) for p in o.ap]
    except Exception:
        return None
    ext = 0
    for s, n in pat[1:]:
        if s < 0 or n < 1:
            return (nm, 0, 1 << 60)
        ext += (n - 1) * s
    return (nm, off, off + ext + 1)


_WRKEYS = ("out", "accum_out")


class _EngView:

    _SKIP = {"nop", "drain", "wait_ge", "sem_inc", "wait_op", "dma_start",
             "dma_start_transpose", "engine_nop"}

    def __init__(self, eng):
        object.__setattr__(self, "_e", eng)
        nm = type(eng).__name__
        autodrain = ("Vector" in nm) or ("Scalar" in nm) or ("GpSimd" in nm)
        object.__setattr__(self, "_autodrain", autodrain)
        object.__setattr__(self, "_wr", [])
        object.__setattr__(self, "_rd", [])
        object.__setattr__(self, "_dirty", False)
        object.__setattr__(self, "_nsd", 0)
        object.__setattr__(self, "_tick", 0)
        object.__setattr__(self, "_lastr", None)
        object.__setattr__(self, "_last_tick", -1)
        object.__setattr__(self, "_last_inc", False)

    @staticmethod
    def _hit(rngs, q):
        nm, lo, hi = q
        for (n2, l2, h2) in rngs:
            if n2 == nm and lo < h2 and l2 < hi:
                return True
        return False

    def __getattr__(self, name):
        a = getattr(self._e, name)
        if not callable(a) or name in self._SKIP or name.startswith("_"):
            return a

        def call(*args, **kw):
            if self._autodrain:
                wr, rd = [], []
                for k, v in kw.items():
                    r = _apr(v)
                    if r is not None:
                        (wr if k in _WRKEYS else rd).append(r)
                for i, v in enumerate(args):
                    r = _apr(v)
                    if r is not None:
                        if name == "memset" and i == 0:
                            wr.append(r)
                        else:
                            wr.append(r)
                            rd.append(r)
                need = False
                for q in rd:
                    if self._hit(self._wr, q):
                        need = True
                        break
                if not need:
                    for q in wr:
                        if self._hit(self._wr, q) or self._hit(self._rd, q):
                            need = True
                            break
                if need and self._dirty:
                    self._e.drain()
                    object.__setattr__(self, "_wr", [])
                    object.__setattr__(self, "_rd", [])
                    object.__setattr__(self, "_nsd", 0)
                r = a(*args, **kw)
                if hasattr(r, "then_inc"):
                    self._wr.extend(wr)
                    self._rd.extend(rd)
                    object.__setattr__(self, "_dirty", True)
                    object.__setattr__(self, "_nsd", self._nsd + 1)
                    return _OpRes(self, r)
                return r
            r = a(*args, **kw)
            if hasattr(r, "then_inc"):
                object.__setattr__(self, "_dirty", True)
                object.__setattr__(self, "_lastr", r)
                object.__setattr__(self, "_last_tick", self._tick)
                object.__setattr__(self, "_last_inc", False)
            return r
        return call

    def wait_ge(self, sem, val):
        return self._e.wait_ge(sem, val)

    def drain(self):
        object.__setattr__(self, "_dirty", False)
        object.__setattr__(self, "_wr", [])
        object.__setattr__(self, "_rd", [])
        object.__setattr__(self, "_nsd", 0)
        return self._e.drain()

    def engine_nop(self):
        return self._e.engine_nop()

    def dma_start(self, *a, **k):
        return self._e.dma_start(*a, **k)

    def nop(self):
        view = self

        class _Shim:
            def then_inc(self, sem, k=1):
                if (not view._autodrain and view._lastr is not None
                        and view._last_tick == view._tick and not view._last_inc):
                    object.__setattr__(view, "_last_inc", True)
                    return view._lastr.then_inc(sem, k)
                object.__setattr__(view, "_dirty", False)
                object.__setattr__(view, "_wr", [])
                object.__setattr__(view, "_rd", [])
                object.__setattr__(view, "_nsd", 0)
                return view._e.drain().then_inc(sem, k)
        return _Shim()


class _OpRes:

    def __init__(self, view, r):
        object.__setattr__(self, "_v", view)
        object.__setattr__(self, "_r", r)

    def then_inc(self, sem, k=1):
        v = self._v
        if v._nsd > 1:
            object.__setattr__(v, "_dirty", False)
            object.__setattr__(v, "_wr", [])
            object.__setattr__(v, "_rd", [])
            object.__setattr__(v, "_nsd", 0)
            return v._e.drain().then_inc(sem, k)
        return self._r.then_inc(sem, k)

    def __getattr__(self, name):
        return getattr(self._r, name)


def t_sl(chunked, ch, sl=None):
    if sl is None:
        return chunked[:, ch * T:(ch + 1) * T]
    return chunked[:, ch * T + sl.start: ch * T + sl.stop]


def nt_sl(nt, w=512):
    return slice(nt * w, (nt + 1) * w)



def load_w16(c, name, kch, cols, dst16, row_off=0, col_sl=None, rows=128,
             dst_guard=None):
    p = c.p
    done = []
    lastmap = getattr(c, "_w16_last", None) or {0: None, 1: None}
    for k in range(kch):
        st = c.wstage[k % 2]
        slot = k % 2
        sem = c.s_wl[slot]
        prev = p.val(sem)
        guard = done[k - 2] if k >= 2 else lastmap[slot]

        def fl(eng, k=k, st=st, sem=sem, prev=prev, guard=guard):
            if guard is not None:
                eng.wait_ge(guard[0], guard[1])
            if prev:
                eng.wait_ge(sem, prev)
            src = c.dram[name]
            r0 = row_off + k * rows
            src = src[r0:r0 + rows, :] if col_sl is None else src[r0:r0 + rows, col_sl]
            eng.dma_start(out=st[0:rows, :cols], in_=src).then_inc(sem, 16)
        p.emit("sync", fl)
        dv = p.bump(sem, 16)
        e = ("vector", "scalar", "gpsimd")[k % 3]
        csem = c.s_cv3[e]

        def cv(eng, k=k, st=st, dv=dv, e=e, sem=sem, csem=csem, dst_guard=dst_guard):
            eng.wait_ge(sem, dv)
            if dst_guard is not None and dst_guard[1]:
                eng.wait_ge(dst_guard[0], dst_guard[1])
            if e == "scalar":
                eng.activation(out=dst16[0:rows, k * cols:(k + 1) * cols], in_=st[0:rows, :cols],
                               func=AF.Copy).then_inc(csem, 1)
            else:
                eng.tensor_copy(out=dst16[0:rows, k * cols:(k + 1) * cols],
                                in_=st[0:rows, :cols]).then_inc(csem, 1)
        p.emit(e, cv)
        done.append((csem, p.bump(csem, 1)))
        lastmap[slot] = done[-1]
    c._w16_last = lastmap
    return done


def load_small(c, name, dst, wait=None):
    p = c.p
    sem = c.s_ls
    prev = p.val(sem)
    spw = (getattr(c, "spill_sem", None), getattr(c, "spill_done", 0))

    def f(eng, prev=prev, wait=wait, spw=spw):
        if spw[0] is not None and spw[1]:
            eng.wait_ge(spw[0], spw[1])
        if wait:
            for s, v in wait:
                if v:
                    eng.wait_ge(s, v)
        if prev:
            eng.wait_ge(sem, prev)
        eng.dma_start(out=dst, in_=c.dram[name]).then_inc(sem, 16)
    p.emit("sync", f)
    tgt = p.bump(sem, 16)
    p.emit("sync", lambda eng, tgt=tgt: eng.wait_ge(sem, tgt))
    return tgt


def row_to_col_dma(c, row_ap, dst, n):
    p = c.p

    def f(eng):
        eng.dma_start(out=dst[:, 0:n],
                      in_=row_ap.rearrange("a (cc pp) -> a pp cc", pp=128)).then_inc(c.s_dma, 16)
    p.emit("gpsimd", f)
    return p.bump(c.s_dma, 16)


def bcast_row(c, row_ap, width, psdst, wait=None, bp=0):
    p = c.p
    ones_row = c.cons32[bp:bp + 1, OFF_ONES:OFF_ONES + 128]

    def f(eng):
        if wait is not None:
            eng.wait_ge(*wait)
        o = 0
        while o < width:
            w = min(512, width - o)
            eng.matmul(out=psdst[:, o:o + w], lhsT=ones_row, rhs=row_ap[0:1, o:o + w],
                       start=True, stop=True)
            o += w
        eng.nop().then_inc(c.s_te, 1)
    p.emit("tensor", f)
    return p.bump(c.s_te, 1)



def emit_ln(c, with_xn16=True):
    p = c.p
    x = c.x
    ones_col = c.cons32[:, OFF_ONES:OFF_ONES + 1]
    sq0 = c.a32.take(512, "sq0")
    sq1 = c.a32.take(512, "sq1")
    rowA = c.a32.take(T, "rowA")
    rowB = c.a32.take(T, "rowB")
    tmp = c.a32.take(512, "lntmp")
    sqb = [sq0, sq1]

    def mm1(eng):
        for ch in range(NCH):
            for nt in range(4):
                eng.matmul(out=c.ps1[0:1, nt_sl(nt)], lhsT=ones_col,
                           rhs=t_sl(x, ch, nt_sl(nt)), start=(ch == 0), stop=(ch == NCH - 1))
        eng.nop().then_inc(c.s_te, 1)
    p.emit("tensor", mm1)
    tev = p.bump(c.s_te, 1)

    def rvm(eng, tev=tev):
        eng.wait_ge(c.s_te, tev)
        eng.tensor_scalar(out=rowA[0:1, 0:T], in0=c.ps1[0:1, 0:T], scalar1=1.0 / D,
                          scalar2=None, op0=ALU.mult).then_inc(c.s_ve, 1)
    p.emit("vector", rvm)
    vevm = p.bump(c.s_ve, 1)

    te0 = p.val(c.s_te)
    step = 0
    for ch in range(NCH):
        for nt in range(4):
            buf = sqb[step % 2]

            def sq(eng, ch=ch, nt=nt, buf=buf, step=step):
                if step >= 2:
                    eng.wait_ge(c.s_te, te0 + step - 1)
                eng.activation(out=buf, in_=t_sl(x, ch, nt_sl(nt)), func=AF.Square).then_inc(c.s_se, 1)
            p.emit("scalar", sq)
            sev = p.bump(c.s_se, 1)

            def mm(eng, ch=ch, nt=nt, buf=buf, sev=sev):
                eng.wait_ge(c.s_se, sev)
                if ch == 0 and nt == 0:
                    eng.wait_ge(c.s_ve, vevm)
                eng.matmul(out=c.ps1[0:1, nt_sl(nt)], lhsT=ones_col,
                           rhs=buf, start=(ch == 0), stop=(ch == NCH - 1))
                eng.nop().then_inc(c.s_te, 1)
            p.emit("tensor", mm)
            p.bump(c.s_te, 1)
            step += 1
    tev = p.val(c.s_te)

    def rv1(eng, tev=tev):
        eng.wait_ge(c.s_te, tev)
        eng.tensor_scalar(out=rowB[0:1, 0:T], in0=c.ps1[0:1, 0:T], scalar1=1.0 / D, scalar2=None, op0=ALU.mult)
        for nt in range(4):
            eng.tensor_tensor(out=sq0[0:1, 0:512], in0=rowA[0:1, nt_sl(nt)],
                              in1=rowA[0:1, nt_sl(nt)], op=ALU.mult)
            eng.tensor_tensor(out=rowB[0:1, nt_sl(nt)], in0=rowB[0:1, nt_sl(nt)],
                              in1=sq0[0:1, 0:512], op=ALU.subtract)
        eng.engine_nop().then_inc(c.s_ve, 1)
    p.emit("vector", rv1)
    vev = p.bump(c.s_ve, 1)

    def rs1(eng, vev=vev):
        eng.wait_ge(c.s_ve, vev)
        eng.activation(out=rowB[0:1, 0:T], in_=rowB[0:1, 0:T], func=AF.Sqrt,
                       bias=c.cc(1e-5)[0:1, :]).then_inc(c.s_se, 1)
    p.emit("scalar", rs1)
    sev = p.bump(c.s_se, 1)

    def rv2(eng, sev=sev):
        eng.wait_ge(c.s_se, sev)
        eng.reciprocal(out=rowB[0:1, 0:T], in_=rowB[0:1, 0:T])
        eng.tensor_tensor(out=rowA[0:1, 0:T], in0=rowA[0:1, 0:T], in1=rowB[0:1, 0:T], op=ALU.mult)
        eng.tensor_scalar(out=rowA[0:1, 0:T], in0=rowA[0:1, 0:T], scalar1=-1.0, scalar2=None,
                          op0=ALU.mult).then_inc(c.s_ve, 1)
    p.emit("vector", rv2)
    vev = p.bump(c.s_ve, 1)


    if with_xn16:
        ones_row = c.cons32[0:1, OFF_ONES:OFF_ONES + 128]
        xn16 = c.xn16
        wait_v = vev
        for half in range(2):
            hsl = [slice(half * 1024, half * 1024 + 512), slice(half * 1024 + 512, (half + 1) * 1024)]

            def bc(eng, hsl=hsl, wait_v=wait_v):
                eng.wait_ge(c.s_ve, wait_v)
                for i in range(2):
                    eng.matmul(out=c.ps2[:, i * 512:(i + 1) * 512], lhsT=ones_row,
                               rhs=rowB[0:1, hsl[i]], start=True, stop=True)
                    eng.matmul(out=c.ps1[:, i * 512:(i + 1) * 512], lhsT=ones_row,
                               rhs=rowA[0:1, hsl[i]], start=True, stop=True)
                eng.nop().then_inc(c.s_te, 1)
            p.emit("tensor", bc)
            tev2 = p.bump(c.s_te, 1)
            for ch in range(NCH):
                for i in range(2):
                    def ap(eng, ch=ch, i=i, hsl=hsl, tev2=tev2):
                        if ch == 0 and i == 0:
                            eng.wait_ge(c.s_te, tev2)
                        eng.tensor_tensor(out=tmp, in0=t_sl(x, ch, hsl[i]),
                                          in1=c.ps2[:, i * 512:(i + 1) * 512], op=ALU.mult)
                        eng.tensor_tensor(out=t_sl(xn16, ch, hsl[i]), in0=tmp,
                                          in1=c.ps1[:, i * 512:(i + 1) * 512],
                                          op=ALU.add).then_inc(c.s_ve, 1)
                    p.emit("vector", ap)
                    p.bump(c.s_ve, 1)
            wait_v = p.val(c.s_ve)
    for b in (sq0, sq1, rowA, rowB, tmp):
        c.a32.drop(b)


def emit_spill_x(c):
    p = c.p
    x = c.x
    sem = p.sem(f"sp{p.nsem()}")

    def f(eng, sem=sem):
        for ch in range(NCH):
            eng.dma_start(out=c.dram_spill[ch * 128:(ch + 1) * 128, :],
                          in_=t_sl(x, ch)).then_inc(sem, 16)
    p.emit("sync", f)
    v = p.bump(sem, 16 * NCH)
    c.spill_sem = sem
    c.spill_done = v
    for e in ("tensor", "vector", "scalar", "gpsimd"):
        p.emit(e, lambda eng, v=v, sem=sem: eng.wait_ge(sem, v))
    c.a32.drop(c.x)
    c.x = None


def emit_restore_x(c):
    p = c.p
    c.x = c.a32.take(NCH * T, "x")
    x = c.x
    sem = p.sem(f"rs{p.nsem()}")

    def f(eng, sem=sem, ssem=c.spill_sem, sv=c.spill_done):
        eng.wait_ge(ssem, sv)
        for ch in range(NCH):
            eng.dma_start(out=t_sl(x, ch),
                          in_=c.dram_spill[ch * 128:(ch + 1) * 128, :]).then_inc(sem, 16)
    p.emit("sync", f)
    v = p.bump(sem, 16 * NCH)
    for e in ("tensor", "vector", "scalar", "gpsimd"):
        p.emit(e, lambda eng, v=v, sem=sem: eng.wait_ge(sem, v))



def _ckpt(c, name):
    if getattr(c, "upto", None) == name:
        if getattr(c, "s16", None) is not None:
            p = c.p
            s16 = c.s16

            def dmp(eng):
                eng.dma_start(out=c.dbg, in_=s16).then_inc(c.s_st, 16)
            p.emit("gpsimd", dmp)
            p.bump(c.s_st, 16)
        if c.x is None:
            emit_restore_x(c)
        c.p.barrier()
        raise _StopEmit()


class _StopEmit(Exception):
    pass


def emit_layer0(c):
    p = c.p
    a16, a32 = c.a16, c.a32
    emit_ln(c, with_xn16=True)
    p.barrier()
    _ckpt(c, "ln")
    emit_spill_x(c)
    _ckpt(c, "spill")
    xn16 = c.xn16

    wk16 = a16.take(8 * D, "wk16")
    load_w16(c, "L0_Wk", 8, D, wk16)
    bkr = a32.take(D, "bkr")
    bkr_s = load_small(c, "L0_bk_row", bkr[0:1, 0:D])
    p.barrier()
    bkbc = a32.take(D, "bkbc")
    tev = bcast_row(c, bkr[0:1, 0:D], D, c.ps2, wait=(c.s_ls, bkr_s))

    def bkev(eng, tev=tev):
        eng.wait_ge(c.s_te, tev)
        eng.tensor_copy(out=bkbc, in_=c.ps2[:, 0:D]).then_inc(c.s_ve, 1)
    p.emit("vector", bkev)
    p.bump(c.s_ve, 1)
    p.barrier()

    ktm = a16.take(16 * D, "ktm")
    kt32 = a32.take(D, "kt32")
    ktmin = a32.take(D, "ktmin")
    ve0 = p.val(c.s_ve)
    for tc in range(16):
        def mm(eng, tc=tc, ve0=ve0):
            if tc >= 1:
                eng.wait_ge(c.s_ve, ve0 + 2 * tc - 1)
            for k in range(NCH):
                lhs = t_sl(xn16, k, slice(tc * 128, (tc + 1) * 128))
                for h in range(2):
                    eng.matmul(out=c.ps1[:, h * 512:(h + 1) * 512], lhsT=lhs,
                               rhs=wk16[:, k * D + h * 512: k * D + (h + 1) * 512],
                               start=(k == 0), stop=(k == NCH - 1))
            eng.nop().then_inc(c.s_te, 1)
        p.emit("tensor", mm)
        tmv = p.bump(c.s_te, 1)

        def ev1(eng, tmv=tmv):
            eng.wait_ge(c.s_te, tmv)
            eng.tensor_tensor(out=kt32, in0=c.ps1[:, 0:D], in1=bkbc, op=ALU.add)
            eng.tensor_scalar(out=ktmin, in0=kt32, scalar1=0.0, scalar2=None,
                              op0=ALU.min).then_inc(c.s_ve, 1)
        p.emit("vector", ev1)
        vv = p.bump(c.s_ve, 1)

        def ev2(eng, vv=vv):
            eng.wait_ge(c.s_ve, vv)
            eng.activation(out=ktmin, in_=ktmin, func=AF.Exp).then_inc(c.s_se, 1)
        p.emit("scalar", ev2)
        sv = p.bump(c.s_se, 1)

        def ev3(eng, tc=tc, sv=sv):
            eng.wait_ge(c.s_se, sv)
            eng.scalar_tensor_tensor(out=ktm[:, tc * D:(tc + 1) * D], in0=kt32, scalar=0.0,
                                     in1=ktmin, op0=ALU.max, op1=ALU.add).then_inc(c.s_ve, 1)
        p.emit("vector", ev3)
        p.bump(c.s_ve, 1)
    ones16_col = c.cons16[:, OFF_ONES16:OFF_ONES16 + 1]
    vev = p.val(c.s_ve)

    def ksum(eng, vev=vev):
        eng.wait_ge(c.s_ve, vev)
        for tc in range(16):
            for h in range(2):
                eng.matmul(out=c.ps2[0:1, h * 512:(h + 1) * 512], lhsT=ones16_col,
                           rhs=ktm[:, tc * D + h * 512: tc * D + (h + 1) * 512],
                           start=(tc == 0), stop=(tc == 15))
        eng.nop().then_inc(c.s_te, 1)
    p.emit("tensor", ksum)
    tev = p.bump(c.s_te, 1)
    ksrow = a32.take(D, "ksrow")

    def ksev(eng, tev=tev):
        eng.wait_ge(c.s_te, tev)
        eng.tensor_scalar(out=ksrow[0:1, 0:D], in0=c.ps2[0:1, 0:D], scalar1=1e-6, scalar2=None,
                          op0=ALU.add).then_inc(c.s_ve, 1)
    p.emit("vector", ksev)
    vev = p.bump(c.s_ve, 1)
    onesc = c.cons32[0:1, OFF_ONES:OFF_ONES + 1]

    def ksd(eng, vev=vev):
        eng.wait_ge(c.s_ve, vev)
        for k in range(NCH):
            eng.matmul(out=c.ps1[:, k:k + 1], lhsT=ksrow[0:1, k * 128:(k + 1) * 128],
                       rhs=onesc, start=True, stop=True)
        eng.nop().then_inc(c.s_te, 1)
    p.emit("tensor", ksd)
    kdv = p.bump(c.s_te, 1)

    def kscv(eng, kdv=kdv):
        eng.wait_ge(c.s_te, kdv)
        eng.tensor_copy(out=c.small16[:, 0:8], in_=c.ps1[:, 0:8]).then_inc(c.s_ve, 1)
    p.emit("vector", kscv)
    p.bump(c.s_ve, 1)
    p.barrier()
    for b in (wk16,):
        a16.drop(b)
    for b in (kt32, ktmin, bkr, bkbc, ksrow):
        a32.drop(b)
    _ckpt(c, "A")

    bvr = a32.take(D, "bvr")
    bvr_s = load_small(c, "L0_bv_row", bvr[0:1, 0:D])
    p.barrier()
    bvbc = a32.take(D, "bvbc")
    tev = bcast_row(c, bvr[0:1, 0:D], D, c.ps2, wait=(c.s_ls, bvr_s))

    def bvev(eng, tev=tev):
        eng.wait_ge(c.s_te, tev)
        eng.tensor_copy(out=bvbc, in_=c.ps2[:, 0:D]).then_inc(c.s_ve, 1)
    p.emit("vector", bvev)
    p.bump(c.s_ve, 1)

    wvq16 = a16.take(8 * 256, "wvq16")
    vp16 = a16.take(256, "vp16")
    kvstage = a16.take(4 * 256, "kvstage")
    for dvq in range(4):
        load_w16(c, "L0_Wv", 8, 256, wvq16, col_sl=slice(dvq * 256, (dvq + 1) * 256))
        p.barrier()
        for dkh in range(2):
            ve_v = p.val(c.s_ve)
            for tc in range(16):
                def vmm(eng, tc=tc, ve_v=ve_v):
                    if tc >= 1:
                        eng.wait_ge(c.s_ve, ve_v + tc)
                    for k in range(NCH):
                        lhs = t_sl(xn16, k, slice(tc * 128, (tc + 1) * 128))
                        eng.matmul(out=c.ps2[:, 0:256], lhsT=lhs,
                                   rhs=wvq16[:, k * 256:(k + 1) * 256],
                                   start=(k == 0), stop=(k == NCH - 1))
                    eng.nop().then_inc(c.s_te, 1)
                p.emit("tensor", vmm)
                tvv = p.bump(c.s_te, 1)

                def vev(eng, tc=tc, tvv=tvv, dvq=dvq):
                    eng.wait_ge(c.s_te, tvv)
                    eng.tensor_tensor(out=vp16, in0=c.ps2[:, 0:256],
                                      in1=bvbc[:, dvq * 256:(dvq + 1) * 256],
                                      op=ALU.add).then_inc(c.s_ve, 1)
                p.emit("vector", vev)
                vvv = p.bump(c.s_ve, 1)

                def kvmm(eng, tc=tc, vvv=vvv, dkh=dkh):
                    eng.wait_ge(c.s_ve, vvv)
                    for dq in range(4):
                        dk = dkh * 4 + dq
                        eng.matmul(out=c.ps1[:, dq * 512: dq * 512 + 256],
                                   lhsT=ktm[:, tc * D + dk * 128: tc * D + (dk + 1) * 128],
                                   rhs=vp16, start=(tc == 0), stop=(tc == 15))
                    eng.nop().then_inc(c.s_te, 1)
                p.emit("tensor", kvmm)
                p.bump(c.s_te, 1)
            tev = p.val(c.s_te)

            def kvev(eng, tev=tev, dkh=dkh, gdp=p.val(c.s_gd)):
                eng.wait_ge(c.s_te, tev)
                if gdp:
                    eng.wait_ge(c.s_gd, gdp)
                for dq in range(4):
                    eng.activation(out=kvstage[:, dq * 256:(dq + 1) * 256],
                                   in_=c.ps1[:, dq * 512: dq * 512 + 256],
                                   func=AF.Copy)
                eng.nop().then_inc(c.s_se, 1)
            p.emit("scalar", kvev)
            sev = p.bump(c.s_se, 1)

            def kvout(eng, dvq=dvq, dkh=dkh, sev=sev):
                eng.wait_ge(c.s_se, sev)
                dst = c.dram_kv.rearrange("pp (dk cc) -> pp dk cc", cc=1024)[:, dkh * 4:(dkh + 1) * 4,
                                                                             dvq * 256:(dvq + 1) * 256]
                eng.dma_start(out=dst,
                              in_=kvstage.rearrange("pp (dk cc) -> pp dk cc", cc=256)).then_inc(c.s_gd, 16)
            p.emit("gpsimd", kvout)
            p.bump(c.s_gd, 16)
            p.barrier()
    gdq = p.val(c.s_gd)
    for e in ("tensor", "vector", "scalar", "gpsimd"):
        p.emit(e, lambda eng, v=gdq: eng.wait_ge(c.s_gd, v))
    for b in (wvq16, vp16, kvstage):
        a16.drop(b)
    for b in (bvr, bvbc):
        a32.drop(b)
    a16.drop(ktm)
    _ckpt(c, "B")
    bq_s = load_small(c, "L0_bq", c.bias_sb[:, 0:8])
    qp16 = a16.take(8 * T, "qp16")
    wqq16 = a16.take(8 * 128, "wqq16")
    q32 = a32.take(512, "q32")
    qmin = a32.take(512, "qmin")
    for och in range(NCH):
        load_w16(c, "L0_Wq", 8, 128, wqq16, col_sl=slice(och * 128, (och + 1) * 128))
        p.barrier()
        se0 = p.val(c.s_se)
        ve0 = p.val(c.s_ve)

        def qmm(eng, och=och):
            for k in range(NCH):
                for nt in range(4):
                    eng.matmul(out=c.ps1[:, nt_sl(nt)],
                               lhsT=wqq16[:, k * 128:(k + 1) * 128],
                               rhs=t_sl(xn16, k, nt_sl(nt)),
                               start=(k == 0), stop=(k == NCH - 1))
            eng.nop().then_inc(c.s_te, 1)
        p.emit("tensor", qmm)
        tqv = p.bump(c.s_te, 1)

        for nt in range(4):
            def qe1(eng, och=och, nt=nt, tqv=tqv, ve0=ve0, bq_s=bq_s):
                if nt == 0:
                    eng.wait_ge(c.s_te, tqv)
                    eng.wait_ge(c.s_ls, bq_s)
                if nt >= 1:
                    eng.wait_ge(c.s_ve, ve0 + 2 * nt)
                eng.activation(out=q32, in_=c.ps1[:, nt_sl(nt)], func=AF.Identity,
                               bias=c.bias_sb[:, och:och + 1]).then_inc(c.s_se, 1)
            p.emit("scalar", qe1)
            sqv = p.bump(c.s_se, 1)

            def qe2(eng, sqv=sqv):
                eng.wait_ge(c.s_se, sqv)
                eng.tensor_scalar(out=qmin, in0=q32, scalar1=0.0, scalar2=None,
                                  op0=ALU.min).then_inc(c.s_ve, 1)
            p.emit("vector", qe2)
            vqv = p.bump(c.s_ve, 1)

            def qe3(eng, vqv=vqv):
                eng.wait_ge(c.s_ve, vqv)
                eng.activation(out=qmin, in_=qmin, func=AF.Exp).then_inc(c.s_se, 1)
            p.emit("scalar", qe3)
            sqv = p.bump(c.s_se, 1)

            def qe4(eng, och=och, nt=nt, sqv=sqv):
                eng.wait_ge(c.s_se, sqv)
                eng.scalar_tensor_tensor(out=qp16[:, och * T + nt * 512: och * T + (nt + 1) * 512],
                                         in0=q32, scalar=0.0, in1=qmin,
                                         op0=ALU.max, op1=ALU.add).then_inc(c.s_ve, 1)
            p.emit("vector", qe4)
            p.bump(c.s_ve, 1)
        p.barrier()
    a32.drop(q32)
    a32.drop(qmin)
    a16.drop(wqq16)
    c.a16.drop(c.xn16)
    c.xn16 = None
    _ckpt(c, "C1a")

    ctx16 = a16.take(8 * T, "ctx16")
    kv16 = a16.take(8 * D, "kv16")

    kvi_s = p.sem(f"kvi{p.nsem()}")

    def kvin(eng, gv=p.val(c.s_gd), kvi_s=kvi_s):
        eng.wait_ge(c.s_gd, gv)
        eng.dma_start(out=kv16, in_=c.dram_kv).then_inc(kvi_s, 16)
    p.emit("sync", kvin)
    kv_v = p.bump(kvi_s, 16)

    def zmm(eng):
        for k in range(NCH):
            for nt in range(4):
                eng.matmul(out=c.ps1[0:1, nt_sl(nt)], lhsT=c.small16[:, k:k + 1],
                           rhs=qp16[:, k * T + nt * 512: k * T + (nt + 1) * 512],
                           start=(k == 0), stop=(k == NCH - 1))
        eng.nop().then_inc(c.s_te, 1)
    p.emit("tensor", zmm)
    tev = p.bump(c.s_te, 1)
    zrow = a32.take(T, "zrow")

    def zrec(eng, tev=tev):
        eng.wait_ge(c.s_te, tev)
        eng.reciprocal(out=zrow[0:1, 0:T], in_=c.ps1[0:1, 0:T]).then_inc(c.s_ve, 1)
    p.emit("vector", zrec)
    vev = p.bump(c.s_ve, 1)
    tev = bcast_row(c, zrow[0:1, 0:T], T, c.ps1, wait=(c.s_ve, vev))
    for ch in range(NCH):
        def qsc(eng, ch=ch, tev=tev):
            if ch == 0:
                eng.wait_ge(c.s_te, tev)
            eng.tensor_tensor(out=t_sl(qp16, ch), in0=t_sl(qp16, ch), in1=c.ps1[:, 0:T],
                              op=ALU.mult)
        p.emit("vector", qsc)
    p.barrier()
    a32.drop(zrow)

    se0 = p.val(c.s_se)
    first = True
    for nt in range(4):
        for ech in range(NCH):
            bank = ech % 4
            idx = nt * 8 + ech

            def cmm(eng, nt=nt, ech=ech, bank=bank, idx=idx, first=first, se0=se0, kv_v=kv_v, kvi_s=kvi_s):
                if first:
                    eng.wait_ge(kvi_s, kv_v)
                if idx >= 4:
                    eng.wait_ge(c.s_se, se0 + idx - 3)
                for dk in range(NCH):
                    eng.matmul(out=c.ps1[:, bank * 512:(bank + 1) * 512],
                               lhsT=kv16[:, dk * D + ech * 128: dk * D + (ech + 1) * 128],
                               rhs=qp16[:, dk * T + nt * 512: dk * T + (nt + 1) * 512],
                               start=(dk == 0), stop=(dk == NCH - 1))
                eng.nop().then_inc(c.s_te, 1)
            p.emit("tensor", cmm)
            first = False
            tcv = p.bump(c.s_te, 1)

            def cev(eng, nt=nt, ech=ech, bank=bank, tcv=tcv):
                eng.wait_ge(c.s_te, tcv)
                eng.activation(out=ctx16[:, ech * T + nt * 512: ech * T + (nt + 1) * 512],
                               in_=c.ps1[:, bank * 512:(bank + 1) * 512],
                               func=AF.Copy).then_inc(c.s_se, 1)
            p.emit("scalar", cev)
            p.bump(c.s_se, 1)
    p.barrier()
    a16.drop(qp16)
    a16.drop(kv16)
    _ckpt(c, "C1b")

    c.s16 = a16.take(8 * T, "s16")
    s16 = c.s16
    wo16 = a16.take(8 * D, "wo16")
    load_w16(c, "L0_Wo", 8, D, wo16)
    bo_s = load_small(c, "L0_bo", c.bias_sb[:, 8:16])
    p.barrier()
    se0 = p.val(c.s_se)
    for och in range(NCH):
        for nt in range(4):
            bank = (och * 4 + nt) % 4
            idx = och * 4 + nt

            def smm(eng, och=och, nt=nt, bank=bank, idx=idx, se0=se0):
                if idx >= 4:
                    eng.wait_ge(c.s_se, se0 + idx - 3)
                for k in range(NCH):
                    eng.matmul(out=c.ps1[:, bank * 512:(bank + 1) * 512],
                               lhsT=wo16[:, k * D + och * 128: k * D + (och + 1) * 128],
                               rhs=ctx16[:, k * T + nt * 512: k * T + (nt + 1) * 512],
                               start=(k == 0), stop=(k == NCH - 1))
                eng.nop().then_inc(c.s_te, 1)
            p.emit("tensor", smm)
            tsv = p.bump(c.s_te, 1)

            def sev_(eng, och=och, nt=nt, bank=bank, tsv=tsv, bo_s=bo_s):
                eng.wait_ge(c.s_te, tsv)
                if och == 0 and nt == 0:
                    eng.wait_ge(c.s_ls, bo_s)
                eng.activation(out=t_sl(s16, och, nt_sl(nt)),
                               in_=c.ps1[:, bank * 512:(bank + 1) * 512],
                               func=AF.Identity, bias=c.bias_sb[:, 8 + och:9 + och]).then_inc(c.s_se, 1)
            p.emit("scalar", sev_)
            p.bump(c.s_se, 1)
    p.barrier()
    a16.drop(ctx16)
    a16.drop(wo16)
    _ckpt(c, "C2")

    emit_restore_x(c)
    ve0 = p.val(c.s_ve)
    for k in range(NCH):
        st = c.wstage[k % 2]
        need = ve0 + (k - 1) if k >= 2 else 0
        sem = c.s_wl[k % 2]
        prev = p.val(sem)

        def fl(eng, k=k, st=st, need=need, sem=sem, prev=prev):
            if need:
                eng.wait_ge(c.s_ve, need)
            if prev:
                eng.wait_ge(sem, prev)
            eng.dma_start(out=st[:, 0:D], in_=c.dram["L0_fw"][k * 128:(k + 1) * 128, :]).then_inc(sem, 16)
        p.emit("sync", fl)
        dvv = p.bump(sem, 16)

        def red(eng, k=k, st=st, dvv=dvv, sem=sem):
            eng.wait_ge(sem, dvv)
            eng.tensor_reduce(out=c.stat32[:, 1:2], in_=st[:, 0:D], axis=AX.X, op=ALU.add,
                              apply_absolute_value=True)
            if k == 0:
                eng.tensor_copy(out=c.stat32[:, 0:1], in_=c.stat32[:, 1:2]).then_inc(c.s_ve, 1)
            else:
                eng.tensor_tensor(out=c.stat32[:, 0:1], in0=c.stat32[:, 0:1],
                                  in1=c.stat32[:, 1:2], op=ALU.add).then_inc(c.s_ve, 1)
        p.emit("vector", red)
        p.bump(c.s_ve, 1)
    vev = p.val(c.s_ve)
    ones_col = c.cons32[:, OFF_ONES:OFF_ONES + 1]
    ones_row = c.cons32[0:1, OFF_ONES:OFF_ONES + 128]

    def wsmm(eng, vev=vev):
        eng.wait_ge(c.s_ve, vev)
        eng.matmul(out=c.ps2[0:1, 0:1], lhsT=ones_col, rhs=c.stat32[:, 0:1], start=True, stop=True)
        eng.nop().then_inc(c.s_te, 1)
    p.emit("tensor", wsmm)
    tev = p.bump(c.s_te, 1)

    def wsev(eng, tev=tev):
        eng.wait_ge(c.s_te, tev)
        eng.tensor_copy(out=c.stat32[0:1, 2:3], in_=c.ps2[0:1, 0:1]).then_inc(c.s_ve, 1)
    p.emit("vector", wsev)
    vev = p.bump(c.s_ve, 1)
    tev = bcast_row(c, c.stat32[0:1, 2:3], 1, c.ps2[:, 0:512], wait=(c.s_ve, vev))

    def wsmath(eng, tev=tev):
        eng.wait_ge(c.s_te, tev)
        eng.tensor_scalar(out=c.stat32[:, 3:4], in0=c.ps2[:, 0:1],
                          scalar1=1.0 / (D * D), scalar2=1e-8, op0=ALU.mult, op1=ALU.add)
        eng.reciprocal(out=c.stat32[:, 4:5], in_=c.stat32[:, 3:4]).then_inc(c.s_ve, 1)
    p.emit("vector", wsmath)
    p.bump(c.s_ve, 1)
    p.barrier()

    fwq16 = a16.take(8 * D, "fwq16")
    wtmp = a32.take(D, "wtmp")
    ve0 = p.val(c.s_ve)
    for k in range(NCH):
        st = c.wstage[k % 2]
        need = ve0 + 2 * (k - 1) if k >= 2 else 0
        sem = c.s_wl[k % 2]
        prev = p.val(sem)

        def fl(eng, k=k, st=st, need=need, sem=sem, prev=prev):
            if need:
                eng.wait_ge(c.s_ve, need)
            if prev:
                eng.wait_ge(sem, prev)
            eng.dma_start(out=st[:, 0:D], in_=c.dram["L0_fw"][k * 128:(k + 1) * 128, :]).then_inc(sem, 16)
        p.emit("sync", fl)
        dvv = p.bump(sem, 16)

        def qz(eng, k=k, st=st, dvv=dvv, sem=sem):
            eng.wait_ge(sem, dvv)
            eng.tensor_scalar(out=wtmp, in0=st[:, 0:D], scalar1=c.stat32[:, 4:5],
                              scalar2=12582912.0, op0=ALU.mult, op1=ALU.add)
            eng.tensor_scalar(out=wtmp, in0=wtmp, scalar1=-12582912.0, scalar2=-1.0,
                              op0=ALU.add, op1=ALU.max)
            eng.tensor_scalar(out=fwq16[:, k * D:(k + 1) * D], in0=wtmp,
                              scalar1=1.0, scalar2=None,
                              op0=ALU.min).then_inc(c.s_ve, 2)
        p.emit("vector", qz)
        p.bump(c.s_ve, 2)
    p.barrier()

    _ckpt(c, "wq")
    emit_ffn0_combine(c, fwq16)
    p.barrier()
    a16.drop(fwq16)
    a32.drop(wtmp)
    a16.drop(c.s16)
    c.s16 = None


def emit_ffn0_combine(c, fwq16):
    p = c.p
    a16, a32 = c.a16, c.a32
    x = c.x
    s16 = c.s16
    id32 = c.cons32[:, OFF_ID:OFF_ID + 128]
    id16 = c.cons16[:, OFF_ID16:OFF_ID16 + 128]
    xtm = a32.take(D, "xtm")
    xtm0 = a32.take(D, "xtm0")
    sqd = a32.take(D, "sqd")
    sg32 = a32.take(512, "sg32")
    aqs = a16.take(D, "aqs")
    aqT = a16.take(8 * 512, "aqT")
    g16a = a16.take(512, "g16a")
    g16b = a16.take(512, "g16b")
    for grp in range(4):
        se0 = p.val(c.s_se)
        ve0 = p.val(c.s_ve)
        for j in range(4):
            tc = grp * 4 + j

            def tx(eng, tc=tc, j=j, ve0=ve0):
                if j > 0:
                    eng.wait_ge(c.s_ve, ve0 + 4 * (j - 1) + 1)
                for ch in range(NCH):
                    eng.transpose(out=c.ps2[:, ch * 128:(ch + 1) * 128],
                                  in_=t_sl(x, ch, slice(tc * 128, (tc + 1) * 128)),
                                  identity=id32)
                eng.nop().then_inc(c.s_te, 1)
            p.emit("tensor", tx)
            tv = p.bump(c.s_te, 1)

            def ev0(eng, tv=tv, j=j, se0=se0):
                eng.wait_ge(c.s_te, tv)
                if j > 0:
                    eng.wait_ge(c.s_se, se0 + 4 * (j - 1) + 3)
                eng.tensor_scalar(out=xtm0, in0=c.ps2[:, 0:D], scalar1=1.0, scalar2=0.0,
                                  op0=ALU.mult, op1=ALU.add,
                                  accum_out=c.stat32[:, 11:12]).then_inc(c.s_ve, 1)
            p.emit("vector", ev0)
            vv0 = p.bump(c.s_ve, 1)

            def sq0(eng, vv0=vv0):
                eng.wait_ge(c.s_ve, vv0)
                eng.activation(out=sqd, in_=xtm0, func=AF.Square,
                               accum_out=c.stat32[:, 12:13]).then_inc(c.s_se, 1)
            p.emit("scalar", sq0)
            sv0 = p.bump(c.s_se, 1)

            def lnst(eng, sv0=sv0):
                eng.wait_ge(c.s_se, sv0)
                eng.tensor_scalar(out=c.stat32[:, 13:14], in0=c.stat32[:, 11:12],
                                  scalar1=1.0 / D, scalar2=None, op0=ALU.mult)
                eng.tensor_scalar(out=c.stat32[:, 14:15], in0=c.stat32[:, 12:13],
                                  scalar1=1.0 / D, scalar2=None, op0=ALU.mult)
                eng.tensor_tensor(out=c.stat32[:, 15:16], in0=c.stat32[:, 13:14],
                                  in1=c.stat32[:, 13:14], op=ALU.mult)
                eng.tensor_tensor(out=c.stat32[:, 14:15], in0=c.stat32[:, 14:15],
                                  in1=c.stat32[:, 15:16], op=ALU.subtract).then_inc(c.s_ve, 1)
            p.emit("vector", lnst)
            vv1 = p.bump(c.s_ve, 1)

            def lnsq(eng, vv1=vv1):
                eng.wait_ge(c.s_ve, vv1)
                eng.activation(out=c.stat32[:, 14:15], in_=c.stat32[:, 14:15], func=AF.Sqrt,
                               bias=c.cc(1e-5)).then_inc(c.s_se, 1)
            p.emit("scalar", lnsq)
            sv1 = p.bump(c.s_se, 1)

            def lnal(eng, sv1=sv1):
                eng.wait_ge(c.s_se, sv1)
                eng.reciprocal(out=c.stat32[:, 15:16], in_=c.stat32[:, 14:15])
                eng.tensor_tensor(out=c.stat32[:, 13:14], in0=c.stat32[:, 13:14],
                                  in1=c.stat32[:, 15:16], op=ALU.mult)
                eng.tensor_scalar(out=c.stat32[:, 13:14], in0=c.stat32[:, 13:14],
                                  scalar1=-1.0, scalar2=None, op0=ALU.mult).then_inc(c.s_ve, 1)
            p.emit("vector", lnal)
            vv2 = p.bump(c.s_ve, 1)

            def lnap(eng, vv2=vv2):
                eng.wait_ge(c.s_ve, vv2)
                eng.activation(out=xtm, in_=xtm0, func=AF.Identity,
                               bias=c.stat32[:, 13:14], scale=c.stat32[:, 15:16]).then_inc(c.s_se, 1)
            p.emit("scalar", lnap)
            sv = p.bump(c.s_se, 1)

            def qnt(eng, sv=sv):
                eng.wait_ge(c.s_se, sv)
                eng.tensor_reduce(out=c.stat32[:, 8:9], in_=xtm, axis=AX.X, op=ALU.max,
                                  apply_absolute_value=True)
                eng.tensor_scalar(out=c.stat32[:, 8:9], in0=c.stat32[:, 8:9], scalar1=1e-8,
                                  scalar2=None, op0=ALU.add)
                eng.reciprocal(out=c.stat32[:, 9:10], in_=c.stat32[:, 8:9])
                eng.tensor_scalar(out=sqd, in0=xtm, scalar1=c.stat32[:, 9:10],
                                  scalar2=127.0, op0=ALU.mult, op1=ALU.mult)
                eng.tensor_scalar(out=sqd, in0=sqd, scalar1=12582912.0, scalar2=-12582912.0,
                                  op0=ALU.add, op1=ALU.add)
                eng.tensor_scalar(out=c.stat32[:, 10:11], in0=c.stat32[:, 8:9],
                                  scalar1=c.stat32[:, 3:4], scalar2=1.0 / 127.0,
                                  op0=ALU.mult, op1=ALU.mult)
                eng.tensor_scalar(out=aqs, in0=sqd, scalar1=c.stat32[:, 10:11],
                                  scalar2=None, op0=ALU.mult).then_inc(c.s_ve, 1)
            p.emit("vector", qnt)
            vv = p.bump(c.s_ve, 1)

            def tb(eng, j=j, vv=vv, se0=se0):
                eng.wait_ge(c.s_ve, vv)
                if j > 0:
                    eng.wait_ge(c.s_se, se0 + 4 * j)
                for ch in range(NCH):
                    eng.transpose(out=c.psh[:, ch * 128:(ch + 1) * 128],
                                  in_=aqs[:, ch * 128:(ch + 1) * 128], identity=id16)
                eng.nop().then_inc(c.s_te, 1)
            p.emit("tensor", tb)
            tv = p.bump(c.s_te, 1)

            def tbe(eng, j=j, tv=tv):
                eng.wait_ge(c.s_te, tv)
                for ch in range(NCH):
                    eng.activation(out=aqT[:, ch * 512 + j * 128: ch * 512 + j * 128 + 128],
                                   in_=c.psh[:, ch * 128:(ch + 1) * 128], func=AF.Copy)
                eng.nop().then_inc(c.s_se, 1)
            p.emit("scalar", tbe)
            p.bump(c.s_se, 1)
        sev = p.val(c.s_se)
        ve1 = p.val(c.s_ve)
        for och in range(NCH):
            g16 = g16a if och % 2 == 0 else g16b

            def ymm(eng, och=och, sev=sev, ve1=ve1):
                eng.wait_ge(c.s_se, sev)
                if och >= 2:
                    eng.wait_ge(c.s_ve, ve1 + 2 * (och - 1))
                co = (och % 2) * 512
                for ich in range(NCH):
                    eng.matmul(out=c.ps1[:, co: co + 512],
                               lhsT=fwq16[:, ich * D + och * 128: ich * D + (och + 1) * 128],
                               rhs=aqT[:, ich * 512:(ich + 1) * 512],
                               start=(ich == 0), stop=(ich == NCH - 1))
                eng.nop().then_inc(c.s_te, 1)
            p.emit("tensor", ymm)
            tyv = p.bump(c.s_te, 1)

            def gsil(eng, och=och, tyv=tyv, g16=g16):
                eng.wait_ge(c.s_te, tyv)
                co = (och % 2) * 512
                eng.activation(out=g16, in_=c.ps1[:, co: co + 512],
                               func=AF.Silu).then_inc(c.s_se, 1)
            p.emit("scalar", gsil)
            syv = p.bump(c.s_se, 1)

            def comb(eng, grp=grp, och=och, syv=syv, g16=g16):
                eng.wait_ge(c.s_se, syv)
                sl = slice(grp * 512, (grp + 1) * 512)
                eng.tensor_tensor(out=sg32, in0=t_sl(s16, och, sl), in1=g16, op=ALU.mult)
                eng.tensor_tensor(out=t_sl(x, och, sl), in0=t_sl(x, och, sl), in1=sg32,
                                  op=ALU.add).then_inc(c.s_ve, 2)
            p.emit("vector", comb)
            p.bump(c.s_ve, 2)
        p.barrier()
    for b in (xtm, xtm0, sqd, sg32):
        a32.drop(b)
    for b in (aqs, aqT, g16a, g16b):
        a16.drop(b)


def nvfp4_ops(c, eng, src32, F, dst16, gs_col, rgs_col, bigA, bigB, sm):
    I32 = mybir.dt.int32
    blk = F // 16
    sv = lambda i: sm[:, i * blk:(i + 1) * blk]
    a3 = lambda t: t.rearrange("p (b k) -> p b k", k=16)
    bview = lambda t: t.unsqueeze(2).broadcast_to([128, blk, 16])
    eng.tensor_reduce(out=sv(0), in_=a3(src32), axis=AX.X, op=ALU.max,
                      apply_absolute_value=True)
    eng.tensor_scalar(out=bigA.bitcast(I32), in0=src32.bitcast(I32), scalar1=0x7FFFFFFF,
                      scalar2=None, op0=ALU.bitwise_and)
    eng.tensor_scalar(out=sv(1), in0=sv(0), scalar1=1.0 / 6.0, scalar2=gs_col,
                      op0=ALU.mult, op1=ALU.mult)
    eng.tensor_scalar(out=sv(1), in0=sv(1), scalar1=448.0, scalar2=None, op0=ALU.min)
    ui = sv(1).bitcast(I32)
    eng.tensor_scalar(out=sv(2).bitcast(I32), in0=ui, scalar1=23, scalar2=255,
                      op0=ALU.logical_shift_right, op1=ALU.bitwise_and)
    eng.tensor_scalar(out=sv(3).bitcast(I32), in0=ui, scalar1=20, scalar2=7,
                      op0=ALU.logical_shift_right, op1=ALU.bitwise_and)
    eng.tensor_scalar(out=sv(4).bitcast(I32), in0=ui, scalar1=0xFFFFF, scalar2=None,
                      op0=ALU.bitwise_and)
    eng.tensor_scalar(out=sv(5), in0=sv(4).bitcast(I32), scalar1=524288.0, scalar2=None,
                      op0=ALU.is_gt)
    eng.tensor_scalar(out=sv(6), in0=sv(4).bitcast(I32), scalar1=524288.0, scalar2=None,
                      op0=ALU.is_equal)
    eng.tensor_scalar(out=sv(4).bitcast(I32), in0=sv(3).bitcast(I32), scalar1=1,
                      scalar2=None, op0=ALU.bitwise_and)
    eng.tensor_tensor(out=sv(6), in0=sv(6), in1=sv(4).bitcast(I32), op=ALU.mult)
    eng.tensor_tensor(out=sv(5), in0=sv(5), in1=sv(6), op=ALU.add)
    eng.scalar_tensor_tensor(out=sv(5), in0=sv(3).bitcast(I32), scalar=8.0, in1=sv(5),
                             op0=ALU.add, op1=ALU.add)
    eng.tensor_scalar(out=sv(3).bitcast(I32), in0=sv(2).bitcast(I32), scalar1=-3,
                      scalar2=0, op0=ALU.add, op1=ALU.max)
    eng.tensor_scalar(out=sv(3).bitcast(I32), in0=sv(3).bitcast(I32), scalar1=23,
                      scalar2=None, op0=ALU.logical_shift_left)
    eng.tensor_tensor(out=sv(2), in0=sv(5), in1=sv(3), op=ALU.mult)
    eng.tensor_scalar(out=sv(6), in0=sv(1), scalar1=512.0, scalar2=8388608.0,
                      op0=ALU.mult, op1=ALU.add)
    eng.tensor_scalar(out=sv(6), in0=sv(6), scalar1=-8388608.0, scalar2=1.0 / 512.0,
                      op0=ALU.add, op1=ALU.mult)
    eng.tensor_scalar(out=sv(4).bitcast(I32), in0=sv(1), scalar1=0.015625, scalar2=None,
                      op0=ALU.is_lt)
    eng.select(out=sv(0), mask=sv(4).bitcast(I32), on_true=sv(6), on_false=sv(2),
               add_drain=True)
    eng.tensor_scalar(out=sv(1), in0=sv(0), scalar1=rgs_col, scalar2=None, op0=ALU.mult)
    eng.tensor_scalar(out=sv(2), in0=sv(1), scalar1=0.0, scalar2=None, op0=ALU.is_equal)
    eng.tensor_tensor(out=sv(2), in0=sv(1), in1=sv(2), op=ALU.add)
    eng.reciprocal(out=sv(2), in_=sv(2))
    eng.tensor_tensor(out=a3(bigA), in0=a3(bigA), in1=bview(sv(2)), op=ALU.mult)
    MAGIC = 8388608.0
    eng.tensor_scalar(out=dst16, in0=bigA, scalar1=5.0, scalar2=None, op0=ALU.is_ge)
    eng.tensor_scalar(out=bigB, in0=bigA, scalar1=2.0, scalar2=MAGIC, op0=ALU.mult, op1=ALU.add)
    eng.tensor_scalar(out=bigB, in0=bigB, scalar1=-MAGIC, scalar2=None, op0=ALU.add)
    eng.tensor_scalar(out=bigB, in0=bigB, scalar1=0.5, scalar2=2.0,
                      op0=ALU.mult, op1=ALU.min)
    eng.tensor_scalar(out=bigA, in0=bigA, scalar1=MAGIC, scalar2=None, op0=ALU.add)
    eng.tensor_scalar(out=bigA, in0=bigA, scalar1=-MAGIC - 2.0, scalar2=0.0,
                      op0=ALU.add, op1=ALU.max)
    eng.scalar_tensor_tensor(out=bigB, in0=bigA, scalar=2.0, in1=bigB,
                             op0=ALU.min, op1=ALU.add)
    eng.scalar_tensor_tensor(out=bigB, in0=dst16, scalar=2.0, in1=bigB,
                             op0=ALU.mult, op1=ALU.add)
    eng.tensor_scalar(out=bigA, in0=src32, scalar1=0.0, scalar2=None, op0=ALU.is_lt)
    eng.tensor_scalar(out=bigA, in0=bigA, scalar1=-2.0, scalar2=1.0, op0=ALU.mult, op1=ALU.add)
    eng.tensor_tensor(out=a3(bigB), in0=a3(bigB), in1=bview(sv(0)), op=ALU.mult)
    eng.tensor_tensor(out=dst16, in0=bigB, in1=bigA, op=ALU.mult)


def stream_wave(c, tag):
    raise NotImplementedError


def emit_layer1(c):
    p = c.p
    a16, a32 = c.a16, c.a32
    F16S = c.dram_f16
    QOFF, KOFF, VOFF, SOFF = 0, 16384, 32768, 49152
    emit_ln(c, with_xn16=True)
    p.barrier()
    emit_spill_x(c)
    xn16 = c.xn16

    wq16 = a16.take(8 * D, "l1wqkv")
    pstage = [a16.take(1024, "pst0"), a16.take(1024, "pst1")]
    psems = [p.sem(f"l1p{p.nsem()}"), p.sem(f"l1p{p.nsem()}b")]
    for part in range(3):
        load_w16(c, "L1_wqkv", 8, D, wq16, col_sl=slice(part * D, (part + 1) * D))
        p.barrier()
        if part < 2:
            se0 = p.val(c.s_se)
            idx = 0
            for och in range(NCH):
                for nt in range(4):
                    bank = idx % 4

                    def mmp(eng, och=och, nt=nt, bank=bank, idx=idx, se0=se0):
                        if idx >= 4:
                            eng.wait_ge(c.s_se, se0 + idx - 3)
                        for k in range(NCH):
                            eng.matmul(out=c.ps1[:, bank * 512:(bank + 1) * 512],
                                       lhsT=wq16[:, k * D + och * 128: k * D + (och + 1) * 128],
                                       rhs=t_sl(xn16, k, nt_sl(nt)),
                                       start=(k == 0), stop=(k == NCH - 1))
                        eng.nop().then_inc(c.s_te, 1)
                    p.emit("tensor", mmp)
                    tv = p.bump(c.s_te, 1)
                    stg = pstage[idx % 2]
                    ps_ = psems[idx % 2]
                    pprev = p.val(ps_)

                    def pev(eng, bank=bank, tv=tv, stg=stg, ps_=ps_, pprev=pprev, part=part):
                        eng.wait_ge(c.s_te, tv)
                        if pprev:
                            eng.wait_ge(ps_, pprev)
                        eng.activation(out=stg[:, 0:512], in_=c.ps1[:, bank * 512:(bank + 1) * 512],
                                       func=AF.Identity,
                                       scale=(1.0 / 32.0 if part == 0 else 1.0)).then_inc(c.s_se, 1)
                    p.emit("scalar", pev)
                    sv = p.bump(c.s_se, 1)

                    def pout(eng, och=och, nt=nt, stg=stg, sv=sv, part=part, ps_=ps_, pprev=pprev):
                        eng.wait_ge(c.s_se, sv)
                        if pprev:
                            eng.wait_ge(ps_, pprev)
                        off = (QOFF if part == 0 else KOFF) + och * 2048 + nt * 512
                        eng.dma_start(out=F16S[:, off:off + 512], in_=stg[:, 0:512]).then_inc(ps_, 16)
                    p.emit("sync", pout)
                    p.bump(ps_, 16)
                    idx += 1
        else:
            se0 = p.val(c.s_se)
            for tc in range(16):
                def mmv(eng, tc=tc, se0=se0):
                    if tc >= 1:
                        eng.wait_ge(c.s_se, se0 + tc)
                    for k in range(NCH):
                        lhs = t_sl(xn16, k, slice(tc * 128, (tc + 1) * 128))
                        for h in range(2):
                            eng.matmul(out=c.ps2[:, h * 512:(h + 1) * 512], lhsT=lhs,
                                       rhs=wq16[:, k * D + h * 512: k * D + (h + 1) * 512],
                                       start=(k == 0), stop=(k == NCH - 1))
                    eng.nop().then_inc(c.s_te, 1)
                p.emit("tensor", mmv)
                tv = p.bump(c.s_te, 1)
                stg = pstage[tc % 2]
                ps_ = psems[tc % 2]
                pprev = p.val(ps_)

                def vev(eng, tv=tv, stg=stg, ps_=ps_, pprev=pprev):
                    eng.wait_ge(c.s_te, tv)
                    if pprev:
                        eng.wait_ge(ps_, pprev)
                    eng.activation(out=stg[:, 0:1024], in_=c.ps2[:, 0:1024],
                                   func=AF.Copy).then_inc(c.s_se, 1)
                p.emit("scalar", vev)
                sv = p.bump(c.s_se, 1)

                def vout(eng, tc=tc, stg=stg, sv=sv, ps_=ps_, pprev=pprev):
                    eng.wait_ge(c.s_se, sv)
                    if pprev:
                        eng.wait_ge(ps_, pprev)
                    eng.dma_start(out=F16S[:, VOFF + tc * 1024: VOFF + (tc + 1) * 1024],
                                  in_=stg[:, 0:1024]).then_inc(ps_, 16)
                p.emit("sync", vout)
                p.bump(ps_, 16)
    p.barrier()
    a16.drop(wq16)
    for b in pstage:
        a16.drop(b)
    c.a16.drop(c.xn16)
    c.xn16 = None

    outw16 = a16.take(8 * D, "l1outw")
    load_w16(c, "L1_outw", 8, D, outw16)
    outb_s = load_small(c, "L1_outb", c.bias_sb[:, 0:8])
    kt16 = a16.take(8 * T, "l1kt")
    ktl_s = p.sem(f"l1kt{p.nsem()}")

    def ktl(eng, pv0=p.val(psems[0]), pv1=p.val(psems[1])):
        eng.wait_ge(psems[0], pv0)
        eng.wait_ge(psems[1], pv1)
        eng.dma_start(out=kt16, in_=F16S[:, KOFF:KOFF + 16384]).then_inc(ktl_s, 16)
    p.emit("sync", ktl)
    ktv = p.bump(ktl_s, 16)
    p.barrier()

    qbuf = [a16.take(1024, "qb0"), a16.take(1024, "qb1")]
    vbuf = [a16.take(1024, "vb%d" % i) for i in range(4)]
    p16 = a16.take(2048, "p16")
    pt16 = a16.take(2048, "pt16")
    y16 = a16.take(1024, "y16")
    ytg = a16.take(8 * 512, "ytg")
    sstg = [a16.take(512, "ss0"), a16.take(512, "ss1")]
    qsem = [p.sem("l1q0"), p.sem("l1q1")]
    vsem = [p.sem("l1v0"), p.sem("l1v1"), p.sem("l1v2"), p.sem("l1v3")]
    av_done = []
    q_done = []
    ssem = [p.sem("l1s0"), p.sem("l1s1")]
    id16 = c.cons16[:, OFF_ID16:OFF_ID16 + 128]
    for g4 in range(4):
        ve_g = p.val(c.s_ve)
        se_g = p.val(c.s_se)
        for j in range(4):
            tq = g4 * 4 + j
            qs = qsem[tq % 2]
            qprev = p.val(qs)

            def qload(eng, tq=tq, qs=qs, qprev=qprev, first=(tq == 0), ktv=ktv,
                      buf=qbuf[tq % 2], pv0=p.val(psems[0]), pv1=p.val(psems[1]),
                      qg=(q_done[-2] if len(q_done) >= 2 else 0)):
                if first:
                    eng.wait_ge(psems[0], pv0)
                    eng.wait_ge(psems[1], pv1)
                    eng.wait_ge(ktl_s, ktv)
                if qprev:
                    eng.wait_ge(qs, qprev)
                if qg:
                    eng.wait_ge(c.s_te, qg)
                srcv = F16S[:, QOFF:QOFF + 16384].rearrange("p (o t) -> p o t", t=2048)
                eng.dma_start(out=buf.rearrange("p (o t) -> p o t", t=128),
                              in_=srcv[:, :, tq * 128:(tq + 1) * 128]).then_inc(qs, 16)
            p.emit("sync", qload)
            qv = p.bump(qs, 16)

            def smm(eng, tq=tq, j=j, qv=qv, qs=qs, se_g=se_g, buf=qbuf[tq % 2]):
                eng.wait_ge(qs, qv)
                if j >= 1:
                    eng.wait_ge(c.s_se, se_g + 3 * (j - 1) + 1)
                for ntk in range(4):
                    for k in range(NCH):
                        eng.matmul(out=c.ps1[:, ntk * 512:(ntk + 1) * 512],
                                   lhsT=buf[:, k * 128:(k + 1) * 128],
                                   rhs=t_sl(kt16, k, nt_sl(ntk)),
                                   start=(k == 0), stop=(k == NCH - 1))
                eng.nop().then_inc(c.s_te, 1)
            p.emit("tensor", smm)
            tv = p.bump(c.s_te, 1)
            q_done.append(tv)

            def smax(eng, tv=tv):
                eng.wait_ge(c.s_te, tv)
                eng.tensor_reduce(out=c.stat32[:, 0:1], in_=c.ps1[:, 0:T], axis=AX.X,
                                  op=ALU.max, negate=True).then_inc(c.s_ve, 1)
            p.emit("vector", smax)
            vv = p.bump(c.s_ve, 1)

            def sexp(eng, vv=vv):
                eng.wait_ge(c.s_ve, vv)
                eng.activation(out=p16, in_=c.ps1[:, 0:T], func=AF.Exp,
                               bias=c.stat32[:, 0:1],
                               accum_out=c.stat32[:, 1:2]).then_inc(c.s_se, 1)
            p.emit("scalar", sexp)
            sv = p.bump(c.s_se, 1)

            def srec(eng, sv=sv):
                eng.wait_ge(c.s_se, sv)
                eng.reciprocal(out=c.stat32[:, 2:3], in_=c.stat32[:, 1:2]).then_inc(c.s_ve, 1)
            p.emit("vector", srec)
            p.bump(c.s_ve, 1)

            def ptr(eng, sv=sv, j=j, se_g=se_g):
                eng.wait_ge(c.s_se, sv)
                if j >= 1:
                    eng.wait_ge(c.s_se, se_g + 3 * j)
                for i in range(16):
                    eng.transpose(out=c.psh[:, i * 128:(i + 1) * 128],
                                  in_=p16[:, i * 128:(i + 1) * 128], identity=id16)
                eng.nop().then_inc(c.s_te, 1)
            p.emit("tensor", ptr)
            tv = p.bump(c.s_te, 1)

            def ptev(eng, tv=tv):
                eng.wait_ge(c.s_te, tv)
                eng.activation(out=pt16, in_=c.psh[:, 0:T], func=AF.Copy).then_inc(c.s_se, 1)
            p.emit("scalar", ptev)
            ptv = p.bump(c.s_se, 1)

            for tk in range(16):
                vs = vsem[tk % 4]
                vprev = p.val(vs)

                def vload(eng, tk=tk, vs=vs, vprev=vprev, buf=vbuf[tk % 4],
                          ag=(av_done[-4] if len(av_done) >= 4 else 0)):
                    if vprev:
                        eng.wait_ge(vs, vprev)
                    if ag:
                        eng.wait_ge(c.s_te, ag)
                    eng.dma_start(out=buf,
                                  in_=F16S[:, VOFF + tk * 1024: VOFF + (tk + 1) * 1024]).then_inc(vs, 16)
                p.emit("sync", vload)
                vv2 = p.bump(vs, 16)

                def avmm(eng, tk=tk, vs=vs, vv2=vv2, ptv=ptv, buf=vbuf[tk % 4]):
                    eng.wait_ge(vs, vv2)
                    if tk == 0:
                        eng.wait_ge(c.s_se, ptv)
                    for h in range(2):
                        eng.matmul(out=c.ps2[:, h * 512:(h + 1) * 512],
                                   lhsT=pt16[:, tk * 128:(tk + 1) * 128],
                                   rhs=buf[:, h * 512:(h + 1) * 512],
                                   start=(tk == 0), stop=(tk == 15))
                    eng.nop().then_inc(c.s_te, 1)
                p.emit("tensor", avmm)
                av_done.append(p.bump(c.s_te, 1))
            tv = p.val(c.s_te)

            def yev(eng, tv=tv):
                eng.wait_ge(c.s_te, tv)
                eng.tensor_scalar(out=y16, in0=c.ps2[:, 0:1024], scalar1=c.stat32[:, 2:3],
                                  scalar2=None, op0=ALU.mult).then_inc(c.s_ve, 1)
            p.emit("vector", yev)
            vv = p.bump(c.s_ve, 1)

            def ytr(eng, vv=vv, j=j, se_g=se_g):
                eng.wait_ge(c.s_ve, vv)
                eng.wait_ge(c.s_se, se_g + 3 * j + 2)
                for i in range(NCH):
                    eng.transpose(out=c.psh[:, i * 128:(i + 1) * 128],
                                  in_=y16[:, i * 128:(i + 1) * 128], identity=id16)
                eng.nop().then_inc(c.s_te, 1)
            p.emit("tensor", ytr)
            tv = p.bump(c.s_te, 1)

            def ytev(eng, tv=tv, j=j):
                eng.wait_ge(c.s_te, tv)
                for i in range(NCH):
                    eng.activation(out=ytg[:, i * 512 + j * 128: i * 512 + j * 128 + 128],
                                   in_=c.psh[:, i * 128:(i + 1) * 128], func=AF.Copy)
                eng.nop().then_inc(c.s_se, 1)
            p.emit("scalar", ytev)
            p.bump(c.s_se, 1)
        sev = p.val(c.s_se)
        for och in range(NCH):
            bank = och % 4

            def somm(eng, och=och, bank=bank, sev=sev):
                eng.wait_ge(c.s_se, sev)
                if och >= 4:
                    eng.wait_ge(c.s_se, sev + och - 3)
                for k in range(NCH):
                    eng.matmul(out=c.ps1[:, bank * 512:(bank + 1) * 512],
                               lhsT=outw16[:, k * D + och * 128: k * D + (och + 1) * 128],
                               rhs=ytg[:, k * 512:(k + 1) * 512],
                               start=(k == 0), stop=(k == NCH - 1))
                eng.nop().then_inc(c.s_te, 1)
            p.emit("tensor", somm)
            tv = p.bump(c.s_te, 1)
            stg = sstg[och % 2]
            ss_ = ssem[och % 2]
            ssp = p.val(ss_)

            def soev(eng, och=och, bank=bank, tv=tv, stg=stg, ss_=ss_, ssp=ssp,
                     obs=outb_s):
                eng.wait_ge(c.s_te, tv)
                if och == 0:
                    eng.wait_ge(c.s_ls, obs)
                if ssp:
                    eng.wait_ge(ss_, ssp)
                eng.activation(out=stg[:, 0:512], in_=c.ps1[:, bank * 512:(bank + 1) * 512],
                               func=AF.Identity, bias=c.bias_sb[:, och:och + 1]).then_inc(c.s_se, 1)
            p.emit("scalar", soev)
            sv = p.bump(c.s_se, 1)

            def sout(eng, och=och, g4=g4, stg=stg, sv=sv, ss_=ss_, ssp=ssp):
                eng.wait_ge(c.s_se, sv)
                if ssp:
                    eng.wait_ge(ss_, ssp)
                eng.dma_start(out=F16S[:, SOFF + och * 2048 + g4 * 512: SOFF + och * 2048 + g4 * 512 + 512],
                              in_=stg[:, 0:512]).then_inc(ss_, 16)
            p.emit("sync", sout)
            p.bump(ss_, 16)
        p.barrier()
    for b in (qbuf[0], qbuf[1], vbuf[0], vbuf[1], vbuf[2], vbuf[3], p16, pt16, y16, ytg, sstg[0], sstg[1], kt16, outw16):
        a16.drop(b)
    c._l1_ssem = ssem
    _ckpt(c, "l1a")

    emit_l1_wquant(c)
    _ckpt(c, "l1w")

    emit_l1_ffn(c)
    p.barrier()


def emit_l1_wquant(c):
    p = c.p
    a16, a32 = c.a16, c.a32
    id16 = c.cons16[:, OFF_ID16:OFF_ID16 + 128]
    id32 = c.cons32[:, OFF_ID:OFF_ID + 128]
    W16 = c.dram_w16
    sin_s = load_small(c, "L1_sin", c.stat32[0:1, 14:15])
    sind_s = load_small(c, "L1_sind", c.stat32[0:1, 15:16])
    bigA = a32.take(2048, "l1qA")
    bigB = a32.take(2048, "l1qB")
    sm = a32.take(7 * 128, "l1qsm")
    wq16row = a16.take(2048, "wq16row")
    wstg = [a16.take(2048, "wts0"), a16.take(2048, "wts1")]
    wos = [p.sem(f"l1wo{p.nsem()}"), p.sem(f"l1wo{p.nsem()}b")]
    for wi, (nm, rows, cols, woff, gcol) in enumerate([
            ("L1_gate", 4096, 1024, 0, 8),
            ("L1_up", 4096, 1024, 32768, 10),
            ("L1_down", 1024, 4096, 65536, 12)]):
        kch = rows // 128
        npc = cols // 1024
        npieces = (kch * npc) // 2

        def _ld2048(eng, st, nm, pi, npc, sem):
            if npc == 1:
                eng.dma_start(out=st[:, 0:1024],
                              in_=c.dram[nm][pi * 256: pi * 256 + 128, :]).then_inc(sem, 16)
                eng.dma_start(out=st[:, 1024:2048],
                              in_=c.dram[nm][pi * 256 + 128: pi * 256 + 256, :]).then_inc(sem, 16)
            else:
                k, pc2 = divmod(pi, npc // 2)
                eng.dma_start(out=st[:, 0:2048],
                              in_=c.dram[nm][k * 128:(k + 1) * 128, pc2 * 2048:(pc2 + 1) * 2048]).then_inc(sem, 32)
        ve0 = p.val(c.s_ve)
        step = 0
        for pi in range(npieces):
                st = c.wstage[step % 2]
                need = ve0 + (step - 1) if step >= 2 else 0
                sem = c.s_wl[step % 2]
                prev = p.val(sem)

                def fl(eng, pi=pi, st=st, need=need, sem=sem, prev=prev, nm=nm,
                       npc=npc):
                    if need:
                        eng.wait_ge(c.s_ve, need)
                    if prev:
                        eng.wait_ge(sem, prev)
                    _ld2048(eng, st, nm, pi, npc, sem)
                p.emit("sync", fl)
                dv = p.bump(sem, 32)

                def red(eng, st=st, dv=dv, sem=sem, first=(step == 0)):
                    eng.wait_ge(sem, dv)
                    eng.tensor_reduce(out=c.stat32[:, 1:2], in_=st[:, 0:2048], axis=AX.X,
                                      op=ALU.max, apply_absolute_value=True)
                    if first:
                        eng.tensor_copy(out=c.stat32[:, 0:1], in_=c.stat32[:, 1:2]).then_inc(c.s_ve, 1)
                    else:
                        eng.tensor_tensor(out=c.stat32[:, 0:1], in0=c.stat32[:, 0:1],
                                          in1=c.stat32[:, 1:2], op=ALU.max).then_inc(c.s_ve, 1)
                p.emit("vector", red)
                p.bump(c.s_ve, 1)
                step += 1
        vev = p.val(c.s_ve)

        def pmax(eng, vev=vev):
            eng.wait_ge(c.s_ve, vev)
            eng.transpose(out=c.ps2[0:1, 0:128], in_=c.stat32[:, 0:1], identity=id32)
            eng.nop().then_inc(c.s_te, 1)
        p.emit("tensor", pmax)
        tv = p.bump(c.s_te, 1)

        def gmath(eng, tv=tv):
            eng.wait_ge(c.s_te, tv)
            eng.reduce_max(out=c.stat32[0:1, 2:3], in_=c.ps2[0:1, 0:128], axis=AX.X)
            eng.reciprocal(out=c.stat32[0:1, 3:4], in_=c.stat32[0:1, 2:3])
            eng.tensor_scalar(out=c.stat32[0:1, 3:4], in0=c.stat32[0:1, 3:4], scalar1=2688.0,
                              scalar2=None, op0=ALU.mult)
            eng.tensor_scalar(out=c.stat32[0:1, 2:3], in0=c.stat32[0:1, 2:3], scalar1=1.0 / 2688.0,
                              scalar2=None, op0=ALU.mult).then_inc(c.s_ve, 1)
        p.emit("vector", gmath)
        vev = p.bump(c.s_ve, 1)
        tv = bcast_row(c, c.stat32[0:1, 3:4], 1, c.ps2[:, 0:512], wait=(c.s_ve, vev))
        tv2 = bcast_row(c, c.stat32[0:1, 2:3], 1, c.ps2[:, 512:1024])

        def gbc(eng, tv2=tv2, gcol=gcol):
            eng.wait_ge(c.s_te, tv2)
            eng.tensor_copy(out=c.stat32[:, gcol:gcol + 1], in_=c.ps2[:, 0:1])
            eng.tensor_copy(out=c.stat32[:, gcol + 1:gcol + 2], in_=c.ps2[:, 512:513]).then_inc(c.s_ve, 1)
        p.emit("vector", gbc)
        p.bump(c.s_ve, 1)
        p.barrier()

        step = 0
        ve0 = p.val(c.s_ve)
        se0 = p.val(c.s_se)
        for pi in range(npieces):
                st = c.wstage[step % 2]
                need = ve0 + (step - 1) if step >= 2 else 0
                sem = c.s_wl[step % 2]
                prev = p.val(sem)

                def fl2(eng, pi=pi, st=st, need=need, sem=sem, prev=prev, nm=nm,
                        npc=npc):
                    if need:
                        eng.wait_ge(c.s_ve, need)
                    if prev:
                        eng.wait_ge(sem, prev)
                    _ld2048(eng, st, nm, pi, npc, sem)
                p.emit("sync", fl2)
                dv = p.bump(sem, 32)

                def qz(eng, st=st, dv=dv, sem=sem, gcol=gcol, step=step, se0=se0):
                    eng.wait_ge(sem, dv)
                    if step >= 1:
                        eng.wait_ge(c.s_se, se0 + step)
                    nvfp4_ops(c, eng, st[:, 0:2048], 2048, wq16row[:, 0:2048],
                              c.stat32[:, gcol:gcol + 1], c.stat32[:, gcol + 1:gcol + 2],
                              bigA[:, 0:2048], bigB[:, 0:2048], sm)
                    eng.nop().then_inc(c.s_ve, 1)
                p.emit("vector", qz)
                qv = p.bump(c.s_ve, 1)

                def trp(eng, qv=qv, step=step, se0=se0):
                    eng.wait_ge(c.s_ve, qv)
                    if step >= 1:
                        eng.wait_ge(c.s_se, se0 + step)
                    for i in range(16):
                        eng.transpose(out=c.psh[:, i * 128:(i + 1) * 128],
                                      in_=wq16row[:, i * 128:(i + 1) * 128], identity=id16)
                    eng.nop().then_inc(c.s_te, 1)
                p.emit("tensor", trp)
                tv = p.bump(c.s_te, 1)
                ws_ = wos[step % 2]
                wsp = p.val(ws_)
                stg2 = wstg[step % 2]

                def trev(eng, tv=tv, ws_=ws_, wsp=wsp, stg2=stg2):
                    eng.wait_ge(c.s_te, tv)
                    if wsp:
                        eng.wait_ge(ws_, wsp)
                    eng.activation(out=stg2, in_=c.psh[:, 0:2048], func=AF.Copy)
                    eng.nop().then_inc(c.s_se, 1)
                p.emit("scalar", trev)
                sv = p.bump(c.s_se, 1)

                def wout(eng, pi=pi, sv=sv, ws_=ws_, wsp=wsp, stg2=stg2,
                         woff=woff, rows=rows, cols=cols, kch=kch, npc=npc):
                    eng.wait_ge(c.s_se, sv)
                    if wsp:
                        eng.wait_ge(ws_, wsp)
                    dst = W16[:, woff:woff + rows * cols // 128].rearrange("p (h r) -> p h r", r=rows)
                    if npc == 1:
                        eng.dma_start(out=dst[:, 0:8, pi * 256: pi * 256 + 128],
                                      in_=stg2[:, 0:1024].rearrange("p (h r) -> p h r", r=128)).then_inc(ws_, 16)
                        eng.dma_start(out=dst[:, 0:8, pi * 256 + 128: pi * 256 + 256],
                                      in_=stg2[:, 1024:2048].rearrange("p (h r) -> p h r", r=128)).then_inc(ws_, 16)
                    else:
                        k, pc2 = divmod(pi, npc // 2)
                        eng.dma_start(out=dst[:, pc2 * 16:(pc2 + 1) * 16, k * 128:(k + 1) * 128],
                                      in_=stg2.rearrange("p (h r) -> p h r", r=128)).then_inc(ws_, 32)
                p.emit("sync", wout)
                p.bump(ws_, 32)
                step += 1
        p.barrier()
    c._l1_wos = [(wos[0], p.val(wos[0])), (wos[1], p.val(wos[1]))]
    sinv = p.val(c.s_ve)

    def smath(eng, sin_s=sin_s, sind_s=sind_s):
        eng.wait_ge(c.s_ls, sin_s)
        eng.wait_ge(c.s_ls, sind_s)
        eng.reciprocal(out=c.stat32[0:1, 0:1], in_=c.stat32[0:1, 14:15])
        eng.reciprocal(out=c.stat32[0:1, 1:2], in_=c.stat32[0:1, 15:16]).then_inc(c.s_ve, 1)
    p.emit("vector", smath)
    vev = p.bump(c.s_ve, 1)
    tv = bcast_row(c, c.stat32[0:1, 0:2], 2, c.ps2[:, 0:512], wait=(c.s_ve, vev))
    tv2 = bcast_row(c, c.stat32[0:1, 14:16], 2, c.ps2[:, 512:1024], wait=(c.s_ve, vev))

    def sfin(eng, tv2=tv2):
        eng.wait_ge(c.s_te, tv2)
        eng.tensor_copy(out=c.stat32[:, 0:2], in_=c.ps2[:, 0:2])
        eng.tensor_copy(out=c.stat32[:, 14:16], in_=c.ps2[:, 512:514])
        eng.tensor_tensor(out=c.stat32[:, 4:5], in0=c.stat32[:, 0:1], in1=c.stat32[:, 9:10], op=ALU.mult)
        eng.tensor_tensor(out=c.stat32[:, 5:6], in0=c.stat32[:, 0:1], in1=c.stat32[:, 11:12], op=ALU.mult)
        eng.tensor_tensor(out=c.stat32[:, 6:7], in0=c.stat32[:, 1:2], in1=c.stat32[:, 13:14], op=ALU.mult).then_inc(c.s_ve, 1)
    p.emit("vector", sfin)
    p.bump(c.s_ve, 1)
    w0 = p.val(wos[0])
    w1 = p.val(wos[1])
    for e in ("tensor", "vector", "scalar", "gpsimd"):
        def wq_q(eng, w0=w0, w1=w1):
            eng.wait_ge(wos[0], w0)
            eng.wait_ge(wos[1], w1)
        p.emit(e, wq_q)
    for b in (bigA, bigB, sm):
        a32.drop(b)
    a16.drop(wq16row)
    for b in wstg:
        a16.drop(b)
    p.barrier()


def emit_l1_ffn(c):
    p = c.p
    a16, a32 = c.a16, c.a32
    F16S = c.dram_f16
    W16 = c.dram_w16
    SOFF = 49152
    id16 = c.cons16[:, OFF_ID16:OFF_ID16 + 128]
    id32 = c.cons32[:, OFF_ID:OFF_ID + 128]
    xg = a32.take(8 * 512, "l1xg")
    bigA = a32.take(1024, "fA")
    bigB = a32.take(1024, "fB")
    gate32 = a32.take(4 * 512, "fgate")
    xtm = a32.take(1024, "fxtm")
    sm = a32.take(7 * 64, "fsm")
    sg32 = a32.take(512, "fsg32")
    vq16 = a16.take(1024, "fvq")
    vqT = a16.take(8 * 512, "fvqT")
    hq16 = a16.take(1024, "fhq")
    hqT = a16.take(32 * 512, "fhqT")
    sgrp = a16.take(8 * 512, "fsg")
    gtm16 = a16.take(1024, "fgtm")
    gT = a16.take(8 * 512, "fgT")
    wbuf = [a16.take(4096, "fw0"), a16.take(4096, "fw1")]
    dbuf = [a16.take(1024, "fd0"), a16.take(1024, "fd1")]
    xgs = p.sem(f"l1xg{p.nsem()}")
    sgs = p.sem(f"l1sg{p.nsem()}")
    xos = p.sem(f"l1xo{p.nsem()}")
    wss = [p.sem(f"l1ws{p.nsem()}"), p.sem(f"l1ws{p.nsem()}b")]
    dss = [p.sem(f"l1ds{p.nsem()}"), p.sem(f"l1ds{p.nsem()}b")]
    wg_done = []
    dn_done = []
    for grp in range(4):
        gsl = slice(grp * 512, (grp + 1) * 512)

        def xload(eng, grp=grp, xop=p.val(xos), ssem=c.spill_sem, ssv=c.spill_done):
            if grp == 0:
                eng.wait_ge(ssem, ssv)
            if xop:
                eng.wait_ge(xos, xop)
            src = c.dram_spill.rearrange("(o pp) t -> pp o t", pp=128)
            eng.dma_start(out=xg.rearrange("p (o t) -> p o t", t=512),
                          in_=src[:, :, grp * 512:(grp + 1) * 512]).then_inc(xgs, 16)
        p.emit("sync", xload)
        xgv = p.bump(xgs, 16)

        def sload(eng, grp=grp, sgp=p.val(sgs), sv0=p.val(c._l1_ssem[0]), sv1=p.val(c._l1_ssem[1])):
            if grp == 0:
                eng.wait_ge(c._l1_ssem[0], sv0)
                eng.wait_ge(c._l1_ssem[1], sv1)
            if sgp:
                eng.wait_ge(sgs, sgp)
            src = F16S[:, SOFF:SOFF + 16384].rearrange("p (o t) -> p o t", t=2048)
            eng.dma_start(out=sgrp.rearrange("p (o t) -> p o t", t=512),
                          in_=src[:, :, grp * 512:(grp + 1) * 512]).then_inc(sgs, 16)
        p.emit("sync", sload)
        sgv = p.bump(sgs, 16)

        se0 = p.val(c.s_se)
        ve0 = p.val(c.s_ve)
        for j in range(4):
            def tx(eng, j=j, xgv=xgv, ve0=ve0):
                if j == 0:
                    eng.wait_ge(xgs, xgv)
                if j > 0:
                    eng.wait_ge(c.s_ve, ve0 + 4 * (j - 1) + 1)
                for ch in range(NCH):
                    eng.transpose(out=c.ps2[:, ch * 128:(ch + 1) * 128],
                                  in_=xg[:, ch * 512 + j * 128: ch * 512 + j * 128 + 128],
                                  identity=id32)
                eng.nop().then_inc(c.s_te, 1)
            p.emit("tensor", tx)
            tv = p.bump(c.s_te, 1)

            def ev0(eng, tv=tv, j=j, se0=se0):
                eng.wait_ge(c.s_te, tv)
                if j > 0:
                    eng.wait_ge(c.s_se, se0 + 4 * (j - 1) + 3)
                eng.tensor_scalar(out=bigA[:, 0:1024], in0=c.ps2[:, 0:D], scalar1=1.0,
                                  scalar2=0.0, op0=ALU.mult, op1=ALU.add,
                                  accum_out=c.stat32[:, 20:21]).then_inc(c.s_ve, 1)
            p.emit("vector", ev0)
            vv0 = p.bump(c.s_ve, 1)

            def sq0(eng, vv0=vv0):
                eng.wait_ge(c.s_ve, vv0)
                eng.activation(out=bigB[:, 0:1024], in_=bigA[:, 0:1024], func=AF.Square,
                               accum_out=c.stat32[:, 21:22]).then_inc(c.s_se, 1)
            p.emit("scalar", sq0)
            sv0 = p.bump(c.s_se, 1)

            def lnst(eng, sv0=sv0):
                eng.wait_ge(c.s_se, sv0)
                eng.tensor_scalar(out=c.stat32[:, 22:23], in0=c.stat32[:, 20:21],
                                  scalar1=1.0 / D, scalar2=None, op0=ALU.mult)
                eng.tensor_scalar(out=c.stat32[:, 23:24], in0=c.stat32[:, 21:22],
                                  scalar1=1.0 / D, scalar2=None, op0=ALU.mult)
                eng.tensor_tensor(out=c.stat32[:, 24:25], in0=c.stat32[:, 22:23],
                                  in1=c.stat32[:, 22:23], op=ALU.mult)
                eng.tensor_tensor(out=c.stat32[:, 23:24], in0=c.stat32[:, 23:24],
                                  in1=c.stat32[:, 24:25], op=ALU.subtract).then_inc(c.s_ve, 1)
            p.emit("vector", lnst)
            vv1 = p.bump(c.s_ve, 1)

            def lnsq(eng, vv1=vv1):
                eng.wait_ge(c.s_ve, vv1)
                eng.activation(out=c.stat32[:, 23:24], in_=c.stat32[:, 23:24], func=AF.Sqrt,
                               bias=c.cc(1e-5)).then_inc(c.s_se, 1)
            p.emit("scalar", lnsq)
            sv1 = p.bump(c.s_se, 1)

            def lnal(eng, sv1=sv1):
                eng.wait_ge(c.s_se, sv1)
                eng.reciprocal(out=c.stat32[:, 24:25], in_=c.stat32[:, 23:24])
                eng.tensor_tensor(out=c.stat32[:, 22:23], in0=c.stat32[:, 22:23],
                                  in1=c.stat32[:, 24:25], op=ALU.mult)
                eng.tensor_scalar(out=c.stat32[:, 22:23], in0=c.stat32[:, 22:23],
                                  scalar1=-1.0, scalar2=None, op0=ALU.mult).then_inc(c.s_ve, 1)
            p.emit("vector", lnal)
            vv2 = p.bump(c.s_ve, 1)

            def lnap(eng, vv2=vv2):
                eng.wait_ge(c.s_ve, vv2)
                eng.activation(out=xtm, in_=bigA[:, 0:1024], func=AF.Identity,
                               bias=c.stat32[:, 22:23], scale=c.stat32[:, 24:25]).then_inc(c.s_se, 1)
            p.emit("scalar", lnap)
            sv2 = p.bump(c.s_se, 1)

            def xnq(eng, sv2=sv2):
                eng.wait_ge(c.s_se, sv2)
                nvfp4_ops(c, eng, xtm, 1024, vq16, c.stat32[:, 14:15], c.stat32[:, 0:1],
                          bigA[:, 0:1024], bigB[:, 0:1024], sm)
                eng.nop().then_inc(c.s_ve, 1)
            p.emit("vector", xnq)
            vq_v = p.bump(c.s_ve, 1)

            def vqt(eng, vq_v=vq_v, j=j, se0=se0):
                eng.wait_ge(c.s_ve, vq_v)
                if j > 0:
                    eng.wait_ge(c.s_se, se0 + 4 * (j - 1) + 4)
                for i in range(NCH):
                    eng.transpose(out=c.psh[:, i * 128:(i + 1) * 128],
                                  in_=vq16[:, i * 128:(i + 1) * 128], identity=id16)
                eng.nop().then_inc(c.s_te, 1)
            p.emit("tensor", vqt)
            tv = p.bump(c.s_te, 1)

            def vqtev(eng, tv=tv, j=j):
                eng.wait_ge(c.s_te, tv)
                for i in range(NCH):
                    eng.activation(out=vqT[:, i * 512 + j * 128: i * 512 + j * 128 + 128],
                                   in_=c.psh[:, i * 128:(i + 1) * 128], func=AF.Copy)
                eng.nop().then_inc(c.s_se, 1)
            p.emit("scalar", vqtev)
            p.bump(c.s_se, 1)
        sevq = p.val(c.s_se)

        for st8 in range(8):
            for gu in range(2):
                ws_ = wss[(st8 * 2 + gu) % 2]
                wsp = p.val(ws_)
                wb = wbuf[(st8 * 2 + gu) % 2]

                def wload(eng, st8=st8, gu=gu, ws_=ws_, wsp=wsp, wb=wb,
                          wg=(wg_done[-2] if len(wg_done) >= 2 else 0),
                          fw=(c._l1_wos + [(s, p.val(s)) for s in c._l1_ssem]
                              if (grp == 0 and st8 == 0 and gu == 0) else None)):
                    if fw:
                        for s_, v_ in fw:
                            if v_:
                                eng.wait_ge(s_, v_)
                    if wsp:
                        eng.wait_ge(ws_, wsp)
                    if wg:
                        eng.wait_ge(c.s_te, wg)
                    woff = gu * 32768
                    srcv = W16[:, woff:woff + 32768].rearrange("p (h r) -> p h r", r=4096)
                    eng.dma_start(out=wb.rearrange("p (h r) -> p h r", r=512),
                                  in_=srcv[:, :, st8 * 512:(st8 + 1) * 512]).then_inc(ws_, 16)
                p.emit("gpsimd", wload)
                wv = p.bump(ws_, 16)

                def wmm(eng, st8=st8, gu=gu, ws_=ws_, wv=wv, wb=wb, sevq=sevq,
                        se1=p.val(c.s_se), ve1=p.val(c.s_ve), first=(st8 == 0 and gu == 0)):
                    eng.wait_ge(ws_, wv)
                    eng.wait_ge(c.s_se, sevq if first else se1)
                    if not first:
                        eng.wait_ge(c.s_ve, ve1)
                    for j in range(4):
                        for k in range(NCH):
                            eng.matmul(out=c.ps1[:, j * 512:(j + 1) * 512],
                                       lhsT=vqT[:, k * 512 + j * 128: k * 512 + j * 128 + 128],
                                       rhs=wb[:, k * 512:(k + 1) * 512],
                                       start=(k == 0), stop=(k == NCH - 1))
                    eng.nop().then_inc(c.s_te, 1)
                p.emit("tensor", wmm)
                tv = p.bump(c.s_te, 1)
                wg_done.append(tv)

                if gu == 0:
                    def gev(eng, tv=tv, ve2=p.val(c.s_ve)):
                        eng.wait_ge(c.s_te, tv)
                        eng.wait_ge(c.s_ve, ve2)
                        eng.activation(out=gate32, in_=c.ps1[:, 0:2048], func=AF.Silu,
                                       scale=c.stat32[:, 4:5]).then_inc(c.s_se, 1)
                    p.emit("scalar", gev)
                    p.bump(c.s_se, 1)
                else:
                    prev_hqt = [0]
                    for jp in range(2):
                        def hj(eng, jp=jp, tv=tv, st8=st8, se2=p.val(c.s_se), ph=prev_hqt[0]):
                            eng.wait_ge(c.s_te, max(tv, ph))
                            if jp == 0:
                                eng.wait_ge(c.s_se, se2)
                            eng.scalar_tensor_tensor(out=bigA[:, 0:1024],
                                                     in0=c.ps1[:, jp * 1024:(jp + 1) * 1024],
                                                     scalar=c.stat32[:, 5:6],
                                                     in1=gate32[:, jp * 1024:(jp + 1) * 1024],
                                                     op0=ALU.mult, op1=ALU.mult)
                            nvfp4_ops(c, eng, bigA[:, 0:1024], 1024, hq16[:, 0:1024],
                                      c.stat32[:, 15:16], c.stat32[:, 1:2],
                                      bigB[:, 0:1024], xtm[:, 0:1024], sm)
                            eng.nop().then_inc(c.s_ve, 1)
                        p.emit("vector", hj)
                        hv = p.bump(c.s_ve, 1)

                        def hqt(eng, hv=hv, se3=p.val(c.s_se)):
                            eng.wait_ge(c.s_ve, hv)
                            eng.wait_ge(c.s_se, se3)
                            for i in range(8):
                                eng.transpose(out=c.psh[:, i * 128:(i + 1) * 128],
                                              in_=hq16[:, i * 128:(i + 1) * 128], identity=id16)
                            eng.nop().then_inc(c.s_te, 1)
                        p.emit("tensor", hqt)
                        tv2 = p.bump(c.s_te, 1)
                        prev_hqt[0] = tv2

                        def hqtev(eng, tv2=tv2, jp=jp, st8=st8):
                            eng.wait_ge(c.s_te, tv2)
                            for i in range(8):
                                ic = st8 * 4 + (i % 4)
                                j = 2 * jp + i // 4
                                eng.activation(out=hqT[:, ic * 512 + j * 128: ic * 512 + j * 128 + 128],
                                               in_=c.psh[:, i * 128:(i + 1) * 128], func=AF.Copy)
                            eng.nop().then_inc(c.s_se, 1)
                        p.emit("scalar", hqtev)
                        p.bump(c.s_se, 1)
        sevh = p.val(c.s_se)

        ve3 = p.val(c.s_ve)
        for oh in range(2):
            for k in range(32):
                ds_ = dss[k % 2]
                dsp = p.val(ds_)
                db = dbuf[k % 2]

                def dload(eng, k=k, oh=oh, ds_=ds_, dsp=dsp, db=db,
                          dg=(dn_done[-2] if len(dn_done) >= 2 else 0),
                          fw=(c._l1_wos + [(s, p.val(s)) for s in c._l1_ssem]
                              if (grp == 0 and oh == 0 and k == 0) else None)):
                    if fw:
                        for s_, v_ in fw:
                            if v_:
                                eng.wait_ge(s_, v_)
                    if dsp:
                        eng.wait_ge(ds_, dsp)
                    if dg:
                        eng.wait_ge(c.s_te, dg)
                    srcv = W16[:, 65536:98304].rearrange("p (h r) -> p h r", r=1024)
                    eng.dma_start(out=db[:, 0:512],
                                  in_=srcv[:, k, oh * 512:(oh + 1) * 512]).then_inc(ds_, 16)
                p.emit("gpsimd", dload)
                dv = p.bump(ds_, 16)

                def dmm(eng, k=k, oh=oh, ds_=ds_, dv=dv, db=db, sevh=sevh,
                        se4=p.val(c.s_se)):
                    eng.wait_ge(ds_, dv)
                    if k == 0:
                        eng.wait_ge(c.s_se, se4)
                    for oo in range(4):
                        eng.matmul(out=c.ps1[:, oo * 512:(oo + 1) * 512],
                                   lhsT=db[:, oo * 128:(oo + 1) * 128],
                                   rhs=hqT[:, k * 512:(k + 1) * 512],
                                   start=(k == 0), stop=(k == 31))
                    eng.nop().then_inc(c.s_te, 1)
                p.emit("tensor", dmm)
                dn_done.append(p.bump(c.s_te, 1))
            tvd = p.val(c.s_te)
            for oo in range(4):
                och = oh * 4 + oo

                def dsil(eng, oo=oo, och=och, tvd=tvd):
                    eng.wait_ge(c.s_te, tvd)
                    eng.activation(out=gT[:, och * 512:(och + 1) * 512],
                                   in_=c.ps1[:, oo * 512:(oo + 1) * 512],
                                   func=AF.Silu, scale=c.stat32[:, 6:7])
                    eng.nop().then_inc(c.s_se, 1)
                p.emit("scalar", dsil)
                p.bump(c.s_se, 1)
        sevd = p.val(c.s_se)
        for och in range(NCH):
            def comb(eng, och=och, sevd=sevd, sgv=sgv, xgv=xgv):
                eng.wait_ge(c.s_se, sevd)
                if och == 0:
                    eng.wait_ge(sgs, sgv)
                    eng.wait_ge(xgs, xgv)
                eng.tensor_tensor(out=sg32, in0=sgrp[:, och * 512:(och + 1) * 512],
                                  in1=gT[:, och * 512:(och + 1) * 512], op=ALU.mult)
                eng.tensor_tensor(out=xg[:, och * 512:(och + 1) * 512],
                                  in0=xg[:, och * 512:(och + 1) * 512], in1=sg32,
                                  op=ALU.add).then_inc(c.s_ve, 2)
            p.emit("vector", comb)
            p.bump(c.s_ve, 2)
        vev = p.val(c.s_ve)

        def xstore(eng, grp=grp, vev=vev):
            eng.wait_ge(c.s_ve, vev)
            dst = c.dram_spill.rearrange("(o pp) t -> pp o t", pp=128)
            eng.dma_start(out=dst[:, :, grp * 512:(grp + 1) * 512],
                          in_=xg.rearrange("p (o t) -> p o t", t=512)).then_inc(xos, 16)
        p.emit("sync", xstore)
        p.bump(xos, 16)
        p.barrier()
    for b in (xg, bigA, bigB, gate32, xtm, sm, sg32):
        a32.drop(b)
    for b in (vq16, vqT, hq16, hqT, sgrp, gtm16, gT, wbuf[0], wbuf[1], dbuf[0], dbuf[1]):
        a16.drop(b)
    sem = p.sem(f"rsf{p.nsem()}")
    c.x = c.a32.take(NCH * T, "x")
    x = c.x

    def fre(eng, sem=sem, xop=p.val(xos)):
        eng.wait_ge(xos, xop)
        for ch in range(NCH):
            eng.dma_start(out=t_sl(x, ch),
                          in_=c.dram_spill[ch * 128:(ch + 1) * 128, :]).then_inc(sem, 16)
    p.emit("sync", fre)
    v = p.bump(sem, 16 * NCH)
    for e in ("tensor", "vector", "scalar", "gpsimd"):
        p.emit(e, lambda eng, v=v, sem=sem: eng.wait_ge(sem, v))
    p.barrier()
def emit_layer2(c):
    p = c.p
    a16, a32 = c.a16, c.a32
    F16S = c.dram_f16
    XBOFF, SZOFF, YZOFF, SOFF = 0, 32768, 65536, 98304
    id32 = c.cons32[:, OFF_ID:OFF_ID + 128]
    id16 = c.cons16[:, OFF_ID16:OFF_ID16 + 128]
    emit_ln(c, with_xn16=True)
    p.barrier()
    emit_spill_x(c)
    xn16 = c.xn16

    nw_s = load_small(c, "L2_normw", c.bias_sb[:, 0:8],
                      wait=[(c.s_se, p.val(c.s_se)), (c.s_ve, p.val(c.s_ve))])
    rrow = a32.take(T, "l2rr")
    tmp32 = a32.take(512, "l2t")
    te0 = p.val(c.s_te)
    step = 0
    sqb2 = [a32.take(512, "l2sq0"), a32.take(512, "l2sq1")]
    for ch in range(NCH):
        for nt in range(4):
            buf = sqb2[step % 2]

            def sq(eng, ch=ch, nt=nt, buf=buf, step=step, te0=te0):
                if step >= 2:
                    eng.wait_ge(c.s_te, te0 + step - 1)
                eng.activation(out=buf, in_=t_sl(xn16, ch, nt_sl(nt)), func=AF.Square).then_inc(c.s_se, 1)
            p.emit("scalar", sq)
            sev = p.bump(c.s_se, 1)

            def mm(eng, nt=nt, ch=ch, buf=buf, sev=sev):
                eng.wait_ge(c.s_se, sev)
                eng.matmul(out=c.ps1[0:1, nt_sl(nt)], lhsT=c.cons32[:, OFF_ONES:OFF_ONES + 1],
                           rhs=buf, start=(ch == 0), stop=(ch == NCH - 1))
                eng.nop().then_inc(c.s_te, 1)
            p.emit("tensor", mm)
            p.bump(c.s_te, 1)
            step += 1
    tev = p.val(c.s_te)

    def rv(eng, tev=tev):
        eng.wait_ge(c.s_te, tev)
        eng.tensor_scalar(out=rrow[0:1, 0:T], in0=c.ps1[0:1, 0:T], scalar1=1.0 / D,
                          scalar2=None, op0=ALU.mult).then_inc(c.s_ve, 1)
    p.emit("vector", rv)
    vev = p.bump(c.s_ve, 1)

    def rs(eng, vev=vev):
        eng.wait_ge(c.s_ve, vev)
        eng.activation(out=rrow[0:1, 0:T], in_=rrow[0:1, 0:T], func=AF.Sqrt,
                       bias=c.cc(1e-5)[0:1, :]).then_inc(c.s_se, 1)
    p.emit("scalar", rs)
    sev = p.bump(c.s_se, 1)

    def rv2(eng, sev=sev):
        eng.wait_ge(c.s_se, sev)
        eng.reciprocal(out=rrow[0:1, 0:T], in_=rrow[0:1, 0:T]).then_inc(c.s_ve, 1)
    p.emit("vector", rv2)
    p.bump(c.s_ve, 1)
    p.barrier()

    wbig = a16.take(8 * 2048, "l2wx")
    pstage = [a16.take(2048, "l2st0"), a16.take(2048, "l2st1")]
    xn2b = a16.take(8 * 512, "l2xn2")
    psems = [p.sem(f"l2p{p.nsem()}"), p.sem(f"l2p{p.nsem()}b")]
    for part, (wname, bcol_s, boff, act, ooff) in enumerate([
            ("L2_Wx", "L2_bx", 8, "id", XBOFF),
            ("L2_Wz", "L2_bz", 8, "silu", SZOFF)]):
        load_w16(c, wname, 8, 2048, wbig)
        b_s = load_small(c, bcol_s, c.bias_sb[:, 8:24],
                         wait=[(c.s_se, p.val(c.s_se))])
        p.barrier()
        for nt in range(4):
            tvb = bcast_row(c, rrow[0:1, nt_sl(nt)], 512, c.ps2[:, 0:512])
            ve0 = p.val(c.s_ve)
            for ch in range(NCH):
                def x2(eng, ch=ch, nt=nt, tvb=tvb, tprev=p.val(c.s_te),
                       nw_first=(part == 0 and nt == 0), nw_s=nw_s):
                    if ch == 0:
                        eng.wait_ge(c.s_te, max(tvb, tprev))
                        if nw_first:
                            eng.wait_ge(c.s_ls, nw_s)
                    eng.tensor_tensor(out=tmp32, in0=t_sl(xn16, ch, nt_sl(nt)),
                                      in1=c.ps2[:, 0:512], op=ALU.mult)
                    eng.tensor_scalar(out=xn2b[:, ch * 512:(ch + 1) * 512], in0=tmp32,
                                      scalar1=c.bias_sb[:, ch:ch + 1], scalar2=None,
                                      op0=ALU.mult).then_inc(c.s_ve, 1)
                p.emit("vector", x2)
                p.bump(c.s_ve, 1)
            vx = p.val(c.s_ve)
            se0 = p.val(c.s_se)
            for oc in range(16):
                bank = oc % 4

                def pmm(eng, oc=oc, nt=nt, bank=bank, vx=vx, se0=se0):
                    eng.wait_ge(c.s_ve, vx)
                    if oc >= 4 or nt > 0:
                        eng.wait_ge(c.s_se, se0 + oc - 3)
                    for k in range(NCH):
                        eng.matmul(out=c.ps1[:, bank * 512:(bank + 1) * 512],
                                   lhsT=wbig[:, k * 2048 + oc * 128: k * 2048 + (oc + 1) * 128],
                                   rhs=xn2b[:, k * 512:(k + 1) * 512],
                                   start=(k == 0), stop=(k == NCH - 1))
                    eng.nop().then_inc(c.s_te, 1)
                p.emit("tensor", pmm)
                tv = p.bump(c.s_te, 1)
                stg = pstage[oc % 2]
                ps_ = psems[oc % 2]
                pprev = p.val(ps_)

                def pev(eng, oc=oc, bank=bank, tv=tv, stg=stg, ps_=ps_, pprev=pprev,
                        act=act, bfirst=(nt == 0 and oc == 0), b_s=b_s):
                    eng.wait_ge(c.s_te, tv)
                    if bfirst:
                        eng.wait_ge(c.s_ls, b_s)
                    if pprev:
                        eng.wait_ge(ps_, pprev)
                    eng.activation(out=stg[:, 0:512], in_=c.ps1[:, bank * 512:(bank + 1) * 512],
                                   func=(AF.Silu if act == "silu" else AF.Identity),
                                   bias=c.bias_sb[:, 8 + oc:9 + oc]).then_inc(c.s_se, 1)
                p.emit("scalar", pev)
                sv = p.bump(c.s_se, 1)

                def pout(eng, oc=oc, nt=nt, stg=stg, sv=sv, ps_=ps_, pprev=pprev, ooff=ooff):
                    eng.wait_ge(c.s_se, sv)
                    if pprev:
                        eng.wait_ge(ps_, pprev)
                    eng.dma_start(out=F16S[:, ooff + oc * 2048 + nt * 512: ooff + oc * 2048 + (nt + 1) * 512],
                                  in_=stg[:, 0:512]).then_inc(ps_, 16)
                p.emit("sync", pout)
                p.bump(ps_, 16)
        p.barrier()
    a16.drop(wbig)
    a16.drop(xn2b)
    for b in pstage:
        a16.drop(b)
    a32.drop(rrow)
    for b in sqb2:
        a32.drop(b)
    a32.drop(tmp32)

    xnsp = p.sem(f"l2xn{p.nsem()}")

    def xnout(eng, xnsp=xnsp):
        eng.dma_start(out=c.dram_w16[:, 0:16384], in_=xn16).then_inc(xnsp, 16)
    p.emit("sync", xnout)
    xnv = p.bump(xnsp, 16)
    for e in ("tensor", "vector", "scalar", "gpsimd"):
        p.emit(e, lambda eng, v=xnv, s=xnsp: eng.wait_ge(s, v))
    c.a16.drop(c.xn16)
    c.xn16 = None

    emit_l2_scan(c, psems)
    _ckpt(c, "l2a")

    emit_l2_out(c)
    _ckpt(c, "l2o")

    emit_l2_wquant(c)
    emit_l2_ffn(c)
    p.barrier()


def emit_l2_scan(c, psems):
    p = c.p
    a16, a32 = c.a16, c.a32
    F16S = c.dram_f16
    XBOFF, SZOFF, YZOFF = 0, 32768, 65536
    LB = 256
    NB = T // LB
    cw_s = load_small(c, "L2_convw", c.convw_sb[:, 0:64])
    cb_s = load_small(c, "L2_convb", c.bias_sb[:, 0:16],
                      wait=[(c.s_se, p.val(c.s_se)), (c.s_ve, p.val(c.s_ve))])
    al_s = load_small(c, "L2_Alog", c.alog_sb[:, 0:256])
    dcol_s = load_small(c, "L2_D", c.bias_sb[:, 16:32],
                        wait=[(c.s_se, p.val(c.s_se)), (c.s_ve, p.val(c.s_ve))])
    wxp16 = a16.take(16 * 96, "wxp")
    load_w16(c, "L2_Wxp", 16, 96, wxp16)
    wdt16 = a16.take(2048, "wdt")
    load_w16(c, "L2_Wdt", 1, 2048, wdt16, rows=64)
    bdt_s = load_small(c, "L2_bdt", c.bdt_sb[:, 0:16])
    aev = p.val(c.s_dma)

    def amath_s(eng, al_s=al_s):
        eng.wait_ge(c.s_ls, al_s)
        eng.activation(out=c.a32_A[:, 0:256], in_=c.alog_sb[:, 0:256],
                       func=AF.Exp).then_inc(c.s_se, 1)
    p.emit("scalar", amath_s)
    sev = p.bump(c.s_se, 1)

    def amath_v(eng, sev=sev):
        eng.wait_ge(c.s_se, sev)
        eng.tensor_scalar(out=c.a32_A[:, 0:256], in0=c.a32_A[:, 0:256], scalar1=-1.0,
                          scalar2=None, op0=ALU.mult).then_inc(c.s_ve, 1)
    p.emit("vector", amath_v)
    p.bump(c.s_ve, 1)
    p.barrier()

    xbb = a16.take(16 * (LB + 4), "xbb")
    xbr = a16.take(16 * LB, "xbr")
    szb = a16.take(16 * LB, "szb")
    btbc = a16.take(16 * LB, "btbc")
    ctbc = a16.take(16 * LB, "ctbc")
    dbx = a16.take(16 * LB, "dbx")
    h16 = a16.take(16 * LB, "h16")
    yzb = a16.take(16 * LB, "yzb")
    dtl16 = a16.take(LB, "dtl")
    btct = a16.take(LB, "btct")
    dt32 = [a32.take(LB, "dt32a"), a32.take(LB, "dt32b")]
    dA = [a32.take(16 * LB, "dAa"), a32.take(16 * LB, "dAb")]
    sps = [a32.take(LB, "spsa"), a32.take(LB, "spsb")]
    gtree = a32.take(2048, "gtree")
    carry = a32.take(16 * 16, "carry")
    u16 = a16.take(LB, "u16")
    yst = [p.sem(f"l2y{p.nsem()}"), p.sem(f"l2y{p.nsem()}b")]
    xbs = [p.sem(f"l2xb{p.nsem()}"), p.sem(f"l2xb{p.nsem()}b")]
    szs = p.sem(f"l2sz{p.nsem()}")

    def czero(eng):
        eng.memset(carry, 0.0)
        eng.nop().then_inc(c.s_ve, 1)
    p.emit("vector", czero)
    p.bump(c.s_ve, 1)
    p.barrier()
    for lb in range(NB):
        xs_ = xbs[lb % 2]
        xsp = p.val(xs_)

        def xbload(eng, lb=lb, xs_=xs_, xsp=xsp, pv0=p.val(psems[0]), pv1=p.val(psems[1])):
            if lb == 0:
                eng.wait_ge(psems[0], pv0)
                eng.wait_ge(psems[1], pv1)
            if xsp:
                eng.wait_ge(xs_, xsp)
            src = F16S[:, XBOFF:XBOFF + 32768].rearrange("p (o t) -> p o t", t=2048)
            lo = max(lb * LB - 4, 0)
            off = 4 if lb > 0 else 0
            eng.dma_start(out=xbb.rearrange("p (o t) -> p o t", t=LB + 4)[:, :, 4 - off:],
                          in_=src[:, :, lb * LB - off: (lb + 1) * LB]).then_inc(xs_, 16)
        p.emit("sync", xbload)
        xbv = p.bump(xs_, 16)

        def szload(eng, lb=lb, szp=p.val(szs)):
            if szp:
                eng.wait_ge(szs, szp)
            src = F16S[:, SZOFF:SZOFF + 32768].rearrange("p (o t) -> p o t", t=2048)
            eng.dma_start(out=szb.rearrange("p (o t) -> p o t", t=LB),
                          in_=src[:, :, lb * LB:(lb + 1) * LB]).then_inc(szs, 16)
        p.emit("sync", szload)
        szv = p.bump(szs, 16)

        if lb == 0:
            def z0(eng, xbv=xbv):
                eng.wait_ge(xbs[0], xbv)
                eng.memset(xbb.rearrange("p (o t) -> p o t", t=LB + 4)[:, :, 0:4], 0.0)
                eng.nop().then_inc(c.s_ve, 1)
            p.emit("vector", z0)
            p.bump(c.s_ve, 1)

        ve0 = p.val(c.s_ve)
        se_cv0 = p.val(c.s_se)
        for oc in range(16):
            def cnv(eng, oc=oc, lb=lb, xbv=xbv, xs_=xs_, cw_s=cw_s, cb_s=cb_s,
                    se_cv0=se_cv0):
                if oc == 0:
                    eng.wait_ge(xs_, xbv)
                    if lb == 0:
                        eng.wait_ge(c.s_ls, cw_s)
                        eng.wait_ge(c.s_ls, cb_s)
                else:
                    eng.wait_ge(c.s_se, se_cv0 + oc)
                xv = xbb[:, oc * (LB + 4):(oc + 1) * (LB + 4)]
                eng.tensor_scalar(out=c.l2conv32[:, 0:256], in0=xv[:, 1:1 + LB],
                                  scalar1=c.convw_sb[:, oc * 4:oc * 4 + 1], scalar2=None,
                                  op0=ALU.mult)
                for tp in range(1, 4):
                    eng.scalar_tensor_tensor(out=c.l2conv32[:, 0:256], in0=xv[:, 1 + tp:1 + tp + LB],
                                             scalar=c.convw_sb[:, oc * 4 + tp:oc * 4 + tp + 1],
                                             in1=c.l2conv32[:, 0:256], op0=ALU.mult, op1=ALU.add)
                eng.nop().then_inc(c.s_ve, 1)
            p.emit("vector", cnv)
            vv = p.bump(c.s_ve, 1)

            def csil(eng, oc=oc, vv=vv):
                eng.wait_ge(c.s_ve, vv)
                eng.activation(out=xbr[:, oc * LB:(oc + 1) * LB], in_=c.l2conv32[:, 0:256],
                               func=AF.Silu, bias=c.bias_sb[:, oc:oc + 1]).then_inc(c.s_se, 1)
            p.emit("scalar", csil)
            p.bump(c.s_se, 1)
        sev = p.val(c.s_se)

        def xdmm(eng, sev=sev):
            eng.wait_ge(c.s_se, sev)
            for k in range(16):
                eng.matmul(out=c.ps2[0:96, 0:LB], lhsT=wxp16[:, k * 96:(k + 1) * 96],
                           rhs=xbr[:, k * LB:(k + 1) * LB], start=(k == 0), stop=(k == 15))
            eng.nop().then_inc(c.s_te, 1)
        p.emit("tensor", xdmm)
        tv = p.bump(c.s_te, 1)

        def xdev(eng, tv=tv):
            eng.wait_ge(c.s_te, tv)
            eng.tensor_copy(out=dtl16[0:64, 0:LB], in_=c.ps2[0:64, 0:LB])
            eng.tensor_copy(out=btct[64:96, 0:LB], in_=c.ps2[64:96, 0:LB]).then_inc(c.s_ve, 1)
        p.emit("vector", xdev)
        vev = p.bump(c.s_ve, 1)

        bcs = c.s_bc
        BCOFF = 114688 + (lb % 2) * LB

        def bc_st(eng, vev=vev, bcs=bcs, BCOFF=BCOFF, prevtot=p.val(bcs)):
            eng.wait_ge(c.s_ve, vev)
            if prevtot:
                eng.wait_ge(bcs, prevtot)
            eng.dma_start(out=c.dram_f16[0:32, BCOFF:BCOFF + LB],
                          in_=btct[64:96, 0:LB]).then_inc(bcs, 16)
        p.emit("sync", bc_st)
        sv_ = p.bump(bcs, 16)

        def bc_rd(eng, bcs=bcs, sv_=sv_, BCOFF=BCOFF):
            eng.wait_ge(bcs, sv_)
            for n in range(16):
                eng.dma_start(out=btbc[:, n * LB:(n + 1) * LB],
                              in_=c.dram_f16[n:n + 1, BCOFF:BCOFF + LB].broadcast_to((128, LB))).then_inc(bcs, 16)
                eng.dma_start(out=ctbc[:, n * LB:(n + 1) * LB],
                              in_=c.dram_f16[16 + n:17 + n, BCOFF:BCOFF + LB].broadcast_to((128, LB))).then_inc(bcs, 16)
        p.emit("sync", bc_rd)
        vbc_sem, vbc = bcs, p.bump(bcs, 16 * 32)
        p.emit("sync", lambda eng, bcs=bcs, vbc=vbc: eng.wait_ge(bcs, vbc))

        se_sc = p.val(c.s_se)
        sp1_se = {}
        dAe_se = {}
        sp2_ve = {}
        scn_ve = {}

        def emit_dt(i):
            sl = i % 2
            dts, dAs, sps_ = dt32[sl], dA[sl], sps[sl]

            def dtmm(eng, i=i, vbc=vbc, vbc_sem=vbc_sem, bdt_s=bdt_s, lb=lb,
                     sw=sp1_se.get(i - 1, 0)):
                if i == 0:
                    eng.wait_ge(vbc_sem, vbc)
                    if lb == 0:
                        eng.wait_ge(c.s_ls, bdt_s)
                if sw:
                    eng.wait_ge(c.s_se, sw)
                eng.matmul(out=c.ps2[:, 0:LB], lhsT=wdt16[0:64, i * 128:(i + 1) * 128],
                           rhs=dtl16[0:64, 0:LB], start=True, stop=True)
                eng.nop().then_inc(c.s_te, 1)
            p.emit("tensor", dtmm)
            tv = p.bump(c.s_te, 1)

            def sp1(eng, i=i, tv=tv, dts=dts, sps_=sps_, v2=scn_ve.get(i - 2, 0)):
                eng.wait_ge(c.s_te, tv)
                if v2:
                    eng.wait_ge(c.s_ve, v2)
                eng.activation(out=dts, in_=c.ps2[:, 0:LB], func=AF.Identity,
                               bias=c.bdt_sb[:, i:i + 1])
                eng.activation(out=sps_, in_=dts, func=AF.Abs)
                eng.activation(out=sps_, in_=sps_, func=AF.Exp, scale=-1.0)
                eng.activation(out=sps_, in_=sps_, func=AF.Ln,
                               bias=c.cc(1.0)).then_inc(c.s_se, 1)
            p.emit("scalar", sp1)
            sp1_se[i] = p.bump(c.s_se, 1)

            def sp2(eng, i=i, dts=dts, sps_=sps_, sw=sp1_se[i]):
                eng.wait_ge(c.s_se, sw)
                eng.scalar_tensor_tensor(out=dts, in0=dts, scalar=0.0, in1=sps_,
                                         op0=ALU.max, op1=ALU.add).then_inc(c.s_ve, 1)
            p.emit("vector", sp2)
            sp2_ve[i] = p.bump(c.s_ve, 1)

            def dAe(eng, i=i, dts=dts, dAs=dAs, vw=sp2_ve[i], v2=scn_ve.get(i - 2, 0)):
                eng.wait_ge(c.s_ve, vw)
                if v2:
                    eng.wait_ge(c.s_ve, v2)
                for n in range(16):
                    eng.activation(out=dAs[:, n * LB:(n + 1) * LB], in_=dts, func=AF.Exp,
                                   scale=c.a32_A[:, i * 16 + n: i * 16 + n + 1])
                eng.nop().then_inc(c.s_se, 1)
            p.emit("scalar", dAe)
            dAe_se[i] = p.bump(c.s_se, 1)

        def emit_scn(oc):
            sl = oc % 2
            dts, dAs = dt32[sl], dA[sl]

            def scn(eng, oc=oc, lb=lb, dts=dts, dAs=dAs, sw=dAe_se[oc],
                    dcol_s=dcol_s, szv=szv,
                    yps=yst[(lb + 1) % 2], ypv=p.val(yst[(lb + 1) % 2])):
                eng.wait_ge(c.s_se, sw)
                if oc == 0:
                    eng.wait_ge(szs, szv)
                    if ypv:
                        eng.wait_ge(yps, ypv)
                    if lb == 0:
                        eng.wait_ge(c.s_ls, dcol_s)
                eng.tensor_tensor(out=u16, in0=dts, in1=xbr[:, oc * LB:(oc + 1) * LB],
                                  op=ALU.mult)
                eng.tensor_tensor(out=dbx.rearrange("p (n t) -> p n t", t=LB),
                                  in0=btbc.rearrange("p (n t) -> p n t", t=LB),
                                  in1=u16.unsqueeze(1).broadcast_to((128, 16, LB)),
                                  op=ALU.mult)
                for n in range(16):
                    eng.tensor_tensor_scan(out=h16[:, n * LB:(n + 1) * LB],
                                           data0=dAs[:, n * LB:(n + 1) * LB],
                                           data1=dbx[:, n * LB:(n + 1) * LB],
                                           initial=(0.0 if lb == 0 else carry[:, oc * 16 + n: oc * 16 + n + 1]),
                                           op0=ALU.mult, op1=ALU.add)
                eng.tensor_copy(out=carry[:, oc * 16:(oc + 1) * 16],
                                in_=h16.rearrange("p (n t) -> p n t", t=LB)[:, :, LB - 1:LB])
                eng.tensor_tensor(out=dbx, in0=h16, in1=ctbc, op=ALU.mult)
                eng.tensor_tensor(out=gtree[:, 0:2048], in0=dbx[:, 0:2048],
                                  in1=dbx[:, 2048:4096], op=ALU.add)
                eng.tensor_tensor(out=gtree[:, 0:1024], in0=gtree[:, 0:1024],
                                  in1=gtree[:, 1024:2048], op=ALU.add)
                eng.tensor_tensor(out=gtree[:, 0:512], in0=gtree[:, 0:512],
                                  in1=gtree[:, 512:1024], op=ALU.add)
                eng.tensor_tensor(out=gtree[:, 0:256], in0=gtree[:, 0:256],
                                  in1=gtree[:, 256:512], op=ALU.add)
                eng.scalar_tensor_tensor(out=gtree[:, 0:256], in0=xbr[:, oc * LB:(oc + 1) * LB],
                                         scalar=c.bias_sb[:, 16 + oc:17 + oc],
                                         in1=gtree[:, 0:256], op0=ALU.mult, op1=ALU.add)
                eng.tensor_tensor(out=yzb[:, oc * LB:(oc + 1) * LB], in0=gtree[:, 0:256],
                                  in1=szb[:, oc * LB:(oc + 1) * LB],
                                  op=ALU.mult).then_inc(c.s_ve, 1)
            p.emit("vector", scn)
            scn_ve[oc] = p.bump(c.s_ve, 1)

        for i in range(16):
            emit_dt(i)
            if i >= 1:
                emit_scn(i - 1)
        emit_scn(15)
        vyz = p.val(c.s_ve)
        ys_ = yst[lb % 2]
        ysp = p.val(ys_)

        def yzout(eng, lb=lb, vyz=vyz, ys_=ys_, ysp=ysp):
            eng.wait_ge(c.s_ve, vyz)
            if ysp:
                eng.wait_ge(ys_, ysp)
            dst = F16S[:, YZOFF:YZOFF + 32768].rearrange("p (o t) -> p o t", t=2048)
            eng.dma_start(out=dst[:, :, lb * LB:(lb + 1) * LB],
                          in_=yzb.rearrange("p (o t) -> p o t", t=LB)).then_inc(ys_, 16)
        p.emit("sync", yzout)
        p.bump(ys_, 16)
        p.barrier()
    c._l2_yst = yst
    y0, y1 = p.val(yst[0]), p.val(yst[1])
    for e in ("tensor", "vector", "scalar", "gpsimd"):
        def yq_(eng, y0=y0, y1=y1):
            eng.wait_ge(yst[0], y0)
            eng.wait_ge(yst[1], y1)
        p.emit(e, yq_)
    for b in (xbb, xbr, szb, btbc, ctbc, dbx, h16, yzb, dtl16, u16, wxp16, wdt16):
        a16.drop(b)
    a16.drop(btct)
    for b in (dt32[0], dt32[1], dA[0], dA[1], sps[0], sps[1], gtree, carry):
        a32.drop(b)


def emit_l2_out(c):
    p = c.p
    a16, a32 = c.a16, c.a32
    F16S = c.dram_f16
    YZOFF, SOFF = 65536, 98304
    c.xn16 = a16.take(NCH * T, "xn16")
    xn16 = c.xn16
    xnsp2 = p.sem(f"l2xn{p.nsem()}")

    def xnin(eng, s=xnsp2):
        eng.dma_start(out=xn16, in_=c.dram_w16[:, 0:16384]).then_inc(s, 16)
    p.emit("sync", xnin)
    xnv2 = p.bump(xnsp2, 16)
    for e in ("tensor", "vector", "scalar", "gpsimd"):
        p.emit(e, lambda eng, v=xnv2, s=xnsp2: eng.wait_ge(s, v))
    wo16 = a16.take(16 * D, "l2wo")
    load_w16(c, "L2_Wout", 16, D, wo16)
    bo_s = load_small(c, "L2_bout", c.bias_sb[:, 0:8],
                      wait=[(c.s_se, p.val(c.s_se)), (c.s_ve, p.val(c.s_ve))])
    yzg = a16.take(16 * 512, "l2yzg")
    sstg = [a16.take(512, "l2ss0"), a16.take(512, "l2ss1")]
    s32 = a32.take(512, "l2s32")
    yzs = p.sem(f"l2yz{p.nsem()}")
    ssem = [p.sem(f"l2s{p.nsem()}"), p.sem(f"l2s{p.nsem()}b")]
    p.barrier()
    for nt in range(4):
        def yzload(eng, nt=nt, yv0=p.val(c._l2_yst[0]), yv1=p.val(c._l2_yst[1]),
                   yzp=p.val(yzs)):
            if nt == 0:
                eng.wait_ge(c._l2_yst[0], yv0)
                eng.wait_ge(c._l2_yst[1], yv1)
            if yzp:
                eng.wait_ge(yzs, yzp)
            src = F16S[:, YZOFF:YZOFF + 32768].rearrange("p (o t) -> p o t", t=2048)
            eng.dma_start(out=yzg.rearrange("p (o t) -> p o t", t=512),
                          in_=src[:, :, nt * 512:(nt + 1) * 512]).then_inc(yzs, 16)
        p.emit("sync", yzload)
        yzv = p.bump(yzs, 16)
        se0 = p.val(c.s_se)
        for och in range(NCH):
            bank = och % 4

            def omm(eng, och=och, bank=bank, yzv=yzv, se0=se0):
                if och == 0:
                    eng.wait_ge(yzs, yzv)
                if och >= 4:
                    eng.wait_ge(c.s_se, se0 + och - 3)
                for k in range(16):
                    eng.matmul(out=c.ps1[:, bank * 512:(bank + 1) * 512],
                               lhsT=wo16[:, k * D + och * 128: k * D + (och + 1) * 128],
                               rhs=yzg[:, k * 512:(k + 1) * 512],
                               start=(k == 0), stop=(k == 15))
                eng.nop().then_inc(c.s_te, 1)
            p.emit("tensor", omm)
            tv = p.bump(c.s_te, 1)

            def oev(eng, och=och, nt=nt, bank=bank, tv=tv, bo_s=bo_s,
                    vprev=p.val(c.s_ve)):
                eng.wait_ge(c.s_te, tv)
                if vprev:
                    eng.wait_ge(c.s_ve, vprev)
                if och == 0 and nt == 0:
                    eng.wait_ge(c.s_ls, bo_s)
                eng.activation(out=s32, in_=c.ps1[:, bank * 512:(bank + 1) * 512],
                               func=AF.Identity, bias=c.bias_sb[:, och:och + 1]).then_inc(c.s_se, 1)
            p.emit("scalar", oev)
            sv = p.bump(c.s_se, 1)
            stg = sstg[och % 2]
            ss_ = ssem[och % 2]
            ssp = p.val(ss_)

            def sadd(eng, och=och, nt=nt, sv=sv, stg=stg, ss_=ss_, ssp=ssp):
                eng.wait_ge(c.s_se, sv)
                if ssp:
                    eng.wait_ge(ss_, ssp)
                eng.tensor_tensor(out=stg[:, 0:512], in0=s32,
                                  in1=t_sl(xn16, och, nt_sl(nt)), op=ALU.add).then_inc(c.s_ve, 1)
            p.emit("vector", sadd)
            vv = p.bump(c.s_ve, 1)

            def sout(eng, och=och, nt=nt, vv=vv, stg=stg, ss_=ss_, ssp=ssp):
                eng.wait_ge(c.s_ve, vv)
                if ssp:
                    eng.wait_ge(ss_, ssp)
                eng.dma_start(out=F16S[:, SOFF + och * 2048 + nt * 512: SOFF + och * 2048 + (nt + 1) * 512],
                              in_=stg[:, 0:512]).then_inc(ss_, 16)
            p.emit("sync", sout)
            p.bump(ss_, 16)
    p.barrier()
    c._l2_ssem = ssem
    s0, s1 = p.val(ssem[0]), p.val(ssem[1])
    for e in ("tensor", "vector", "scalar", "gpsimd"):
        def sq_(eng, s0=s0, s1=s1):
            eng.wait_ge(ssem[0], s0)
            eng.wait_ge(ssem[1], s1)
        p.emit(e, sq_)
    a16.drop(wo16)
    a16.drop(yzg)
    for b in sstg:
        a16.drop(b)
    a32.drop(s32)
    c.a16.drop(c.xn16)
    c.xn16 = None


def emit_l2_wquant(c):
    p = c.p
    a16, a32 = c.a16, c.a32
    id32 = c.cons32[:, OFF_ID:OFF_ID + 128]
    W16 = c.dram_w16
    wtmp = a32.take(1024, "l2wt")
    wq16row = a16.take(1024, "l2wqr")
    wstg = [a16.take(1024, "l2wts0"), a16.take(1024, "l2wts1")]
    wos = [p.sem(f"l2wo{p.nsem()}"), p.sem(f"l2wo{p.nsem()}b")]
    w16l = dict(getattr(c, "_w16_last", None) or {0: None, 1: None})
    for wi, (nm, rows, woff, scol) in enumerate([
            ("L2_gate", 1024, 0, 8), ("L2_up", 1024, 32768, 9), ("L2_down", 4096, 65536, 10)]):
        kch = rows // 128
        cols = 4096 if rows == 1024 else 1024
        npc = cols // 1024
        nelem = float(rows * cols)
        ve0 = p.val(c.s_ve)
        step = 0
        for k in range(kch):
            for pc in range(npc):
                st = c.wstage[step % 2]
                need = ve0 + (step - 1) if step >= 2 else ve0
                sem = c.s_wl[step % 2]
                prev = p.val(sem)

                def fl(eng, k=k, pc=pc, st=st, need=need, sem=sem, prev=prev, nm=nm,
                       xg=(w16l[step % 2] if (step < 2 and wi == 0) else None)):
                    if xg:
                        eng.wait_ge(xg[0], xg[1])
                    if need:
                        eng.wait_ge(c.s_ve, need)
                    if prev:
                        eng.wait_ge(sem, prev)
                    eng.dma_start(out=st[:, 0:1024],
                                  in_=c.dram[nm][k * 128:(k + 1) * 128, pc * 1024:(pc + 1) * 1024]).then_inc(sem, 16)
                p.emit("sync", fl)
                dv = p.bump(sem, 16)

                def red(eng, st=st, dv=dv, sem=sem, first=(step == 0)):
                    eng.wait_ge(sem, dv)
                    eng.tensor_reduce(out=c.stat32[:, 1:2], in_=st[:, 0:1024], axis=AX.X,
                                      op=ALU.add, apply_absolute_value=True)
                    if first:
                        eng.tensor_copy(out=c.stat32[:, 0:1], in_=c.stat32[:, 1:2]).then_inc(c.s_ve, 1)
                    else:
                        eng.tensor_tensor(out=c.stat32[:, 0:1], in0=c.stat32[:, 0:1],
                                          in1=c.stat32[:, 1:2], op=ALU.add).then_inc(c.s_ve, 1)
                p.emit("vector", red)
                p.bump(c.s_ve, 1)
                step += 1
        vev = p.val(c.s_ve)
        ones_col = c.cons32[:, OFF_ONES:OFF_ONES + 1]

        def psum_(eng, vev=vev):
            eng.wait_ge(c.s_ve, vev)
            eng.matmul(out=c.ps2[0:1, 0:1], lhsT=ones_col, rhs=c.stat32[:, 0:1],
                       start=True, stop=True)
            eng.nop().then_inc(c.s_te, 1)
        p.emit("tensor", psum_)
        tv = p.bump(c.s_te, 1)

        def gmath(eng, tv=tv, nelem=nelem):
            eng.wait_ge(c.s_te, tv)
            eng.tensor_scalar(out=c.stat32[0:1, 2:3], in0=c.ps2[0:1, 0:1],
                              scalar1=1.0 / nelem, scalar2=1e-5, op0=ALU.mult, op1=ALU.max)
            eng.reciprocal(out=c.stat32[0:1, 3:4], in_=c.stat32[0:1, 2:3]).then_inc(c.s_ve, 1)
        p.emit("vector", gmath)
        vev = p.bump(c.s_ve, 1)
        tv = bcast_row(c, c.stat32[0:1, 2:4], 2, c.ps2[:, 0:512], wait=(c.s_ve, vev))

        def gbc(eng, tv=tv, scol=scol):
            eng.wait_ge(c.s_te, tv)
            eng.tensor_copy(out=c.stat32[:, scol:scol + 1], in_=c.ps2[:, 0:1])
            eng.tensor_copy(out=c.stat32[:, 16:17], in_=c.ps2[:, 1:2]).then_inc(c.s_ve, 1)
        p.emit("vector", gbc)
        p.bump(c.s_ve, 1)
        p.barrier()

        step = 0
        ve0 = p.val(c.s_ve)
        for k in range(kch):
            for pc in range(npc):
                st = c.wstage[step % 2]
                need = ve0 + (step - 1) if step >= 2 else ve0
                sem = c.s_wl[step % 2]
                prev = p.val(sem)

                def fl2(eng, k=k, pc=pc, st=st, need=need, sem=sem, prev=prev, nm=nm):
                    if need:
                        eng.wait_ge(c.s_ve, need)
                    if prev:
                        eng.wait_ge(sem, prev)
                    eng.dma_start(out=st[:, 0:1024],
                                  in_=c.dram[nm][k * 128:(k + 1) * 128, pc * 1024:(pc + 1) * 1024]).then_inc(sem, 16)
                p.emit("sync", fl2)
                dv = p.bump(sem, 16)
                ws_ = wos[step % 2]
                wsp = p.val(ws_)

                def qz(eng, st=st, dv=dv, sem=sem, ws_=ws_, wsp=wsp, stg2=wstg[step % 2]):
                    eng.wait_ge(sem, dv)
                    if wsp:
                        eng.wait_ge(ws_, wsp)
                    eng.tensor_scalar(out=wtmp, in0=st[:, 0:1024], scalar1=c.stat32[:, 16:17],
                                      scalar2=12582912.0, op0=ALU.mult, op1=ALU.add)
                    eng.tensor_scalar(out=wtmp, in0=wtmp, scalar1=-12582912.0, scalar2=-1.0,
                                      op0=ALU.add, op1=ALU.max)
                    eng.tensor_scalar(out=stg2, in0=wtmp, scalar1=1.0, scalar2=None,
                                      op0=ALU.min).then_inc(c.s_ve, 1)
                p.emit("vector", qz)
                qv = p.bump(c.s_ve, 1)

                def wout(eng, k=k, pc=pc, qv=qv, ws_=ws_, wsp=wsp, stg2=wstg[step % 2],
                         woff=woff, cols=cols, npc=npc):
                    eng.wait_ge(c.s_ve, qv)
                    if wsp:
                        eng.wait_ge(ws_, wsp)
                    off = woff + k * cols + pc * 1024
                    eng.dma_start(out=W16[:, off:off + 1024], in_=stg2).then_inc(ws_, 16)
                p.emit("sync", wout)
                p.bump(ws_, 16)
                step += 1
        p.barrier()
    w0, w1 = p.val(wos[0]), p.val(wos[1])
    for e in ("tensor", "vector", "scalar", "gpsimd"):
        def wq_q(eng, w0=w0, w1=w1):
            eng.wait_ge(wos[0], w0)
            eng.wait_ge(wos[1], w1)
        p.emit(e, wq_q)
    a32.drop(wtmp)
    a16.drop(wq16row)
    for b in wstg:
        a16.drop(b)
    vv_ = p.val(c.s_ve)
    c._w16_last = {0: (c.s_ve, vv_), 1: (c.s_ve, vv_)}
    c._l2_wos = [(wos[0], p.val(wos[0])), (wos[1], p.val(wos[1]))]
    p.barrier()


def emit_l2_ffn(c):
    p = c.p
    a16, a32 = c.a16, c.a32
    F16S = c.dram_f16
    W16 = c.dram_w16
    SOFF, HOFF = 98304, 0
    id16 = c.cons16[:, OFF_ID16:OFF_ID16 + 128]
    id32 = c.cons32[:, OFF_ID:OFF_ID + 128]
    xg = a32.take(8 * 512, "l2xg")
    bigA = a32.take(1024, "l2fA")
    gate32 = a32.take(4 * 512, "l2fg")
    xtm = a32.take(1024, "l2fx")
    sg32 = a32.take(512, "l2fs32")
    vq16 = a16.take(1024, "l2fvq")
    vqT = a16.take(8 * 512, "l2fvqT")
    hq16 = a16.take(512, "l2fhq")
    hqT = a16.take(32 * 512, "l2fhqT")
    sgrp = a16.take(8 * 512, "l2fsg")
    gT = a16.take(8 * 512, "l2fgT")
    h16sb = [a16.take(512, "l2fh16a"), a16.take(512, "l2fh16b")]
    wbuf = [a16.take(4096, "l2fw0"), a16.take(4096, "l2fw1")]
    dbuf = [a16.take(512, "l2fd0"), a16.take(512, "l2fd1")]
    xgs = p.sem(f"l2xg{p.nsem()}")
    sgs = p.sem(f"l2sg{p.nsem()}")
    xos = p.sem(f"l2xo{p.nsem()}")
    wss = [p.sem(f"l2ws{p.nsem()}"), p.sem(f"l2ws{p.nsem()}b")]
    dss = [p.sem(f"l2ds{p.nsem()}"), p.sem(f"l2ds{p.nsem()}b")]
    hss = [p.sem(f"l2hs{p.nsem()}"), p.sem(f"l2hs{p.nsem()}b")]
    hrs = [p.sem(f"l2hr{p.nsem()}"), p.sem(f"l2hr{p.nsem()}b")]
    wg_done = []
    dn_done = []
    hq_done = []
    for grp in range(4):
        def xload(eng, grp=grp, xop=p.val(xos), ssem=c.spill_sem, ssv=c.spill_done):
            if grp == 0:
                eng.wait_ge(ssem, ssv)
            if xop:
                eng.wait_ge(xos, xop)
            src = c.dram_spill.rearrange("(o pp) t -> pp o t", pp=128)
            eng.dma_start(out=xg.rearrange("p (o t) -> p o t", t=512),
                          in_=src[:, :, grp * 512:(grp + 1) * 512]).then_inc(xgs, 16)
        p.emit("sync", xload)
        xgv = p.bump(xgs, 16)

        def sload(eng, grp=grp, sgp=p.val(sgs), sv0=p.val(c._l2_ssem[0]), sv1=p.val(c._l2_ssem[1])):
            if grp == 0:
                eng.wait_ge(c._l2_ssem[0], sv0)
                eng.wait_ge(c._l2_ssem[1], sv1)
            if sgp:
                eng.wait_ge(sgs, sgp)
            src = F16S[:, SOFF:SOFF + 16384].rearrange("p (o t) -> p o t", t=2048)
            eng.dma_start(out=sgrp.rearrange("p (o t) -> p o t", t=512),
                          in_=src[:, :, grp * 512:(grp + 1) * 512]).then_inc(sgs, 16)
        p.emit("sync", sload)
        sgv = p.bump(sgs, 16)

        se0 = p.val(c.s_se)
        ve0 = p.val(c.s_ve)
        for j in range(4):
            def tx(eng, j=j, xgv=xgv, ve0=ve0):
                if j == 0:
                    eng.wait_ge(xgs, xgv)
                if j > 0:
                    eng.wait_ge(c.s_ve, ve0 + 4 * (j - 1) + 1)
                for ch in range(NCH):
                    eng.transpose(out=c.ps2[:, ch * 128:(ch + 1) * 128],
                                  in_=xg[:, ch * 512 + j * 128: ch * 512 + j * 128 + 128],
                                  identity=id32)
                eng.nop().then_inc(c.s_te, 1)
            p.emit("tensor", tx)
            tv = p.bump(c.s_te, 1)

            def ev0(eng, tv=tv, j=j, se0=se0):
                eng.wait_ge(c.s_te, tv)
                if j > 0:
                    eng.wait_ge(c.s_se, se0 + 4 * (j - 1) + 3)
                eng.tensor_scalar(out=bigA[:, 0:1024], in0=c.ps2[:, 0:D], scalar1=1.0,
                                  scalar2=0.0, op0=ALU.mult, op1=ALU.add,
                                  accum_out=c.stat32[:, 20:21]).then_inc(c.s_ve, 1)
            p.emit("vector", ev0)
            vv0 = p.bump(c.s_ve, 1)

            def sq0(eng, vv0=vv0):
                eng.wait_ge(c.s_ve, vv0)
                eng.activation(out=xtm, in_=bigA[:, 0:1024], func=AF.Square,
                               accum_out=c.stat32[:, 21:22]).then_inc(c.s_se, 1)
            p.emit("scalar", sq0)
            sv0 = p.bump(c.s_se, 1)

            def lnst(eng, sv0=sv0):
                eng.wait_ge(c.s_se, sv0)
                eng.tensor_scalar(out=c.stat32[:, 22:23], in0=c.stat32[:, 20:21],
                                  scalar1=1.0 / D, scalar2=None, op0=ALU.mult)
                eng.tensor_scalar(out=c.stat32[:, 23:24], in0=c.stat32[:, 21:22],
                                  scalar1=1.0 / D, scalar2=None, op0=ALU.mult)
                eng.tensor_tensor(out=c.stat32[:, 24:25], in0=c.stat32[:, 22:23],
                                  in1=c.stat32[:, 22:23], op=ALU.mult)
                eng.tensor_tensor(out=c.stat32[:, 23:24], in0=c.stat32[:, 23:24],
                                  in1=c.stat32[:, 24:25], op=ALU.subtract).then_inc(c.s_ve, 1)
            p.emit("vector", lnst)
            vv1 = p.bump(c.s_ve, 1)

            def lnsq(eng, vv1=vv1):
                eng.wait_ge(c.s_ve, vv1)
                eng.activation(out=c.stat32[:, 23:24], in_=c.stat32[:, 23:24], func=AF.Sqrt,
                               bias=c.cc(1e-5)).then_inc(c.s_se, 1)
            p.emit("scalar", lnsq)
            sv1 = p.bump(c.s_se, 1)

            def lnal(eng, sv1=sv1):
                eng.wait_ge(c.s_se, sv1)
                eng.reciprocal(out=c.stat32[:, 24:25], in_=c.stat32[:, 23:24])
                eng.tensor_tensor(out=c.stat32[:, 22:23], in0=c.stat32[:, 22:23],
                                  in1=c.stat32[:, 24:25], op=ALU.mult)
                eng.tensor_scalar(out=c.stat32[:, 22:23], in0=c.stat32[:, 22:23],
                                  scalar1=-1.0, scalar2=None, op0=ALU.mult).then_inc(c.s_ve, 1)
            p.emit("vector", lnal)
            vv2 = p.bump(c.s_ve, 1)

            def lnap(eng, vv2=vv2):
                eng.wait_ge(c.s_ve, vv2)
                eng.activation(out=xtm, in_=bigA[:, 0:1024], func=AF.Identity,
                               bias=c.stat32[:, 22:23], scale=c.stat32[:, 24:25]).then_inc(c.s_se, 1)
            p.emit("scalar", lnap)
            sv2 = p.bump(c.s_se, 1)

            def xnq(eng, sv2=sv2):
                eng.wait_ge(c.s_se, sv2)
                eng.tensor_reduce(out=c.stat32[:, 25:26], in_=xtm, axis=AX.X, op=ALU.max,
                                  apply_absolute_value=True)
                eng.tensor_scalar(out=c.stat32[:, 25:26], in0=c.stat32[:, 25:26],
                                  scalar1=1e-5, scalar2=None, op0=ALU.max)
                eng.reciprocal(out=c.stat32[:, 26:27], in_=c.stat32[:, 25:26])
                eng.tensor_scalar(out=c.stat32[:, 26:27], in0=c.stat32[:, 26:27],
                                  scalar1=127.0, scalar2=None, op0=ALU.mult)
                eng.tensor_scalar(out=c.stat32[:, 25:26], in0=c.stat32[:, 25:26],
                                  scalar1=1.0 / 127.0, scalar2=None, op0=ALU.mult)
                eng.tensor_scalar(out=bigA[:, 0:1024], in0=xtm, scalar1=c.stat32[:, 26:27],
                                  scalar2=12582912.0, op0=ALU.mult, op1=ALU.add)
                eng.tensor_scalar(out=bigA[:, 0:1024], in0=bigA[:, 0:1024], scalar1=-12582912.0,
                                  scalar2=None, op0=ALU.add)
                eng.tensor_scalar(out=vq16, in0=bigA[:, 0:1024], scalar1=c.stat32[:, 25:26],
                                  scalar2=None, op0=ALU.mult).then_inc(c.s_ve, 1)
            p.emit("vector", xnq)
            vq_v = p.bump(c.s_ve, 1)

            def vqt(eng, vq_v=vq_v, j=j, se0=se0):
                eng.wait_ge(c.s_ve, vq_v)
                if j > 0:
                    eng.wait_ge(c.s_se, se0 + 4 * (j - 1) + 4)
                for i in range(NCH):
                    eng.transpose(out=c.psh[:, i * 128:(i + 1) * 128],
                                  in_=vq16[:, i * 128:(i + 1) * 128], identity=id16)
                eng.nop().then_inc(c.s_te, 1)
            p.emit("tensor", vqt)
            tv = p.bump(c.s_te, 1)

            def vqtev(eng, tv=tv, j=j):
                eng.wait_ge(c.s_te, tv)
                for i in range(NCH):
                    eng.activation(out=vqT[:, i * 512 + j * 128: i * 512 + j * 128 + 128],
                                   in_=c.psh[:, i * 128:(i + 1) * 128], func=AF.Copy)
                eng.nop().then_inc(c.s_se, 1)
            p.emit("scalar", vqtev)
            p.bump(c.s_se, 1)
        sevq = p.val(c.s_se)

        def hz(eng, sevq=sevq):
            eng.wait_ge(c.s_se, sevq)
            eng.memset(c.stat32[:, 27:31], 0.0)
            eng.nop().then_inc(c.s_ve, 1)
        p.emit("vector", hz)
        p.bump(c.s_ve, 1)

        for st8 in range(8):
            for gu in range(2):
                ws_ = wss[(st8 * 2 + gu) % 2]
                wsp = p.val(ws_)
                wb = wbuf[(st8 * 2 + gu) % 2]

                def wload(eng, st8=st8, gu=gu, ws_=ws_, wsp=wsp, wb=wb,
                          wg=(wg_done[-2] if len(wg_done) >= 2 else 0),
                          fw=(c._l2_wos + [(s, p.val(s)) for s in c._l2_ssem]
                              if (grp == 0 and st8 == 0 and gu == 0) else None)):
                    if fw:
                        for s_, v_ in fw:
                            if v_:
                                eng.wait_ge(s_, v_)
                    if wsp:
                        eng.wait_ge(ws_, wsp)
                    if wg:
                        eng.wait_ge(c.s_te, wg)
                    woff = gu * 32768
                    srcv = W16[:, woff:woff + 32768].rearrange("p (h r) -> p h r", r=4096)
                    eng.dma_start(out=wb.rearrange("p (h r) -> p h r", r=512),
                                  in_=srcv[:, :, st8 * 512:(st8 + 1) * 512]).then_inc(ws_, 16)
                p.emit("gpsimd", wload)
                wv = p.bump(ws_, 16)

                def wmm(eng, ws_=ws_, wv=wv, wb=wb, sevq=sevq, se1=p.val(c.s_se),
                        ve1=p.val(c.s_ve), first=(st8 == 0 and gu == 0)):
                    eng.wait_ge(ws_, wv)
                    eng.wait_ge(c.s_se, sevq if first else se1)
                    if not first:
                        eng.wait_ge(c.s_ve, ve1)
                    for j in range(4):
                        for k in range(NCH):
                            eng.matmul(out=c.ps1[:, j * 512:(j + 1) * 512],
                                       lhsT=vqT[:, k * 512 + j * 128: k * 512 + j * 128 + 128],
                                       rhs=wb[:, k * 512:(k + 1) * 512],
                                       start=(k == 0), stop=(k == NCH - 1))
                    eng.nop().then_inc(c.s_te, 1)
                p.emit("tensor", wmm)
                tv = p.bump(c.s_te, 1)
                wg_done.append(tv)

                if gu == 0:
                    def gev(eng, tv=tv, ve2=p.val(c.s_ve)):
                        eng.wait_ge(c.s_te, tv)
                        eng.wait_ge(c.s_ve, ve2)
                        eng.activation(out=gate32, in_=c.ps1[:, 0:2048], func=AF.Silu,
                                       scale=c.stat32[:, 8:9]).then_inc(c.s_se, 1)
                    p.emit("scalar", gev)
                    p.bump(c.s_se, 1)
                else:
                    for j in range(4):
                        hs_ = hss[j % 2]
                        hsp = p.val(hs_)

                        def hj(eng, j=j, tv=tv, se2=p.val(c.s_se), hs_=hs_, hsp=hsp, hbuf=h16sb[j % 2]):
                            eng.wait_ge(c.s_te, tv)
                            if j == 0:
                                eng.wait_ge(c.s_se, se2)
                            if hsp:
                                eng.wait_ge(hs_, hsp)
                            eng.scalar_tensor_tensor(out=hbuf, in0=c.ps1[:, j * 512:(j + 1) * 512],
                                                     scalar=c.stat32[:, 9:10],
                                                     in1=gate32[:, j * 512:(j + 1) * 512],
                                                     op0=ALU.mult, op1=ALU.mult)
                            eng.tensor_reduce(out=c.stat32[:, 31:32], in_=hbuf, axis=AX.X,
                                              op=ALU.max, apply_absolute_value=True)
                            eng.tensor_tensor(out=c.stat32[:, 27 + j:28 + j],
                                              in0=c.stat32[:, 27 + j:28 + j],
                                              in1=c.stat32[:, 31:32], op=ALU.max).then_inc(c.s_ve, 1)
                        p.emit("vector", hj)
                        hv = p.bump(c.s_ve, 1)

                        def hout(eng, j=j, st8=st8, grp=grp, hv=hv, hs_=hs_, hsp=hsp, hbuf=h16sb[j % 2]):
                            eng.wait_ge(c.s_ve, hv)
                            if hsp:
                                eng.wait_ge(hs_, hsp)
                            off = HOFF + j * 4096 + st8 * 512
                            eng.dma_start(out=F16S[:, off:off + 512], in_=hbuf).then_inc(hs_, 16)
                        p.emit("gpsimd", hout)
                        p.bump(hs_, 16)
        vevh = p.val(c.s_ve)
        h0, h1 = p.val(hss[0]), p.val(hss[1])

        def hsc(eng, vevh=vevh):
            eng.wait_ge(c.s_ve, vevh)
            for j in range(4):
                eng.tensor_scalar(out=c.stat32[:, 27 + j:28 + j], in0=c.stat32[:, 27 + j:28 + j],
                                  scalar1=1e-5, scalar2=None, op0=ALU.max)
            eng.reciprocal(out=c.stat32[:, 31:32], in_=c.stat32[:, 27:28])
            eng.tensor_scalar(out=c.stat32[:, 31:32], in0=c.stat32[:, 31:32], scalar1=127.0,
                              scalar2=None, op0=ALU.mult).then_inc(c.s_ve, 1)
        p.emit("vector", hsc)
        p.bump(c.s_ve, 1)

        se3 = p.val(c.s_se)
        hqi = 0
        hqt_done = []
        for j in range(4):
            for st8 in range(8):
                hr_ = hrs[hqi % 2]
                hrp = p.val(hr_)

                def hback(eng, j=j, st8=st8, hr_=hr_, hrp=hrp, h0=h0, h1=h1, hqi=hqi,
                          hg=(hqt_done[-1] if hqt_done else 0)):
                    if hqi == 0:
                        eng.wait_ge(hss[0], h0)
                        eng.wait_ge(hss[1], h1)
                    if hrp:
                        eng.wait_ge(hr_, hrp)
                    if hg:
                        eng.wait_ge(c.s_te, hg)
                    off = HOFF + j * 4096 + st8 * 512
                    eng.dma_start(out=hq16, in_=F16S[:, off:off + 512]).then_inc(hr_, 16)
                p.emit("gpsimd", hback)
                hrv = p.bump(hr_, 16)

                def hqz(eng, j=j, hrv=hrv, hr_=hr_, vevh2=p.val(c.s_ve)):
                    eng.wait_ge(hr_, hrv)
                    if True:
                        eng.wait_ge(c.s_ve, vevh2)
                    eng.reciprocal(out=c.stat32[:, 31:32], in_=c.stat32[:, 27 + j:28 + j])
                    eng.tensor_scalar(out=bigA[:, 0:512], in0=hq16,
                                      scalar1=c.stat32[:, 31:32], scalar2=127.0,
                                      op0=ALU.mult, op1=ALU.mult)
                    eng.tensor_scalar(out=bigA[:, 0:512], in0=bigA[:, 0:512],
                                      scalar1=12582912.0, scalar2=-12582912.0,
                                      op0=ALU.add, op1=ALU.add)
                    eng.tensor_scalar(out=c.stat32[:, 31:32], in0=c.stat32[:, 27 + j:28 + j],
                                      scalar1=1.0 / 127.0, scalar2=None, op0=ALU.mult)
                    eng.tensor_scalar(out=hq16, in0=bigA[:, 0:512], scalar1=c.stat32[:, 31:32],
                                      scalar2=None, op0=ALU.mult).then_inc(c.s_ve, 1)
                p.emit("vector", hqz)
                hqv = p.bump(c.s_ve, 1)
                hq_done.append(hqv)

                def hqt(eng, hqv=hqv, se4=p.val(c.s_se)):
                    eng.wait_ge(c.s_ve, hqv)
                    eng.wait_ge(c.s_se, se4)
                    for i in range(4):
                        eng.transpose(out=c.psh[:, i * 128:(i + 1) * 128],
                                      in_=hq16[:, i * 128:(i + 1) * 128], identity=id16)
                    eng.nop().then_inc(c.s_te, 1)
                p.emit("tensor", hqt)
                tv2 = p.bump(c.s_te, 1)
                hqt_done.append(tv2)

                def hqtev(eng, tv2=tv2, j=j, st8=st8):
                    eng.wait_ge(c.s_te, tv2)
                    for i in range(4):
                        ic = st8 * 4 + i
                        eng.activation(out=hqT[:, ic * 512 + j * 128: ic * 512 + j * 128 + 128],
                                       in_=c.psh[:, i * 128:(i + 1) * 128], func=AF.Copy)
                    eng.nop().then_inc(c.s_se, 1)
                p.emit("scalar", hqtev)
                p.bump(c.s_se, 1)
                hqi += 1
        sevh2 = p.val(c.s_se)

        for oh in range(2):
            for k in range(32):
                ds_ = dss[k % 2]
                dsp = p.val(ds_)
                db = dbuf[k % 2]

                def dload(eng, k=k, oh=oh, ds_=ds_, dsp=dsp, db=db,
                          dg=(dn_done[-2] if len(dn_done) >= 2 else 0),
                          fw=(c._l2_wos + [(s, p.val(s)) for s in c._l2_ssem]
                              if (grp == 0 and oh == 0 and k == 0) else None)):
                    if fw:
                        for s_, v_ in fw:
                            if v_:
                                eng.wait_ge(s_, v_)
                    if dsp:
                        eng.wait_ge(ds_, dsp)
                    if dg:
                        eng.wait_ge(c.s_te, dg)
                    off = 65536 + k * 1024 + oh * 512
                    eng.dma_start(out=db, in_=W16[:, off:off + 512]).then_inc(ds_, 16)
                p.emit("gpsimd", dload)
                dv = p.bump(ds_, 16)

                def dmm(eng, k=k, ds_=ds_, dv=dv, db=db, sevh2=sevh2, se4=p.val(c.s_se)):
                    eng.wait_ge(ds_, dv)
                    if k == 0:
                        eng.wait_ge(c.s_se, se4)
                    for oo in range(4):
                        eng.matmul(out=c.ps1[:, oo * 512:(oo + 1) * 512],
                                   lhsT=db[:, oo * 128:(oo + 1) * 128],
                                   rhs=hqT[:, k * 512:(k + 1) * 512],
                                   start=(k == 0), stop=(k == 31))
                    eng.nop().then_inc(c.s_te, 1)
                p.emit("tensor", dmm)
                dn_done.append(p.bump(c.s_te, 1))
            tvd = p.val(c.s_te)
            for oo in range(4):
                och = oh * 4 + oo

                def dsil(eng, oo=oo, och=och, tvd=tvd):
                    eng.wait_ge(c.s_te, tvd)
                    eng.activation(out=gT[:, och * 512:(och + 1) * 512],
                                   in_=c.ps1[:, oo * 512:(oo + 1) * 512],
                                   func=AF.Silu, scale=c.stat32[:, 10:11])
                    eng.nop().then_inc(c.s_se, 1)
                p.emit("scalar", dsil)
                p.bump(c.s_se, 1)
        sevd = p.val(c.s_se)
        for och in range(NCH):
            def comb(eng, och=och, sevd=sevd, sgv=sgv, xgv=xgv):
                eng.wait_ge(c.s_se, sevd)
                if och == 0:
                    eng.wait_ge(sgs, sgv)
                    eng.wait_ge(xgs, xgv)
                eng.tensor_tensor(out=sg32, in0=sgrp[:, och * 512:(och + 1) * 512],
                                  in1=gT[:, och * 512:(och + 1) * 512], op=ALU.mult)
                eng.tensor_tensor(out=xg[:, och * 512:(och + 1) * 512],
                                  in0=xg[:, och * 512:(och + 1) * 512], in1=sg32,
                                  op=ALU.add).then_inc(c.s_ve, 2)
            p.emit("vector", comb)
            p.bump(c.s_ve, 2)
        vev = p.val(c.s_ve)

        def xstore(eng, grp=grp, vev=vev):
            eng.wait_ge(c.s_ve, vev)
            dst = c.dram_spill.rearrange("(o pp) t -> pp o t", pp=128)
            eng.dma_start(out=dst[:, :, grp * 512:(grp + 1) * 512],
                          in_=xg.rearrange("p (o t) -> p o t", t=512)).then_inc(xos, 16)
        p.emit("sync", xstore)
        p.bump(xos, 16)
        p.barrier()
    for b in (xg, bigA, gate32, xtm, sg32):
        a32.drop(b)
    for b in (vq16, vqT, hq16, hqT, sgrp, gT, h16sb[0], h16sb[1], wbuf[0], wbuf[1], dbuf[0], dbuf[1]):
        a16.drop(b)
    sem = p.sem(f"rsf{p.nsem()}")
    c.x = c.a32.take(NCH * T, "x")
    x = c.x

    def fre(eng, sem=sem, xop=p.val(xos)):
        eng.wait_ge(xos, xop)
        for ch in range(NCH):
            eng.dma_start(out=t_sl(x, ch),
                          in_=c.dram_spill[ch * 128:(ch + 1) * 128, :]).then_inc(sem, 16)
    p.emit("sync", fre)
    v = p.bump(sem, 16 * NCH)
    for e in ("tensor", "vector", "scalar", "gpsimd"):
        p.emit(e, lambda eng, v=v, sem=sem: eng.wait_ge(sem, v))
    p.barrier()



def emit_layer3(c):
    p = c.p
    a16, a32 = c.a16, c.a32
    F16S = c.dram_f16
    QOFF, KOFF, VOFF, YOFF, SOFF, MOFF = 0, 16384, 32768, 49152, 65536, 81920
    id16 = c.cons16[:, OFF_ID16:OFF_ID16 + 128]
    id32 = c.cons32[:, OFF_ID:OFF_ID + 128]
    emit_ln(c, with_xn16=True)
    p.barrier()
    emit_spill_x(c)
    xn16 = c.xn16

    wq16 = a16.take(8 * D, "l3wqkv")
    pstage = [a16.take(1024, "l3ps0"), a16.take(1024, "l3ps1")]
    psems = [p.sem(f"l3p{p.nsem()}"), p.sem(f"l3p{p.nsem()}b")]
    bq_s = load_small(c, "L3_bqkv", c.bias_sb[:, 0:24],
                      wait=[(c.s_se, p.val(c.s_se)), (c.s_ve, p.val(c.s_ve))])
    bvr = a32.take(1024, "l3bvr")
    bvbc = a32.take(1024, "l3bvbc")
    bvr_s = load_small(c, "L3_bv_row", bvr[0:1, 0:D])
    for part in range(3):
        load_w16(c, "L3_wqkv", 8, D, wq16, col_sl=slice(part * D, (part + 1) * D))
        p.barrier()
        if part < 2:
            se0 = p.val(c.s_se)
            idx = 0
            for och in range(NCH):
                for nt in range(4):
                    bank = idx % 4

                    def mmp(eng, och=och, nt=nt, bank=bank, idx=idx, se0=se0):
                        if idx >= 4:
                            eng.wait_ge(c.s_se, se0 + idx - 3)
                        for k in range(NCH):
                            eng.matmul(out=c.ps1[:, bank * 512:(bank + 1) * 512],
                                       lhsT=wq16[:, k * D + och * 128: k * D + (och + 1) * 128],
                                       rhs=t_sl(xn16, k, nt_sl(nt)),
                                       start=(k == 0), stop=(k == NCH - 1))
                        eng.nop().then_inc(c.s_te, 1)
                    p.emit("tensor", mmp)
                    tv = p.bump(c.s_te, 1)
                    stg = pstage[idx % 2]
                    ps_ = psems[idx % 2]
                    pprev = p.val(ps_)

                    def pev(eng, och=och, bank=bank, tv=tv, stg=stg, ps_=ps_, pprev=pprev,
                            part=part, first=(part == 0 and idx == 0), bq_s=bq_s):
                        eng.wait_ge(c.s_te, tv)
                        if first:
                            eng.wait_ge(c.s_ls, bq_s)
                        if pprev:
                            eng.wait_ge(ps_, pprev)
                        eng.activation(out=stg[:, 0:512], in_=c.ps1[:, bank * 512:(bank + 1) * 512],
                                       func=AF.Identity,
                                       bias=c.bias_sb[:, part * 8 + och: part * 8 + och + 1],
                                       scale=(0.125 if part == 0 else 1.0)).then_inc(c.s_se, 1)
                    p.emit("scalar", pev)
                    sv = p.bump(c.s_se, 1)

                    def pout(eng, och=och, nt=nt, stg=stg, sv=sv, part=part, ps_=ps_, pprev=pprev):
                        eng.wait_ge(c.s_se, sv)
                        if pprev:
                            eng.wait_ge(ps_, pprev)
                        off = (QOFF if part == 0 else KOFF) + och * 2048 + nt * 512
                        eng.dma_start(out=F16S[:, off:off + 512], in_=stg[:, 0:512]).then_inc(ps_, 16)
                    p.emit("sync", pout)
                    p.bump(ps_, 16)
                    idx += 1
        else:
            tvb = bcast_row(c, bvr[0:1, 0:D], D, c.ps2, wait=(c.s_ls, bvr_s))

            def bvev(eng, tvb=tvb):
                eng.wait_ge(c.s_te, tvb)
                eng.tensor_copy(out=bvbc, in_=c.ps2[:, 0:D]).then_inc(c.s_ve, 1)
            p.emit("vector", bvev)
            p.bump(c.s_ve, 1)
            ve0 = p.val(c.s_ve)
            for tc in range(16):
                def mmv(eng, tc=tc, ve0=ve0):
                    eng.wait_ge(c.s_ve, ve0 + tc)
                    for k in range(NCH):
                        lhs = t_sl(xn16, k, slice(tc * 128, (tc + 1) * 128))
                        for hh in range(2):
                            eng.matmul(out=c.ps2[:, hh * 512:(hh + 1) * 512], lhsT=lhs,
                                       rhs=wq16[:, k * D + hh * 512: k * D + (hh + 1) * 512],
                                       start=(k == 0), stop=(k == NCH - 1))
                    eng.nop().then_inc(c.s_te, 1)
                p.emit("tensor", mmv)
                tv = p.bump(c.s_te, 1)
                stg = pstage[tc % 2]
                ps_ = psems[tc % 2]
                pprev = p.val(ps_)

                def vev(eng, tv=tv, stg=stg, ps_=ps_, pprev=pprev):
                    eng.wait_ge(c.s_te, tv)
                    if pprev:
                        eng.wait_ge(ps_, pprev)
                    eng.tensor_tensor(out=stg[:, 0:1024], in0=c.ps2[:, 0:1024],
                                      in1=bvbc, op=ALU.add).then_inc(c.s_ve, 1)
                p.emit("vector", vev)
                sv = p.bump(c.s_ve, 1)

                def vout(eng, tc=tc, stg=stg, sv=sv, ps_=ps_, pprev=pprev):
                    eng.wait_ge(c.s_ve, sv)
                    if pprev:
                        eng.wait_ge(ps_, pprev)
                    eng.dma_start(out=F16S[:, VOFF + tc * 1024: VOFF + (tc + 1) * 1024],
                                  in_=stg[:, 0:1024]).then_inc(ps_, 16)
                p.emit("sync", vout)
                p.bump(ps_, 16)
    p.barrier()
    a16.drop(wq16)
    for b in pstage:
        a16.drop(b)
    a32.drop(bvr)
    a32.drop(bvbc)
    qk0, qk1 = p.val(psems[0]), p.val(psems[1])
    for e in ("tensor", "vector", "scalar", "gpsimd"):
        def pq_(eng, qk0=qk0, qk1=qk1):
            eng.wait_ge(psems[0], qk0)
            eng.wait_ge(psems[1], qk1)
        p.emit(e, pq_)
    p.barrier()

    w116 = a16.take(8 * 512, "l3w1")
    load_w16(c, "L3_W1", 8, 512, w116)
    b1_s = load_small(c, "L3_b1", c.bias_sb[:, 24:28],
                      wait=[(c.s_se, p.val(c.s_se))])
    sw2 = a32.take(32, "l3w2s")
    w216 = a16.take(32, "l3w2")
    mrow16 = a16.take(T, "l3mrow")
    hbuf = a16.take(2048, "l3hbuf")
    lg32 = a32.take(512, "l3lg")
    lgt32 = a32.take(32, "l3lgt")
    b2bc = a32.take(8, "l3b2bc")
    mt16 = a16.take(8, "l3mt")
    w2s = p.sem(f"l3w2{p.nsem()}")

    def w2load(eng, w2s=w2s, ssem=c.spill_sem, ssv=c.spill_done):
        eng.wait_ge(ssem, ssv)
        eng.dma_start(out=sw2.rearrange("p (cc e) -> p cc e", e=8),
                      in_=c.dram["L3_W2"].rearrange("(cc p) e -> p cc e", p=128)).then_inc(w2s, 16)
    p.emit("sync", w2load)
    w2v = p.bump(w2s, 16)

    def w2cv(eng, w2v=w2v, w2s=w2s):
        eng.wait_ge(w2s, w2v)
        eng.tensor_copy(out=w216, in_=sw2).then_inc(c.s_ve, 1)
    p.emit("vector", w2cv)
    p.bump(c.s_ve, 1)
    b2r = a32.take(8, "l3b2r")
    b2_s = load_small(c, "L3_b2", b2r[0:1, 0:8])
    tvb = bcast_row(c, b2r[0:1, 0:8], 8, c.ps2[:, 0:512], wait=(c.s_ls, b2_s))

    def b2ev(eng, tvb=tvb):
        eng.wait_ge(c.s_te, tvb)
        eng.tensor_copy(out=b2bc, in_=c.ps2[:, 0:8]).then_inc(c.s_ve, 1)
    p.emit("vector", b2ev)
    p.bump(c.s_ve, 1)
    p.barrier()
    for nt in range(4):
        se0 = p.val(c.s_se)
        for fch in range(4):
            def rmm(eng, nt=nt, fch=fch, se0=se0, tprev=p.val(c.s_te)):
                if fch == 0:
                    eng.wait_ge(c.s_te, tprev)
                    eng.wait_ge(c.s_se, se0)
                for k in range(NCH):
                    eng.matmul(out=c.ps1[:, fch * 512:(fch + 1) * 512],
                               lhsT=w116[:, k * 512 + fch * 128: k * 512 + (fch + 1) * 128],
                               rhs=t_sl(xn16, k, nt_sl(nt)),
                               start=(k == 0), stop=(k == NCH - 1))
                eng.nop().then_inc(c.s_te, 1)
            p.emit("tensor", rmm)
            tv = p.bump(c.s_te, 1)

            def rev(eng, fch=fch, tv=tv, b1_s=b1_s, first=(nt == 0 and fch == 0)):
                eng.wait_ge(c.s_te, tv)
                if first:
                    eng.wait_ge(c.s_ls, b1_s)
                eng.activation(out=hbuf[:, fch * 512:(fch + 1) * 512],
                               in_=c.ps1[:, fch * 512:(fch + 1) * 512],
                               func=AF.Relu,
                               bias=c.bias_sb[:, 24 + fch:25 + fch]).then_inc(c.s_se, 1)
            p.emit("scalar", rev)
            p.bump(c.s_se, 1)
        sevh = p.val(c.s_se)

        def lmm(eng, sevh=sevh, vprev=p.val(c.s_ve)):
            eng.wait_ge(c.s_se, sevh)
            eng.wait_ge(c.s_ve, vprev)
            for fch in range(4):
                eng.matmul(out=c.ps2[0:8, 0:512], lhsT=w216[:, fch * 8:(fch + 1) * 8],
                           rhs=hbuf[:, fch * 512:(fch + 1) * 512],
                           start=(fch == 0), stop=(fch == 3))
            eng.nop().then_inc(c.s_te, 1)
        p.emit("tensor", lmm)
        tv = p.bump(c.s_te, 1)

        def levac(eng, tv=tv):
            eng.wait_ge(c.s_te, tv)
            eng.tensor_copy(out=lg32[0:8, 0:512], in_=c.ps2[0:8, 0:512]).then_inc(c.s_ve, 1)
        p.emit("vector", levac)
        lv = p.bump(c.s_ve, 1)
        for j in range(4):
            def ltr(eng, j=j, lv=lv, sprev=p.val(c.s_se)):
                if j == 0:
                    eng.wait_ge(c.s_ve, lv)
                    eng.wait_ge(c.s_se, sprev)
                eng.transpose(out=c.ps2[:, 512 + j * 8: 512 + (j + 1) * 8],
                              in_=lg32[0:8, j * 128:(j + 1) * 128],
                              identity=id32[0:8, 0:8])
                eng.nop().then_inc(c.s_te, 1)
            p.emit("tensor", ltr)
            tv2 = p.bump(c.s_te, 1)

            def lsel(eng, j=j, tv2=tv2):
                eng.wait_ge(c.s_te, tv2)
                eng.tensor_tensor(out=lgt32[:, 0:8], in0=c.ps2[:, 512 + j * 8: 512 + (j + 1) * 8],
                                  in1=b2bc, op=ALU.add)
                eng.tensor_reduce(out=c.stat32[:, 0:1], in_=lgt32[:, 0:8], axis=AX.X,
                                  op=ALU.max)
                eng.tensor_scalar(out=mt16, in0=lgt32[:, 0:8], scalar1=c.stat32[:, 0:1],
                                  scalar2=None, op0=ALU.is_equal).then_inc(c.s_ve, 1)
            p.emit("vector", lsel)
            mv = p.bump(c.s_ve, 1)

            def mtr(eng, j=j, mv=mv):
                eng.wait_ge(c.s_ve, mv)
                eng.transpose(out=c.psh[0:8, j * 128:(j + 1) * 128], in_=mt16[:, 0:8],
                              identity=id16)
                eng.nop().then_inc(c.s_te, 1)
            p.emit("tensor", mtr)
            tv3 = p.bump(c.s_te, 1)

            def mev(eng, j=j, nt=nt, tv3=tv3):
                eng.wait_ge(c.s_te, tv3)
                eng.activation(out=mrow16[0:8, nt * 512 + j * 128: nt * 512 + (j + 1) * 128],
                               in_=c.psh[0:8, j * 128:(j + 1) * 128],
                               func=AF.Copy).then_inc(c.s_se, 1)
            p.emit("scalar", mev)
            p.bump(c.s_se, 1)
    sevm = p.val(c.s_se)
    mds = p.sem(f"l3md{p.nsem()}")

    def mdma(eng, sevm=sevm, mds=mds):
        eng.wait_ge(c.s_se, sevm)
        eng.dma_start(out=F16S[0:8, MOFF:MOFF + T], in_=mrow16[0:8, 0:T]).then_inc(mds, 16)
    p.emit("sync", mdma)
    mdv = p.bump(mds, 16)
    p.barrier()
    a16.drop(w116)
    a16.drop(hbuf)
    a16.drop(w216)
    a16.drop(mt16)
    a32.drop(sw2)
    a32.drop(lg32)
    a32.drop(lgt32)
    a32.drop(b2bc)
    a32.drop(b2r)
    p.barrier()
    _ckpt(c, "l3r")

    qkbuf = [a16.take(5120, "l3qk0"), a16.take(5120, "l3qk1")]
    S16 = a16.take(2048, "l3S")
    pt16 = a16.take(2048, "l3pt")
    ctx16 = a16.take(64, "l3ctx")
    ysm = [a16.take(128, "l3y0"), a16.take(128, "l3y1")]
    qks = [p.sem(f"l3qk{p.nsem()}"), p.sem(f"l3qk{p.nsem()}b")]
    ysem = [p.sem(f"l3ys{p.nsem()}"), p.sem(f"l3ys{p.nsem()}b")]
    qk_done = {}
    last_te_of_head = {}

    def emit_qkvload(h):
        sl = h % 2
        sem = qks[sl]
        prev = p.val(sem)
        teg = last_te_of_head.get(h - 2, 0)

        def qkl(eng, h=h, sem=sem, prev=prev, teg=teg):
            if teg:
                eng.wait_ge(c.s_te, teg)
            if prev:
                eng.wait_ge(sem, prev)
            qb = qkbuf[h % 2]
            r0 = (h % 2) * 64
            ch = h // 2
            eng.dma_start(out=qb[0:64, 0:2048],
                          in_=F16S[r0:r0 + 64, QOFF + ch * 2048: QOFF + (ch + 1) * 2048]).then_inc(sem, 16)
            eng.dma_start(out=qb[0:64, 2048:4096],
                          in_=F16S[r0:r0 + 64, KOFF + ch * 2048: KOFF + (ch + 1) * 2048]).then_inc(sem, 16)
            eng.dma_start(out=qb.rearrange("p (q d) -> p q d", d=64)[:, 64:80, :],
                          in_=F16S[:, VOFF:VOFF + 16384].rearrange("p (tt d) -> p tt d", d=1024)[:, :, h * 64:(h + 1) * 64]).then_inc(sem, 16)
        p.emit("sync", qkl)
        qk_done[h] = p.bump(sem, 48)

    emit_qkvload(0)
    emit_qkvload(1)
    prev_sev = 0
    prev_smask = 0
    prev_yev = 0
    for h in range(16):
        qb = qkbuf[h % 2]
        qh = qb[0:64, 0:2048]
        kh = qb[0:64, 2048:4096]
        vh = qb[:, 4096:5120]
        for tq in range(16):
            nb = tq // 4 + 1
            sb = (tq % 2) * 4

            def smm(eng, h=h, tq=tq, nb=nb, qh=qh, kh=kh,
                    qkv=(qk_done[h] if tq == 0 else 0), sem=qks[h % 2],
                    ps_=prev_sev, pm_=prev_smask):
                if qkv:
                    eng.wait_ge(sem, qkv)
                if ps_:
                    eng.wait_ge(c.s_se, ps_)
                if pm_:
                    eng.wait_ge(c.s_ve, pm_)
                for b in range(nb):
                    eng.matmul(out=c.ps1[:, b * 512:(b + 1) * 512],
                               lhsT=qh[:, tq * 128:(tq + 1) * 128],
                               rhs=kh[:, b * 512:(b + 1) * 512],
                               start=True, stop=True)
                eng.nop().then_inc(c.s_te, 1)
            p.emit("tensor", smm)
            tv = p.bump(c.s_te, 1)

            sev_v = 0
            if nb > 1:
                def sev(eng, nb=nb, tv=tv):
                    eng.wait_ge(c.s_te, tv)
                    eng.activation(out=S16[:, 0:(nb - 1) * 512],
                                   in_=c.ps1[:, 0:(nb - 1) * 512],
                                   func=AF.Copy).then_inc(c.s_se, 1)
                p.emit("scalar", sev)
                sev_v = p.bump(c.s_se, 1)

            def smask(eng, nb=nb, tq=tq, tv=tv, sev_v=sev_v, sb=sb):
                eng.wait_ge(c.s_te, tv)
                mk = c.cons16[:, OFF_MASK16 + (tq % 4) * 512: OFF_MASK16 + (tq % 4 + 1) * 512]
                eng.scalar_tensor_tensor(out=S16[:, (nb - 1) * 512: nb * 512],
                                         in0=c.ps1[:, (nb - 1) * 512: nb * 512],
                                         scalar=1.0, in1=mk, op0=ALU.mult, op1=ALU.add)
                if sev_v:
                    eng.wait_ge(c.s_se, sev_v)
                eng.tensor_reduce(out=c.stat32[:, sb:sb + 1], in_=S16[:, 0:nb * 512], axis=AX.X,
                                  op=ALU.max)
                eng.tensor_scalar(out=c.stat32[:, sb + 1:sb + 2], in0=c.stat32[:, sb:sb + 1],
                                  scalar1=-1.0, scalar2=None, op0=ALU.mult).then_inc(c.s_ve, 1)
            p.emit("vector", smask)
            vv = p.bump(c.s_ve, 1)
            smask_v = vv

            def sexp(eng, nb=nb, vv=vv, sb=sb):
                eng.wait_ge(c.s_ve, vv)
                eng.activation(out=S16[:, 0:nb * 512], in_=S16[:, 0:nb * 512], func=AF.Exp,
                               bias=c.stat32[:, sb + 1:sb + 2],
                               accum_out=c.stat32[:, sb + 2:sb + 3]).then_inc(c.s_se, 1)
            p.emit("scalar", sexp)
            sv = p.bump(c.s_se, 1)
            sexp_v = sv

            def srec(eng, sv=sv, sb=sb):
                eng.wait_ge(c.s_se, sv)
                eng.reciprocal(out=c.stat32[:, sb + 3:sb + 4],
                               in_=c.stat32[:, sb + 2:sb + 3]).then_inc(c.s_ve, 1)
            p.emit("vector", srec)
            rv = p.bump(c.s_ve, 1)

            def ptr(eng, nb=nb, sexp_v=sexp_v, py_=prev_yev):
                eng.wait_ge(c.s_se, sexp_v)
                if py_:
                    eng.wait_ge(c.s_se, py_)
                for k in range(nb * 4):
                    eng.transpose(out=c.psh[:, k * 128:(k + 1) * 128],
                                  in_=S16[:, k * 128:(k + 1) * 128], identity=id16)
                eng.nop().then_inc(c.s_te, 1)
            p.emit("tensor", ptr)
            tv = p.bump(c.s_te, 1)

            def ptev(eng, nb=nb, tv=tv):
                eng.wait_ge(c.s_te, tv)
                eng.activation(out=pt16[:, 0:nb * 512], in_=c.psh[:, 0:nb * 512],
                               func=AF.Copy).then_inc(c.s_se, 1)
            p.emit("scalar", ptev)
            pv = p.bump(c.s_se, 1)

            def avmm(eng, nb=nb, vh=vh, pv=pv):
                eng.wait_ge(c.s_se, pv)
                for k in range(nb * 4):
                    eng.matmul(out=c.ps2[:, 0:64], lhsT=pt16[:, k * 128:(k + 1) * 128],
                               rhs=vh[:, k * 64:(k + 1) * 64],
                               start=(k == 0), stop=(k == nb * 4 - 1))
                eng.nop().then_inc(c.s_te, 1)
            p.emit("tensor", avmm)
            tv = p.bump(c.s_te, 1)

            def cev(eng, tv=tv, rvv=rv, sb=sb):
                eng.wait_ge(c.s_te, tv)
                eng.wait_ge(c.s_ve, rvv)
                eng.activation(out=ctx16, in_=c.ps2[:, 0:64], func=AF.Identity,
                               scale=c.stat32[:, sb + 3:sb + 4]).then_inc(c.s_se, 1)
            p.emit("scalar", cev)
            cv = p.bump(c.s_se, 1)

            def ctr(eng, h=h, cv=cv):
                eng.wait_ge(c.s_se, cv)
                r0 = (h % 2) * 64
                eng.transpose(out=c.psh[r0:r0 + 64, 0:128], in_=ctx16[:, 0:64],
                              identity=id16)
                eng.nop().then_inc(c.s_te, 1)
            p.emit("tensor", ctr)
            tv = p.bump(c.s_te, 1)
            prev_sev = sexp_v
            prev_smask = smask_v
            ysl = (h * 16 + tq) % 2
            ys_ = ysem[ysl]
            yprev = p.val(ys_)

            def yev(eng, h=h, tv=tv, ysl=ysl, ys_=ys_, yprev=yprev):
                eng.wait_ge(c.s_te, tv)
                if yprev:
                    eng.wait_ge(ys_, yprev)
                r0 = (h % 2) * 64
                eng.activation(out=ysm[ysl][r0:r0 + 64, 0:128],
                               in_=c.psh[r0:r0 + 64, 0:128],
                               func=AF.Copy).then_inc(c.s_se, 1)
            p.emit("scalar", yev)
            yv = p.bump(c.s_se, 1)
            prev_yev = yv

            def ydma(eng, h=h, tq=tq, yv=yv, ysl=ysl, ys_=ys_, yprev=yprev):
                eng.wait_ge(c.s_se, yv)
                if yprev:
                    eng.wait_ge(ys_, yprev)
                r0 = (h % 2) * 64
                off = YOFF + (h // 2) * 2048 + tq * 128
                eng.dma_start(out=F16S[r0:r0 + 64, off:off + 128],
                              in_=ysm[ysl][r0:r0 + 64, 0:128]).then_inc(ys_, 16)
            p.emit("sync", ydma)
            p.bump(ys_, 16)
        last_te_of_head[h] = p.val(c.s_te)
        if h + 2 < 16:
            emit_qkvload(h + 2)
    p.barrier()
    y0, y1 = p.val(ysem[0]), p.val(ysem[1])
    for e in ("tensor", "vector", "scalar", "gpsimd"):
        def yq_(eng, y0=y0, y1=y1):
            eng.wait_ge(ysem[0], y0)
            eng.wait_ge(ysem[1], y1)
        p.emit(e, yq_)
    for b in qkbuf:
        a16.drop(b)
    a16.drop(S16)
    a16.drop(pt16)
    a16.drop(ctx16)
    for b in ysm:
        a16.drop(b)
    p.barrier()
    _ckpt(c, "l3a")

    wo16 = a16.take(8 * D, "l3wo")
    load_w16(c, "L3_outw", 8, D, wo16)
    bo_s = load_small(c, "L3_outb", c.bias_sb[:, 0:8],
                      wait=[(c.s_se, p.val(c.s_se)), (c.s_ve, p.val(c.s_ve))])
    yg = a16.take(8 * 512, "l3yg")
    sstg = [a16.take(512, "l3so0"), a16.take(512, "l3so1")]
    ygs = p.sem(f"l3yg{p.nsem()}")
    ssem = [p.sem(f"l3s{p.nsem()}"), p.sem(f"l3s{p.nsem()}b")]
    p.barrier()
    for nt in range(4):
        def yload(eng, nt=nt, ygp=p.val(ygs), yd0=p.val(ysem[0]), yd1=p.val(ysem[1])):
            if nt == 0:
                eng.wait_ge(ysem[0], yd0)
                eng.wait_ge(ysem[1], yd1)
            if ygp:
                eng.wait_ge(ygs, ygp)
            src = F16S[:, YOFF:YOFF + 16384].rearrange("p (o t) -> p o t", t=2048)
            eng.dma_start(out=yg.rearrange("p (o t) -> p o t", t=512),
                          in_=src[:, :, nt * 512:(nt + 1) * 512]).then_inc(ygs, 16)
        p.emit("sync", yload)
        ygv = p.bump(ygs, 16)
        se0 = p.val(c.s_se)
        for och in range(NCH):
            bank = och % 4

            def omm(eng, och=och, nt=nt, bank=bank, ygv=ygv, se0=se0):
                if och == 0:
                    eng.wait_ge(ygs, ygv)
                if och >= 4 or nt > 0:
                    eng.wait_ge(c.s_se, se0 + och - 3)
                for k in range(NCH):
                    eng.matmul(out=c.ps1[:, bank * 512:(bank + 1) * 512],
                               lhsT=wo16[:, k * D + och * 128: k * D + (och + 1) * 128],
                               rhs=yg[:, k * 512:(k + 1) * 512],
                               start=(k == 0), stop=(k == NCH - 1))
                eng.nop().then_inc(c.s_te, 1)
            p.emit("tensor", omm)
            tv = p.bump(c.s_te, 1)
            stg = sstg[och % 2]
            ss_ = ssem[och % 2]
            ssp = p.val(ss_)

            def oev(eng, och=och, nt=nt, bank=bank, tv=tv, bo_s=bo_s, stg=stg,
                    ss_=ss_, ssp=ssp):
                eng.wait_ge(c.s_te, tv)
                if och == 0 and nt == 0:
                    eng.wait_ge(c.s_ls, bo_s)
                if ssp:
                    eng.wait_ge(ss_, ssp)
                eng.activation(out=stg[:, 0:512], in_=c.ps1[:, bank * 512:(bank + 1) * 512],
                               func=AF.Identity,
                               bias=c.bias_sb[:, och:och + 1]).then_inc(c.s_se, 1)
            p.emit("scalar", oev)
            sv = p.bump(c.s_se, 1)

            def sout(eng, och=och, nt=nt, sv=sv, stg=stg, ss_=ss_, ssp=ssp):
                eng.wait_ge(c.s_se, sv)
                if ssp:
                    eng.wait_ge(ss_, ssp)
                eng.dma_start(out=F16S[:, SOFF + och * 2048 + nt * 512: SOFF + och * 2048 + (nt + 1) * 512],
                              in_=stg[:, 0:512]).then_inc(ss_, 16)
            p.emit("sync", sout)
            p.bump(ss_, 16)
    p.barrier()
    s0, s1 = p.val(ssem[0]), p.val(ssem[1])
    for e in ("tensor", "vector", "scalar", "gpsimd"):
        def sq_(eng, s0=s0, s1=s1):
            eng.wait_ge(ssem[0], s0)
            eng.wait_ge(ssem[1], s1)
        p.emit(e, sq_)
    a16.drop(wo16)
    a16.drop(yg)
    for b in sstg:
        a16.drop(b)
    p.barrier()
    _ckpt(c, "l3o")

    g32 = a32.take(NCH * T, "l3g")
    eb32 = a32.take(1024, "l3eb32")
    expb16 = a16.take(1024, "l3eb16")
    eb_s = load_small(c, "L3_expb", eb32[0:8, 0:1024])

    def ebcv(eng, eb_s=eb_s):
        eng.wait_ge(c.s_ls, eb_s)
        eng.tensor_copy(out=expb16[0:8, 0:1024], in_=eb32[0:8, 0:1024]).then_inc(c.s_ve, 1)
    p.emit("vector", ebcv)
    p.bump(c.s_ve, 1)
    p.barrier()
    for nt in range(4):
        for och in range(NCH):
            bank2 = (nt * NCH + och) % 2

            def bmm(eng, nt=nt, och=och, bank2=bank2, vprev=p.val(c.s_ve)):
                eng.wait_ge(c.s_ve, vprev)
                eng.matmul(out=c.ps2[:, bank2 * 512:(bank2 + 1) * 512],
                           lhsT=expb16[0:8, och * 128:(och + 1) * 128],
                           rhs=mrow16[0:8, nt * 512:(nt + 1) * 512],
                           start=True, stop=True)
                eng.nop().then_inc(c.s_te, 1)
            p.emit("tensor", bmm)
            tv = p.bump(c.s_te, 1)

            def gbias(eng, nt=nt, och=och, bank2=bank2, tv=tv):
                eng.wait_ge(c.s_te, tv)
                eng.tensor_copy(out=t_sl(g32, och, nt_sl(nt)),
                                in_=c.ps2[:, bank2 * 512:(bank2 + 1) * 512]).then_inc(c.s_ve, 1)
            p.emit("vector", gbias)
            p.bump(c.s_ve, 1)
    p.barrier()
    for e in ("tensor", "vector", "scalar", "gpsimd"):
        p.emit(e, lambda eng, v=mdv, s=mds: eng.wait_ge(s, v))
    a16.drop(mrow16)
    we16b = [a16.take(8 * D, "l3we0"), a16.take(8 * D, "l3we1")]
    xe16 = a16.take(8 * 512, "l3xe")
    maskbc = a16.take(512, "l3mbc")
    mbs = [p.sem(f"l3mb{p.nsem()}"), p.sem(f"l3mb{p.nsem()}b")]
    emm_last = {}
    for e8 in range(8):
        we16 = we16b[e8 % 2]
        wdone = load_w16(c, "L3_expw", 8, D, we16, row_off=e8 * D,
                         dst_guard=(c.s_te, emm_last.get(e8 - 2, 0)))
        wlast = {}
        for s_, v_ in wdone:
            wlast[s_] = max(wlast.get(s_, 0), v_)
        wlast = list(wlast.items())
        for nt in range(4):
            mi = (e8 * 4 + nt) % 2
            mb_ = mbs[mi]
            mbp = p.val(mb_)

            def mload(eng, e8=e8, nt=nt, mb_=mb_, mbp=mbp, vprev=p.val(c.s_ve),
                      mdv=mdv, mds=mds):
                if e8 == 0 and nt == 0:
                    eng.wait_ge(mds, mdv)
                if mbp:
                    eng.wait_ge(mb_, mbp)
                eng.wait_ge(c.s_ve, vprev)
                eng.dma_start(out=maskbc,
                              in_=F16S[e8:e8 + 1, MOFF + nt * 512: MOFF + (nt + 1) * 512].broadcast_to((128, 512))).then_inc(mb_, 16)
            p.emit("sync", mload)
            mbv = p.bump(mb_, 16)

            def xeb(eng, nt=nt, mbv=mbv, mb_=mb_, teprev=p.val(c.s_te)):
                eng.wait_ge(mb_, mbv)
                eng.wait_ge(c.s_te, teprev)
                for k in range(NCH):
                    eng.tensor_tensor(out=xe16[:, k * 512:(k + 1) * 512],
                                      in0=t_sl(xn16, k, nt_sl(nt)), in1=maskbc,
                                      op=ALU.mult)
                eng.nop().then_inc(c.s_ve, 1)
            p.emit("vector", xeb)
            xv = p.bump(c.s_ve, 1)
            ve0 = p.val(c.s_ve)
            for oh in range(NCH):
                bank = oh % 4

                def emm(eng, oh=oh, bank=bank, xv=xv, ve0=ve0, we16=we16,
                        wl=(wlast if nt == 0 and oh == 0 else None)):
                    if oh == 0:
                        eng.wait_ge(c.s_ve, xv)
                    if wl:
                        for s_, v_ in wl:
                            eng.wait_ge(s_, v_)
                    if oh >= 4:
                        eng.wait_ge(c.s_ve, ve0 + oh - 3)
                    for k in range(NCH):
                        eng.matmul(out=c.ps1[:, bank * 512:(bank + 1) * 512],
                                   lhsT=we16[:, k * D + oh * 128: k * D + (oh + 1) * 128],
                                   rhs=xe16[:, k * 512:(k + 1) * 512],
                                   start=(k == 0), stop=(k == NCH - 1))
                    eng.nop().then_inc(c.s_te, 1)
                p.emit("tensor", emm)
                tv = p.bump(c.s_te, 1)
                emm_last[e8] = tv

                def gadd(eng, oh=oh, nt=nt, bank=bank, tv=tv):
                    eng.wait_ge(c.s_te, tv)
                    eng.tensor_tensor(out=t_sl(g32, oh, nt_sl(nt)),
                                      in0=t_sl(g32, oh, nt_sl(nt)),
                                      in1=c.ps1[:, bank * 512:(bank + 1) * 512],
                                      op=ALU.add).then_inc(c.s_ve, 1)
                p.emit("vector", gadd)
                p.bump(c.s_ve, 1)
    p.barrier()
    a32.drop(eb32)
    a16.drop(expb16)
    a16.drop(we16b[0])
    a16.drop(we16b[1])
    a16.drop(xe16)
    a16.drop(maskbc)
    p.barrier()
    _ckpt(c, "l3e")

    xg = a32.take(NCH * 512, "l3xg")
    sg32 = a32.take(512, "l3sg32")
    g16t = a16.take(512, "l3g16")
    sgrp = a16.take(NCH * 512, "l3sgrp")
    xgs = p.sem(f"l3xg{p.nsem()}")
    sgs = p.sem(f"l3sg{p.nsem()}")
    xos = p.sem(f"l3xo{p.nsem()}")
    for grp in range(4):
        def xload(eng, grp=grp, xop=p.val(xos), ssem=c.spill_sem, ssv=c.spill_done):
            if grp == 0:
                eng.wait_ge(ssem, ssv)
            if xop:
                eng.wait_ge(xos, xop)
            src = c.dram_spill.rearrange("(o pp) t -> pp o t", pp=128)
            eng.dma_start(out=xg.rearrange("p (o t) -> p o t", t=512),
                          in_=src[:, :, grp * 512:(grp + 1) * 512]).then_inc(xgs, 16)
        p.emit("sync", xload)
        xgv = p.bump(xgs, 16)

        def sload(eng, grp=grp, sgp=p.val(sgs), sv0=p.val(ssem[0]), sv1=p.val(ssem[1])):
            if grp == 0:
                eng.wait_ge(ssem[0], sv0)
                eng.wait_ge(ssem[1], sv1)
            if sgp:
                eng.wait_ge(sgs, sgp)
            src = F16S[:, SOFF:SOFF + 16384].rearrange("p (o t) -> p o t", t=2048)
            eng.dma_start(out=sgrp.rearrange("p (o t) -> p o t", t=512),
                          in_=src[:, :, grp * 512:(grp + 1) * 512]).then_inc(sgs, 16)
        p.emit("sync", sload)
        sgv = p.bump(sgs, 16)
        for och in range(NCH):
            def gsil(eng, grp=grp, och=och, vprev=p.val(c.s_ve)):
                eng.wait_ge(c.s_ve, vprev)
                eng.activation(out=g16t, in_=t_sl(g32, och, slice(grp * 512, (grp + 1) * 512)),
                               func=AF.Silu).then_inc(c.s_se, 1)
            p.emit("scalar", gsil)
            gv = p.bump(c.s_se, 1)

            def comb(eng, och=och, gv=gv, sgv=sgv, xgv=xgv):
                eng.wait_ge(c.s_se, gv)
                if och == 0:
                    eng.wait_ge(sgs, sgv)
                    eng.wait_ge(xgs, xgv)
                eng.tensor_tensor(out=sg32, in0=sgrp[:, och * 512:(och + 1) * 512],
                                  in1=g16t, op=ALU.mult)
                eng.tensor_tensor(out=xg[:, och * 512:(och + 1) * 512],
                                  in0=xg[:, och * 512:(och + 1) * 512], in1=sg32,
                                  op=ALU.add).then_inc(c.s_ve, 1)
            p.emit("vector", comb)
            p.bump(c.s_ve, 1)
        vev = p.val(c.s_ve)

        def xstore(eng, grp=grp, vev=vev):
            eng.wait_ge(c.s_ve, vev)
            dst = c.dram_spill.rearrange("(o pp) t -> pp o t", pp=128)
            eng.dma_start(out=dst[:, :, grp * 512:(grp + 1) * 512],
                          in_=xg.rearrange("p (o t) -> p o t", t=512)).then_inc(xos, 16)
        p.emit("sync", xstore)
        p.bump(xos, 16)
        p.barrier()
    a32.drop(xg)
    a32.drop(sg32)
    a32.drop(g32)
    a16.drop(g16t)
    a16.drop(sgrp)
    c.a16.drop(c.xn16)
    c.xn16 = None
    sem = p.sem(f"l3rf{p.nsem()}")
    c.x = c.a32.take(NCH * T, "x")
    x = c.x

    def fre(eng, sem=sem, xop=p.val(xos)):
        eng.wait_ge(xos, xop)
        for ch in range(NCH):
            eng.dma_start(out=t_sl(x, ch),
                          in_=c.dram_spill[ch * 128:(ch + 1) * 128, :]).then_inc(sem, 16)
    p.emit("sync", fre)
    v = p.bump(sem, 16 * NCH)
    for e in ("tensor", "vector", "scalar", "gpsimd"):
        p.emit(e, lambda eng, v=v, sem=sem: eng.wait_ge(sem, v))
    p.barrier()



def build_core_program(n_layers=4, upto=None):
    from contextlib import ExitStack
    nc = bass.Bass("TRN2", target_bir_lowering=False)
    es = ExitStack()
    p = Prog(nc, es)
    c = Ctx()
    c.nc, c.p, c.es = nc, p, es

    dram = {}

    def din(name, shape, dtype=F32):
        dram[name] = nc.dram_tensor(name, list(shape), dtype, kind="ExternalInput").ap()

    din("xT", (D, T))
    c.out = nc.dram_tensor("outT", [D, T], F32, kind="ExternalOutput").ap()
    c.dram_spill = nc.dram_tensor("xspill", [D, T], F32, kind="Internal").ap()
    c.dram_kv = nc.dram_tensor("kvscratch", [128, 8 * D], F16, kind="Internal").ap()
    c.dbg = nc.dram_tensor("dbg", [128, NCH * T], F16, kind="ExternalOutput").ap()
    c.dram_f16 = nc.dram_tensor("f16scratch", [128, 131072], F16, kind="Internal").ap()
    c.dram_w16 = nc.dram_tensor("w16scratch", [128, 98304], F16, kind="Internal").ap()
    din("consts32", (128, CW32))
    din("consts16", (128, CW16), F16)

    din("L0_Wq", (D, D)); din("L0_Wk", (D, D)); din("L0_Wv", (D, D)); din("L0_Wo", (D, D))
    din("L0_bq", (128, NCH)); din("L0_bo", (128, NCH))
    din("L0_bk_row", (1, D)); din("L0_bv_row", (1, D))
    din("L0_fw", (D, D))
    din("L1_wqkv", (D, 3 * D)); din("L1_outw", (D, D)); din("L1_outb", (128, NCH))
    din("L1_gate", (4096, D)); din("L1_up", (4096, D)); din("L1_down", (D, 4096))
    din("L1_sin", (1, 1)); din("L1_sind", (1, 1))
    din("L2_normw", (128, NCH)); din("L2_Wx", (D, 2048)); din("L2_bx", (128, 16))
    din("L2_Wz", (D, 2048)); din("L2_bz", (128, 16))
    din("L2_convw", (128, 64)); din("L2_convb", (128, 16))
    din("L2_Wxp", (2048, 96)); din("L2_Wdt", (64, 2048)); din("L2_bdt", (128, 16))
    din("L2_Alog", (128, 256)); din("L2_D", (128, 16))
    din("L2_Wout", (2048, D)); din("L2_bout", (128, NCH))
    din("L2_gate", (D, 4096)); din("L2_up", (D, 4096)); din("L2_down", (4096, D))
    din("L3_wqkv", (D, 3 * D)); din("L3_bqkv", (128, 24)); din("L3_bv_row", (1, D))
    din("L3_outw", (D, D)); din("L3_outb", (128, NCH))
    din("L3_W1", (D, 512)); din("L3_b1", (128, 4)); din("L3_W2", (512, 8)); din("L3_b2", (1, 8))
    din("L3_expw", (8 * D, D)); din("L3_expb", (8, D))
    c.dram = dram

    def sb(name, shape, dtype=F32):
        return es.enter_context(nc.sbuf_tensor(name, list(shape), dtype))

    pool32 = sb("pool32", (128, P32_COLS))
    pool16 = sb("pool16", (128, P16_COLS), F16)
    c.a32 = Alloc(pool32, P32_COLS, "a32")
    c.a16 = Alloc(pool16, P16_COLS, "a16")
    c.cons32 = sb("cons32", (128, CW32))
    c.cons16 = sb("cons16", (128, CW16), F16)
    c.cc = lambda v: c.cons32[:, OFF_SC + CONST_SCALARS.index(v): OFF_SC + CONST_SCALARS.index(v) + 1]
    c.alpha = sb("alpha", (128, 16))
    c.beta = sb("beta", (128, 16))
    c.stat32 = sb("stat32", (128, 32))
    c.bias_sb = sb("bias_sb", (128, 32))
    c.small16 = sb("small16", (128, 64), F16)
    c.pool8 = sb("pool8", (128, 4096), I8)
    c.convw_sb = sb("convw", (128, 64))
    c.alog_sb = sb("alog", (128, 256))
    c.a32_A = sb("a32A", (128, 256))
    c.bdt_sb = sb("bdt", (128, 16))
    c.l2conv32 = sb("l2cv", (128, 256))
    c.wstage = [sb("wst0", (128, 2048)), sb("wst1", (128, 2048))]

    c.ps1 = es.enter_context(nc.psum_tensor("ps1", [128, 2048], F32))
    c.ps2 = es.enter_context(nc.psum_tensor("ps2", [128, 1024], F32))
    c.psh = es.enter_context(nc.psum_tensor("psh", [128, 2048], F16))

    c.s_wl = [p.sem("wl0"), p.sem("wl1")]
    c.s_cv3 = {e: p.sem(f"cv_{e}") for e in ("vector", "scalar", "gpsimd")}
    for s in ("dma", "te", "ve", "se", "ge", "cv", "st", "gd", "ls", "bc"):
        setattr(c, "s_" + s, p.sem(s))

    c.x = c.a32.take(NCH * T, "x")
    c.xn16 = c.a16.take(NCH * T, "xn16")
    c.s16 = None
    x = c.x

    s_ld = p.sem("init_ld")

    def f(eng):
        for ch in range(NCH):
            eng.dma_start(out=t_sl(x, ch),
                          in_=dram["xT"][ch * 128:(ch + 1) * 128, :]).then_inc(s_ld, 16)
        eng.dma_start(out=c.cons32[:], in_=dram["consts32"]).then_inc(s_ld, 16)
        eng.dma_start(out=c.cons16[:], in_=dram["consts16"]).then_inc(s_ld, 16)
    p.emit("sync", f)
    iv = p.bump(s_ld, 16 * (NCH + 2))
    for e in ("tensor", "vector", "scalar", "gpsimd"):
        p.emit(e, lambda eng, v=iv: eng.wait_ge(s_ld, v))

    emitters = [emit_layer0, emit_layer1, emit_layer2, emit_layer3]
    c.upto = upto
    try:
        for li in range(n_layers):
            c.layer = li
            if li > 0 and c.xn16 is None:
                c.xn16 = c.a16.take(NCH * T, "xn16")
            emitters[li](c)
            p.barrier()
    except _StopEmit:
        pass

    xf = c.x

    def fs(eng):
        for ch in range(NCH):
            eng.dma_start(out=c.out[ch * 128:(ch + 1) * 128, :],
                          in_=t_sl(xf, ch)).then_inc(c.s_st, 16)
    p.emit("gpsimd", fs)
    p.bump(c.s_st, 16 * NCH)
    fin = p.val(c.s_st)
    p.emit("gpsimd", lambda eng, v=fin: eng.wait_ge(c.s_st, v))

    blk = es.enter_context(nc.Block())
    p.build(blk)
    es.close()
    return nc



def _prep_weights(inputs):
    g = {}
    f = np.ascontiguousarray

    def t(a):
        return f(np.asarray(a).T)

    for n in ("Wq", "Wk", "Wv", "Wo"):
        g[f"L0_{n}"] = t(inputs[f"L0_attn_{n}"])
    g["L0_bq"] = f(np.asarray(inputs["L0_attn_bq"]).reshape(NCH, 128).T)
    g["L0_bo"] = f(np.asarray(inputs["L0_attn_bo"]).reshape(NCH, 128).T)
    g["L0_bk_row"] = f(np.asarray(inputs["L0_attn_bk"]).reshape(1, D))
    g["L0_bv_row"] = f(np.asarray(inputs["L0_attn_bv"]).reshape(1, D))
    g["L0_fw"] = t(inputs["L0_ffn_weight"])
    g["L1_wqkv"] = f(np.asarray(inputs["L1_attn_weight_qkv"]))
    g["L1_outw"] = t(inputs["L1_attn_out_w"])
    g["L1_outb"] = f(np.asarray(inputs["L1_attn_out_b"]).reshape(NCH, 128).T)
    g["L1_gate"] = f(np.asarray(inputs["L1_ffn_gate_w"]))
    g["L1_up"] = f(np.asarray(inputs["L1_ffn_up_w"]))
    g["L1_down"] = f(np.asarray(inputs["L1_ffn_down_w"]))
    g["L1_sin"] = f(np.asarray(inputs["L1_ffn_s_in"]).reshape(1, 1))
    g["L1_sind"] = f(np.asarray(inputs["L1_ffn_s_in_down"]).reshape(1, 1))
    g["L2_normw"] = f(np.asarray(inputs["L2_attn_norm_w"]).reshape(NCH, 128).T)
    g["L2_Wx"] = t(inputs["L2_attn_Wx"])
    g["L2_bx"] = f(np.asarray(inputs["L2_attn_bx"]).reshape(16, 128).T)
    g["L2_Wz"] = t(inputs["L2_attn_Wz"])
    g["L2_bz"] = f(np.asarray(inputs["L2_attn_bz"]).reshape(16, 128).T)
    g["L2_convw"] = f(np.asarray(inputs["L2_attn_conv_w"]).reshape(16, 128, 4).transpose(1, 0, 2).reshape(128, 64))
    g["L2_convb"] = f(np.asarray(inputs["L2_attn_conv_b"]).reshape(16, 128).T)
    g["L2_Wxp"] = t(inputs["L2_attn_Wxp"])
    g["L2_Wdt"] = t(inputs["L2_attn_Wdt"])
    g["L2_bdt"] = f(np.asarray(inputs["L2_attn_bdt"]).reshape(16, 128).T)
    g["L2_Alog"] = f(np.asarray(inputs["L2_attn_A_log"]).reshape(16, 128, 16).transpose(1, 0, 2).reshape(128, 256))
    g["L2_D"] = f(np.asarray(inputs["L2_attn_D"]).reshape(16, 128).T)
    g["L2_Wout"] = t(inputs["L2_attn_Wout"])
    g["L2_bout"] = f(np.asarray(inputs["L2_attn_bout"]).reshape(NCH, 128).T)
    g["L2_gate"] = t(inputs["L2_ffn_w_gate"])
    g["L2_up"] = t(inputs["L2_ffn_w_up"])
    g["L2_down"] = t(inputs["L2_ffn_w_down"])
    g["L3_wqkv"] = t(inputs["L3_attn_w_qkv"])
    bq3 = np.asarray(inputs["L3_attn_b_qkv"], np.float32).copy()
    bq3[:1024] *= 0.125
    g["L3_bqkv"] = f(bq3.reshape(24, 128).T)
    g["L3_bv_row"] = f(np.asarray(inputs["L3_attn_b_qkv"], np.float32)[None, 2048:3072])
    g["L3_outw"] = t(inputs["L3_attn_w_out"])
    g["L3_outb"] = f(np.asarray(inputs["L3_attn_b_out"]).reshape(NCH, 128).T)
    g["L3_W1"] = f(np.asarray(inputs["L3_ffn_W1"]))
    g["L3_b1"] = f(np.asarray(inputs["L3_ffn_b1"]).reshape(4, 128).T)
    g["L3_W2"] = f(np.asarray(inputs["L3_ffn_W2"]))
    g["L3_b2"] = f(np.asarray(inputs["L3_ffn_b2"]).reshape(1, 8))
    g["L3_expw"] = f(np.asarray(inputs["L3_ffn_expert_W"]).reshape(8 * 1024, 1024))
    g["L3_expb"] = f(np.asarray(inputs["L3_ffn_expert_b"]))
    g["consts32"] = _consts32()
    g["consts16"] = _consts16()
    return g


def kernel(**inputs):
    g = _prep_weights(inputs)
    x = np.asarray(inputs["L0_attn_x"])
    nc = build_core_program(4)
    in_maps = []
    for b in range(B):
        m = dict(g)
        m["xT"] = np.ascontiguousarray(x[b].T)
        in_maps.append(m)
    res = run_bass_kernel_spmd(nc, in_maps, core_ids=list(range(B)))
    out = np.stack([np.ascontiguousarray(np.asarray(r["outT"]).T) for r in res.results], axis=0)
    return out.astype(np.float32)


# version 17
# speedup vs baseline: 1.0521x; 1.0521x over previous
import numpy as np

import concourse.bass as bass
import concourse.mybir as mybir
from concourse.bass_utils import run_bass_kernel_spmd

F32 = mybir.dt.float32
F16 = mybir.dt.float16
I8 = mybir.dt.int8
AF = mybir.ActivationFunctionType
ALU = mybir.AluOpType
AX = mybir.AxisListType

T = 2048
D = 1024
NCH = D // 128
B = 8

ENGINES = ("tensor", "vector", "scalar", "gpsimd", "sync")

CONST_SCALARS = [1e-5, 1.0, 0.5, 1e-8, 1e-30, 1e-6]
CW32 = 128 + 128 + 1 + len(CONST_SCALARS)
OFF_ID = 0
OFF_ONES = 128
OFF_IOTA = 256
OFF_SC = 257
CW16 = 128 + 128 + 4 * 512
OFF_ID16 = 0
OFF_ONES16 = 128
OFF_MASK16 = 256

P32_COLS = 22016
P16_COLS = 41984


def _consts32():
    ident = np.eye(128, dtype=np.float32)
    ones = np.ones((128, 128), dtype=np.float32)
    iota8 = np.zeros((128, 1), dtype=np.float32)
    iota8[:8, 0] = np.arange(8)
    sc = np.tile(np.asarray(CONST_SCALARS, dtype=np.float32)[None, :], (128, 1))
    return np.ascontiguousarray(np.concatenate([ident, ones, iota8, sc], axis=1))


def _consts16():
    ident = np.eye(128, dtype=np.float32)
    ones = np.ones((128, 128), dtype=np.float32)
    masks = []
    for v in range(4):
        m = np.zeros((128, 512), dtype=np.float32)
        for r in range(128):
            c0 = v * 128 + r + 1
            if c0 < 512:
                m[r, c0:] = -30000.0
        masks.append(m)
    return np.ascontiguousarray(np.concatenate([ident, ones] + masks, axis=1).astype(np.float16))


class Alloc:
    def __init__(self, tile, cols, name):
        self.tile, self.cols, self.name = tile, cols, name
        self.freel = [(0, cols)]
        self.live = {}

    def take(self, n, tag=""):
        n = int(n)
        for i, (s, ln) in enumerate(self.freel):
            if ln >= n:
                if ln == n:
                    self.freel.pop(i)
                else:
                    self.freel[i] = (s + n, ln - n)
                self.live[s] = (n, tag)
                ap = self.tile[:, s:s + n]
                if not hasattr(self, "_reg"):
                    self._reg = {}
                self._reg[id(ap)] = (s, ap)
                return ap
        raise RuntimeError(f"pool {self.name}: need {n}, free={self.freel} live={self.live}")

    def drop(self, ap):
        s, _ = self._reg.pop(id(ap))
        n, _ = self.live.pop(s)
        self.freel.append((s, n))
        self.freel.sort()
        m = []
        for st, ln in self.freel:
            if m and m[-1][0] + m[-1][1] == st:
                m[-1] = (m[-1][0], m[-1][1] + ln)
            else:
                m.append((st, ln))
        self.freel = m


class Ctx:
    pass


class Prog:
    def __init__(self, nc, es):
        self.nc, self.es = nc, es
        self.ops = []
        self.cnt = {}
        self._bar_idx = 0
        self.eng_done = {e: self.sem(f"done_{e}") for e in ENGINES}

    def sem(self, name):
        s = self.es.enter_context(self.nc.semaphore(name))
        self.cnt[id(s)] = 0
        self._nsem = getattr(self, "_nsem", 0) + 1
        return s

    def nsem(self):
        return getattr(self, "_nsem", 0)

    def val(self, s):
        return self.cnt[id(s)]

    def bump(self, s, k=1):
        self.cnt[id(s)] += k
        return self.cnt[id(s)]

    def emit(self, engine, fn):
        self.ops.append((engine, fn))

    def barrier(self):
        active = sorted(set(e for e, _ in self.ops[self._bar_idx:]) - {"sync"})
        for e in active:
            s = self.eng_done[e]
            self.bump(s, 1)
            self.emit(e, lambda eng, s=s: eng.nop().then_inc(s, 1))
        snap = {e: self.val(self.eng_done[e]) for e in ENGINES}
        for e in ENGINES:
            def wait(eng, e=e, snap=snap):
                for oe in ENGINES:
                    if oe != e and snap[oe] > 0:
                        eng.wait_ge(self.eng_done[oe], snap[oe])
            self.emit(e, wait)
        self._bar_idx = len(self.ops)

    def build(self, block):
        for e in ENGINES:
            def run(eng, e=e):
                ev = _EngView(eng)
                for oe, fn in self.ops:
                    if oe == e:
                        object.__setattr__(ev, "_tick", ev._tick + 1)
                        fn(ev)
            getattr(block, e)(run)


def _apr(o):
    try:
        t = o.tensor
        nm = t.name
        off = o.offset
        pat = [list(p) for p in o.ap]
    except Exception:
        return None
    ext = 0
    for s, n in pat[1:]:
        if s < 0 or n < 1:
            return (nm, 0, 1 << 60)
        ext += (n - 1) * s
    return (nm, off, off + ext + 1)


_WRKEYS = ("out", "accum_out")


class _EngView:

    _SKIP = {"nop", "drain", "wait_ge", "sem_inc", "wait_op", "dma_start",
             "dma_start_transpose", "engine_nop"}

    def __init__(self, eng):
        object.__setattr__(self, "_e", eng)
        nm = type(eng).__name__
        autodrain = ("Vector" in nm) or ("Scalar" in nm) or ("GpSimd" in nm)
        object.__setattr__(self, "_autodrain", autodrain)
        object.__setattr__(self, "_wr", [])
        object.__setattr__(self, "_rd", [])
        object.__setattr__(self, "_dirty", False)
        object.__setattr__(self, "_nsd", 0)
        object.__setattr__(self, "_tick", 0)
        object.__setattr__(self, "_lastr", None)
        object.__setattr__(self, "_last_tick", -1)
        object.__setattr__(self, "_last_inc", False)

    @staticmethod
    def _hit(rngs, q):
        nm, lo, hi = q
        for (n2, l2, h2) in rngs:
            if n2 == nm and lo < h2 and l2 < hi:
                return True
        return False

    def __getattr__(self, name):
        a = getattr(self._e, name)
        if not callable(a) or name in self._SKIP or name.startswith("_"):
            return a

        def call(*args, **kw):
            if self._autodrain:
                wr, rd = [], []
                for k, v in kw.items():
                    r = _apr(v)
                    if r is not None:
                        (wr if k in _WRKEYS else rd).append(r)
                for i, v in enumerate(args):
                    r = _apr(v)
                    if r is not None:
                        if name == "memset" and i == 0:
                            wr.append(r)
                        else:
                            wr.append(r)
                            rd.append(r)
                need = False
                for q in rd:
                    if self._hit(self._wr, q):
                        need = True
                        break
                if not need:
                    for q in wr:
                        if self._hit(self._wr, q) or self._hit(self._rd, q):
                            need = True
                            break
                if need and self._dirty:
                    self._e.drain()
                    object.__setattr__(self, "_wr", [])
                    object.__setattr__(self, "_rd", [])
                    object.__setattr__(self, "_nsd", 0)
                r = a(*args, **kw)
                if hasattr(r, "then_inc"):
                    self._wr.extend(wr)
                    self._rd.extend(rd)
                    object.__setattr__(self, "_dirty", True)
                    object.__setattr__(self, "_nsd", self._nsd + 1)
                    return _OpRes(self, r)
                return r
            r = a(*args, **kw)
            if hasattr(r, "then_inc"):
                object.__setattr__(self, "_dirty", True)
                object.__setattr__(self, "_lastr", r)
                object.__setattr__(self, "_last_tick", self._tick)
                object.__setattr__(self, "_last_inc", False)
            return r
        return call

    def wait_ge(self, sem, val):
        return self._e.wait_ge(sem, val)

    def drain(self):
        object.__setattr__(self, "_dirty", False)
        object.__setattr__(self, "_wr", [])
        object.__setattr__(self, "_rd", [])
        object.__setattr__(self, "_nsd", 0)
        return self._e.drain()

    def engine_nop(self):
        return self._e.engine_nop()

    def dma_start(self, *a, **k):
        return self._e.dma_start(*a, **k)

    def nop(self):
        view = self

        class _Shim:
            def then_inc(self, sem, k=1):
                if (not view._autodrain and view._lastr is not None
                        and view._last_tick == view._tick and not view._last_inc):
                    object.__setattr__(view, "_last_inc", True)
                    return view._lastr.then_inc(sem, k)
                object.__setattr__(view, "_dirty", False)
                object.__setattr__(view, "_wr", [])
                object.__setattr__(view, "_rd", [])
                object.__setattr__(view, "_nsd", 0)
                return view._e.drain().then_inc(sem, k)
        return _Shim()


class _OpRes:

    def __init__(self, view, r):
        object.__setattr__(self, "_v", view)
        object.__setattr__(self, "_r", r)

    def then_inc(self, sem, k=1):
        v = self._v
        if v._nsd > 1:
            object.__setattr__(v, "_dirty", False)
            object.__setattr__(v, "_wr", [])
            object.__setattr__(v, "_rd", [])
            object.__setattr__(v, "_nsd", 0)
            return v._e.drain().then_inc(sem, k)
        return self._r.then_inc(sem, k)

    def __getattr__(self, name):
        return getattr(self._r, name)


def t_sl(chunked, ch, sl=None):
    if sl is None:
        return chunked[:, ch * T:(ch + 1) * T]
    return chunked[:, ch * T + sl.start: ch * T + sl.stop]


def nt_sl(nt, w=512):
    return slice(nt * w, (nt + 1) * w)



def load_w16(c, name, kch, cols, dst16, row_off=0, col_sl=None, rows=128,
             dst_guard=None):
    p = c.p
    done = []
    lastmap = getattr(c, "_w16_last", None) or {0: None, 1: None}
    for k in range(kch):
        st = c.wstage[k % 2]
        slot = k % 2
        sem = c.s_wl[slot]
        prev = p.val(sem)
        guard = done[k - 2] if k >= 2 else lastmap[slot]

        def fl(eng, k=k, st=st, sem=sem, prev=prev, guard=guard):
            if guard is not None:
                eng.wait_ge(guard[0], guard[1])
            if prev:
                eng.wait_ge(sem, prev)
            src = c.dram[name]
            r0 = row_off + k * rows
            src = src[r0:r0 + rows, :] if col_sl is None else src[r0:r0 + rows, col_sl]
            eng.dma_start(out=st[0:rows, :cols], in_=src).then_inc(sem, 16)
        p.emit("sync", fl)
        dv = p.bump(sem, 16)
        e = ("vector", "scalar", "gpsimd")[k % 3]
        csem = c.s_cv3[e]

        def cv(eng, k=k, st=st, dv=dv, e=e, sem=sem, csem=csem, dst_guard=dst_guard):
            eng.wait_ge(sem, dv)
            if dst_guard is not None and dst_guard[1]:
                eng.wait_ge(dst_guard[0], dst_guard[1])
            if e == "scalar":
                eng.activation(out=dst16[0:rows, k * cols:(k + 1) * cols], in_=st[0:rows, :cols],
                               func=AF.Copy).then_inc(csem, 1)
            else:
                eng.tensor_copy(out=dst16[0:rows, k * cols:(k + 1) * cols],
                                in_=st[0:rows, :cols]).then_inc(csem, 1)
        p.emit(e, cv)
        done.append((csem, p.bump(csem, 1)))
        lastmap[slot] = done[-1]
    c._w16_last = lastmap
    return done


def load_small(c, name, dst, wait=None):
    p = c.p
    sem = c.s_ls
    prev = p.val(sem)
    spw = (getattr(c, "spill_sem", None), getattr(c, "spill_done", 0))

    def f(eng, prev=prev, wait=wait, spw=spw):
        if spw[0] is not None and spw[1]:
            eng.wait_ge(spw[0], spw[1])
        if wait:
            for s, v in wait:
                if v:
                    eng.wait_ge(s, v)
        if prev:
            eng.wait_ge(sem, prev)
        eng.dma_start(out=dst, in_=c.dram[name]).then_inc(sem, 16)
    p.emit("sync", f)
    tgt = p.bump(sem, 16)
    p.emit("sync", lambda eng, tgt=tgt: eng.wait_ge(sem, tgt))
    return tgt


def row_to_col_dma(c, row_ap, dst, n):
    p = c.p

    def f(eng):
        eng.dma_start(out=dst[:, 0:n],
                      in_=row_ap.rearrange("a (cc pp) -> a pp cc", pp=128)).then_inc(c.s_dma, 16)
    p.emit("gpsimd", f)
    return p.bump(c.s_dma, 16)


def bcast_row(c, row_ap, width, psdst, wait=None, bp=0):
    p = c.p
    ones_row = c.cons32[bp:bp + 1, OFF_ONES:OFF_ONES + 128]

    def f(eng):
        if wait is not None:
            eng.wait_ge(*wait)
        o = 0
        while o < width:
            w = min(512, width - o)
            eng.matmul(out=psdst[:, o:o + w], lhsT=ones_row, rhs=row_ap[0:1, o:o + w],
                       start=True, stop=True)
            o += w
        eng.nop().then_inc(c.s_te, 1)
    p.emit("tensor", f)
    return p.bump(c.s_te, 1)



def emit_ln(c, with_xn16=True):
    p = c.p
    x = c.x
    ones_col = c.cons32[:, OFF_ONES:OFF_ONES + 1]
    sq0 = c.a32.take(512, "sq0")
    sq1 = c.a32.take(512, "sq1")
    rowA = c.a32.take(T, "rowA")
    rowB = c.a32.take(T, "rowB")
    tmp = c.a32.take(512, "lntmp")
    sqb = [sq0, sq1]

    def mm1(eng):
        for ch in range(NCH):
            for nt in range(4):
                eng.matmul(out=c.ps1[0:1, nt_sl(nt)], lhsT=ones_col,
                           rhs=t_sl(x, ch, nt_sl(nt)), start=(ch == 0), stop=(ch == NCH - 1))
        eng.nop().then_inc(c.s_te, 1)
    p.emit("tensor", mm1)
    tev = p.bump(c.s_te, 1)

    def rvm(eng, tev=tev):
        eng.wait_ge(c.s_te, tev)
        eng.tensor_scalar(out=rowA[0:1, 0:T], in0=c.ps1[0:1, 0:T], scalar1=1.0 / D,
                          scalar2=None, op0=ALU.mult).then_inc(c.s_ve, 1)
    p.emit("vector", rvm)
    vevm = p.bump(c.s_ve, 1)

    te0 = p.val(c.s_te)
    step = 0
    for ch in range(NCH):
        for nt in range(4):
            buf = sqb[step % 2]

            def sq(eng, ch=ch, nt=nt, buf=buf, step=step):
                if step >= 2:
                    eng.wait_ge(c.s_te, te0 + step - 1)
                eng.activation(out=buf, in_=t_sl(x, ch, nt_sl(nt)), func=AF.Square).then_inc(c.s_se, 1)
            p.emit("scalar", sq)
            sev = p.bump(c.s_se, 1)

            def mm(eng, ch=ch, nt=nt, buf=buf, sev=sev):
                eng.wait_ge(c.s_se, sev)
                if ch == 0 and nt == 0:
                    eng.wait_ge(c.s_ve, vevm)
                eng.matmul(out=c.ps1[0:1, nt_sl(nt)], lhsT=ones_col,
                           rhs=buf, start=(ch == 0), stop=(ch == NCH - 1))
                eng.nop().then_inc(c.s_te, 1)
            p.emit("tensor", mm)
            p.bump(c.s_te, 1)
            step += 1
    tev = p.val(c.s_te)

    def rv1(eng, tev=tev):
        eng.wait_ge(c.s_te, tev)
        eng.tensor_scalar(out=rowB[0:1, 0:T], in0=c.ps1[0:1, 0:T], scalar1=1.0 / D, scalar2=None, op0=ALU.mult)
        for nt in range(4):
            eng.tensor_tensor(out=sq0[0:1, 0:512], in0=rowA[0:1, nt_sl(nt)],
                              in1=rowA[0:1, nt_sl(nt)], op=ALU.mult)
            eng.tensor_tensor(out=rowB[0:1, nt_sl(nt)], in0=rowB[0:1, nt_sl(nt)],
                              in1=sq0[0:1, 0:512], op=ALU.subtract)
        eng.engine_nop().then_inc(c.s_ve, 1)
    p.emit("vector", rv1)
    vev = p.bump(c.s_ve, 1)

    def rs1(eng, vev=vev):
        eng.wait_ge(c.s_ve, vev)
        eng.activation(out=rowB[0:1, 0:T], in_=rowB[0:1, 0:T], func=AF.Sqrt,
                       bias=c.cc(1e-5)[0:1, :]).then_inc(c.s_se, 1)
    p.emit("scalar", rs1)
    sev = p.bump(c.s_se, 1)

    def rv2(eng, sev=sev):
        eng.wait_ge(c.s_se, sev)
        eng.reciprocal(out=rowB[0:1, 0:T], in_=rowB[0:1, 0:T])
        eng.tensor_tensor(out=rowA[0:1, 0:T], in0=rowA[0:1, 0:T], in1=rowB[0:1, 0:T], op=ALU.mult)
        eng.tensor_scalar(out=rowA[0:1, 0:T], in0=rowA[0:1, 0:T], scalar1=-1.0, scalar2=None,
                          op0=ALU.mult).then_inc(c.s_ve, 1)
    p.emit("vector", rv2)
    vev = p.bump(c.s_ve, 1)


    if with_xn16:
        ones_row = c.cons32[0:1, OFF_ONES:OFF_ONES + 128]
        xn16 = c.xn16
        wait_v = vev
        for half in range(2):
            hsl = [slice(half * 1024, half * 1024 + 512), slice(half * 1024 + 512, (half + 1) * 1024)]

            def bc(eng, hsl=hsl, wait_v=wait_v):
                eng.wait_ge(c.s_ve, wait_v)
                for i in range(2):
                    eng.matmul(out=c.ps2[:, i * 512:(i + 1) * 512], lhsT=ones_row,
                               rhs=rowB[0:1, hsl[i]], start=True, stop=True)
                    eng.matmul(out=c.ps1[:, i * 512:(i + 1) * 512], lhsT=ones_row,
                               rhs=rowA[0:1, hsl[i]], start=True, stop=True)
                eng.nop().then_inc(c.s_te, 1)
            p.emit("tensor", bc)
            tev2 = p.bump(c.s_te, 1)
            for ch in range(NCH):
                for i in range(2):
                    def ap(eng, ch=ch, i=i, hsl=hsl, tev2=tev2):
                        if ch == 0 and i == 0:
                            eng.wait_ge(c.s_te, tev2)
                        eng.tensor_tensor(out=tmp, in0=t_sl(x, ch, hsl[i]),
                                          in1=c.ps2[:, i * 512:(i + 1) * 512], op=ALU.mult)
                        eng.tensor_tensor(out=t_sl(xn16, ch, hsl[i]), in0=tmp,
                                          in1=c.ps1[:, i * 512:(i + 1) * 512],
                                          op=ALU.add).then_inc(c.s_ve, 1)
                    p.emit("vector", ap)
                    p.bump(c.s_ve, 1)
            wait_v = p.val(c.s_ve)
    for b in (sq0, sq1, rowA, rowB, tmp):
        c.a32.drop(b)


def emit_spill_x(c):
    p = c.p
    x = c.x
    sem = p.sem(f"sp{p.nsem()}")

    def f(eng, sem=sem):
        for ch in range(NCH):
            eng.dma_start(out=c.dram_spill[ch * 128:(ch + 1) * 128, :],
                          in_=t_sl(x, ch)).then_inc(sem, 16)
    p.emit("sync", f)
    v = p.bump(sem, 16 * NCH)
    c.spill_sem = sem
    c.spill_done = v
    for e in ("tensor", "vector", "scalar", "gpsimd"):
        p.emit(e, lambda eng, v=v, sem=sem: eng.wait_ge(sem, v))
    c.a32.drop(c.x)
    c.x = None


def emit_restore_x(c):
    p = c.p
    c.x = c.a32.take(NCH * T, "x")
    x = c.x
    sem = p.sem(f"rs{p.nsem()}")

    def f(eng, sem=sem, ssem=c.spill_sem, sv=c.spill_done):
        eng.wait_ge(ssem, sv)
        for ch in range(NCH):
            eng.dma_start(out=t_sl(x, ch),
                          in_=c.dram_spill[ch * 128:(ch + 1) * 128, :]).then_inc(sem, 16)
    p.emit("sync", f)
    v = p.bump(sem, 16 * NCH)
    for e in ("tensor", "vector", "scalar", "gpsimd"):
        p.emit(e, lambda eng, v=v, sem=sem: eng.wait_ge(sem, v))



def _ckpt(c, name):
    if getattr(c, "upto", None) == name:
        if getattr(c, "s16", None) is not None:
            p = c.p
            s16 = c.s16

            def dmp(eng):
                eng.dma_start(out=c.dbg, in_=s16).then_inc(c.s_st, 16)
            p.emit("gpsimd", dmp)
            p.bump(c.s_st, 16)
        if c.x is None:
            emit_restore_x(c)
        c.p.barrier()
        raise _StopEmit()


class _StopEmit(Exception):
    pass


def emit_layer0(c):
    p = c.p
    a16, a32 = c.a16, c.a32
    emit_ln(c, with_xn16=True)
    p.barrier()
    _ckpt(c, "ln")
    emit_spill_x(c)
    _ckpt(c, "spill")
    xn16 = c.xn16

    wk16 = a16.take(8 * D, "wk16")
    load_w16(c, "L0_Wk", 8, D, wk16)
    bkr = a32.take(D, "bkr")
    bkr_s = load_small(c, "L0_bk_row", bkr[0:1, 0:D])
    p.barrier()
    bkbc = a32.take(D, "bkbc")
    tev = bcast_row(c, bkr[0:1, 0:D], D, c.ps2, wait=(c.s_ls, bkr_s))

    def bkev(eng, tev=tev):
        eng.wait_ge(c.s_te, tev)
        eng.tensor_copy(out=bkbc, in_=c.ps2[:, 0:D]).then_inc(c.s_ve, 1)
    p.emit("vector", bkev)
    p.bump(c.s_ve, 1)
    p.barrier()

    ktm = a16.take(16 * D, "ktm")
    kt32 = a32.take(D, "kt32")
    ktmin = a32.take(D, "ktmin")
    ve0 = p.val(c.s_ve)
    for tc in range(16):
        def mm(eng, tc=tc, ve0=ve0):
            if tc >= 1:
                eng.wait_ge(c.s_ve, ve0 + 2 * tc - 1)
            for k in range(NCH):
                lhs = t_sl(xn16, k, slice(tc * 128, (tc + 1) * 128))
                for h in range(2):
                    eng.matmul(out=c.ps1[:, h * 512:(h + 1) * 512], lhsT=lhs,
                               rhs=wk16[:, k * D + h * 512: k * D + (h + 1) * 512],
                               start=(k == 0), stop=(k == NCH - 1))
            eng.nop().then_inc(c.s_te, 1)
        p.emit("tensor", mm)
        tmv = p.bump(c.s_te, 1)

        def ev1(eng, tmv=tmv):
            eng.wait_ge(c.s_te, tmv)
            eng.tensor_tensor(out=kt32, in0=c.ps1[:, 0:D], in1=bkbc, op=ALU.add)
            eng.tensor_scalar(out=ktmin, in0=kt32, scalar1=0.0, scalar2=None,
                              op0=ALU.min).then_inc(c.s_ve, 1)
        p.emit("vector", ev1)
        vv = p.bump(c.s_ve, 1)

        def ev2(eng, vv=vv):
            eng.wait_ge(c.s_ve, vv)
            eng.activation(out=ktmin, in_=ktmin, func=AF.Exp).then_inc(c.s_se, 1)
        p.emit("scalar", ev2)
        sv = p.bump(c.s_se, 1)

        def ev3(eng, tc=tc, sv=sv):
            eng.wait_ge(c.s_se, sv)
            eng.scalar_tensor_tensor(out=ktm[:, tc * D:(tc + 1) * D], in0=kt32, scalar=0.0,
                                     in1=ktmin, op0=ALU.max, op1=ALU.add).then_inc(c.s_ve, 1)
        p.emit("vector", ev3)
        p.bump(c.s_ve, 1)
    ones16_col = c.cons16[:, OFF_ONES16:OFF_ONES16 + 1]
    vev = p.val(c.s_ve)

    def ksum(eng, vev=vev):
        eng.wait_ge(c.s_ve, vev)
        for tc in range(16):
            for h in range(2):
                eng.matmul(out=c.ps2[0:1, h * 512:(h + 1) * 512], lhsT=ones16_col,
                           rhs=ktm[:, tc * D + h * 512: tc * D + (h + 1) * 512],
                           start=(tc == 0), stop=(tc == 15))
        eng.nop().then_inc(c.s_te, 1)
    p.emit("tensor", ksum)
    tev = p.bump(c.s_te, 1)
    ksrow = a32.take(D, "ksrow")

    def ksev(eng, tev=tev):
        eng.wait_ge(c.s_te, tev)
        eng.tensor_scalar(out=ksrow[0:1, 0:D], in0=c.ps2[0:1, 0:D], scalar1=1e-6, scalar2=None,
                          op0=ALU.add).then_inc(c.s_ve, 1)
    p.emit("vector", ksev)
    vev = p.bump(c.s_ve, 1)
    onesc = c.cons32[0:1, OFF_ONES:OFF_ONES + 1]

    def ksd(eng, vev=vev):
        eng.wait_ge(c.s_ve, vev)
        for k in range(NCH):
            eng.matmul(out=c.ps1[:, k:k + 1], lhsT=ksrow[0:1, k * 128:(k + 1) * 128],
                       rhs=onesc, start=True, stop=True)
        eng.nop().then_inc(c.s_te, 1)
    p.emit("tensor", ksd)
    kdv = p.bump(c.s_te, 1)

    def kscv(eng, kdv=kdv):
        eng.wait_ge(c.s_te, kdv)
        eng.tensor_copy(out=c.small16[:, 0:8], in_=c.ps1[:, 0:8]).then_inc(c.s_ve, 1)
    p.emit("vector", kscv)
    p.bump(c.s_ve, 1)
    p.barrier()
    for b in (wk16,):
        a16.drop(b)
    for b in (kt32, ktmin, bkr, bkbc, ksrow):
        a32.drop(b)
    _ckpt(c, "A")

    bvr = a32.take(D, "bvr")
    bvr_s = load_small(c, "L0_bv_row", bvr[0:1, 0:D])
    p.barrier()
    bvbc = a32.take(D, "bvbc")
    tev = bcast_row(c, bvr[0:1, 0:D], D, c.ps2, wait=(c.s_ls, bvr_s))

    def bvev(eng, tev=tev):
        eng.wait_ge(c.s_te, tev)
        eng.tensor_copy(out=bvbc, in_=c.ps2[:, 0:D]).then_inc(c.s_ve, 1)
    p.emit("vector", bvev)
    p.bump(c.s_ve, 1)

    wvq16 = a16.take(8 * 256, "wvq16")
    vp16 = a16.take(256, "vp16")
    kvstage = a16.take(4 * 256, "kvstage")
    for dvq in range(4):
        load_w16(c, "L0_Wv", 8, 256, wvq16, col_sl=slice(dvq * 256, (dvq + 1) * 256))
        p.barrier()
        for dkh in range(2):
            ve_v = p.val(c.s_ve)
            for tc in range(16):
                def vmm(eng, tc=tc, ve_v=ve_v):
                    if tc >= 1:
                        eng.wait_ge(c.s_ve, ve_v + tc)
                    for k in range(NCH):
                        lhs = t_sl(xn16, k, slice(tc * 128, (tc + 1) * 128))
                        eng.matmul(out=c.ps2[:, 0:256], lhsT=lhs,
                                   rhs=wvq16[:, k * 256:(k + 1) * 256],
                                   start=(k == 0), stop=(k == NCH - 1))
                    eng.nop().then_inc(c.s_te, 1)
                p.emit("tensor", vmm)
                tvv = p.bump(c.s_te, 1)

                def vev(eng, tc=tc, tvv=tvv, dvq=dvq):
                    eng.wait_ge(c.s_te, tvv)
                    eng.tensor_tensor(out=vp16, in0=c.ps2[:, 0:256],
                                      in1=bvbc[:, dvq * 256:(dvq + 1) * 256],
                                      op=ALU.add).then_inc(c.s_ve, 1)
                p.emit("vector", vev)
                vvv = p.bump(c.s_ve, 1)

                def kvmm(eng, tc=tc, vvv=vvv, dkh=dkh):
                    eng.wait_ge(c.s_ve, vvv)
                    for dq in range(4):
                        dk = dkh * 4 + dq
                        eng.matmul(out=c.ps1[:, dq * 512: dq * 512 + 256],
                                   lhsT=ktm[:, tc * D + dk * 128: tc * D + (dk + 1) * 128],
                                   rhs=vp16, start=(tc == 0), stop=(tc == 15))
                    eng.nop().then_inc(c.s_te, 1)
                p.emit("tensor", kvmm)
                p.bump(c.s_te, 1)
            tev = p.val(c.s_te)

            def kvev(eng, tev=tev, dkh=dkh, gdp=p.val(c.s_gd)):
                eng.wait_ge(c.s_te, tev)
                if gdp:
                    eng.wait_ge(c.s_gd, gdp)
                for dq in range(4):
                    eng.activation(out=kvstage[:, dq * 256:(dq + 1) * 256],
                                   in_=c.ps1[:, dq * 512: dq * 512 + 256],
                                   func=AF.Copy)
                eng.nop().then_inc(c.s_se, 1)
            p.emit("scalar", kvev)
            sev = p.bump(c.s_se, 1)

            def kvout(eng, dvq=dvq, dkh=dkh, sev=sev):
                eng.wait_ge(c.s_se, sev)
                dst = c.dram_kv.rearrange("pp (dk cc) -> pp dk cc", cc=1024)[:, dkh * 4:(dkh + 1) * 4,
                                                                             dvq * 256:(dvq + 1) * 256]
                eng.dma_start(out=dst,
                              in_=kvstage.rearrange("pp (dk cc) -> pp dk cc", cc=256)).then_inc(c.s_gd, 16)
            p.emit("gpsimd", kvout)
            p.bump(c.s_gd, 16)
            p.barrier()
    gdq = p.val(c.s_gd)
    for e in ("tensor", "vector", "scalar", "gpsimd"):
        p.emit(e, lambda eng, v=gdq: eng.wait_ge(c.s_gd, v))
    for b in (wvq16, vp16, kvstage):
        a16.drop(b)
    for b in (bvr, bvbc):
        a32.drop(b)
    a16.drop(ktm)
    _ckpt(c, "B")
    bq_s = load_small(c, "L0_bq", c.bias_sb[:, 0:8])
    qp16 = a16.take(8 * T, "qp16")
    wqq16 = a16.take(8 * 128, "wqq16")
    q32 = a32.take(512, "q32")
    qmin = a32.take(512, "qmin")
    for och in range(NCH):
        load_w16(c, "L0_Wq", 8, 128, wqq16, col_sl=slice(och * 128, (och + 1) * 128))
        p.barrier()
        se0 = p.val(c.s_se)
        ve0 = p.val(c.s_ve)

        def qmm(eng, och=och):
            for k in range(NCH):
                for nt in range(4):
                    eng.matmul(out=c.ps1[:, nt_sl(nt)],
                               lhsT=wqq16[:, k * 128:(k + 1) * 128],
                               rhs=t_sl(xn16, k, nt_sl(nt)),
                               start=(k == 0), stop=(k == NCH - 1))
            eng.nop().then_inc(c.s_te, 1)
        p.emit("tensor", qmm)
        tqv = p.bump(c.s_te, 1)

        for nt in range(4):
            def qe1(eng, och=och, nt=nt, tqv=tqv, ve0=ve0, bq_s=bq_s):
                if nt == 0:
                    eng.wait_ge(c.s_te, tqv)
                    eng.wait_ge(c.s_ls, bq_s)
                if nt >= 1:
                    eng.wait_ge(c.s_ve, ve0 + 2 * nt)
                eng.activation(out=q32, in_=c.ps1[:, nt_sl(nt)], func=AF.Identity,
                               bias=c.bias_sb[:, och:och + 1]).then_inc(c.s_se, 1)
            p.emit("scalar", qe1)
            sqv = p.bump(c.s_se, 1)

            def qe2(eng, sqv=sqv):
                eng.wait_ge(c.s_se, sqv)
                eng.tensor_scalar(out=qmin, in0=q32, scalar1=0.0, scalar2=None,
                                  op0=ALU.min).then_inc(c.s_ve, 1)
            p.emit("vector", qe2)
            vqv = p.bump(c.s_ve, 1)

            def qe3(eng, vqv=vqv):
                eng.wait_ge(c.s_ve, vqv)
                eng.activation(out=qmin, in_=qmin, func=AF.Exp).then_inc(c.s_se, 1)
            p.emit("scalar", qe3)
            sqv = p.bump(c.s_se, 1)

            def qe4(eng, och=och, nt=nt, sqv=sqv):
                eng.wait_ge(c.s_se, sqv)
                eng.scalar_tensor_tensor(out=qp16[:, och * T + nt * 512: och * T + (nt + 1) * 512],
                                         in0=q32, scalar=0.0, in1=qmin,
                                         op0=ALU.max, op1=ALU.add).then_inc(c.s_ve, 1)
            p.emit("vector", qe4)
            p.bump(c.s_ve, 1)
        p.barrier()
    a32.drop(q32)
    a32.drop(qmin)
    a16.drop(wqq16)
    c.a16.drop(c.xn16)
    c.xn16 = None
    _ckpt(c, "C1a")

    ctx16 = a16.take(8 * T, "ctx16")
    kv16 = a16.take(8 * D, "kv16")

    kvi_s = p.sem(f"kvi{p.nsem()}")

    def kvin(eng, gv=p.val(c.s_gd), kvi_s=kvi_s):
        eng.wait_ge(c.s_gd, gv)
        eng.dma_start(out=kv16, in_=c.dram_kv).then_inc(kvi_s, 16)
    p.emit("sync", kvin)
    kv_v = p.bump(kvi_s, 16)

    def zmm(eng):
        for k in range(NCH):
            for nt in range(4):
                eng.matmul(out=c.ps1[0:1, nt_sl(nt)], lhsT=c.small16[:, k:k + 1],
                           rhs=qp16[:, k * T + nt * 512: k * T + (nt + 1) * 512],
                           start=(k == 0), stop=(k == NCH - 1))
        eng.nop().then_inc(c.s_te, 1)
    p.emit("tensor", zmm)
    tev = p.bump(c.s_te, 1)
    zrow = a32.take(T, "zrow")

    def zrec(eng, tev=tev):
        eng.wait_ge(c.s_te, tev)
        eng.reciprocal(out=zrow[0:1, 0:T], in_=c.ps1[0:1, 0:T]).then_inc(c.s_ve, 1)
    p.emit("vector", zrec)
    vev = p.bump(c.s_ve, 1)
    tev = bcast_row(c, zrow[0:1, 0:T], T, c.ps1, wait=(c.s_ve, vev))
    for ch in range(NCH):
        def qsc(eng, ch=ch, tev=tev):
            if ch == 0:
                eng.wait_ge(c.s_te, tev)
            eng.tensor_tensor(out=t_sl(qp16, ch), in0=t_sl(qp16, ch), in1=c.ps1[:, 0:T],
                              op=ALU.mult)
        p.emit("vector", qsc)
    p.barrier()
    a32.drop(zrow)

    se0 = p.val(c.s_se)
    first = True
    for nt in range(4):
        for ech in range(NCH):
            bank = ech % 4
            idx = nt * 8 + ech

            def cmm(eng, nt=nt, ech=ech, bank=bank, idx=idx, first=first, se0=se0, kv_v=kv_v, kvi_s=kvi_s):
                if first:
                    eng.wait_ge(kvi_s, kv_v)
                if idx >= 4:
                    eng.wait_ge(c.s_se, se0 + idx - 3)
                for dk in range(NCH):
                    eng.matmul(out=c.ps1[:, bank * 512:(bank + 1) * 512],
                               lhsT=kv16[:, dk * D + ech * 128: dk * D + (ech + 1) * 128],
                               rhs=qp16[:, dk * T + nt * 512: dk * T + (nt + 1) * 512],
                               start=(dk == 0), stop=(dk == NCH - 1))
                eng.nop().then_inc(c.s_te, 1)
            p.emit("tensor", cmm)
            first = False
            tcv = p.bump(c.s_te, 1)

            def cev(eng, nt=nt, ech=ech, bank=bank, tcv=tcv):
                eng.wait_ge(c.s_te, tcv)
                eng.activation(out=ctx16[:, ech * T + nt * 512: ech * T + (nt + 1) * 512],
                               in_=c.ps1[:, bank * 512:(bank + 1) * 512],
                               func=AF.Copy).then_inc(c.s_se, 1)
            p.emit("scalar", cev)
            p.bump(c.s_se, 1)
    p.barrier()
    a16.drop(qp16)
    a16.drop(kv16)
    _ckpt(c, "C1b")

    c.s16 = a16.take(8 * T, "s16")
    s16 = c.s16
    wo16 = a16.take(8 * D, "wo16")
    load_w16(c, "L0_Wo", 8, D, wo16)
    bo_s = load_small(c, "L0_bo", c.bias_sb[:, 8:16])
    p.barrier()
    se0 = p.val(c.s_se)
    for och in range(NCH):
        for nt in range(4):
            bank = (och * 4 + nt) % 4
            idx = och * 4 + nt

            def smm(eng, och=och, nt=nt, bank=bank, idx=idx, se0=se0):
                if idx >= 4:
                    eng.wait_ge(c.s_se, se0 + idx - 3)
                for k in range(NCH):
                    eng.matmul(out=c.ps1[:, bank * 512:(bank + 1) * 512],
                               lhsT=wo16[:, k * D + och * 128: k * D + (och + 1) * 128],
                               rhs=ctx16[:, k * T + nt * 512: k * T + (nt + 1) * 512],
                               start=(k == 0), stop=(k == NCH - 1))
                eng.nop().then_inc(c.s_te, 1)
            p.emit("tensor", smm)
            tsv = p.bump(c.s_te, 1)

            def sev_(eng, och=och, nt=nt, bank=bank, tsv=tsv, bo_s=bo_s):
                eng.wait_ge(c.s_te, tsv)
                if och == 0 and nt == 0:
                    eng.wait_ge(c.s_ls, bo_s)
                eng.activation(out=t_sl(s16, och, nt_sl(nt)),
                               in_=c.ps1[:, bank * 512:(bank + 1) * 512],
                               func=AF.Identity, bias=c.bias_sb[:, 8 + och:9 + och]).then_inc(c.s_se, 1)
            p.emit("scalar", sev_)
            p.bump(c.s_se, 1)
    p.barrier()
    a16.drop(ctx16)
    a16.drop(wo16)
    _ckpt(c, "C2")

    emit_restore_x(c)
    ve0 = p.val(c.s_ve)
    for k in range(NCH):
        st = c.wstage[k % 2]
        need = ve0 + (k - 1) if k >= 2 else 0
        sem = c.s_wl[k % 2]
        prev = p.val(sem)

        def fl(eng, k=k, st=st, need=need, sem=sem, prev=prev):
            if need:
                eng.wait_ge(c.s_ve, need)
            if prev:
                eng.wait_ge(sem, prev)
            eng.dma_start(out=st[:, 0:D], in_=c.dram["L0_fw"][k * 128:(k + 1) * 128, :]).then_inc(sem, 16)
        p.emit("sync", fl)
        dvv = p.bump(sem, 16)

        def red(eng, k=k, st=st, dvv=dvv, sem=sem):
            eng.wait_ge(sem, dvv)
            eng.tensor_reduce(out=c.stat32[:, 1:2], in_=st[:, 0:D], axis=AX.X, op=ALU.add,
                              apply_absolute_value=True)
            if k == 0:
                eng.tensor_copy(out=c.stat32[:, 0:1], in_=c.stat32[:, 1:2]).then_inc(c.s_ve, 1)
            else:
                eng.tensor_tensor(out=c.stat32[:, 0:1], in0=c.stat32[:, 0:1],
                                  in1=c.stat32[:, 1:2], op=ALU.add).then_inc(c.s_ve, 1)
        p.emit("vector", red)
        p.bump(c.s_ve, 1)
    vev = p.val(c.s_ve)
    ones_col = c.cons32[:, OFF_ONES:OFF_ONES + 1]
    ones_row = c.cons32[0:1, OFF_ONES:OFF_ONES + 128]

    def wsmm(eng, vev=vev):
        eng.wait_ge(c.s_ve, vev)
        eng.matmul(out=c.ps2[0:1, 0:1], lhsT=ones_col, rhs=c.stat32[:, 0:1], start=True, stop=True)
        eng.nop().then_inc(c.s_te, 1)
    p.emit("tensor", wsmm)
    tev = p.bump(c.s_te, 1)

    def wsev(eng, tev=tev):
        eng.wait_ge(c.s_te, tev)
        eng.tensor_copy(out=c.stat32[0:1, 2:3], in_=c.ps2[0:1, 0:1]).then_inc(c.s_ve, 1)
    p.emit("vector", wsev)
    vev = p.bump(c.s_ve, 1)
    tev = bcast_row(c, c.stat32[0:1, 2:3], 1, c.ps2[:, 0:512], wait=(c.s_ve, vev))

    def wsmath(eng, tev=tev):
        eng.wait_ge(c.s_te, tev)
        eng.tensor_scalar(out=c.stat32[:, 3:4], in0=c.ps2[:, 0:1],
                          scalar1=1.0 / (D * D), scalar2=1e-8, op0=ALU.mult, op1=ALU.add)
        eng.reciprocal(out=c.stat32[:, 4:5], in_=c.stat32[:, 3:4]).then_inc(c.s_ve, 1)
    p.emit("vector", wsmath)
    p.bump(c.s_ve, 1)
    p.barrier()

    fwq16 = a16.take(8 * D, "fwq16")
    wtmp = a32.take(D, "wtmp")
    ve0 = p.val(c.s_ve)
    for k in range(NCH):
        st = c.wstage[k % 2]
        need = ve0 + 2 * (k - 1) if k >= 2 else 0
        sem = c.s_wl[k % 2]
        prev = p.val(sem)

        def fl(eng, k=k, st=st, need=need, sem=sem, prev=prev):
            if need:
                eng.wait_ge(c.s_ve, need)
            if prev:
                eng.wait_ge(sem, prev)
            eng.dma_start(out=st[:, 0:D], in_=c.dram["L0_fw"][k * 128:(k + 1) * 128, :]).then_inc(sem, 16)
        p.emit("sync", fl)
        dvv = p.bump(sem, 16)

        def qz(eng, k=k, st=st, dvv=dvv, sem=sem):
            eng.wait_ge(sem, dvv)
            eng.tensor_scalar(out=wtmp, in0=st[:, 0:D], scalar1=c.stat32[:, 4:5],
                              scalar2=12582912.0, op0=ALU.mult, op1=ALU.add)
            eng.tensor_scalar(out=wtmp, in0=wtmp, scalar1=-12582912.0, scalar2=-1.0,
                              op0=ALU.add, op1=ALU.max)
            eng.tensor_scalar(out=fwq16[:, k * D:(k + 1) * D], in0=wtmp,
                              scalar1=1.0, scalar2=None,
                              op0=ALU.min).then_inc(c.s_ve, 2)
        p.emit("vector", qz)
        p.bump(c.s_ve, 2)
    p.barrier()

    _ckpt(c, "wq")
    emit_ffn0_combine(c, fwq16)
    p.barrier()
    a16.drop(fwq16)
    a32.drop(wtmp)
    a16.drop(c.s16)
    c.s16 = None


def emit_ffn0_combine(c, fwq16):
    p = c.p
    a16, a32 = c.a16, c.a32
    x = c.x
    s16 = c.s16
    id32 = c.cons32[:, OFF_ID:OFF_ID + 128]
    id16 = c.cons16[:, OFF_ID16:OFF_ID16 + 128]
    xtm = a32.take(D, "xtm")
    xtm0 = a32.take(D, "xtm0")
    sqd = a32.take(D, "sqd")
    sg32 = a32.take(512, "sg32")
    aqs = a16.take(D, "aqs")
    aqT = a16.take(8 * 512, "aqT")
    g16a = a16.take(512, "g16a")
    g16b = a16.take(512, "g16b")
    for grp in range(4):
        se0 = p.val(c.s_se)
        ve0 = p.val(c.s_ve)
        for j in range(4):
            tc = grp * 4 + j

            def tx(eng, tc=tc, j=j, ve0=ve0):
                if j > 0:
                    eng.wait_ge(c.s_ve, ve0 + 4 * (j - 1) + 1)
                for ch in range(NCH):
                    eng.transpose(out=c.ps2[:, ch * 128:(ch + 1) * 128],
                                  in_=t_sl(x, ch, slice(tc * 128, (tc + 1) * 128)),
                                  identity=id32)
                eng.nop().then_inc(c.s_te, 1)
            p.emit("tensor", tx)
            tv = p.bump(c.s_te, 1)

            def ev0(eng, tv=tv, j=j, se0=se0):
                eng.wait_ge(c.s_te, tv)
                if j > 0:
                    eng.wait_ge(c.s_se, se0 + 4 * (j - 1) + 3)
                eng.tensor_scalar(out=xtm0, in0=c.ps2[:, 0:D], scalar1=1.0, scalar2=0.0,
                                  op0=ALU.mult, op1=ALU.add,
                                  accum_out=c.stat32[:, 11:12]).then_inc(c.s_ve, 1)
            p.emit("vector", ev0)
            vv0 = p.bump(c.s_ve, 1)

            def sq0(eng, vv0=vv0):
                eng.wait_ge(c.s_ve, vv0)
                eng.activation(out=sqd, in_=xtm0, func=AF.Square,
                               accum_out=c.stat32[:, 12:13]).then_inc(c.s_se, 1)
            p.emit("scalar", sq0)
            sv0 = p.bump(c.s_se, 1)

            def lnst(eng, sv0=sv0):
                eng.wait_ge(c.s_se, sv0)
                eng.tensor_scalar(out=c.stat32[:, 13:14], in0=c.stat32[:, 11:12],
                                  scalar1=1.0 / D, scalar2=None, op0=ALU.mult)
                eng.tensor_scalar(out=c.stat32[:, 14:15], in0=c.stat32[:, 12:13],
                                  scalar1=1.0 / D, scalar2=None, op0=ALU.mult)
                eng.tensor_tensor(out=c.stat32[:, 15:16], in0=c.stat32[:, 13:14],
                                  in1=c.stat32[:, 13:14], op=ALU.mult)
                eng.tensor_tensor(out=c.stat32[:, 14:15], in0=c.stat32[:, 14:15],
                                  in1=c.stat32[:, 15:16], op=ALU.subtract).then_inc(c.s_ve, 1)
            p.emit("vector", lnst)
            vv1 = p.bump(c.s_ve, 1)

            def lnsq(eng, vv1=vv1):
                eng.wait_ge(c.s_ve, vv1)
                eng.activation(out=c.stat32[:, 14:15], in_=c.stat32[:, 14:15], func=AF.Sqrt,
                               bias=c.cc(1e-5)).then_inc(c.s_se, 1)
            p.emit("scalar", lnsq)
            sv1 = p.bump(c.s_se, 1)

            def lnal(eng, sv1=sv1):
                eng.wait_ge(c.s_se, sv1)
                eng.reciprocal(out=c.stat32[:, 15:16], in_=c.stat32[:, 14:15])
                eng.tensor_tensor(out=c.stat32[:, 13:14], in0=c.stat32[:, 13:14],
                                  in1=c.stat32[:, 15:16], op=ALU.mult)
                eng.tensor_scalar(out=c.stat32[:, 13:14], in0=c.stat32[:, 13:14],
                                  scalar1=-1.0, scalar2=None, op0=ALU.mult).then_inc(c.s_ve, 1)
            p.emit("vector", lnal)
            vv2 = p.bump(c.s_ve, 1)

            def lnap(eng, vv2=vv2):
                eng.wait_ge(c.s_ve, vv2)
                eng.activation(out=xtm, in_=xtm0, func=AF.Identity,
                               bias=c.stat32[:, 13:14], scale=c.stat32[:, 15:16]).then_inc(c.s_se, 1)
            p.emit("scalar", lnap)
            sv = p.bump(c.s_se, 1)

            def qnt(eng, sv=sv):
                eng.wait_ge(c.s_se, sv)
                eng.tensor_reduce(out=c.stat32[:, 8:9], in_=xtm, axis=AX.X, op=ALU.max,
                                  apply_absolute_value=True)
                eng.tensor_scalar(out=c.stat32[:, 8:9], in0=c.stat32[:, 8:9], scalar1=1e-8,
                                  scalar2=None, op0=ALU.add)
                eng.reciprocal(out=c.stat32[:, 9:10], in_=c.stat32[:, 8:9])
                eng.tensor_scalar(out=sqd, in0=xtm, scalar1=c.stat32[:, 9:10],
                                  scalar2=127.0, op0=ALU.mult, op1=ALU.mult)
                eng.tensor_scalar(out=sqd, in0=sqd, scalar1=12582912.0, scalar2=-12582912.0,
                                  op0=ALU.add, op1=ALU.add)
                eng.tensor_scalar(out=c.stat32[:, 10:11], in0=c.stat32[:, 8:9],
                                  scalar1=c.stat32[:, 3:4], scalar2=1.0 / 127.0,
                                  op0=ALU.mult, op1=ALU.mult)
                eng.tensor_scalar(out=aqs, in0=sqd, scalar1=c.stat32[:, 10:11],
                                  scalar2=None, op0=ALU.mult).then_inc(c.s_ve, 1)
            p.emit("vector", qnt)
            vv = p.bump(c.s_ve, 1)

            def tb(eng, j=j, vv=vv, se0=se0):
                eng.wait_ge(c.s_ve, vv)
                if j > 0:
                    eng.wait_ge(c.s_se, se0 + 4 * j)
                for ch in range(NCH):
                    eng.transpose(out=c.psh[:, ch * 128:(ch + 1) * 128],
                                  in_=aqs[:, ch * 128:(ch + 1) * 128], identity=id16)
                eng.nop().then_inc(c.s_te, 1)
            p.emit("tensor", tb)
            tv = p.bump(c.s_te, 1)

            def tbe(eng, j=j, tv=tv):
                eng.wait_ge(c.s_te, tv)
                for ch in range(NCH):
                    eng.activation(out=aqT[:, ch * 512 + j * 128: ch * 512 + j * 128 + 128],
                                   in_=c.psh[:, ch * 128:(ch + 1) * 128], func=AF.Copy)
                eng.nop().then_inc(c.s_se, 1)
            p.emit("scalar", tbe)
            p.bump(c.s_se, 1)
        sev = p.val(c.s_se)
        ve1 = p.val(c.s_ve)
        for och in range(NCH):
            g16 = g16a if och % 2 == 0 else g16b

            def ymm(eng, och=och, sev=sev, ve1=ve1):
                eng.wait_ge(c.s_se, sev)
                if och >= 2:
                    eng.wait_ge(c.s_ve, ve1 + 2 * (och - 1))
                co = (och % 2) * 512
                for ich in range(NCH):
                    eng.matmul(out=c.ps1[:, co: co + 512],
                               lhsT=fwq16[:, ich * D + och * 128: ich * D + (och + 1) * 128],
                               rhs=aqT[:, ich * 512:(ich + 1) * 512],
                               start=(ich == 0), stop=(ich == NCH - 1))
                eng.nop().then_inc(c.s_te, 1)
            p.emit("tensor", ymm)
            tyv = p.bump(c.s_te, 1)

            def gsil(eng, och=och, tyv=tyv, g16=g16):
                eng.wait_ge(c.s_te, tyv)
                co = (och % 2) * 512
                eng.activation(out=g16, in_=c.ps1[:, co: co + 512],
                               func=AF.Silu).then_inc(c.s_se, 1)
            p.emit("scalar", gsil)
            syv = p.bump(c.s_se, 1)

            def comb(eng, grp=grp, och=och, syv=syv, g16=g16):
                eng.wait_ge(c.s_se, syv)
                sl = slice(grp * 512, (grp + 1) * 512)
                eng.tensor_tensor(out=sg32, in0=t_sl(s16, och, sl), in1=g16, op=ALU.mult)
                eng.tensor_tensor(out=t_sl(x, och, sl), in0=t_sl(x, och, sl), in1=sg32,
                                  op=ALU.add).then_inc(c.s_ve, 2)
            p.emit("vector", comb)
            p.bump(c.s_ve, 2)
        p.barrier()
    for b in (xtm, xtm0, sqd, sg32):
        a32.drop(b)
    for b in (aqs, aqT, g16a, g16b):
        a16.drop(b)


def nvfp4_ops(c, eng, src32, F, dst16, gs_col, rgs_col, bigA, bigB, sm):
    I32 = mybir.dt.int32
    blk = F // 16
    sv = lambda i: sm[:, i * blk:(i + 1) * blk]
    a3 = lambda t: t.rearrange("p (b k) -> p b k", k=16)
    bview = lambda t: t.unsqueeze(2).broadcast_to([128, blk, 16])
    eng.tensor_reduce(out=sv(0), in_=a3(src32), axis=AX.X, op=ALU.max,
                      apply_absolute_value=True)
    eng.tensor_scalar(out=bigA.bitcast(I32), in0=src32.bitcast(I32), scalar1=0x7FFFFFFF,
                      scalar2=None, op0=ALU.bitwise_and)
    eng.tensor_scalar(out=sv(1), in0=sv(0), scalar1=1.0 / 6.0, scalar2=gs_col,
                      op0=ALU.mult, op1=ALU.mult)
    eng.tensor_scalar(out=sv(1), in0=sv(1), scalar1=448.0, scalar2=None, op0=ALU.min)
    ui = sv(1).bitcast(I32)
    eng.tensor_scalar(out=sv(2).bitcast(I32), in0=ui, scalar1=23, scalar2=255,
                      op0=ALU.logical_shift_right, op1=ALU.bitwise_and)
    eng.tensor_scalar(out=sv(3).bitcast(I32), in0=ui, scalar1=20, scalar2=7,
                      op0=ALU.logical_shift_right, op1=ALU.bitwise_and)
    eng.tensor_scalar(out=sv(4).bitcast(I32), in0=ui, scalar1=0xFFFFF, scalar2=None,
                      op0=ALU.bitwise_and)
    eng.tensor_scalar(out=sv(5), in0=sv(4).bitcast(I32), scalar1=524288.0, scalar2=None,
                      op0=ALU.is_gt)
    eng.tensor_scalar(out=sv(6), in0=sv(4).bitcast(I32), scalar1=524288.0, scalar2=None,
                      op0=ALU.is_equal)
    eng.tensor_scalar(out=sv(4).bitcast(I32), in0=sv(3).bitcast(I32), scalar1=1,
                      scalar2=None, op0=ALU.bitwise_and)
    eng.tensor_tensor(out=sv(6), in0=sv(6), in1=sv(4).bitcast(I32), op=ALU.mult)
    eng.tensor_tensor(out=sv(5), in0=sv(5), in1=sv(6), op=ALU.add)
    eng.scalar_tensor_tensor(out=sv(5), in0=sv(3).bitcast(I32), scalar=8.0, in1=sv(5),
                             op0=ALU.add, op1=ALU.add)
    eng.tensor_scalar(out=sv(3).bitcast(I32), in0=sv(2).bitcast(I32), scalar1=-3,
                      scalar2=0, op0=ALU.add, op1=ALU.max)
    eng.tensor_scalar(out=sv(3).bitcast(I32), in0=sv(3).bitcast(I32), scalar1=23,
                      scalar2=None, op0=ALU.logical_shift_left)
    eng.tensor_tensor(out=sv(2), in0=sv(5), in1=sv(3), op=ALU.mult)
    eng.tensor_scalar(out=sv(6), in0=sv(1), scalar1=512.0, scalar2=8388608.0,
                      op0=ALU.mult, op1=ALU.add)
    eng.tensor_scalar(out=sv(6), in0=sv(6), scalar1=-8388608.0, scalar2=1.0 / 512.0,
                      op0=ALU.add, op1=ALU.mult)
    eng.tensor_scalar(out=sv(4).bitcast(I32), in0=sv(1), scalar1=0.015625, scalar2=None,
                      op0=ALU.is_lt)
    eng.select(out=sv(0), mask=sv(4).bitcast(I32), on_true=sv(6), on_false=sv(2),
               add_drain=True)
    eng.tensor_scalar(out=sv(1), in0=sv(0), scalar1=rgs_col, scalar2=None, op0=ALU.mult)
    eng.tensor_scalar(out=sv(2), in0=sv(1), scalar1=0.0, scalar2=None, op0=ALU.is_equal)
    eng.tensor_tensor(out=sv(2), in0=sv(1), in1=sv(2), op=ALU.add)
    eng.reciprocal(out=sv(2), in_=sv(2))
    eng.tensor_tensor(out=a3(bigA), in0=a3(bigA), in1=bview(sv(2)), op=ALU.mult)
    MAGIC = 8388608.0
    eng.tensor_scalar(out=dst16, in0=bigA, scalar1=5.0, scalar2=None, op0=ALU.is_ge)
    eng.tensor_scalar(out=bigB, in0=bigA, scalar1=2.0, scalar2=MAGIC, op0=ALU.mult, op1=ALU.add)
    eng.tensor_scalar(out=bigB, in0=bigB, scalar1=-MAGIC, scalar2=None, op0=ALU.add)
    eng.tensor_scalar(out=bigB, in0=bigB, scalar1=0.5, scalar2=2.0,
                      op0=ALU.mult, op1=ALU.min)
    eng.tensor_scalar(out=bigA, in0=bigA, scalar1=MAGIC, scalar2=None, op0=ALU.add)
    eng.tensor_scalar(out=bigA, in0=bigA, scalar1=-MAGIC - 2.0, scalar2=0.0,
                      op0=ALU.add, op1=ALU.max)
    eng.scalar_tensor_tensor(out=bigB, in0=bigA, scalar=2.0, in1=bigB,
                             op0=ALU.min, op1=ALU.add)
    eng.scalar_tensor_tensor(out=bigB, in0=dst16, scalar=2.0, in1=bigB,
                             op0=ALU.mult, op1=ALU.add)
    eng.tensor_scalar(out=bigA, in0=src32, scalar1=0.0, scalar2=None, op0=ALU.is_lt)
    eng.tensor_scalar(out=bigA, in0=bigA, scalar1=-2.0, scalar2=1.0, op0=ALU.mult, op1=ALU.add)
    eng.tensor_tensor(out=a3(bigB), in0=a3(bigB), in1=bview(sv(0)), op=ALU.mult)
    eng.tensor_tensor(out=dst16, in0=bigB, in1=bigA, op=ALU.mult)


def stream_wave(c, tag):
    raise NotImplementedError


def emit_layer1(c):
    p = c.p
    a16, a32 = c.a16, c.a32
    F16S = c.dram_f16
    QOFF, KOFF, VOFF, SOFF = 0, 16384, 32768, 49152
    emit_ln(c, with_xn16=True)
    p.barrier()
    emit_spill_x(c)
    xn16 = c.xn16

    wq16 = a16.take(8 * D, "l1wqkv")
    pstage = [a16.take(1024, "pst0"), a16.take(1024, "pst1")]
    psems = [p.sem(f"l1p{p.nsem()}"), p.sem(f"l1p{p.nsem()}b")]
    for part in range(3):
        load_w16(c, "L1_wqkv", 8, D, wq16, col_sl=slice(part * D, (part + 1) * D))
        p.barrier()
        if part < 2:
            se0 = p.val(c.s_se)
            idx = 0
            for och in range(NCH):
                for nt in range(4):
                    bank = idx % 4

                    def mmp(eng, och=och, nt=nt, bank=bank, idx=idx, se0=se0):
                        if idx >= 4:
                            eng.wait_ge(c.s_se, se0 + idx - 3)
                        for k in range(NCH):
                            eng.matmul(out=c.ps1[:, bank * 512:(bank + 1) * 512],
                                       lhsT=wq16[:, k * D + och * 128: k * D + (och + 1) * 128],
                                       rhs=t_sl(xn16, k, nt_sl(nt)),
                                       start=(k == 0), stop=(k == NCH - 1))
                        eng.nop().then_inc(c.s_te, 1)
                    p.emit("tensor", mmp)
                    tv = p.bump(c.s_te, 1)
                    stg = pstage[idx % 2]
                    ps_ = psems[idx % 2]
                    pprev = p.val(ps_)

                    def pev(eng, bank=bank, tv=tv, stg=stg, ps_=ps_, pprev=pprev, part=part):
                        eng.wait_ge(c.s_te, tv)
                        if pprev:
                            eng.wait_ge(ps_, pprev)
                        eng.activation(out=stg[:, 0:512], in_=c.ps1[:, bank * 512:(bank + 1) * 512],
                                       func=AF.Identity,
                                       scale=(1.0 / 32.0 if part == 0 else 1.0)).then_inc(c.s_se, 1)
                    p.emit("scalar", pev)
                    sv = p.bump(c.s_se, 1)

                    def pout(eng, och=och, nt=nt, stg=stg, sv=sv, part=part, ps_=ps_, pprev=pprev):
                        eng.wait_ge(c.s_se, sv)
                        if pprev:
                            eng.wait_ge(ps_, pprev)
                        off = (QOFF if part == 0 else KOFF) + och * 2048 + nt * 512
                        eng.dma_start(out=F16S[:, off:off + 512], in_=stg[:, 0:512]).then_inc(ps_, 16)
                    p.emit("sync", pout)
                    p.bump(ps_, 16)
                    idx += 1
        else:
            se0 = p.val(c.s_se)
            for tc in range(16):
                def mmv(eng, tc=tc, se0=se0):
                    if tc >= 1:
                        eng.wait_ge(c.s_se, se0 + tc)
                    for k in range(NCH):
                        lhs = t_sl(xn16, k, slice(tc * 128, (tc + 1) * 128))
                        for h in range(2):
                            eng.matmul(out=c.ps2[:, h * 512:(h + 1) * 512], lhsT=lhs,
                                       rhs=wq16[:, k * D + h * 512: k * D + (h + 1) * 512],
                                       start=(k == 0), stop=(k == NCH - 1))
                    eng.nop().then_inc(c.s_te, 1)
                p.emit("tensor", mmv)
                tv = p.bump(c.s_te, 1)
                stg = pstage[tc % 2]
                ps_ = psems[tc % 2]
                pprev = p.val(ps_)

                def vev(eng, tv=tv, stg=stg, ps_=ps_, pprev=pprev):
                    eng.wait_ge(c.s_te, tv)
                    if pprev:
                        eng.wait_ge(ps_, pprev)
                    eng.activation(out=stg[:, 0:1024], in_=c.ps2[:, 0:1024],
                                   func=AF.Copy).then_inc(c.s_se, 1)
                p.emit("scalar", vev)
                sv = p.bump(c.s_se, 1)

                def vout(eng, tc=tc, stg=stg, sv=sv, ps_=ps_, pprev=pprev):
                    eng.wait_ge(c.s_se, sv)
                    if pprev:
                        eng.wait_ge(ps_, pprev)
                    eng.dma_start(out=F16S[:, VOFF + tc * 1024: VOFF + (tc + 1) * 1024],
                                  in_=stg[:, 0:1024]).then_inc(ps_, 16)
                p.emit("sync", vout)
                p.bump(ps_, 16)
    p.barrier()
    a16.drop(wq16)
    for b in pstage:
        a16.drop(b)
    c.a16.drop(c.xn16)
    c.xn16 = None

    outw16 = a16.take(8 * D, "l1outw")
    load_w16(c, "L1_outw", 8, D, outw16)
    outb_s = load_small(c, "L1_outb", c.bias_sb[:, 0:8])
    kt16 = a16.take(8 * T, "l1kt")
    ktl_s = p.sem(f"l1kt{p.nsem()}")

    def ktl(eng, pv0=p.val(psems[0]), pv1=p.val(psems[1])):
        eng.wait_ge(psems[0], pv0)
        eng.wait_ge(psems[1], pv1)
        eng.dma_start(out=kt16, in_=F16S[:, KOFF:KOFF + 16384]).then_inc(ktl_s, 16)
    p.emit("sync", ktl)
    ktv = p.bump(ktl_s, 16)
    p.barrier()

    qbuf = [a16.take(1024, "qb0"), a16.take(1024, "qb1")]
    vbuf = [a16.take(1024, "vb%d" % i) for i in range(4)]
    p16 = a16.take(2048, "p16")
    pt16 = a16.take(2048, "pt16")
    y16 = a16.take(1024, "y16")
    ytg = a16.take(8 * 512, "ytg")
    sstg = [a16.take(512, "ss0"), a16.take(512, "ss1")]
    qsem = [p.sem("l1q0"), p.sem("l1q1")]
    vsem = [p.sem("l1v0"), p.sem("l1v1"), p.sem("l1v2"), p.sem("l1v3")]
    av_done = []
    q_done = []
    ssem = [p.sem("l1s0"), p.sem("l1s1")]
    id16 = c.cons16[:, OFF_ID16:OFF_ID16 + 128]
    for g4 in range(4):
        ve_g = p.val(c.s_ve)
        se_g = p.val(c.s_se)
        for j in range(4):
            tq = g4 * 4 + j
            qs = qsem[tq % 2]
            qprev = p.val(qs)

            def qload(eng, tq=tq, qs=qs, qprev=qprev, first=(tq == 0), ktv=ktv,
                      buf=qbuf[tq % 2], pv0=p.val(psems[0]), pv1=p.val(psems[1]),
                      qg=(q_done[-2] if len(q_done) >= 2 else 0)):
                if first:
                    eng.wait_ge(psems[0], pv0)
                    eng.wait_ge(psems[1], pv1)
                    eng.wait_ge(ktl_s, ktv)
                if qprev:
                    eng.wait_ge(qs, qprev)
                if qg:
                    eng.wait_ge(c.s_te, qg)
                srcv = F16S[:, QOFF:QOFF + 16384].rearrange("p (o t) -> p o t", t=2048)
                eng.dma_start(out=buf.rearrange("p (o t) -> p o t", t=128),
                              in_=srcv[:, :, tq * 128:(tq + 1) * 128]).then_inc(qs, 16)
            p.emit("sync", qload)
            qv = p.bump(qs, 16)

            def smm(eng, tq=tq, j=j, qv=qv, qs=qs, se_g=se_g, buf=qbuf[tq % 2]):
                eng.wait_ge(qs, qv)
                if j >= 1:
                    eng.wait_ge(c.s_se, se_g + 3 * (j - 1) + 1)
                for ntk in range(4):
                    for k in range(NCH):
                        eng.matmul(out=c.ps1[:, ntk * 512:(ntk + 1) * 512],
                                   lhsT=buf[:, k * 128:(k + 1) * 128],
                                   rhs=t_sl(kt16, k, nt_sl(ntk)),
                                   start=(k == 0), stop=(k == NCH - 1))
                eng.nop().then_inc(c.s_te, 1)
            p.emit("tensor", smm)
            tv = p.bump(c.s_te, 1)
            q_done.append(tv)

            def smax(eng, tv=tv):
                eng.wait_ge(c.s_te, tv)
                eng.tensor_reduce(out=c.stat32[:, 0:1], in_=c.ps1[:, 0:T], axis=AX.X,
                                  op=ALU.max, negate=True).then_inc(c.s_ve, 1)
            p.emit("vector", smax)
            vv = p.bump(c.s_ve, 1)

            def sexp(eng, vv=vv):
                eng.wait_ge(c.s_ve, vv)
                eng.activation(out=p16, in_=c.ps1[:, 0:T], func=AF.Exp,
                               bias=c.stat32[:, 0:1],
                               accum_out=c.stat32[:, 1:2]).then_inc(c.s_se, 1)
            p.emit("scalar", sexp)
            sv = p.bump(c.s_se, 1)

            def srec(eng, sv=sv):
                eng.wait_ge(c.s_se, sv)
                eng.reciprocal(out=c.stat32[:, 2:3], in_=c.stat32[:, 1:2]).then_inc(c.s_ve, 1)
            p.emit("vector", srec)
            p.bump(c.s_ve, 1)

            def ptr(eng, sv=sv, j=j, se_g=se_g):
                eng.wait_ge(c.s_se, sv)
                if j >= 1:
                    eng.wait_ge(c.s_se, se_g + 3 * j)
                for i in range(16):
                    eng.transpose(out=c.psh[:, i * 128:(i + 1) * 128],
                                  in_=p16[:, i * 128:(i + 1) * 128], identity=id16)
                eng.nop().then_inc(c.s_te, 1)
            p.emit("tensor", ptr)
            tv = p.bump(c.s_te, 1)

            def ptev(eng, tv=tv):
                eng.wait_ge(c.s_te, tv)
                eng.activation(out=pt16, in_=c.psh[:, 0:T], func=AF.Copy).then_inc(c.s_se, 1)
            p.emit("scalar", ptev)
            ptv = p.bump(c.s_se, 1)

            for tk in range(16):
                vs = vsem[tk % 4]
                vprev = p.val(vs)

                def vload(eng, tk=tk, vs=vs, vprev=vprev, buf=vbuf[tk % 4],
                          ag=(av_done[-4] if len(av_done) >= 4 else 0)):
                    if vprev:
                        eng.wait_ge(vs, vprev)
                    if ag:
                        eng.wait_ge(c.s_te, ag)
                    eng.dma_start(out=buf,
                                  in_=F16S[:, VOFF + tk * 1024: VOFF + (tk + 1) * 1024]).then_inc(vs, 16)
                p.emit("sync", vload)
                vv2 = p.bump(vs, 16)

                def avmm(eng, tk=tk, vs=vs, vv2=vv2, ptv=ptv, buf=vbuf[tk % 4]):
                    eng.wait_ge(vs, vv2)
                    if tk == 0:
                        eng.wait_ge(c.s_se, ptv)
                    for h in range(2):
                        eng.matmul(out=c.ps2[:, h * 512:(h + 1) * 512],
                                   lhsT=pt16[:, tk * 128:(tk + 1) * 128],
                                   rhs=buf[:, h * 512:(h + 1) * 512],
                                   start=(tk == 0), stop=(tk == 15))
                    eng.nop().then_inc(c.s_te, 1)
                p.emit("tensor", avmm)
                av_done.append(p.bump(c.s_te, 1))
            tv = p.val(c.s_te)

            def yev(eng, tv=tv):
                eng.wait_ge(c.s_te, tv)
                eng.tensor_scalar(out=y16, in0=c.ps2[:, 0:1024], scalar1=c.stat32[:, 2:3],
                                  scalar2=None, op0=ALU.mult).then_inc(c.s_ve, 1)
            p.emit("vector", yev)
            vv = p.bump(c.s_ve, 1)

            def ytr(eng, vv=vv, j=j, se_g=se_g):
                eng.wait_ge(c.s_ve, vv)
                eng.wait_ge(c.s_se, se_g + 3 * j + 2)
                for i in range(NCH):
                    eng.transpose(out=c.psh[:, i * 128:(i + 1) * 128],
                                  in_=y16[:, i * 128:(i + 1) * 128], identity=id16)
                eng.nop().then_inc(c.s_te, 1)
            p.emit("tensor", ytr)
            tv = p.bump(c.s_te, 1)

            def ytev(eng, tv=tv, j=j):
                eng.wait_ge(c.s_te, tv)
                for i in range(NCH):
                    eng.activation(out=ytg[:, i * 512 + j * 128: i * 512 + j * 128 + 128],
                                   in_=c.psh[:, i * 128:(i + 1) * 128], func=AF.Copy)
                eng.nop().then_inc(c.s_se, 1)
            p.emit("scalar", ytev)
            p.bump(c.s_se, 1)
        sev = p.val(c.s_se)
        for och in range(NCH):
            bank = och % 4

            def somm(eng, och=och, bank=bank, sev=sev):
                eng.wait_ge(c.s_se, sev)
                if och >= 4:
                    eng.wait_ge(c.s_se, sev + och - 3)
                for k in range(NCH):
                    eng.matmul(out=c.ps1[:, bank * 512:(bank + 1) * 512],
                               lhsT=outw16[:, k * D + och * 128: k * D + (och + 1) * 128],
                               rhs=ytg[:, k * 512:(k + 1) * 512],
                               start=(k == 0), stop=(k == NCH - 1))
                eng.nop().then_inc(c.s_te, 1)
            p.emit("tensor", somm)
            tv = p.bump(c.s_te, 1)
            stg = sstg[och % 2]
            ss_ = ssem[och % 2]
            ssp = p.val(ss_)

            def soev(eng, och=och, bank=bank, tv=tv, stg=stg, ss_=ss_, ssp=ssp,
                     obs=outb_s):
                eng.wait_ge(c.s_te, tv)
                if och == 0:
                    eng.wait_ge(c.s_ls, obs)
                if ssp:
                    eng.wait_ge(ss_, ssp)
                eng.activation(out=stg[:, 0:512], in_=c.ps1[:, bank * 512:(bank + 1) * 512],
                               func=AF.Identity, bias=c.bias_sb[:, och:och + 1]).then_inc(c.s_se, 1)
            p.emit("scalar", soev)
            sv = p.bump(c.s_se, 1)

            def sout(eng, och=och, g4=g4, stg=stg, sv=sv, ss_=ss_, ssp=ssp):
                eng.wait_ge(c.s_se, sv)
                if ssp:
                    eng.wait_ge(ss_, ssp)
                eng.dma_start(out=F16S[:, SOFF + och * 2048 + g4 * 512: SOFF + och * 2048 + g4 * 512 + 512],
                              in_=stg[:, 0:512]).then_inc(ss_, 16)
            p.emit("sync", sout)
            p.bump(ss_, 16)
        p.barrier()
    for b in (qbuf[0], qbuf[1], vbuf[0], vbuf[1], vbuf[2], vbuf[3], p16, pt16, y16, ytg, sstg[0], sstg[1], kt16, outw16):
        a16.drop(b)
    c._l1_ssem = ssem
    _ckpt(c, "l1a")

    emit_l1_wquant(c)
    _ckpt(c, "l1w")

    emit_l1_ffn(c)
    p.barrier()


def emit_l1_wquant(c):
    p = c.p
    a16, a32 = c.a16, c.a32
    id16 = c.cons16[:, OFF_ID16:OFF_ID16 + 128]
    id32 = c.cons32[:, OFF_ID:OFF_ID + 128]
    W16 = c.dram_w16
    sin_s = load_small(c, "L1_sin", c.stat32[0:1, 14:15])
    sind_s = load_small(c, "L1_sind", c.stat32[0:1, 15:16])
    bigA = a32.take(2048, "l1qA")
    bigB = a32.take(2048, "l1qB")
    sm = a32.take(7 * 128, "l1qsm")
    wq16row = a16.take(2048, "wq16row")
    wstg = [a16.take(2048, "wts0"), a16.take(2048, "wts1")]
    wos = [p.sem(f"l1wo{p.nsem()}"), p.sem(f"l1wo{p.nsem()}b")]
    for wi, (nm, rows, cols, woff, gcol) in enumerate([
            ("L1_gate", 4096, 1024, 0, 8),
            ("L1_up", 4096, 1024, 32768, 10),
            ("L1_down", 1024, 4096, 65536, 12)]):
        kch = rows // 128
        npc = cols // 1024
        npieces = (kch * npc) // 2

        def _ld2048(eng, st, nm, pi, npc, sem):
            if npc == 1:
                eng.dma_start(out=st[:, 0:1024],
                              in_=c.dram[nm][pi * 256: pi * 256 + 128, :]).then_inc(sem, 16)
                eng.dma_start(out=st[:, 1024:2048],
                              in_=c.dram[nm][pi * 256 + 128: pi * 256 + 256, :]).then_inc(sem, 16)
            else:
                k, pc2 = divmod(pi, npc // 2)
                eng.dma_start(out=st[:, 0:2048],
                              in_=c.dram[nm][k * 128:(k + 1) * 128, pc2 * 2048:(pc2 + 1) * 2048]).then_inc(sem, 32)
        ve0 = p.val(c.s_ve)
        step = 0
        for pi in range(npieces):
                st = c.wstage[step % 2]
                need = ve0 + (step - 1) if step >= 2 else 0
                sem = c.s_wl[step % 2]
                prev = p.val(sem)

                def fl(eng, pi=pi, st=st, need=need, sem=sem, prev=prev, nm=nm,
                       npc=npc):
                    if need:
                        eng.wait_ge(c.s_ve, need)
                    if prev:
                        eng.wait_ge(sem, prev)
                    _ld2048(eng, st, nm, pi, npc, sem)
                p.emit("sync", fl)
                dv = p.bump(sem, 32)

                def red(eng, st=st, dv=dv, sem=sem, first=(step == 0)):
                    eng.wait_ge(sem, dv)
                    eng.tensor_reduce(out=c.stat32[:, 1:2], in_=st[:, 0:2048], axis=AX.X,
                                      op=ALU.max, apply_absolute_value=True)
                    if first:
                        eng.tensor_copy(out=c.stat32[:, 0:1], in_=c.stat32[:, 1:2]).then_inc(c.s_ve, 1)
                    else:
                        eng.tensor_tensor(out=c.stat32[:, 0:1], in0=c.stat32[:, 0:1],
                                          in1=c.stat32[:, 1:2], op=ALU.max).then_inc(c.s_ve, 1)
                p.emit("vector", red)
                p.bump(c.s_ve, 1)
                step += 1
        vev = p.val(c.s_ve)

        def pmax(eng, vev=vev):
            eng.wait_ge(c.s_ve, vev)
            eng.transpose(out=c.ps2[0:1, 0:128], in_=c.stat32[:, 0:1], identity=id32)
            eng.nop().then_inc(c.s_te, 1)
        p.emit("tensor", pmax)
        tv = p.bump(c.s_te, 1)

        def gmath(eng, tv=tv):
            eng.wait_ge(c.s_te, tv)
            eng.reduce_max(out=c.stat32[0:1, 2:3], in_=c.ps2[0:1, 0:128], axis=AX.X)
            eng.reciprocal(out=c.stat32[0:1, 3:4], in_=c.stat32[0:1, 2:3])
            eng.tensor_scalar(out=c.stat32[0:1, 3:4], in0=c.stat32[0:1, 3:4], scalar1=2688.0,
                              scalar2=None, op0=ALU.mult)
            eng.tensor_scalar(out=c.stat32[0:1, 2:3], in0=c.stat32[0:1, 2:3], scalar1=1.0 / 2688.0,
                              scalar2=None, op0=ALU.mult).then_inc(c.s_ve, 1)
        p.emit("vector", gmath)
        vev = p.bump(c.s_ve, 1)
        tv = bcast_row(c, c.stat32[0:1, 3:4], 1, c.ps2[:, 0:512], wait=(c.s_ve, vev))
        tv2 = bcast_row(c, c.stat32[0:1, 2:3], 1, c.ps2[:, 512:1024])

        def gbc(eng, tv2=tv2, gcol=gcol):
            eng.wait_ge(c.s_te, tv2)
            eng.tensor_copy(out=c.stat32[:, gcol:gcol + 1], in_=c.ps2[:, 0:1])
            eng.tensor_copy(out=c.stat32[:, gcol + 1:gcol + 2], in_=c.ps2[:, 512:513]).then_inc(c.s_ve, 1)
        p.emit("vector", gbc)
        p.bump(c.s_ve, 1)
        p.barrier()

        step = 0
        ve0 = p.val(c.s_ve)
        se0 = p.val(c.s_se)
        for pi in range(npieces):
                st = c.wstage[step % 2]
                need = ve0 + (step - 1) if step >= 2 else 0
                sem = c.s_wl[step % 2]
                prev = p.val(sem)

                def fl2(eng, pi=pi, st=st, need=need, sem=sem, prev=prev, nm=nm,
                        npc=npc):
                    if need:
                        eng.wait_ge(c.s_ve, need)
                    if prev:
                        eng.wait_ge(sem, prev)
                    _ld2048(eng, st, nm, pi, npc, sem)
                p.emit("sync", fl2)
                dv = p.bump(sem, 32)

                def qz(eng, st=st, dv=dv, sem=sem, gcol=gcol, step=step, se0=se0):
                    eng.wait_ge(sem, dv)
                    if step >= 1:
                        eng.wait_ge(c.s_se, se0 + step)
                    nvfp4_ops(c, eng, st[:, 0:2048], 2048, wq16row[:, 0:2048],
                              c.stat32[:, gcol:gcol + 1], c.stat32[:, gcol + 1:gcol + 2],
                              bigA[:, 0:2048], bigB[:, 0:2048], sm)
                    eng.nop().then_inc(c.s_ve, 1)
                p.emit("vector", qz)
                qv = p.bump(c.s_ve, 1)

                def trp(eng, qv=qv, step=step, se0=se0):
                    eng.wait_ge(c.s_ve, qv)
                    if step >= 1:
                        eng.wait_ge(c.s_se, se0 + step)
                    for i in range(16):
                        eng.transpose(out=c.psh[:, i * 128:(i + 1) * 128],
                                      in_=wq16row[:, i * 128:(i + 1) * 128], identity=id16)
                    eng.nop().then_inc(c.s_te, 1)
                p.emit("tensor", trp)
                tv = p.bump(c.s_te, 1)
                ws_ = wos[step % 2]
                wsp = p.val(ws_)
                stg2 = wstg[step % 2]

                def trev(eng, tv=tv, ws_=ws_, wsp=wsp, stg2=stg2):
                    eng.wait_ge(c.s_te, tv)
                    if wsp:
                        eng.wait_ge(ws_, wsp)
                    eng.activation(out=stg2, in_=c.psh[:, 0:2048], func=AF.Copy)
                    eng.nop().then_inc(c.s_se, 1)
                p.emit("scalar", trev)
                sv = p.bump(c.s_se, 1)

                def wout(eng, pi=pi, sv=sv, ws_=ws_, wsp=wsp, stg2=stg2,
                         woff=woff, rows=rows, cols=cols, kch=kch, npc=npc):
                    eng.wait_ge(c.s_se, sv)
                    if wsp:
                        eng.wait_ge(ws_, wsp)
                    dst = W16[:, woff:woff + rows * cols // 128].rearrange("p (h r) -> p h r", r=rows)
                    if npc == 1:
                        eng.dma_start(out=dst[:, 0:8, pi * 256: pi * 256 + 128],
                                      in_=stg2[:, 0:1024].rearrange("p (h r) -> p h r", r=128)).then_inc(ws_, 16)
                        eng.dma_start(out=dst[:, 0:8, pi * 256 + 128: pi * 256 + 256],
                                      in_=stg2[:, 1024:2048].rearrange("p (h r) -> p h r", r=128)).then_inc(ws_, 16)
                    else:
                        k, pc2 = divmod(pi, npc // 2)
                        eng.dma_start(out=dst[:, pc2 * 16:(pc2 + 1) * 16, k * 128:(k + 1) * 128],
                                      in_=stg2.rearrange("p (h r) -> p h r", r=128)).then_inc(ws_, 32)
                p.emit("sync", wout)
                p.bump(ws_, 32)
                step += 1
        p.barrier()
    sinv = p.val(c.s_ve)

    def smath(eng, sin_s=sin_s, sind_s=sind_s):
        eng.wait_ge(c.s_ls, sin_s)
        eng.wait_ge(c.s_ls, sind_s)
        eng.reciprocal(out=c.stat32[0:1, 0:1], in_=c.stat32[0:1, 14:15])
        eng.reciprocal(out=c.stat32[0:1, 1:2], in_=c.stat32[0:1, 15:16]).then_inc(c.s_ve, 1)
    p.emit("vector", smath)
    vev = p.bump(c.s_ve, 1)
    tv = bcast_row(c, c.stat32[0:1, 0:2], 2, c.ps2[:, 0:512], wait=(c.s_ve, vev))
    tv2 = bcast_row(c, c.stat32[0:1, 14:16], 2, c.ps2[:, 512:1024], wait=(c.s_ve, vev))

    def sfin(eng, tv2=tv2):
        eng.wait_ge(c.s_te, tv2)
        eng.tensor_copy(out=c.stat32[:, 0:2], in_=c.ps2[:, 0:2])
        eng.tensor_copy(out=c.stat32[:, 14:16], in_=c.ps2[:, 512:514])
        eng.tensor_tensor(out=c.stat32[:, 4:5], in0=c.stat32[:, 0:1], in1=c.stat32[:, 9:10], op=ALU.mult)
        eng.tensor_tensor(out=c.stat32[:, 5:6], in0=c.stat32[:, 0:1], in1=c.stat32[:, 11:12], op=ALU.mult)
        eng.tensor_tensor(out=c.stat32[:, 6:7], in0=c.stat32[:, 1:2], in1=c.stat32[:, 13:14], op=ALU.mult).then_inc(c.s_ve, 1)
    p.emit("vector", sfin)
    p.bump(c.s_ve, 1)
    w0 = p.val(wos[0])
    w1 = p.val(wos[1])
    for e in ("tensor", "vector", "scalar", "gpsimd"):
        def wq_q(eng, w0=w0, w1=w1):
            eng.wait_ge(wos[0], w0)
            eng.wait_ge(wos[1], w1)
        p.emit(e, wq_q)
    for b in (bigA, bigB, sm):
        a32.drop(b)
    a16.drop(wq16row)
    for b in wstg:
        a16.drop(b)
    p.barrier()


def emit_l1_ffn(c):
    p = c.p
    a16, a32 = c.a16, c.a32
    F16S = c.dram_f16
    W16 = c.dram_w16
    SOFF = 49152
    id16 = c.cons16[:, OFF_ID16:OFF_ID16 + 128]
    id32 = c.cons32[:, OFF_ID:OFF_ID + 128]
    xg = a32.take(8 * 512, "l1xg")
    bigA = a32.take(2048, "fA")
    bigB = a32.take(2048, "fB")
    gate32 = a32.take(4 * 512, "fgate")
    xtm = a32.take(2048, "fxtm")
    sm = a32.take(7 * 128, "fsm")
    sg32 = a32.take(512, "fsg32")
    vq16 = a16.take(1024, "fvq")
    vqT = a16.take(8 * 512, "fvqT")
    hq16 = a16.take(2048, "fhq")
    hqT = a16.take(32 * 512, "fhqT")
    sgrp = a16.take(8 * 512, "fsg")
    gT = a16.take(8 * 512, "fgT")
    wbuf = [a16.take(4096, "fw0"), a16.take(4096, "fw1")]
    dbuf = [a16.take(1024, "fd0"), a16.take(1024, "fd1")]
    xgs = p.sem(f"l1xg{p.nsem()}")
    sgs = p.sem(f"l1sg{p.nsem()}")
    xos = p.sem(f"l1xo{p.nsem()}")
    wss = [p.sem(f"l1ws{p.nsem()}"), p.sem(f"l1ws{p.nsem()}b")]
    dss = [p.sem(f"l1ds{p.nsem()}"), p.sem(f"l1ds{p.nsem()}b")]
    wg_done = []
    dn_done = []
    for grp in range(4):
        gsl = slice(grp * 512, (grp + 1) * 512)

        def xload(eng, grp=grp, xop=p.val(xos), ssem=c.spill_sem, ssv=c.spill_done):
            if grp == 0:
                eng.wait_ge(ssem, ssv)
            if xop:
                eng.wait_ge(xos, xop)
            src = c.dram_spill.rearrange("(o pp) t -> pp o t", pp=128)
            eng.dma_start(out=xg.rearrange("p (o t) -> p o t", t=512),
                          in_=src[:, :, grp * 512:(grp + 1) * 512]).then_inc(xgs, 16)
        p.emit("sync", xload)
        xgv = p.bump(xgs, 16)

        def sload(eng, grp=grp, sgp=p.val(sgs), sv0=p.val(c._l1_ssem[0]), sv1=p.val(c._l1_ssem[1])):
            if grp == 0:
                eng.wait_ge(c._l1_ssem[0], sv0)
                eng.wait_ge(c._l1_ssem[1], sv1)
            if sgp:
                eng.wait_ge(sgs, sgp)
            src = F16S[:, SOFF:SOFF + 16384].rearrange("p (o t) -> p o t", t=2048)
            eng.dma_start(out=sgrp.rearrange("p (o t) -> p o t", t=512),
                          in_=src[:, :, grp * 512:(grp + 1) * 512]).then_inc(sgs, 16)
        p.emit("sync", sload)
        sgv = p.bump(sgs, 16)

        se0 = p.val(c.s_se)
        ve0 = p.val(c.s_ve)
        for j in range(4):
            def tx(eng, j=j, xgv=xgv, ve0=ve0):
                if j == 0:
                    eng.wait_ge(xgs, xgv)
                if j > 0:
                    eng.wait_ge(c.s_ve, ve0 + 4 * (j - 1) + 1)
                for ch in range(NCH):
                    eng.transpose(out=c.ps2[:, ch * 128:(ch + 1) * 128],
                                  in_=xg[:, ch * 512 + j * 128: ch * 512 + j * 128 + 128],
                                  identity=id32)
                eng.nop().then_inc(c.s_te, 1)
            p.emit("tensor", tx)
            tv = p.bump(c.s_te, 1)

            def ev0(eng, tv=tv, j=j, se0=se0):
                eng.wait_ge(c.s_te, tv)
                if j > 0:
                    eng.wait_ge(c.s_se, se0 + 4 * (j - 1) + 3)
                eng.tensor_scalar(out=bigA[:, 0:1024], in0=c.ps2[:, 0:D], scalar1=1.0,
                                  scalar2=0.0, op0=ALU.mult, op1=ALU.add,
                                  accum_out=c.stat32[:, 20:21]).then_inc(c.s_ve, 1)
            p.emit("vector", ev0)
            vv0 = p.bump(c.s_ve, 1)

            def sq0(eng, vv0=vv0):
                eng.wait_ge(c.s_ve, vv0)
                eng.activation(out=bigB[:, 0:1024], in_=bigA[:, 0:1024], func=AF.Square,
                               accum_out=c.stat32[:, 21:22]).then_inc(c.s_se, 1)
            p.emit("scalar", sq0)
            sv0 = p.bump(c.s_se, 1)

            def lnst(eng, sv0=sv0):
                eng.wait_ge(c.s_se, sv0)
                eng.tensor_scalar(out=c.stat32[:, 22:23], in0=c.stat32[:, 20:21],
                                  scalar1=1.0 / D, scalar2=None, op0=ALU.mult)
                eng.tensor_scalar(out=c.stat32[:, 23:24], in0=c.stat32[:, 21:22],
                                  scalar1=1.0 / D, scalar2=None, op0=ALU.mult)
                eng.tensor_tensor(out=c.stat32[:, 24:25], in0=c.stat32[:, 22:23],
                                  in1=c.stat32[:, 22:23], op=ALU.mult)
                eng.tensor_tensor(out=c.stat32[:, 23:24], in0=c.stat32[:, 23:24],
                                  in1=c.stat32[:, 24:25], op=ALU.subtract).then_inc(c.s_ve, 1)
            p.emit("vector", lnst)
            vv1 = p.bump(c.s_ve, 1)

            def lnsq(eng, vv1=vv1):
                eng.wait_ge(c.s_ve, vv1)
                eng.activation(out=c.stat32[:, 23:24], in_=c.stat32[:, 23:24], func=AF.Sqrt,
                               bias=c.cc(1e-5)).then_inc(c.s_se, 1)
            p.emit("scalar", lnsq)
            sv1 = p.bump(c.s_se, 1)

            def lnal(eng, sv1=sv1):
                eng.wait_ge(c.s_se, sv1)
                eng.reciprocal(out=c.stat32[:, 24:25], in_=c.stat32[:, 23:24])
                eng.tensor_tensor(out=c.stat32[:, 22:23], in0=c.stat32[:, 22:23],
                                  in1=c.stat32[:, 24:25], op=ALU.mult)
                eng.tensor_scalar(out=c.stat32[:, 22:23], in0=c.stat32[:, 22:23],
                                  scalar1=-1.0, scalar2=None, op0=ALU.mult).then_inc(c.s_ve, 1)
            p.emit("vector", lnal)
            vv2 = p.bump(c.s_ve, 1)

            def lnap(eng, vv2=vv2):
                eng.wait_ge(c.s_ve, vv2)
                eng.activation(out=xtm[:, 0:1024], in_=bigA[:, 0:1024], func=AF.Identity,
                               bias=c.stat32[:, 22:23], scale=c.stat32[:, 24:25]).then_inc(c.s_se, 1)
            p.emit("scalar", lnap)
            sv2 = p.bump(c.s_se, 1)

            def xnq(eng, sv2=sv2):
                eng.wait_ge(c.s_se, sv2)
                nvfp4_ops(c, eng, xtm[:, 0:1024], 1024, vq16, c.stat32[:, 14:15],
                          c.stat32[:, 0:1], bigA[:, 0:1024], bigB[:, 0:1024], sm)
                eng.nop().then_inc(c.s_ve, 1)
            p.emit("vector", xnq)
            vq_v = p.bump(c.s_ve, 1)

            def vqt(eng, vq_v=vq_v, j=j, se0=se0):
                eng.wait_ge(c.s_ve, vq_v)
                if j > 0:
                    eng.wait_ge(c.s_se, se0 + 4 * (j - 1) + 4)
                for i in range(NCH):
                    eng.transpose(out=c.psh[:, i * 128:(i + 1) * 128],
                                  in_=vq16[:, i * 128:(i + 1) * 128], identity=id16)
                eng.nop().then_inc(c.s_te, 1)
            p.emit("tensor", vqt)
            tv = p.bump(c.s_te, 1)

            def vqtev(eng, tv=tv, j=j):
                eng.wait_ge(c.s_te, tv)
                for i in range(NCH):
                    eng.activation(out=vqT[:, i * 512 + j * 128: i * 512 + j * 128 + 128],
                                   in_=c.psh[:, i * 128:(i + 1) * 128], func=AF.Copy)
                eng.nop().then_inc(c.s_se, 1)
            p.emit("scalar", vqtev)
            p.bump(c.s_se, 1)
        sevq = p.val(c.s_se)

        for st8 in range(8):
            for gu in range(2):
                ws_ = wss[(st8 * 2 + gu) % 2]
                wsp = p.val(ws_)
                wb = wbuf[(st8 * 2 + gu) % 2]

                def wload(eng, st8=st8, gu=gu, ws_=ws_, wsp=wsp, wb=wb,
                          wg=(wg_done[-2] if len(wg_done) >= 2 else 0)):
                    if wsp:
                        eng.wait_ge(ws_, wsp)
                    if wg:
                        eng.wait_ge(c.s_te, wg)
                    woff = gu * 32768
                    srcv = W16[:, woff:woff + 32768].rearrange("p (h r) -> p h r", r=4096)
                    eng.dma_start(out=wb.rearrange("p (h r) -> p h r", r=512),
                                  in_=srcv[:, :, st8 * 512:(st8 + 1) * 512]).then_inc(ws_, 16)
                p.emit("sync", wload)
                wv = p.bump(ws_, 16)

                def wmm(eng, st8=st8, gu=gu, ws_=ws_, wv=wv, wb=wb, sevq=sevq,
                        se1=p.val(c.s_se), ve1=p.val(c.s_ve), first=(st8 == 0 and gu == 0)):
                    eng.wait_ge(ws_, wv)
                    eng.wait_ge(c.s_se, sevq if first else se1)
                    if not first:
                        eng.wait_ge(c.s_ve, ve1)
                    for j in range(4):
                        for k in range(NCH):
                            eng.matmul(out=c.ps1[:, j * 512:(j + 1) * 512],
                                       lhsT=vqT[:, k * 512 + j * 128: k * 512 + j * 128 + 128],
                                       rhs=wb[:, k * 512:(k + 1) * 512],
                                       start=(k == 0), stop=(k == NCH - 1))
                    eng.nop().then_inc(c.s_te, 1)
                p.emit("tensor", wmm)
                tv = p.bump(c.s_te, 1)
                wg_done.append(tv)

                if gu == 0:
                    def gev(eng, tv=tv, ve2=p.val(c.s_ve)):
                        eng.wait_ge(c.s_te, tv)
                        eng.wait_ge(c.s_ve, ve2)
                        eng.activation(out=gate32, in_=c.ps1[:, 0:2048], func=AF.Silu,
                                       scale=c.stat32[:, 4:5]).then_inc(c.s_se, 1)
                    p.emit("scalar", gev)
                    p.bump(c.s_se, 1)
                else:
                    def hj(eng, tv=tv, st8=st8, se2=p.val(c.s_se)):
                        eng.wait_ge(c.s_te, tv)
                        eng.wait_ge(c.s_se, se2)
                        eng.scalar_tensor_tensor(out=bigA[:, 0:2048],
                                                 in0=c.ps1[:, 0:2048],
                                                 scalar=c.stat32[:, 5:6],
                                                 in1=gate32[:, 0:2048],
                                                 op0=ALU.mult, op1=ALU.mult)
                        nvfp4_ops(c, eng, bigA[:, 0:2048], 2048, hq16[:, 0:2048],
                                  c.stat32[:, 15:16], c.stat32[:, 1:2],
                                  bigB[:, 0:2048], xtm[:, 0:2048], sm)
                        eng.nop().then_inc(c.s_ve, 1)
                    p.emit("vector", hj)
                    hv = p.bump(c.s_ve, 1)

                    def hqt(eng, hv=hv):
                        eng.wait_ge(c.s_ve, hv)
                        for i in range(16):
                            eng.transpose(out=c.psh[:, i * 128:(i + 1) * 128],
                                          in_=hq16[:, i * 128:(i + 1) * 128], identity=id16)
                        eng.nop().then_inc(c.s_te, 1)
                    p.emit("tensor", hqt)
                    tv2 = p.bump(c.s_te, 1)

                    def hqtev(eng, tv2=tv2, st8=st8):
                        eng.wait_ge(c.s_te, tv2)
                        for i in range(16):
                            ic = st8 * 4 + (i % 4)
                            j = i // 4
                            eng.activation(out=hqT[:, ic * 512 + j * 128: ic * 512 + j * 128 + 128],
                                           in_=c.psh[:, i * 128:(i + 1) * 128], func=AF.Copy)
                        eng.nop().then_inc(c.s_se, 1)
                    p.emit("scalar", hqtev)
                    p.bump(c.s_se, 1)
        sevh = p.val(c.s_se)

        ve3 = p.val(c.s_ve)
        for oh in range(2):
            for k in range(32):
                ds_ = dss[k % 2]
                dsp = p.val(ds_)
                db = dbuf[k % 2]

                def dload(eng, k=k, oh=oh, ds_=ds_, dsp=dsp, db=db,
                          dg=(dn_done[-2] if len(dn_done) >= 2 else 0)):
                    if dsp:
                        eng.wait_ge(ds_, dsp)
                    if dg:
                        eng.wait_ge(c.s_te, dg)
                    srcv = W16[:, 65536:98304].rearrange("p (h r) -> p h r", r=1024)
                    eng.dma_start(out=db[:, 0:512],
                                  in_=srcv[:, k, oh * 512:(oh + 1) * 512]).then_inc(ds_, 16)
                p.emit("sync", dload)
                dv = p.bump(ds_, 16)

                def dmm(eng, k=k, oh=oh, ds_=ds_, dv=dv, db=db, sevh=sevh,
                        se4=p.val(c.s_se)):
                    eng.wait_ge(ds_, dv)
                    if k == 0:
                        eng.wait_ge(c.s_se, se4)
                    for oo in range(4):
                        eng.matmul(out=c.ps1[:, oo * 512:(oo + 1) * 512],
                                   lhsT=db[:, oo * 128:(oo + 1) * 128],
                                   rhs=hqT[:, k * 512:(k + 1) * 512],
                                   start=(k == 0), stop=(k == 31))
                    eng.nop().then_inc(c.s_te, 1)
                p.emit("tensor", dmm)
                dn_done.append(p.bump(c.s_te, 1))
            tvd = p.val(c.s_te)
            for oo in range(4):
                och = oh * 4 + oo

                def dsil(eng, oo=oo, och=och, tvd=tvd):
                    eng.wait_ge(c.s_te, tvd)
                    eng.activation(out=gT[:, och * 512:(och + 1) * 512],
                                   in_=c.ps1[:, oo * 512:(oo + 1) * 512],
                                   func=AF.Silu, scale=c.stat32[:, 6:7])
                    eng.nop().then_inc(c.s_se, 1)
                p.emit("scalar", dsil)
                p.bump(c.s_se, 1)
        sevd = p.val(c.s_se)
        for och in range(NCH):
            def comb(eng, och=och, sevd=sevd, sgv=sgv, xgv=xgv):
                eng.wait_ge(c.s_se, sevd)
                if och == 0:
                    eng.wait_ge(sgs, sgv)
                    eng.wait_ge(xgs, xgv)
                eng.tensor_tensor(out=sg32, in0=sgrp[:, och * 512:(och + 1) * 512],
                                  in1=gT[:, och * 512:(och + 1) * 512], op=ALU.mult)
                eng.tensor_tensor(out=xg[:, och * 512:(och + 1) * 512],
                                  in0=xg[:, och * 512:(och + 1) * 512], in1=sg32,
                                  op=ALU.add).then_inc(c.s_ve, 2)
            p.emit("vector", comb)
            p.bump(c.s_ve, 2)
        vev = p.val(c.s_ve)

        def xstore(eng, grp=grp, vev=vev):
            eng.wait_ge(c.s_ve, vev)
            dst = c.dram_spill.rearrange("(o pp) t -> pp o t", pp=128)
            eng.dma_start(out=dst[:, :, grp * 512:(grp + 1) * 512],
                          in_=xg.rearrange("p (o t) -> p o t", t=512)).then_inc(xos, 16)
        p.emit("sync", xstore)
        p.bump(xos, 16)
        p.barrier()
    for b in (xg, bigA, bigB, gate32, xtm, sm, sg32):
        a32.drop(b)
    for b in (vq16, vqT, hq16, hqT, sgrp, gT, wbuf[0], wbuf[1], dbuf[0], dbuf[1]):
        a16.drop(b)
    sem = p.sem(f"rsf{p.nsem()}")
    c.x = c.a32.take(NCH * T, "x")
    x = c.x

    def fre(eng, sem=sem, xop=p.val(xos)):
        eng.wait_ge(xos, xop)
        for ch in range(NCH):
            eng.dma_start(out=t_sl(x, ch),
                          in_=c.dram_spill[ch * 128:(ch + 1) * 128, :]).then_inc(sem, 16)
    p.emit("sync", fre)
    v = p.bump(sem, 16 * NCH)
    for e in ("tensor", "vector", "scalar", "gpsimd"):
        p.emit(e, lambda eng, v=v, sem=sem: eng.wait_ge(sem, v))
    p.barrier()
def emit_layer2(c):
    p = c.p
    a16, a32 = c.a16, c.a32
    F16S = c.dram_f16
    XBOFF, SZOFF, YZOFF, SOFF = 0, 32768, 65536, 98304
    id32 = c.cons32[:, OFF_ID:OFF_ID + 128]
    id16 = c.cons16[:, OFF_ID16:OFF_ID16 + 128]
    emit_ln(c, with_xn16=True)
    p.barrier()
    emit_spill_x(c)
    xn16 = c.xn16

    nw_s = load_small(c, "L2_normw", c.bias_sb[:, 0:8],
                      wait=[(c.s_se, p.val(c.s_se)), (c.s_ve, p.val(c.s_ve))])
    rrow = a32.take(T, "l2rr")
    tmp32 = a32.take(512, "l2t")
    te0 = p.val(c.s_te)
    step = 0
    sqb2 = [a32.take(512, "l2sq0"), a32.take(512, "l2sq1")]
    for ch in range(NCH):
        for nt in range(4):
            buf = sqb2[step % 2]

            def sq(eng, ch=ch, nt=nt, buf=buf, step=step, te0=te0):
                if step >= 2:
                    eng.wait_ge(c.s_te, te0 + step - 1)
                eng.activation(out=buf, in_=t_sl(xn16, ch, nt_sl(nt)), func=AF.Square).then_inc(c.s_se, 1)
            p.emit("scalar", sq)
            sev = p.bump(c.s_se, 1)

            def mm(eng, nt=nt, ch=ch, buf=buf, sev=sev):
                eng.wait_ge(c.s_se, sev)
                eng.matmul(out=c.ps1[0:1, nt_sl(nt)], lhsT=c.cons32[:, OFF_ONES:OFF_ONES + 1],
                           rhs=buf, start=(ch == 0), stop=(ch == NCH - 1))
                eng.nop().then_inc(c.s_te, 1)
            p.emit("tensor", mm)
            p.bump(c.s_te, 1)
            step += 1
    tev = p.val(c.s_te)

    def rv(eng, tev=tev):
        eng.wait_ge(c.s_te, tev)
        eng.tensor_scalar(out=rrow[0:1, 0:T], in0=c.ps1[0:1, 0:T], scalar1=1.0 / D,
                          scalar2=None, op0=ALU.mult).then_inc(c.s_ve, 1)
    p.emit("vector", rv)
    vev = p.bump(c.s_ve, 1)

    def rs(eng, vev=vev):
        eng.wait_ge(c.s_ve, vev)
        eng.activation(out=rrow[0:1, 0:T], in_=rrow[0:1, 0:T], func=AF.Sqrt,
                       bias=c.cc(1e-5)[0:1, :]).then_inc(c.s_se, 1)
    p.emit("scalar", rs)
    sev = p.bump(c.s_se, 1)

    def rv2(eng, sev=sev):
        eng.wait_ge(c.s_se, sev)
        eng.reciprocal(out=rrow[0:1, 0:T], in_=rrow[0:1, 0:T]).then_inc(c.s_ve, 1)
    p.emit("vector", rv2)
    p.bump(c.s_ve, 1)
    p.barrier()

    wbig = a16.take(8 * 2048, "l2wx")
    pstage = [a16.take(2048, "l2st0"), a16.take(2048, "l2st1")]
    xn2b = a16.take(8 * 512, "l2xn2")
    psems = [p.sem(f"l2p{p.nsem()}"), p.sem(f"l2p{p.nsem()}b")]
    for part, (wname, bcol_s, boff, act, ooff) in enumerate([
            ("L2_Wx", "L2_bx", 8, "id", XBOFF),
            ("L2_Wz", "L2_bz", 8, "silu", SZOFF)]):
        load_w16(c, wname, 8, 2048, wbig)
        b_s = load_small(c, bcol_s, c.bias_sb[:, 8:24],
                         wait=[(c.s_se, p.val(c.s_se))])
        p.barrier()
        for nt in range(4):
            tvb = bcast_row(c, rrow[0:1, nt_sl(nt)], 512, c.ps2[:, 0:512])
            ve0 = p.val(c.s_ve)
            for ch in range(NCH):
                def x2(eng, ch=ch, nt=nt, tvb=tvb, tprev=p.val(c.s_te),
                       nw_first=(part == 0 and nt == 0), nw_s=nw_s):
                    if ch == 0:
                        eng.wait_ge(c.s_te, max(tvb, tprev))
                        if nw_first:
                            eng.wait_ge(c.s_ls, nw_s)
                    eng.tensor_tensor(out=tmp32, in0=t_sl(xn16, ch, nt_sl(nt)),
                                      in1=c.ps2[:, 0:512], op=ALU.mult)
                    eng.tensor_scalar(out=xn2b[:, ch * 512:(ch + 1) * 512], in0=tmp32,
                                      scalar1=c.bias_sb[:, ch:ch + 1], scalar2=None,
                                      op0=ALU.mult).then_inc(c.s_ve, 1)
                p.emit("vector", x2)
                p.bump(c.s_ve, 1)
            vx = p.val(c.s_ve)
            se0 = p.val(c.s_se)
            for oc in range(16):
                bank = oc % 4

                def pmm(eng, oc=oc, nt=nt, bank=bank, vx=vx, se0=se0):
                    eng.wait_ge(c.s_ve, vx)
                    if oc >= 4 or nt > 0:
                        eng.wait_ge(c.s_se, se0 + oc - 3)
                    for k in range(NCH):
                        eng.matmul(out=c.ps1[:, bank * 512:(bank + 1) * 512],
                                   lhsT=wbig[:, k * 2048 + oc * 128: k * 2048 + (oc + 1) * 128],
                                   rhs=xn2b[:, k * 512:(k + 1) * 512],
                                   start=(k == 0), stop=(k == NCH - 1))
                    eng.nop().then_inc(c.s_te, 1)
                p.emit("tensor", pmm)
                tv = p.bump(c.s_te, 1)
                stg = pstage[oc % 2]
                ps_ = psems[oc % 2]
                pprev = p.val(ps_)

                def pev(eng, oc=oc, bank=bank, tv=tv, stg=stg, ps_=ps_, pprev=pprev,
                        act=act, bfirst=(nt == 0 and oc == 0), b_s=b_s):
                    eng.wait_ge(c.s_te, tv)
                    if bfirst:
                        eng.wait_ge(c.s_ls, b_s)
                    if pprev:
                        eng.wait_ge(ps_, pprev)
                    eng.activation(out=stg[:, 0:512], in_=c.ps1[:, bank * 512:(bank + 1) * 512],
                                   func=(AF.Silu if act == "silu" else AF.Identity),
                                   bias=c.bias_sb[:, 8 + oc:9 + oc]).then_inc(c.s_se, 1)
                p.emit("scalar", pev)
                sv = p.bump(c.s_se, 1)

                def pout(eng, oc=oc, nt=nt, stg=stg, sv=sv, ps_=ps_, pprev=pprev, ooff=ooff):
                    eng.wait_ge(c.s_se, sv)
                    if pprev:
                        eng.wait_ge(ps_, pprev)
                    eng.dma_start(out=F16S[:, ooff + oc * 2048 + nt * 512: ooff + oc * 2048 + (nt + 1) * 512],
                                  in_=stg[:, 0:512]).then_inc(ps_, 16)
                p.emit("sync", pout)
                p.bump(ps_, 16)
        p.barrier()
    a16.drop(wbig)
    a16.drop(xn2b)
    for b in pstage:
        a16.drop(b)
    a32.drop(rrow)
    for b in sqb2:
        a32.drop(b)
    a32.drop(tmp32)

    xnsp = p.sem(f"l2xn{p.nsem()}")

    def xnout(eng, xnsp=xnsp):
        eng.dma_start(out=c.dram_w16[:, 0:16384], in_=xn16).then_inc(xnsp, 16)
    p.emit("sync", xnout)
    xnv = p.bump(xnsp, 16)
    for e in ("tensor", "vector", "scalar", "gpsimd"):
        p.emit(e, lambda eng, v=xnv, s=xnsp: eng.wait_ge(s, v))
    c.a16.drop(c.xn16)
    c.xn16 = None

    emit_l2_scan(c, psems)
    _ckpt(c, "l2a")

    emit_l2_out(c)
    _ckpt(c, "l2o")

    emit_l2_wquant(c)
    emit_l2_ffn(c)
    p.barrier()


def emit_l2_scan(c, psems):
    p = c.p
    a16, a32 = c.a16, c.a32
    F16S = c.dram_f16
    XBOFF, SZOFF, YZOFF = 0, 32768, 65536
    LB = 256
    NB = T // LB
    cw_s = load_small(c, "L2_convw", c.convw_sb[:, 0:64])
    cb_s = load_small(c, "L2_convb", c.bias_sb[:, 0:16],
                      wait=[(c.s_se, p.val(c.s_se)), (c.s_ve, p.val(c.s_ve))])
    al_s = load_small(c, "L2_Alog", c.alog_sb[:, 0:256])
    dcol_s = load_small(c, "L2_D", c.bias_sb[:, 16:32],
                        wait=[(c.s_se, p.val(c.s_se)), (c.s_ve, p.val(c.s_ve))])
    wxp16 = a16.take(16 * 96, "wxp")
    load_w16(c, "L2_Wxp", 16, 96, wxp16)
    wdt16 = a16.take(2048, "wdt")
    load_w16(c, "L2_Wdt", 1, 2048, wdt16, rows=64)
    bdt_s = load_small(c, "L2_bdt", c.bdt_sb[:, 0:16])
    aev = p.val(c.s_dma)

    def amath_s(eng, al_s=al_s):
        eng.wait_ge(c.s_ls, al_s)
        eng.activation(out=c.a32_A[:, 0:256], in_=c.alog_sb[:, 0:256],
                       func=AF.Exp).then_inc(c.s_se, 1)
    p.emit("scalar", amath_s)
    sev = p.bump(c.s_se, 1)

    def amath_v(eng, sev=sev):
        eng.wait_ge(c.s_se, sev)
        eng.tensor_scalar(out=c.a32_A[:, 0:256], in0=c.a32_A[:, 0:256], scalar1=-1.0,
                          scalar2=None, op0=ALU.mult).then_inc(c.s_ve, 1)
    p.emit("vector", amath_v)
    p.bump(c.s_ve, 1)
    p.barrier()

    xbb = a16.take(16 * (LB + 4), "xbb")
    xbr = a16.take(16 * LB, "xbr")
    szb = a16.take(16 * LB, "szb")
    btbc = a16.take(16 * LB, "btbc")
    ctbc = a16.take(16 * LB, "ctbc")
    dbx = a16.take(16 * LB, "dbx")
    h16 = a16.take(16 * LB, "h16")
    yzb = a16.take(16 * LB, "yzb")
    dtl16 = a16.take(LB, "dtl")
    btct = a16.take(LB, "btct")
    dt32 = [a32.take(LB, "dt32a"), a32.take(LB, "dt32b")]
    dA = [a32.take(16 * LB, "dAa"), a32.take(16 * LB, "dAb")]
    sps = [a32.take(LB, "spsa"), a32.take(LB, "spsb")]
    gtree = a32.take(2048, "gtree")
    carry = a32.take(16 * 16, "carry")
    u16 = a16.take(LB, "u16")
    yst = [p.sem(f"l2y{p.nsem()}"), p.sem(f"l2y{p.nsem()}b")]
    xbs = [p.sem(f"l2xb{p.nsem()}"), p.sem(f"l2xb{p.nsem()}b")]
    szs = p.sem(f"l2sz{p.nsem()}")

    def czero(eng):
        eng.memset(carry, 0.0)
        eng.nop().then_inc(c.s_ve, 1)
    p.emit("vector", czero)
    p.bump(c.s_ve, 1)
    p.barrier()
    for lb in range(NB):
        xs_ = xbs[lb % 2]
        xsp = p.val(xs_)

        def xbload(eng, lb=lb, xs_=xs_, xsp=xsp, pv0=p.val(psems[0]), pv1=p.val(psems[1])):
            if lb == 0:
                eng.wait_ge(psems[0], pv0)
                eng.wait_ge(psems[1], pv1)
            if xsp:
                eng.wait_ge(xs_, xsp)
            src = F16S[:, XBOFF:XBOFF + 32768].rearrange("p (o t) -> p o t", t=2048)
            lo = max(lb * LB - 4, 0)
            off = 4 if lb > 0 else 0
            eng.dma_start(out=xbb.rearrange("p (o t) -> p o t", t=LB + 4)[:, :, 4 - off:],
                          in_=src[:, :, lb * LB - off: (lb + 1) * LB]).then_inc(xs_, 16)
        p.emit("sync", xbload)
        xbv = p.bump(xs_, 16)

        def szload(eng, lb=lb, szp=p.val(szs)):
            if szp:
                eng.wait_ge(szs, szp)
            src = F16S[:, SZOFF:SZOFF + 32768].rearrange("p (o t) -> p o t", t=2048)
            eng.dma_start(out=szb.rearrange("p (o t) -> p o t", t=LB),
                          in_=src[:, :, lb * LB:(lb + 1) * LB]).then_inc(szs, 16)
        p.emit("sync", szload)
        szv = p.bump(szs, 16)

        if lb == 0:
            def z0(eng, xbv=xbv):
                eng.wait_ge(xbs[0], xbv)
                eng.memset(xbb.rearrange("p (o t) -> p o t", t=LB + 4)[:, :, 0:4], 0.0)
                eng.nop().then_inc(c.s_ve, 1)
            p.emit("vector", z0)
            p.bump(c.s_ve, 1)

        ve0 = p.val(c.s_ve)
        se_cv0 = p.val(c.s_se)
        for oc in range(16):
            def cnv(eng, oc=oc, lb=lb, xbv=xbv, xs_=xs_, cw_s=cw_s, cb_s=cb_s,
                    se_cv0=se_cv0):
                if oc == 0:
                    eng.wait_ge(xs_, xbv)
                    if lb == 0:
                        eng.wait_ge(c.s_ls, cw_s)
                        eng.wait_ge(c.s_ls, cb_s)
                else:
                    eng.wait_ge(c.s_se, se_cv0 + oc)
                xv = xbb[:, oc * (LB + 4):(oc + 1) * (LB + 4)]
                eng.tensor_scalar(out=c.l2conv32[:, 0:256], in0=xv[:, 1:1 + LB],
                                  scalar1=c.convw_sb[:, oc * 4:oc * 4 + 1], scalar2=None,
                                  op0=ALU.mult)
                for tp in range(1, 4):
                    eng.scalar_tensor_tensor(out=c.l2conv32[:, 0:256], in0=xv[:, 1 + tp:1 + tp + LB],
                                             scalar=c.convw_sb[:, oc * 4 + tp:oc * 4 + tp + 1],
                                             in1=c.l2conv32[:, 0:256], op0=ALU.mult, op1=ALU.add)
                eng.nop().then_inc(c.s_ve, 1)
            p.emit("vector", cnv)
            vv = p.bump(c.s_ve, 1)

            def csil(eng, oc=oc, vv=vv):
                eng.wait_ge(c.s_ve, vv)
                eng.activation(out=xbr[:, oc * LB:(oc + 1) * LB], in_=c.l2conv32[:, 0:256],
                               func=AF.Silu, bias=c.bias_sb[:, oc:oc + 1]).then_inc(c.s_se, 1)
            p.emit("scalar", csil)
            p.bump(c.s_se, 1)
        sev = p.val(c.s_se)

        def xdmm(eng, sev=sev):
            eng.wait_ge(c.s_se, sev)
            for k in range(16):
                eng.matmul(out=c.ps2[0:96, 0:LB], lhsT=wxp16[:, k * 96:(k + 1) * 96],
                           rhs=xbr[:, k * LB:(k + 1) * LB], start=(k == 0), stop=(k == 15))
            eng.nop().then_inc(c.s_te, 1)
        p.emit("tensor", xdmm)
        tv = p.bump(c.s_te, 1)

        def xdev(eng, tv=tv):
            eng.wait_ge(c.s_te, tv)
            eng.tensor_copy(out=dtl16[0:64, 0:LB], in_=c.ps2[0:64, 0:LB])
            eng.tensor_copy(out=btct[64:96, 0:LB], in_=c.ps2[64:96, 0:LB]).then_inc(c.s_ve, 1)
        p.emit("vector", xdev)
        vev = p.bump(c.s_ve, 1)

        bcs = c.s_bc
        BCOFF = 114688 + (lb % 2) * LB

        def bc_st(eng, vev=vev, bcs=bcs, BCOFF=BCOFF, prevtot=p.val(bcs)):
            eng.wait_ge(c.s_ve, vev)
            if prevtot:
                eng.wait_ge(bcs, prevtot)
            eng.dma_start(out=c.dram_f16[0:32, BCOFF:BCOFF + LB],
                          in_=btct[64:96, 0:LB]).then_inc(bcs, 16)
        p.emit("sync", bc_st)
        sv_ = p.bump(bcs, 16)

        def bc_rd(eng, bcs=bcs, sv_=sv_, BCOFF=BCOFF):
            eng.wait_ge(bcs, sv_)
            for n in range(16):
                eng.dma_start(out=btbc[:, n * LB:(n + 1) * LB],
                              in_=c.dram_f16[n:n + 1, BCOFF:BCOFF + LB].broadcast_to((128, LB))).then_inc(bcs, 16)
                eng.dma_start(out=ctbc[:, n * LB:(n + 1) * LB],
                              in_=c.dram_f16[16 + n:17 + n, BCOFF:BCOFF + LB].broadcast_to((128, LB))).then_inc(bcs, 16)
        p.emit("sync", bc_rd)
        vbc_sem, vbc = bcs, p.bump(bcs, 16 * 32)
        p.emit("sync", lambda eng, bcs=bcs, vbc=vbc: eng.wait_ge(bcs, vbc))

        se_sc = p.val(c.s_se)
        sp1_se = {}
        dAe_se = {}
        sp2_ve = {}
        scn_ve = {}

        def emit_dt(i):
            sl = i % 2
            dts, dAs, sps_ = dt32[sl], dA[sl], sps[sl]

            def dtmm(eng, i=i, vbc=vbc, vbc_sem=vbc_sem, bdt_s=bdt_s, lb=lb,
                     sw=sp1_se.get(i - 1, 0)):
                if i == 0:
                    eng.wait_ge(vbc_sem, vbc)
                    if lb == 0:
                        eng.wait_ge(c.s_ls, bdt_s)
                if sw:
                    eng.wait_ge(c.s_se, sw)
                eng.matmul(out=c.ps2[:, 0:LB], lhsT=wdt16[0:64, i * 128:(i + 1) * 128],
                           rhs=dtl16[0:64, 0:LB], start=True, stop=True)
                eng.nop().then_inc(c.s_te, 1)
            p.emit("tensor", dtmm)
            tv = p.bump(c.s_te, 1)

            def sp1(eng, i=i, tv=tv, dts=dts, sps_=sps_, v2=scn_ve.get(i - 2, 0)):
                eng.wait_ge(c.s_te, tv)
                if v2:
                    eng.wait_ge(c.s_ve, v2)
                eng.activation(out=dts, in_=c.ps2[:, 0:LB], func=AF.Identity,
                               bias=c.bdt_sb[:, i:i + 1])
                eng.activation(out=sps_, in_=dts, func=AF.Abs)
                eng.activation(out=sps_, in_=sps_, func=AF.Exp, scale=-1.0)
                eng.activation(out=sps_, in_=sps_, func=AF.Ln,
                               bias=c.cc(1.0)).then_inc(c.s_se, 1)
            p.emit("scalar", sp1)
            sp1_se[i] = p.bump(c.s_se, 1)

            def sp2(eng, i=i, dts=dts, sps_=sps_, sw=sp1_se[i]):
                eng.wait_ge(c.s_se, sw)
                eng.scalar_tensor_tensor(out=dts, in0=dts, scalar=0.0, in1=sps_,
                                         op0=ALU.max, op1=ALU.add).then_inc(c.s_ve, 1)
            p.emit("vector", sp2)
            sp2_ve[i] = p.bump(c.s_ve, 1)

            def dAe(eng, i=i, dts=dts, dAs=dAs, vw=sp2_ve[i], v2=scn_ve.get(i - 2, 0)):
                eng.wait_ge(c.s_ve, vw)
                if v2:
                    eng.wait_ge(c.s_ve, v2)
                for n in range(16):
                    eng.activation(out=dAs[:, n * LB:(n + 1) * LB], in_=dts, func=AF.Exp,
                                   scale=c.a32_A[:, i * 16 + n: i * 16 + n + 1])
                eng.nop().then_inc(c.s_se, 1)
            p.emit("scalar", dAe)
            dAe_se[i] = p.bump(c.s_se, 1)

        def emit_scn(oc):
            sl = oc % 2
            dts, dAs = dt32[sl], dA[sl]

            def scn(eng, oc=oc, lb=lb, dts=dts, dAs=dAs, sw=dAe_se[oc],
                    dcol_s=dcol_s, szv=szv,
                    yps=yst[(lb + 1) % 2], ypv=p.val(yst[(lb + 1) % 2])):
                eng.wait_ge(c.s_se, sw)
                if oc == 0:
                    eng.wait_ge(szs, szv)
                    if ypv:
                        eng.wait_ge(yps, ypv)
                    if lb == 0:
                        eng.wait_ge(c.s_ls, dcol_s)
                eng.tensor_tensor(out=u16, in0=dts, in1=xbr[:, oc * LB:(oc + 1) * LB],
                                  op=ALU.mult)
                eng.tensor_tensor(out=dbx.rearrange("p (n t) -> p n t", t=LB),
                                  in0=btbc.rearrange("p (n t) -> p n t", t=LB),
                                  in1=u16.unsqueeze(1).broadcast_to((128, 16, LB)),
                                  op=ALU.mult)
                for n in range(16):
                    eng.tensor_tensor_scan(out=h16[:, n * LB:(n + 1) * LB],
                                           data0=dAs[:, n * LB:(n + 1) * LB],
                                           data1=dbx[:, n * LB:(n + 1) * LB],
                                           initial=(0.0 if lb == 0 else carry[:, oc * 16 + n: oc * 16 + n + 1]),
                                           op0=ALU.mult, op1=ALU.add)
                eng.tensor_copy(out=carry[:, oc * 16:(oc + 1) * 16],
                                in_=h16.rearrange("p (n t) -> p n t", t=LB)[:, :, LB - 1:LB])
                eng.tensor_tensor(out=dbx, in0=h16, in1=ctbc, op=ALU.mult)
                eng.tensor_tensor(out=gtree[:, 0:2048], in0=dbx[:, 0:2048],
                                  in1=dbx[:, 2048:4096], op=ALU.add)
                eng.tensor_tensor(out=gtree[:, 0:1024], in0=gtree[:, 0:1024],
                                  in1=gtree[:, 1024:2048], op=ALU.add)
                eng.tensor_tensor(out=gtree[:, 0:512], in0=gtree[:, 0:512],
                                  in1=gtree[:, 512:1024], op=ALU.add)
                eng.tensor_tensor(out=gtree[:, 0:256], in0=gtree[:, 0:256],
                                  in1=gtree[:, 256:512], op=ALU.add)
                eng.scalar_tensor_tensor(out=gtree[:, 0:256], in0=xbr[:, oc * LB:(oc + 1) * LB],
                                         scalar=c.bias_sb[:, 16 + oc:17 + oc],
                                         in1=gtree[:, 0:256], op0=ALU.mult, op1=ALU.add)
                eng.tensor_tensor(out=yzb[:, oc * LB:(oc + 1) * LB], in0=gtree[:, 0:256],
                                  in1=szb[:, oc * LB:(oc + 1) * LB],
                                  op=ALU.mult).then_inc(c.s_ve, 1)
            p.emit("vector", scn)
            scn_ve[oc] = p.bump(c.s_ve, 1)

        for i in range(16):
            emit_dt(i)
            if i >= 1:
                emit_scn(i - 1)
        emit_scn(15)
        vyz = p.val(c.s_ve)
        ys_ = yst[lb % 2]
        ysp = p.val(ys_)

        def yzout(eng, lb=lb, vyz=vyz, ys_=ys_, ysp=ysp):
            eng.wait_ge(c.s_ve, vyz)
            if ysp:
                eng.wait_ge(ys_, ysp)
            dst = F16S[:, YZOFF:YZOFF + 32768].rearrange("p (o t) -> p o t", t=2048)
            eng.dma_start(out=dst[:, :, lb * LB:(lb + 1) * LB],
                          in_=yzb.rearrange("p (o t) -> p o t", t=LB)).then_inc(ys_, 16)
        p.emit("sync", yzout)
        p.bump(ys_, 16)
        p.barrier()
    c._l2_yst = yst
    y0, y1 = p.val(yst[0]), p.val(yst[1])
    for e in ("tensor", "vector", "scalar", "gpsimd"):
        def yq_(eng, y0=y0, y1=y1):
            eng.wait_ge(yst[0], y0)
            eng.wait_ge(yst[1], y1)
        p.emit(e, yq_)
    for b in (xbb, xbr, szb, btbc, ctbc, dbx, h16, yzb, dtl16, u16, wxp16, wdt16):
        a16.drop(b)
    a16.drop(btct)
    for b in (dt32[0], dt32[1], dA[0], dA[1], sps[0], sps[1], gtree, carry):
        a32.drop(b)


def emit_l2_out(c):
    p = c.p
    a16, a32 = c.a16, c.a32
    F16S = c.dram_f16
    YZOFF, SOFF = 65536, 98304
    c.xn16 = a16.take(NCH * T, "xn16")
    xn16 = c.xn16
    xnsp2 = p.sem(f"l2xn{p.nsem()}")

    def xnin(eng, s=xnsp2):
        eng.dma_start(out=xn16, in_=c.dram_w16[:, 0:16384]).then_inc(s, 16)
    p.emit("sync", xnin)
    xnv2 = p.bump(xnsp2, 16)
    for e in ("tensor", "vector", "scalar", "gpsimd"):
        p.emit(e, lambda eng, v=xnv2, s=xnsp2: eng.wait_ge(s, v))
    wo16 = a16.take(16 * D, "l2wo")
    load_w16(c, "L2_Wout", 16, D, wo16)
    bo_s = load_small(c, "L2_bout", c.bias_sb[:, 0:8],
                      wait=[(c.s_se, p.val(c.s_se)), (c.s_ve, p.val(c.s_ve))])
    yzg = a16.take(16 * 512, "l2yzg")
    sstg = [a16.take(512, "l2ss0"), a16.take(512, "l2ss1")]
    s32 = a32.take(512, "l2s32")
    yzs = p.sem(f"l2yz{p.nsem()}")
    ssem = [p.sem(f"l2s{p.nsem()}"), p.sem(f"l2s{p.nsem()}b")]
    p.barrier()
    for nt in range(4):
        def yzload(eng, nt=nt, yv0=p.val(c._l2_yst[0]), yv1=p.val(c._l2_yst[1]),
                   yzp=p.val(yzs)):
            if nt == 0:
                eng.wait_ge(c._l2_yst[0], yv0)
                eng.wait_ge(c._l2_yst[1], yv1)
            if yzp:
                eng.wait_ge(yzs, yzp)
            src = F16S[:, YZOFF:YZOFF + 32768].rearrange("p (o t) -> p o t", t=2048)
            eng.dma_start(out=yzg.rearrange("p (o t) -> p o t", t=512),
                          in_=src[:, :, nt * 512:(nt + 1) * 512]).then_inc(yzs, 16)
        p.emit("sync", yzload)
        yzv = p.bump(yzs, 16)
        se0 = p.val(c.s_se)
        for och in range(NCH):
            bank = och % 4

            def omm(eng, och=och, bank=bank, yzv=yzv, se0=se0):
                if och == 0:
                    eng.wait_ge(yzs, yzv)
                if och >= 4:
                    eng.wait_ge(c.s_se, se0 + och - 3)
                for k in range(16):
                    eng.matmul(out=c.ps1[:, bank * 512:(bank + 1) * 512],
                               lhsT=wo16[:, k * D + och * 128: k * D + (och + 1) * 128],
                               rhs=yzg[:, k * 512:(k + 1) * 512],
                               start=(k == 0), stop=(k == 15))
                eng.nop().then_inc(c.s_te, 1)
            p.emit("tensor", omm)
            tv = p.bump(c.s_te, 1)

            def oev(eng, och=och, nt=nt, bank=bank, tv=tv, bo_s=bo_s,
                    vprev=p.val(c.s_ve)):
                eng.wait_ge(c.s_te, tv)
                if vprev:
                    eng.wait_ge(c.s_ve, vprev)
                if och == 0 and nt == 0:
                    eng.wait_ge(c.s_ls, bo_s)
                eng.activation(out=s32, in_=c.ps1[:, bank * 512:(bank + 1) * 512],
                               func=AF.Identity, bias=c.bias_sb[:, och:och + 1]).then_inc(c.s_se, 1)
            p.emit("scalar", oev)
            sv = p.bump(c.s_se, 1)
            stg = sstg[och % 2]
            ss_ = ssem[och % 2]
            ssp = p.val(ss_)

            def sadd(eng, och=och, nt=nt, sv=sv, stg=stg, ss_=ss_, ssp=ssp):
                eng.wait_ge(c.s_se, sv)
                if ssp:
                    eng.wait_ge(ss_, ssp)
                eng.tensor_tensor(out=stg[:, 0:512], in0=s32,
                                  in1=t_sl(xn16, och, nt_sl(nt)), op=ALU.add).then_inc(c.s_ve, 1)
            p.emit("vector", sadd)
            vv = p.bump(c.s_ve, 1)

            def sout(eng, och=och, nt=nt, vv=vv, stg=stg, ss_=ss_, ssp=ssp):
                eng.wait_ge(c.s_ve, vv)
                if ssp:
                    eng.wait_ge(ss_, ssp)
                eng.dma_start(out=F16S[:, SOFF + och * 2048 + nt * 512: SOFF + och * 2048 + (nt + 1) * 512],
                              in_=stg[:, 0:512]).then_inc(ss_, 16)
            p.emit("sync", sout)
            p.bump(ss_, 16)
    p.barrier()
    c._l2_ssem = ssem
    s0, s1 = p.val(ssem[0]), p.val(ssem[1])
    for e in ("tensor", "vector", "scalar", "gpsimd"):
        def sq_(eng, s0=s0, s1=s1):
            eng.wait_ge(ssem[0], s0)
            eng.wait_ge(ssem[1], s1)
        p.emit(e, sq_)
    a16.drop(wo16)
    a16.drop(yzg)
    for b in sstg:
        a16.drop(b)
    a32.drop(s32)
    c.a16.drop(c.xn16)
    c.xn16 = None


def emit_l2_wquant(c):
    p = c.p
    a16, a32 = c.a16, c.a32
    id32 = c.cons32[:, OFF_ID:OFF_ID + 128]
    W16 = c.dram_w16
    wtmp = a32.take(1024, "l2wt")
    wq16row = a16.take(1024, "l2wqr")
    wstg = [a16.take(1024, "l2wts0"), a16.take(1024, "l2wts1")]
    wos = [p.sem(f"l2wo{p.nsem()}"), p.sem(f"l2wo{p.nsem()}b")]
    w16l = dict(getattr(c, "_w16_last", None) or {0: None, 1: None})
    for wi, (nm, rows, woff, scol) in enumerate([
            ("L2_gate", 1024, 0, 8), ("L2_up", 1024, 32768, 9), ("L2_down", 4096, 65536, 10)]):
        kch = rows // 128
        cols = 4096 if rows == 1024 else 1024
        npc = cols // 1024
        nelem = float(rows * cols)
        ve0 = p.val(c.s_ve)
        step = 0
        for k in range(kch):
            for pc in range(npc):
                st = c.wstage[step % 2]
                need = ve0 + (step - 1) if step >= 2 else ve0
                sem = c.s_wl[step % 2]
                prev = p.val(sem)

                def fl(eng, k=k, pc=pc, st=st, need=need, sem=sem, prev=prev, nm=nm,
                       xg=(w16l[step % 2] if (step < 2 and wi == 0) else None)):
                    if xg:
                        eng.wait_ge(xg[0], xg[1])
                    if need:
                        eng.wait_ge(c.s_ve, need)
                    if prev:
                        eng.wait_ge(sem, prev)
                    eng.dma_start(out=st[:, 0:1024],
                                  in_=c.dram[nm][k * 128:(k + 1) * 128, pc * 1024:(pc + 1) * 1024]).then_inc(sem, 16)
                p.emit("sync", fl)
                dv = p.bump(sem, 16)

                def red(eng, st=st, dv=dv, sem=sem, first=(step == 0)):
                    eng.wait_ge(sem, dv)
                    eng.tensor_reduce(out=c.stat32[:, 1:2], in_=st[:, 0:1024], axis=AX.X,
                                      op=ALU.add, apply_absolute_value=True)
                    if first:
                        eng.tensor_copy(out=c.stat32[:, 0:1], in_=c.stat32[:, 1:2]).then_inc(c.s_ve, 1)
                    else:
                        eng.tensor_tensor(out=c.stat32[:, 0:1], in0=c.stat32[:, 0:1],
                                          in1=c.stat32[:, 1:2], op=ALU.add).then_inc(c.s_ve, 1)
                p.emit("vector", red)
                p.bump(c.s_ve, 1)
                step += 1
        vev = p.val(c.s_ve)
        ones_col = c.cons32[:, OFF_ONES:OFF_ONES + 1]

        def psum_(eng, vev=vev):
            eng.wait_ge(c.s_ve, vev)
            eng.matmul(out=c.ps2[0:1, 0:1], lhsT=ones_col, rhs=c.stat32[:, 0:1],
                       start=True, stop=True)
            eng.nop().then_inc(c.s_te, 1)
        p.emit("tensor", psum_)
        tv = p.bump(c.s_te, 1)

        def gmath(eng, tv=tv, nelem=nelem):
            eng.wait_ge(c.s_te, tv)
            eng.tensor_scalar(out=c.stat32[0:1, 2:3], in0=c.ps2[0:1, 0:1],
                              scalar1=1.0 / nelem, scalar2=1e-5, op0=ALU.mult, op1=ALU.max)
            eng.reciprocal(out=c.stat32[0:1, 3:4], in_=c.stat32[0:1, 2:3]).then_inc(c.s_ve, 1)
        p.emit("vector", gmath)
        vev = p.bump(c.s_ve, 1)
        tv = bcast_row(c, c.stat32[0:1, 2:4], 2, c.ps2[:, 0:512], wait=(c.s_ve, vev))

        def gbc(eng, tv=tv, scol=scol):
            eng.wait_ge(c.s_te, tv)
            eng.tensor_copy(out=c.stat32[:, scol:scol + 1], in_=c.ps2[:, 0:1])
            eng.tensor_copy(out=c.stat32[:, 16:17], in_=c.ps2[:, 1:2]).then_inc(c.s_ve, 1)
        p.emit("vector", gbc)
        p.bump(c.s_ve, 1)
        p.barrier()

        step = 0
        ve0 = p.val(c.s_ve)
        for k in range(kch):
            for pc in range(npc):
                st = c.wstage[step % 2]
                need = ve0 + (step - 1) if step >= 2 else ve0
                sem = c.s_wl[step % 2]
                prev = p.val(sem)

                def fl2(eng, k=k, pc=pc, st=st, need=need, sem=sem, prev=prev, nm=nm):
                    if need:
                        eng.wait_ge(c.s_ve, need)
                    if prev:
                        eng.wait_ge(sem, prev)
                    eng.dma_start(out=st[:, 0:1024],
                                  in_=c.dram[nm][k * 128:(k + 1) * 128, pc * 1024:(pc + 1) * 1024]).then_inc(sem, 16)
                p.emit("sync", fl2)
                dv = p.bump(sem, 16)
                ws_ = wos[step % 2]
                wsp = p.val(ws_)

                def qz(eng, st=st, dv=dv, sem=sem, ws_=ws_, wsp=wsp, stg2=wstg[step % 2]):
                    eng.wait_ge(sem, dv)
                    if wsp:
                        eng.wait_ge(ws_, wsp)
                    eng.tensor_scalar(out=wtmp, in0=st[:, 0:1024], scalar1=c.stat32[:, 16:17],
                                      scalar2=12582912.0, op0=ALU.mult, op1=ALU.add)
                    eng.tensor_scalar(out=wtmp, in0=wtmp, scalar1=-12582912.0, scalar2=-1.0,
                                      op0=ALU.add, op1=ALU.max)
                    eng.tensor_scalar(out=stg2, in0=wtmp, scalar1=1.0, scalar2=None,
                                      op0=ALU.min).then_inc(c.s_ve, 1)
                p.emit("vector", qz)
                qv = p.bump(c.s_ve, 1)

                def wout(eng, k=k, pc=pc, qv=qv, ws_=ws_, wsp=wsp, stg2=wstg[step % 2],
                         woff=woff, cols=cols, npc=npc):
                    eng.wait_ge(c.s_ve, qv)
                    if wsp:
                        eng.wait_ge(ws_, wsp)
                    off = woff + k * cols + pc * 1024
                    eng.dma_start(out=W16[:, off:off + 1024], in_=stg2).then_inc(ws_, 16)
                p.emit("sync", wout)
                p.bump(ws_, 16)
                step += 1
        p.barrier()
    w0, w1 = p.val(wos[0]), p.val(wos[1])
    for e in ("tensor", "vector", "scalar", "gpsimd"):
        def wq_q(eng, w0=w0, w1=w1):
            eng.wait_ge(wos[0], w0)
            eng.wait_ge(wos[1], w1)
        p.emit(e, wq_q)
    a32.drop(wtmp)
    a16.drop(wq16row)
    for b in wstg:
        a16.drop(b)
    vv_ = p.val(c.s_ve)
    c._w16_last = {0: (c.s_ve, vv_), 1: (c.s_ve, vv_)}
    p.barrier()


def emit_l2_ffn(c):
    p = c.p
    a16, a32 = c.a16, c.a32
    F16S = c.dram_f16
    W16 = c.dram_w16
    SOFF, HOFF = 98304, 0
    id16 = c.cons16[:, OFF_ID16:OFF_ID16 + 128]
    id32 = c.cons32[:, OFF_ID:OFF_ID + 128]
    xg = a32.take(8 * 512, "l2xg")
    bigA = a32.take(1024, "l2fA")
    gate32 = a32.take(4 * 512, "l2fg")
    xtm = a32.take(1024, "l2fx")
    sg32 = a32.take(512, "l2fs32")
    vq16 = a16.take(1024, "l2fvq")
    vqT = a16.take(8 * 512, "l2fvqT")
    hq16 = a16.take(1024, "l2fhq")
    hqT = a16.take(32 * 512, "l2fhqT")
    sgrp = a16.take(8 * 512, "l2fsg")
    gT = a16.take(8 * 512, "l2fgT")
    h16sb = [a16.take(512, "l2fh16a"), a16.take(512, "l2fh16b")]
    wbuf = [a16.take(4096, "l2fw0"), a16.take(4096, "l2fw1")]
    dbuf = [a16.take(512, "l2fd0"), a16.take(512, "l2fd1")]
    xgs = p.sem(f"l2xg{p.nsem()}")
    sgs = p.sem(f"l2sg{p.nsem()}")
    xos = p.sem(f"l2xo{p.nsem()}")
    wss = [p.sem(f"l2ws{p.nsem()}"), p.sem(f"l2ws{p.nsem()}b")]
    dss = [p.sem(f"l2ds{p.nsem()}"), p.sem(f"l2ds{p.nsem()}b")]
    hss = [p.sem(f"l2hs{p.nsem()}"), p.sem(f"l2hs{p.nsem()}b")]
    hrs = [p.sem(f"l2hr{p.nsem()}"), p.sem(f"l2hr{p.nsem()}b")]
    wg_done = []
    dn_done = []
    hq_done = []
    for grp in range(4):
        def xload(eng, grp=grp, xop=p.val(xos), ssem=c.spill_sem, ssv=c.spill_done):
            if grp == 0:
                eng.wait_ge(ssem, ssv)
            if xop:
                eng.wait_ge(xos, xop)
            src = c.dram_spill.rearrange("(o pp) t -> pp o t", pp=128)
            eng.dma_start(out=xg.rearrange("p (o t) -> p o t", t=512),
                          in_=src[:, :, grp * 512:(grp + 1) * 512]).then_inc(xgs, 16)
        p.emit("sync", xload)
        xgv = p.bump(xgs, 16)

        def sload(eng, grp=grp, sgp=p.val(sgs), sv0=p.val(c._l2_ssem[0]), sv1=p.val(c._l2_ssem[1])):
            if grp == 0:
                eng.wait_ge(c._l2_ssem[0], sv0)
                eng.wait_ge(c._l2_ssem[1], sv1)
            if sgp:
                eng.wait_ge(sgs, sgp)
            src = F16S[:, SOFF:SOFF + 16384].rearrange("p (o t) -> p o t", t=2048)
            eng.dma_start(out=sgrp.rearrange("p (o t) -> p o t", t=512),
                          in_=src[:, :, grp * 512:(grp + 1) * 512]).then_inc(sgs, 16)
        p.emit("sync", sload)
        sgv = p.bump(sgs, 16)

        se0 = p.val(c.s_se)
        ve0 = p.val(c.s_ve)
        for j in range(4):
            def tx(eng, j=j, xgv=xgv, ve0=ve0):
                if j == 0:
                    eng.wait_ge(xgs, xgv)
                if j > 0:
                    eng.wait_ge(c.s_ve, ve0 + 4 * (j - 1) + 1)
                for ch in range(NCH):
                    eng.transpose(out=c.ps2[:, ch * 128:(ch + 1) * 128],
                                  in_=xg[:, ch * 512 + j * 128: ch * 512 + j * 128 + 128],
                                  identity=id32)
                eng.nop().then_inc(c.s_te, 1)
            p.emit("tensor", tx)
            tv = p.bump(c.s_te, 1)

            def ev0(eng, tv=tv, j=j, se0=se0):
                eng.wait_ge(c.s_te, tv)
                if j > 0:
                    eng.wait_ge(c.s_se, se0 + 4 * (j - 1) + 3)
                eng.tensor_scalar(out=bigA[:, 0:1024], in0=c.ps2[:, 0:D], scalar1=1.0,
                                  scalar2=0.0, op0=ALU.mult, op1=ALU.add,
                                  accum_out=c.stat32[:, 20:21]).then_inc(c.s_ve, 1)
            p.emit("vector", ev0)
            vv0 = p.bump(c.s_ve, 1)

            def sq0(eng, vv0=vv0):
                eng.wait_ge(c.s_ve, vv0)
                eng.activation(out=xtm, in_=bigA[:, 0:1024], func=AF.Square,
                               accum_out=c.stat32[:, 21:22]).then_inc(c.s_se, 1)
            p.emit("scalar", sq0)
            sv0 = p.bump(c.s_se, 1)

            def lnst(eng, sv0=sv0):
                eng.wait_ge(c.s_se, sv0)
                eng.tensor_scalar(out=c.stat32[:, 22:23], in0=c.stat32[:, 20:21],
                                  scalar1=1.0 / D, scalar2=None, op0=ALU.mult)
                eng.tensor_scalar(out=c.stat32[:, 23:24], in0=c.stat32[:, 21:22],
                                  scalar1=1.0 / D, scalar2=None, op0=ALU.mult)
                eng.tensor_tensor(out=c.stat32[:, 24:25], in0=c.stat32[:, 22:23],
                                  in1=c.stat32[:, 22:23], op=ALU.mult)
                eng.tensor_tensor(out=c.stat32[:, 23:24], in0=c.stat32[:, 23:24],
                                  in1=c.stat32[:, 24:25], op=ALU.subtract).then_inc(c.s_ve, 1)
            p.emit("vector", lnst)
            vv1 = p.bump(c.s_ve, 1)

            def lnsq(eng, vv1=vv1):
                eng.wait_ge(c.s_ve, vv1)
                eng.activation(out=c.stat32[:, 23:24], in_=c.stat32[:, 23:24], func=AF.Sqrt,
                               bias=c.cc(1e-5)).then_inc(c.s_se, 1)
            p.emit("scalar", lnsq)
            sv1 = p.bump(c.s_se, 1)

            def lnal(eng, sv1=sv1):
                eng.wait_ge(c.s_se, sv1)
                eng.reciprocal(out=c.stat32[:, 24:25], in_=c.stat32[:, 23:24])
                eng.tensor_tensor(out=c.stat32[:, 22:23], in0=c.stat32[:, 22:23],
                                  in1=c.stat32[:, 24:25], op=ALU.mult)
                eng.tensor_scalar(out=c.stat32[:, 22:23], in0=c.stat32[:, 22:23],
                                  scalar1=-1.0, scalar2=None, op0=ALU.mult).then_inc(c.s_ve, 1)
            p.emit("vector", lnal)
            vv2 = p.bump(c.s_ve, 1)

            def lnap(eng, vv2=vv2):
                eng.wait_ge(c.s_ve, vv2)
                eng.activation(out=xtm, in_=bigA[:, 0:1024], func=AF.Identity,
                               bias=c.stat32[:, 22:23], scale=c.stat32[:, 24:25]).then_inc(c.s_se, 1)
            p.emit("scalar", lnap)
            sv2 = p.bump(c.s_se, 1)

            def xnq(eng, sv2=sv2):
                eng.wait_ge(c.s_se, sv2)
                eng.tensor_reduce(out=c.stat32[:, 25:26], in_=xtm, axis=AX.X, op=ALU.max,
                                  apply_absolute_value=True)
                eng.tensor_scalar(out=c.stat32[:, 25:26], in0=c.stat32[:, 25:26],
                                  scalar1=1e-5, scalar2=None, op0=ALU.max)
                eng.reciprocal(out=c.stat32[:, 26:27], in_=c.stat32[:, 25:26])
                eng.tensor_scalar(out=c.stat32[:, 26:27], in0=c.stat32[:, 26:27],
                                  scalar1=127.0, scalar2=None, op0=ALU.mult)
                eng.tensor_scalar(out=c.stat32[:, 25:26], in0=c.stat32[:, 25:26],
                                  scalar1=1.0 / 127.0, scalar2=None, op0=ALU.mult)
                eng.tensor_scalar(out=bigA[:, 0:1024], in0=xtm, scalar1=c.stat32[:, 26:27],
                                  scalar2=12582912.0, op0=ALU.mult, op1=ALU.add)
                eng.tensor_scalar(out=bigA[:, 0:1024], in0=bigA[:, 0:1024], scalar1=-12582912.0,
                                  scalar2=None, op0=ALU.add)
                eng.tensor_scalar(out=vq16, in0=bigA[:, 0:1024], scalar1=c.stat32[:, 25:26],
                                  scalar2=None, op0=ALU.mult).then_inc(c.s_ve, 1)
            p.emit("vector", xnq)
            vq_v = p.bump(c.s_ve, 1)

            def vqt(eng, vq_v=vq_v, j=j, se0=se0):
                eng.wait_ge(c.s_ve, vq_v)
                if j > 0:
                    eng.wait_ge(c.s_se, se0 + 4 * (j - 1) + 4)
                for i in range(NCH):
                    eng.transpose(out=c.psh[:, i * 128:(i + 1) * 128],
                                  in_=vq16[:, i * 128:(i + 1) * 128], identity=id16)
                eng.nop().then_inc(c.s_te, 1)
            p.emit("tensor", vqt)
            tv = p.bump(c.s_te, 1)

            def vqtev(eng, tv=tv, j=j):
                eng.wait_ge(c.s_te, tv)
                for i in range(NCH):
                    eng.activation(out=vqT[:, i * 512 + j * 128: i * 512 + j * 128 + 128],
                                   in_=c.psh[:, i * 128:(i + 1) * 128], func=AF.Copy)
                eng.nop().then_inc(c.s_se, 1)
            p.emit("scalar", vqtev)
            p.bump(c.s_se, 1)
        sevq = p.val(c.s_se)

        def hz(eng, sevq=sevq):
            eng.wait_ge(c.s_se, sevq)
            eng.memset(c.stat32[:, 27:31], 0.0)
            eng.nop().then_inc(c.s_ve, 1)
        p.emit("vector", hz)
        p.bump(c.s_ve, 1)

        for st8 in range(8):
            for gu in range(2):
                ws_ = wss[(st8 * 2 + gu) % 2]
                wsp = p.val(ws_)
                wb = wbuf[(st8 * 2 + gu) % 2]

                def wload(eng, st8=st8, gu=gu, ws_=ws_, wsp=wsp, wb=wb,
                          wg=(wg_done[-2] if len(wg_done) >= 2 else 0)):
                    if wsp:
                        eng.wait_ge(ws_, wsp)
                    if wg:
                        eng.wait_ge(c.s_te, wg)
                    woff = gu * 32768
                    srcv = W16[:, woff:woff + 32768].rearrange("p (h r) -> p h r", r=4096)
                    eng.dma_start(out=wb.rearrange("p (h r) -> p h r", r=512),
                                  in_=srcv[:, :, st8 * 512:(st8 + 1) * 512]).then_inc(ws_, 16)
                p.emit("sync", wload)
                wv = p.bump(ws_, 16)

                def wmm(eng, ws_=ws_, wv=wv, wb=wb, sevq=sevq, se1=p.val(c.s_se),
                        ve1=p.val(c.s_ve), first=(st8 == 0 and gu == 0)):
                    eng.wait_ge(ws_, wv)
                    eng.wait_ge(c.s_se, sevq if first else se1)
                    if not first:
                        eng.wait_ge(c.s_ve, ve1)
                    for j in range(4):
                        for k in range(NCH):
                            eng.matmul(out=c.ps1[:, j * 512:(j + 1) * 512],
                                       lhsT=vqT[:, k * 512 + j * 128: k * 512 + j * 128 + 128],
                                       rhs=wb[:, k * 512:(k + 1) * 512],
                                       start=(k == 0), stop=(k == NCH - 1))
                    eng.nop().then_inc(c.s_te, 1)
                p.emit("tensor", wmm)
                tv = p.bump(c.s_te, 1)
                wg_done.append(tv)

                if gu == 0:
                    def gev(eng, tv=tv, ve2=p.val(c.s_ve)):
                        eng.wait_ge(c.s_te, tv)
                        eng.wait_ge(c.s_ve, ve2)
                        eng.activation(out=gate32, in_=c.ps1[:, 0:2048], func=AF.Silu,
                                       scale=c.stat32[:, 8:9]).then_inc(c.s_se, 1)
                    p.emit("scalar", gev)
                    p.bump(c.s_se, 1)
                else:
                    for j in range(4):
                        hs_ = hss[j % 2]
                        hsp = p.val(hs_)

                        def hj(eng, j=j, tv=tv, se2=p.val(c.s_se), hs_=hs_, hsp=hsp, hbuf=h16sb[j % 2]):
                            eng.wait_ge(c.s_te, tv)
                            if j == 0:
                                eng.wait_ge(c.s_se, se2)
                            if hsp:
                                eng.wait_ge(hs_, hsp)
                            eng.scalar_tensor_tensor(out=hbuf, in0=c.ps1[:, j * 512:(j + 1) * 512],
                                                     scalar=c.stat32[:, 9:10],
                                                     in1=gate32[:, j * 512:(j + 1) * 512],
                                                     op0=ALU.mult, op1=ALU.mult)
                            eng.tensor_reduce(out=c.stat32[:, 31:32], in_=hbuf, axis=AX.X,
                                              op=ALU.max, apply_absolute_value=True)
                            eng.tensor_tensor(out=c.stat32[:, 27 + j:28 + j],
                                              in0=c.stat32[:, 27 + j:28 + j],
                                              in1=c.stat32[:, 31:32], op=ALU.max).then_inc(c.s_ve, 1)
                        p.emit("vector", hj)
                        hv = p.bump(c.s_ve, 1)

                        def hout(eng, j=j, st8=st8, grp=grp, hv=hv, hs_=hs_, hsp=hsp, hbuf=h16sb[j % 2]):
                            eng.wait_ge(c.s_ve, hv)
                            if hsp:
                                eng.wait_ge(hs_, hsp)
                            off = HOFF + j * 4096 + st8 * 512
                            eng.dma_start(out=F16S[:, off:off + 512], in_=hbuf).then_inc(hs_, 16)
                        p.emit("sync", hout)
                        p.bump(hs_, 16)
        vevh = p.val(c.s_ve)
        h0, h1 = p.val(hss[0]), p.val(hss[1])

        def hsc(eng, vevh=vevh):
            eng.wait_ge(c.s_ve, vevh)
            for j in range(4):
                eng.tensor_scalar(out=c.stat32[:, 27 + j:28 + j], in0=c.stat32[:, 27 + j:28 + j],
                                  scalar1=1e-5, scalar2=None, op0=ALU.max)
            eng.reciprocal(out=c.stat32[:, 31:32], in_=c.stat32[:, 27:28])
            eng.tensor_scalar(out=c.stat32[:, 31:32], in0=c.stat32[:, 31:32], scalar1=127.0,
                              scalar2=None, op0=ALU.mult).then_inc(c.s_ve, 1)
        p.emit("vector", hsc)
        p.bump(c.s_ve, 1)

        se3 = p.val(c.s_se)
        hqi = 0
        hqt_done = []
        for j in range(4):
            for st4 in range(4):
                hr_ = hrs[hqi % 2]
                hrp = p.val(hr_)

                def hback(eng, j=j, st4=st4, hr_=hr_, hrp=hrp, h0=h0, h1=h1, hqi=hqi,
                          hg=(hqt_done[-1] if hqt_done else 0)):
                    if hqi == 0:
                        eng.wait_ge(hss[0], h0)
                        eng.wait_ge(hss[1], h1)
                    if hrp:
                        eng.wait_ge(hr_, hrp)
                    if hg:
                        eng.wait_ge(c.s_te, hg)
                    off = HOFF + j * 4096 + st4 * 1024
                    eng.dma_start(out=hq16, in_=F16S[:, off:off + 1024]).then_inc(hr_, 16)
                p.emit("sync", hback)
                hrv = p.bump(hr_, 16)

                def hqz(eng, j=j, hrv=hrv, hr_=hr_, vevh2=p.val(c.s_ve)):
                    eng.wait_ge(hr_, hrv)
                    eng.wait_ge(c.s_ve, vevh2)
                    eng.reciprocal(out=c.stat32[:, 31:32], in_=c.stat32[:, 27 + j:28 + j])
                    eng.tensor_scalar(out=bigA[:, 0:1024], in0=hq16,
                                      scalar1=c.stat32[:, 31:32], scalar2=127.0,
                                      op0=ALU.mult, op1=ALU.mult)
                    eng.tensor_scalar(out=bigA[:, 0:1024], in0=bigA[:, 0:1024],
                                      scalar1=12582912.0, scalar2=-12582912.0,
                                      op0=ALU.add, op1=ALU.add)
                    eng.tensor_scalar(out=c.stat32[:, 31:32], in0=c.stat32[:, 27 + j:28 + j],
                                      scalar1=1.0 / 127.0, scalar2=None, op0=ALU.mult)
                    eng.tensor_scalar(out=hq16, in0=bigA[:, 0:1024], scalar1=c.stat32[:, 31:32],
                                      scalar2=None, op0=ALU.mult).then_inc(c.s_ve, 1)
                p.emit("vector", hqz)
                hqv = p.bump(c.s_ve, 1)
                hq_done.append(hqv)

                def hqt(eng, hqv=hqv, se4=p.val(c.s_se)):
                    eng.wait_ge(c.s_ve, hqv)
                    eng.wait_ge(c.s_se, se4)
                    for i in range(8):
                        eng.transpose(out=c.psh[:, i * 128:(i + 1) * 128],
                                      in_=hq16[:, i * 128:(i + 1) * 128], identity=id16)
                    eng.nop().then_inc(c.s_te, 1)
                p.emit("tensor", hqt)
                tv2 = p.bump(c.s_te, 1)
                hqt_done.append(tv2)

                def hqtev(eng, tv2=tv2, j=j, st4=st4):
                    eng.wait_ge(c.s_te, tv2)
                    for i in range(8):
                        ic = (st4 * 2 + i // 4) * 4 + (i % 4)
                        eng.activation(out=hqT[:, ic * 512 + j * 128: ic * 512 + j * 128 + 128],
                                       in_=c.psh[:, i * 128:(i + 1) * 128], func=AF.Copy)
                    eng.nop().then_inc(c.s_se, 1)
                p.emit("scalar", hqtev)
                p.bump(c.s_se, 1)
                hqi += 1
        sevh2 = p.val(c.s_se)

        for oh in range(2):
            for k in range(32):
                ds_ = dss[k % 2]
                dsp = p.val(ds_)
                db = dbuf[k % 2]

                def dload(eng, k=k, oh=oh, ds_=ds_, dsp=dsp, db=db,
                          dg=(dn_done[-2] if len(dn_done) >= 2 else 0)):
                    if dsp:
                        eng.wait_ge(ds_, dsp)
                    if dg:
                        eng.wait_ge(c.s_te, dg)
                    off = 65536 + k * 1024 + oh * 512
                    eng.dma_start(out=db, in_=W16[:, off:off + 512]).then_inc(ds_, 16)
                p.emit("sync", dload)
                dv = p.bump(ds_, 16)

                def dmm(eng, k=k, ds_=ds_, dv=dv, db=db, sevh2=sevh2, se4=p.val(c.s_se)):
                    eng.wait_ge(ds_, dv)
                    if k == 0:
                        eng.wait_ge(c.s_se, se4)
                    for oo in range(4):
                        eng.matmul(out=c.ps1[:, oo * 512:(oo + 1) * 512],
                                   lhsT=db[:, oo * 128:(oo + 1) * 128],
                                   rhs=hqT[:, k * 512:(k + 1) * 512],
                                   start=(k == 0), stop=(k == 31))
                    eng.nop().then_inc(c.s_te, 1)
                p.emit("tensor", dmm)
                dn_done.append(p.bump(c.s_te, 1))
            tvd = p.val(c.s_te)
            for oo in range(4):
                och = oh * 4 + oo

                def dsil(eng, oo=oo, och=och, tvd=tvd):
                    eng.wait_ge(c.s_te, tvd)
                    eng.activation(out=gT[:, och * 512:(och + 1) * 512],
                                   in_=c.ps1[:, oo * 512:(oo + 1) * 512],
                                   func=AF.Silu, scale=c.stat32[:, 10:11])
                    eng.nop().then_inc(c.s_se, 1)
                p.emit("scalar", dsil)
                p.bump(c.s_se, 1)
        sevd = p.val(c.s_se)
        for och in range(NCH):
            def comb(eng, och=och, sevd=sevd, sgv=sgv, xgv=xgv):
                eng.wait_ge(c.s_se, sevd)
                if och == 0:
                    eng.wait_ge(sgs, sgv)
                    eng.wait_ge(xgs, xgv)
                eng.tensor_tensor(out=sg32, in0=sgrp[:, och * 512:(och + 1) * 512],
                                  in1=gT[:, och * 512:(och + 1) * 512], op=ALU.mult)
                eng.tensor_tensor(out=xg[:, och * 512:(och + 1) * 512],
                                  in0=xg[:, och * 512:(och + 1) * 512], in1=sg32,
                                  op=ALU.add).then_inc(c.s_ve, 2)
            p.emit("vector", comb)
            p.bump(c.s_ve, 2)
        vev = p.val(c.s_ve)

        def xstore(eng, grp=grp, vev=vev):
            eng.wait_ge(c.s_ve, vev)
            dst = c.dram_spill.rearrange("(o pp) t -> pp o t", pp=128)
            eng.dma_start(out=dst[:, :, grp * 512:(grp + 1) * 512],
                          in_=xg.rearrange("p (o t) -> p o t", t=512)).then_inc(xos, 16)
        p.emit("sync", xstore)
        p.bump(xos, 16)
        p.barrier()
    for b in (xg, bigA, gate32, xtm, sg32):
        a32.drop(b)
    for b in (vq16, vqT, hq16, hqT, sgrp, gT, h16sb[0], h16sb[1], wbuf[0], wbuf[1], dbuf[0], dbuf[1]):
        a16.drop(b)
    sem = p.sem(f"rsf{p.nsem()}")
    c.x = c.a32.take(NCH * T, "x")
    x = c.x

    def fre(eng, sem=sem, xop=p.val(xos)):
        eng.wait_ge(xos, xop)
        for ch in range(NCH):
            eng.dma_start(out=t_sl(x, ch),
                          in_=c.dram_spill[ch * 128:(ch + 1) * 128, :]).then_inc(sem, 16)
    p.emit("sync", fre)
    v = p.bump(sem, 16 * NCH)
    for e in ("tensor", "vector", "scalar", "gpsimd"):
        p.emit(e, lambda eng, v=v, sem=sem: eng.wait_ge(sem, v))
    p.barrier()



def emit_layer3(c):
    p = c.p
    a16, a32 = c.a16, c.a32
    F16S = c.dram_f16
    QOFF, KOFF, VOFF, YOFF, SOFF, MOFF = 0, 16384, 32768, 49152, 65536, 81920
    id16 = c.cons16[:, OFF_ID16:OFF_ID16 + 128]
    id32 = c.cons32[:, OFF_ID:OFF_ID + 128]
    emit_ln(c, with_xn16=True)
    p.barrier()
    emit_spill_x(c)
    xn16 = c.xn16

    wq16 = a16.take(8 * D, "l3wqkv")
    pstage = [a16.take(1024, "l3ps0"), a16.take(1024, "l3ps1")]
    psems = [p.sem(f"l3p{p.nsem()}"), p.sem(f"l3p{p.nsem()}b")]
    bq_s = load_small(c, "L3_bqkv", c.bias_sb[:, 0:24],
                      wait=[(c.s_se, p.val(c.s_se)), (c.s_ve, p.val(c.s_ve))])
    bvr = a32.take(1024, "l3bvr")
    bvbc = a32.take(1024, "l3bvbc")
    bvr_s = load_small(c, "L3_bv_row", bvr[0:1, 0:D])
    for part in range(3):
        load_w16(c, "L3_wqkv", 8, D, wq16, col_sl=slice(part * D, (part + 1) * D))
        p.barrier()
        if part < 2:
            se0 = p.val(c.s_se)
            idx = 0
            for och in range(NCH):
                for nt in range(4):
                    bank = idx % 4

                    def mmp(eng, och=och, nt=nt, bank=bank, idx=idx, se0=se0):
                        if idx >= 4:
                            eng.wait_ge(c.s_se, se0 + idx - 3)
                        for k in range(NCH):
                            eng.matmul(out=c.ps1[:, bank * 512:(bank + 1) * 512],
                                       lhsT=wq16[:, k * D + och * 128: k * D + (och + 1) * 128],
                                       rhs=t_sl(xn16, k, nt_sl(nt)),
                                       start=(k == 0), stop=(k == NCH - 1))
                        eng.nop().then_inc(c.s_te, 1)
                    p.emit("tensor", mmp)
                    tv = p.bump(c.s_te, 1)
                    stg = pstage[idx % 2]
                    ps_ = psems[idx % 2]
                    pprev = p.val(ps_)

                    def pev(eng, och=och, bank=bank, tv=tv, stg=stg, ps_=ps_, pprev=pprev,
                            part=part, first=(part == 0 and idx == 0), bq_s=bq_s):
                        eng.wait_ge(c.s_te, tv)
                        if first:
                            eng.wait_ge(c.s_ls, bq_s)
                        if pprev:
                            eng.wait_ge(ps_, pprev)
                        eng.activation(out=stg[:, 0:512], in_=c.ps1[:, bank * 512:(bank + 1) * 512],
                                       func=AF.Identity,
                                       bias=c.bias_sb[:, part * 8 + och: part * 8 + och + 1],
                                       scale=(0.125 if part == 0 else 1.0)).then_inc(c.s_se, 1)
                    p.emit("scalar", pev)
                    sv = p.bump(c.s_se, 1)

                    def pout(eng, och=och, nt=nt, stg=stg, sv=sv, part=part, ps_=ps_, pprev=pprev):
                        eng.wait_ge(c.s_se, sv)
                        if pprev:
                            eng.wait_ge(ps_, pprev)
                        off = (QOFF if part == 0 else KOFF) + och * 2048 + nt * 512
                        eng.dma_start(out=F16S[:, off:off + 512], in_=stg[:, 0:512]).then_inc(ps_, 16)
                    p.emit("sync", pout)
                    p.bump(ps_, 16)
                    idx += 1
        else:
            tvb = bcast_row(c, bvr[0:1, 0:D], D, c.ps2, wait=(c.s_ls, bvr_s))

            def bvev(eng, tvb=tvb):
                eng.wait_ge(c.s_te, tvb)
                eng.tensor_copy(out=bvbc, in_=c.ps2[:, 0:D]).then_inc(c.s_ve, 1)
            p.emit("vector", bvev)
            p.bump(c.s_ve, 1)
            ve0 = p.val(c.s_ve)
            for tc in range(16):
                def mmv(eng, tc=tc, ve0=ve0):
                    eng.wait_ge(c.s_ve, ve0 + tc)
                    for k in range(NCH):
                        lhs = t_sl(xn16, k, slice(tc * 128, (tc + 1) * 128))
                        for hh in range(2):
                            eng.matmul(out=c.ps2[:, hh * 512:(hh + 1) * 512], lhsT=lhs,
                                       rhs=wq16[:, k * D + hh * 512: k * D + (hh + 1) * 512],
                                       start=(k == 0), stop=(k == NCH - 1))
                    eng.nop().then_inc(c.s_te, 1)
                p.emit("tensor", mmv)
                tv = p.bump(c.s_te, 1)
                stg = pstage[tc % 2]
                ps_ = psems[tc % 2]
                pprev = p.val(ps_)

                def vev(eng, tv=tv, stg=stg, ps_=ps_, pprev=pprev):
                    eng.wait_ge(c.s_te, tv)
                    if pprev:
                        eng.wait_ge(ps_, pprev)
                    eng.tensor_tensor(out=stg[:, 0:1024], in0=c.ps2[:, 0:1024],
                                      in1=bvbc, op=ALU.add).then_inc(c.s_ve, 1)
                p.emit("vector", vev)
                sv = p.bump(c.s_ve, 1)

                def vout(eng, tc=tc, stg=stg, sv=sv, ps_=ps_, pprev=pprev):
                    eng.wait_ge(c.s_ve, sv)
                    if pprev:
                        eng.wait_ge(ps_, pprev)
                    eng.dma_start(out=F16S[:, VOFF + tc * 1024: VOFF + (tc + 1) * 1024],
                                  in_=stg[:, 0:1024]).then_inc(ps_, 16)
                p.emit("sync", vout)
                p.bump(ps_, 16)
    p.barrier()
    a16.drop(wq16)
    for b in pstage:
        a16.drop(b)
    a32.drop(bvr)
    a32.drop(bvbc)
    qk0, qk1 = p.val(psems[0]), p.val(psems[1])
    for e in ("tensor", "vector", "scalar", "gpsimd"):
        def pq_(eng, qk0=qk0, qk1=qk1):
            eng.wait_ge(psems[0], qk0)
            eng.wait_ge(psems[1], qk1)
        p.emit(e, pq_)
    p.barrier()

    w116 = a16.take(8 * 512, "l3w1")
    load_w16(c, "L3_W1", 8, 512, w116)
    b1_s = load_small(c, "L3_b1", c.bias_sb[:, 24:28],
                      wait=[(c.s_se, p.val(c.s_se))])
    sw2 = a32.take(32, "l3w2s")
    w216 = a16.take(32, "l3w2")
    mrow16 = a16.take(T, "l3mrow")
    hbuf = a16.take(2048, "l3hbuf")
    lg32 = a32.take(512, "l3lg")
    lgt32 = a32.take(32, "l3lgt")
    b2bc = a32.take(8, "l3b2bc")
    mt16 = a16.take(8, "l3mt")
    w2s = p.sem(f"l3w2{p.nsem()}")

    def w2load(eng, w2s=w2s, ssem=c.spill_sem, ssv=c.spill_done):
        eng.wait_ge(ssem, ssv)
        eng.dma_start(out=sw2.rearrange("p (cc e) -> p cc e", e=8),
                      in_=c.dram["L3_W2"].rearrange("(cc p) e -> p cc e", p=128)).then_inc(w2s, 16)
    p.emit("sync", w2load)
    w2v = p.bump(w2s, 16)

    def w2cv(eng, w2v=w2v, w2s=w2s):
        eng.wait_ge(w2s, w2v)
        eng.tensor_copy(out=w216, in_=sw2).then_inc(c.s_ve, 1)
    p.emit("vector", w2cv)
    p.bump(c.s_ve, 1)
    b2r = a32.take(8, "l3b2r")
    b2_s = load_small(c, "L3_b2", b2r[0:1, 0:8])
    tvb = bcast_row(c, b2r[0:1, 0:8], 8, c.ps2[:, 0:512], wait=(c.s_ls, b2_s))

    def b2ev(eng, tvb=tvb):
        eng.wait_ge(c.s_te, tvb)
        eng.tensor_copy(out=b2bc, in_=c.ps2[:, 0:8]).then_inc(c.s_ve, 1)
    p.emit("vector", b2ev)
    p.bump(c.s_ve, 1)
    p.barrier()
    for nt in range(4):
        se0 = p.val(c.s_se)
        for fch in range(4):
            def rmm(eng, nt=nt, fch=fch, se0=se0, tprev=p.val(c.s_te)):
                if fch == 0:
                    eng.wait_ge(c.s_te, tprev)
                    eng.wait_ge(c.s_se, se0)
                for k in range(NCH):
                    eng.matmul(out=c.ps1[:, fch * 512:(fch + 1) * 512],
                               lhsT=w116[:, k * 512 + fch * 128: k * 512 + (fch + 1) * 128],
                               rhs=t_sl(xn16, k, nt_sl(nt)),
                               start=(k == 0), stop=(k == NCH - 1))
                eng.nop().then_inc(c.s_te, 1)
            p.emit("tensor", rmm)
            tv = p.bump(c.s_te, 1)

            def rev(eng, fch=fch, tv=tv, b1_s=b1_s, first=(nt == 0 and fch == 0)):
                eng.wait_ge(c.s_te, tv)
                if first:
                    eng.wait_ge(c.s_ls, b1_s)
                eng.activation(out=hbuf[:, fch * 512:(fch + 1) * 512],
                               in_=c.ps1[:, fch * 512:(fch + 1) * 512],
                               func=AF.Relu,
                               bias=c.bias_sb[:, 24 + fch:25 + fch]).then_inc(c.s_se, 1)
            p.emit("scalar", rev)
            p.bump(c.s_se, 1)
        sevh = p.val(c.s_se)

        def lmm(eng, sevh=sevh, vprev=p.val(c.s_ve)):
            eng.wait_ge(c.s_se, sevh)
            eng.wait_ge(c.s_ve, vprev)
            for fch in range(4):
                eng.matmul(out=c.ps2[0:8, 0:512], lhsT=w216[:, fch * 8:(fch + 1) * 8],
                           rhs=hbuf[:, fch * 512:(fch + 1) * 512],
                           start=(fch == 0), stop=(fch == 3))
            eng.nop().then_inc(c.s_te, 1)
        p.emit("tensor", lmm)
        tv = p.bump(c.s_te, 1)

        def levac(eng, tv=tv):
            eng.wait_ge(c.s_te, tv)
            eng.tensor_copy(out=lg32[0:8, 0:512], in_=c.ps2[0:8, 0:512]).then_inc(c.s_ve, 1)
        p.emit("vector", levac)
        lv = p.bump(c.s_ve, 1)
        for j in range(4):
            def ltr(eng, j=j, lv=lv, sprev=p.val(c.s_se)):
                if j == 0:
                    eng.wait_ge(c.s_ve, lv)
                    eng.wait_ge(c.s_se, sprev)
                eng.transpose(out=c.ps2[:, 512 + j * 8: 512 + (j + 1) * 8],
                              in_=lg32[0:8, j * 128:(j + 1) * 128],
                              identity=id32[0:8, 0:8])
                eng.nop().then_inc(c.s_te, 1)
            p.emit("tensor", ltr)
            tv2 = p.bump(c.s_te, 1)

            def lsel(eng, j=j, tv2=tv2):
                eng.wait_ge(c.s_te, tv2)
                eng.tensor_tensor(out=lgt32[:, 0:8], in0=c.ps2[:, 512 + j * 8: 512 + (j + 1) * 8],
                                  in1=b2bc, op=ALU.add)
                eng.tensor_reduce(out=c.stat32[:, 0:1], in_=lgt32[:, 0:8], axis=AX.X,
                                  op=ALU.max)
                eng.tensor_scalar(out=mt16, in0=lgt32[:, 0:8], scalar1=c.stat32[:, 0:1],
                                  scalar2=None, op0=ALU.is_equal).then_inc(c.s_ve, 1)
            p.emit("vector", lsel)
            mv = p.bump(c.s_ve, 1)

            def mtr(eng, j=j, mv=mv):
                eng.wait_ge(c.s_ve, mv)
                eng.transpose(out=c.psh[0:8, j * 128:(j + 1) * 128], in_=mt16[:, 0:8],
                              identity=id16)
                eng.nop().then_inc(c.s_te, 1)
            p.emit("tensor", mtr)
            tv3 = p.bump(c.s_te, 1)

            def mev(eng, j=j, nt=nt, tv3=tv3):
                eng.wait_ge(c.s_te, tv3)
                eng.activation(out=mrow16[0:8, nt * 512 + j * 128: nt * 512 + (j + 1) * 128],
                               in_=c.psh[0:8, j * 128:(j + 1) * 128],
                               func=AF.Copy).then_inc(c.s_se, 1)
            p.emit("scalar", mev)
            p.bump(c.s_se, 1)
    sevm = p.val(c.s_se)
    mds = p.sem(f"l3md{p.nsem()}")

    def mdma(eng, sevm=sevm, mds=mds):
        eng.wait_ge(c.s_se, sevm)
        eng.dma_start(out=F16S[0:8, MOFF:MOFF + T], in_=mrow16[0:8, 0:T]).then_inc(mds, 16)
    p.emit("sync", mdma)
    mdv = p.bump(mds, 16)
    p.barrier()
    a16.drop(w116)
    a16.drop(hbuf)
    a16.drop(w216)
    a16.drop(mt16)
    a32.drop(sw2)
    a32.drop(lg32)
    a32.drop(lgt32)
    a32.drop(b2bc)
    a32.drop(b2r)
    p.barrier()
    _ckpt(c, "l3r")

    qkbuf = [a16.take(5120, "l3qk0"), a16.take(5120, "l3qk1")]
    S16 = a16.take(2048, "l3S")
    pt16 = a16.take(2048, "l3pt")
    ctx16 = a16.take(64, "l3ctx")
    ysm = [a16.take(128, "l3y0"), a16.take(128, "l3y1")]
    qks = [p.sem(f"l3qk{p.nsem()}"), p.sem(f"l3qk{p.nsem()}b")]
    ysem = [p.sem(f"l3ys{p.nsem()}"), p.sem(f"l3ys{p.nsem()}b")]
    qk_done = {}
    last_te_of_head = {}

    def emit_qkvload(h):
        sl = h % 2
        sem = qks[sl]
        prev = p.val(sem)
        teg = last_te_of_head.get(h - 2, 0)

        def qkl(eng, h=h, sem=sem, prev=prev, teg=teg):
            if teg:
                eng.wait_ge(c.s_te, teg)
            if prev:
                eng.wait_ge(sem, prev)
            qb = qkbuf[h % 2]
            r0 = (h % 2) * 64
            ch = h // 2
            eng.dma_start(out=qb[0:64, 0:2048],
                          in_=F16S[r0:r0 + 64, QOFF + ch * 2048: QOFF + (ch + 1) * 2048]).then_inc(sem, 16)
            eng.dma_start(out=qb[0:64, 2048:4096],
                          in_=F16S[r0:r0 + 64, KOFF + ch * 2048: KOFF + (ch + 1) * 2048]).then_inc(sem, 16)
            eng.dma_start(out=qb.rearrange("p (q d) -> p q d", d=64)[:, 64:80, :],
                          in_=F16S[:, VOFF:VOFF + 16384].rearrange("p (tt d) -> p tt d", d=1024)[:, :, h * 64:(h + 1) * 64]).then_inc(sem, 16)
        p.emit("sync", qkl)
        qk_done[h] = p.bump(sem, 48)

    emit_qkvload(0)
    emit_qkvload(1)
    prev_sev = 0
    prev_smask = 0
    prev_yev = 0
    for h in range(16):
        qb = qkbuf[h % 2]
        qh = qb[0:64, 0:2048]
        kh = qb[0:64, 2048:4096]
        vh = qb[:, 4096:5120]
        for tq in range(16):
            nb = tq // 4 + 1
            sb = (tq % 2) * 4

            def smm(eng, h=h, tq=tq, nb=nb, qh=qh, kh=kh,
                    qkv=(qk_done[h] if tq == 0 else 0), sem=qks[h % 2],
                    ps_=prev_sev, pm_=prev_smask):
                if qkv:
                    eng.wait_ge(sem, qkv)
                if ps_:
                    eng.wait_ge(c.s_se, ps_)
                if pm_:
                    eng.wait_ge(c.s_ve, pm_)
                for b in range(nb):
                    eng.matmul(out=c.ps1[:, b * 512:(b + 1) * 512],
                               lhsT=qh[:, tq * 128:(tq + 1) * 128],
                               rhs=kh[:, b * 512:(b + 1) * 512],
                               start=True, stop=True)
                eng.nop().then_inc(c.s_te, 1)
            p.emit("tensor", smm)
            tv = p.bump(c.s_te, 1)

            sev_v = 0
            if nb > 1:
                def sev(eng, nb=nb, tv=tv):
                    eng.wait_ge(c.s_te, tv)
                    eng.activation(out=S16[:, 0:(nb - 1) * 512],
                                   in_=c.ps1[:, 0:(nb - 1) * 512],
                                   func=AF.Copy).then_inc(c.s_se, 1)
                p.emit("scalar", sev)
                sev_v = p.bump(c.s_se, 1)

            def smask(eng, nb=nb, tq=tq, tv=tv, sev_v=sev_v, sb=sb):
                eng.wait_ge(c.s_te, tv)
                mk = c.cons16[:, OFF_MASK16 + (tq % 4) * 512: OFF_MASK16 + (tq % 4 + 1) * 512]
                eng.scalar_tensor_tensor(out=S16[:, (nb - 1) * 512: nb * 512],
                                         in0=c.ps1[:, (nb - 1) * 512: nb * 512],
                                         scalar=1.0, in1=mk, op0=ALU.mult, op1=ALU.add)
                if sev_v:
                    eng.wait_ge(c.s_se, sev_v)
                eng.tensor_reduce(out=c.stat32[:, sb:sb + 1], in_=S16[:, 0:nb * 512], axis=AX.X,
                                  op=ALU.max)
                eng.tensor_scalar(out=c.stat32[:, sb + 1:sb + 2], in0=c.stat32[:, sb:sb + 1],
                                  scalar1=-1.0, scalar2=None, op0=ALU.mult).then_inc(c.s_ve, 1)
            p.emit("vector", smask)
            vv = p.bump(c.s_ve, 1)
            smask_v = vv

            def sexp(eng, nb=nb, vv=vv, sb=sb):
                eng.wait_ge(c.s_ve, vv)
                eng.activation(out=S16[:, 0:nb * 512], in_=S16[:, 0:nb * 512], func=AF.Exp,
                               bias=c.stat32[:, sb + 1:sb + 2],
                               accum_out=c.stat32[:, sb + 2:sb + 3]).then_inc(c.s_se, 1)
            p.emit("scalar", sexp)
            sv = p.bump(c.s_se, 1)
            sexp_v = sv

            def srec(eng, sv=sv, sb=sb):
                eng.wait_ge(c.s_se, sv)
                eng.reciprocal(out=c.stat32[:, sb + 3:sb + 4],
                               in_=c.stat32[:, sb + 2:sb + 3]).then_inc(c.s_ve, 1)
            p.emit("vector", srec)
            rv = p.bump(c.s_ve, 1)

            def ptr(eng, nb=nb, sexp_v=sexp_v, py_=prev_yev):
                eng.wait_ge(c.s_se, sexp_v)
                if py_:
                    eng.wait_ge(c.s_se, py_)
                for k in range(nb * 4):
                    eng.transpose(out=c.psh[:, k * 128:(k + 1) * 128],
                                  in_=S16[:, k * 128:(k + 1) * 128], identity=id16)
                eng.nop().then_inc(c.s_te, 1)
            p.emit("tensor", ptr)
            tv = p.bump(c.s_te, 1)

            def ptev(eng, nb=nb, tv=tv):
                eng.wait_ge(c.s_te, tv)
                eng.activation(out=pt16[:, 0:nb * 512], in_=c.psh[:, 0:nb * 512],
                               func=AF.Copy).then_inc(c.s_se, 1)
            p.emit("scalar", ptev)
            pv = p.bump(c.s_se, 1)

            def avmm(eng, nb=nb, vh=vh, pv=pv):
                eng.wait_ge(c.s_se, pv)
                for k in range(nb * 4):
                    eng.matmul(out=c.ps2[:, 0:64], lhsT=pt16[:, k * 128:(k + 1) * 128],
                               rhs=vh[:, k * 64:(k + 1) * 64],
                               start=(k == 0), stop=(k == nb * 4 - 1))
                eng.nop().then_inc(c.s_te, 1)
            p.emit("tensor", avmm)
            tv = p.bump(c.s_te, 1)

            def cev(eng, tv=tv, rvv=rv, sb=sb):
                eng.wait_ge(c.s_te, tv)
                eng.wait_ge(c.s_ve, rvv)
                eng.activation(out=ctx16, in_=c.ps2[:, 0:64], func=AF.Identity,
                               scale=c.stat32[:, sb + 3:sb + 4]).then_inc(c.s_se, 1)
            p.emit("scalar", cev)
            cv = p.bump(c.s_se, 1)

            def ctr(eng, h=h, cv=cv):
                eng.wait_ge(c.s_se, cv)
                r0 = (h % 2) * 64
                eng.transpose(out=c.psh[r0:r0 + 64, 0:128], in_=ctx16[:, 0:64],
                              identity=id16)
                eng.nop().then_inc(c.s_te, 1)
            p.emit("tensor", ctr)
            tv = p.bump(c.s_te, 1)
            prev_sev = sexp_v
            prev_smask = smask_v
            ysl = (h * 16 + tq) % 2
            ys_ = ysem[ysl]
            yprev = p.val(ys_)

            def yev(eng, h=h, tv=tv, ysl=ysl, ys_=ys_, yprev=yprev):
                eng.wait_ge(c.s_te, tv)
                if yprev:
                    eng.wait_ge(ys_, yprev)
                r0 = (h % 2) * 64
                eng.activation(out=ysm[ysl][r0:r0 + 64, 0:128],
                               in_=c.psh[r0:r0 + 64, 0:128],
                               func=AF.Copy).then_inc(c.s_se, 1)
            p.emit("scalar", yev)
            yv = p.bump(c.s_se, 1)
            prev_yev = yv

            def ydma(eng, h=h, tq=tq, yv=yv, ysl=ysl, ys_=ys_, yprev=yprev):
                eng.wait_ge(c.s_se, yv)
                if yprev:
                    eng.wait_ge(ys_, yprev)
                r0 = (h % 2) * 64
                off = YOFF + (h // 2) * 2048 + tq * 128
                eng.dma_start(out=F16S[r0:r0 + 64, off:off + 128],
                              in_=ysm[ysl][r0:r0 + 64, 0:128]).then_inc(ys_, 16)
            p.emit("sync", ydma)
            p.bump(ys_, 16)
        last_te_of_head[h] = p.val(c.s_te)
        if h + 2 < 16:
            emit_qkvload(h + 2)
    p.barrier()
    y0, y1 = p.val(ysem[0]), p.val(ysem[1])
    for e in ("tensor", "vector", "scalar", "gpsimd"):
        def yq_(eng, y0=y0, y1=y1):
            eng.wait_ge(ysem[0], y0)
            eng.wait_ge(ysem[1], y1)
        p.emit(e, yq_)
    for b in qkbuf:
        a16.drop(b)
    a16.drop(S16)
    a16.drop(pt16)
    a16.drop(ctx16)
    for b in ysm:
        a16.drop(b)
    p.barrier()
    _ckpt(c, "l3a")

    wo16 = a16.take(8 * D, "l3wo")
    load_w16(c, "L3_outw", 8, D, wo16)
    bo_s = load_small(c, "L3_outb", c.bias_sb[:, 0:8],
                      wait=[(c.s_se, p.val(c.s_se)), (c.s_ve, p.val(c.s_ve))])
    yg = a16.take(8 * 512, "l3yg")
    sstg = [a16.take(512, "l3so0"), a16.take(512, "l3so1")]
    ygs = p.sem(f"l3yg{p.nsem()}")
    ssem = [p.sem(f"l3s{p.nsem()}"), p.sem(f"l3s{p.nsem()}b")]
    p.barrier()
    for nt in range(4):
        def yload(eng, nt=nt, ygp=p.val(ygs), yd0=p.val(ysem[0]), yd1=p.val(ysem[1])):
            if nt == 0:
                eng.wait_ge(ysem[0], yd0)
                eng.wait_ge(ysem[1], yd1)
            if ygp:
                eng.wait_ge(ygs, ygp)
            src = F16S[:, YOFF:YOFF + 16384].rearrange("p (o t) -> p o t", t=2048)
            eng.dma_start(out=yg.rearrange("p (o t) -> p o t", t=512),
                          in_=src[:, :, nt * 512:(nt + 1) * 512]).then_inc(ygs, 16)
        p.emit("sync", yload)
        ygv = p.bump(ygs, 16)
        se0 = p.val(c.s_se)
        for och in range(NCH):
            bank = och % 4

            def omm(eng, och=och, nt=nt, bank=bank, ygv=ygv, se0=se0):
                if och == 0:
                    eng.wait_ge(ygs, ygv)
                if och >= 4 or nt > 0:
                    eng.wait_ge(c.s_se, se0 + och - 3)
                for k in range(NCH):
                    eng.matmul(out=c.ps1[:, bank * 512:(bank + 1) * 512],
                               lhsT=wo16[:, k * D + och * 128: k * D + (och + 1) * 128],
                               rhs=yg[:, k * 512:(k + 1) * 512],
                               start=(k == 0), stop=(k == NCH - 1))
                eng.nop().then_inc(c.s_te, 1)
            p.emit("tensor", omm)
            tv = p.bump(c.s_te, 1)
            stg = sstg[och % 2]
            ss_ = ssem[och % 2]
            ssp = p.val(ss_)

            def oev(eng, och=och, nt=nt, bank=bank, tv=tv, bo_s=bo_s, stg=stg,
                    ss_=ss_, ssp=ssp):
                eng.wait_ge(c.s_te, tv)
                if och == 0 and nt == 0:
                    eng.wait_ge(c.s_ls, bo_s)
                if ssp:
                    eng.wait_ge(ss_, ssp)
                eng.activation(out=stg[:, 0:512], in_=c.ps1[:, bank * 512:(bank + 1) * 512],
                               func=AF.Identity,
                               bias=c.bias_sb[:, och:och + 1]).then_inc(c.s_se, 1)
            p.emit("scalar", oev)
            sv = p.bump(c.s_se, 1)

            def sout(eng, och=och, nt=nt, sv=sv, stg=stg, ss_=ss_, ssp=ssp):
                eng.wait_ge(c.s_se, sv)
                if ssp:
                    eng.wait_ge(ss_, ssp)
                eng.dma_start(out=F16S[:, SOFF + och * 2048 + nt * 512: SOFF + och * 2048 + (nt + 1) * 512],
                              in_=stg[:, 0:512]).then_inc(ss_, 16)
            p.emit("sync", sout)
            p.bump(ss_, 16)
    p.barrier()
    s0, s1 = p.val(ssem[0]), p.val(ssem[1])
    for e in ("tensor", "vector", "scalar", "gpsimd"):
        def sq_(eng, s0=s0, s1=s1):
            eng.wait_ge(ssem[0], s0)
            eng.wait_ge(ssem[1], s1)
        p.emit(e, sq_)
    a16.drop(wo16)
    a16.drop(yg)
    for b in sstg:
        a16.drop(b)
    p.barrier()
    _ckpt(c, "l3o")

    g32 = a32.take(NCH * T, "l3g")
    eb32 = a32.take(1024, "l3eb32")
    expb16 = a16.take(1024, "l3eb16")
    eb_s = load_small(c, "L3_expb", eb32[0:8, 0:1024])

    def ebcv(eng, eb_s=eb_s):
        eng.wait_ge(c.s_ls, eb_s)
        eng.tensor_copy(out=expb16[0:8, 0:1024], in_=eb32[0:8, 0:1024]).then_inc(c.s_ve, 1)
    p.emit("vector", ebcv)
    p.bump(c.s_ve, 1)
    p.barrier()
    for nt in range(4):
        for och in range(NCH):
            bank2 = (nt * NCH + och) % 2

            def bmm(eng, nt=nt, och=och, bank2=bank2, vprev=p.val(c.s_ve)):
                eng.wait_ge(c.s_ve, vprev)
                eng.matmul(out=c.ps2[:, bank2 * 512:(bank2 + 1) * 512],
                           lhsT=expb16[0:8, och * 128:(och + 1) * 128],
                           rhs=mrow16[0:8, nt * 512:(nt + 1) * 512],
                           start=True, stop=True)
                eng.nop().then_inc(c.s_te, 1)
            p.emit("tensor", bmm)
            tv = p.bump(c.s_te, 1)

            def gbias(eng, nt=nt, och=och, bank2=bank2, tv=tv):
                eng.wait_ge(c.s_te, tv)
                eng.tensor_copy(out=t_sl(g32, och, nt_sl(nt)),
                                in_=c.ps2[:, bank2 * 512:(bank2 + 1) * 512]).then_inc(c.s_ve, 1)
            p.emit("vector", gbias)
            p.bump(c.s_ve, 1)
    p.barrier()
    for e in ("tensor", "vector", "scalar", "gpsimd"):
        p.emit(e, lambda eng, v=mdv, s=mds: eng.wait_ge(s, v))
    a16.drop(mrow16)
    we16b = [a16.take(8 * D, "l3we0"), a16.take(8 * D, "l3we1")]
    xe16 = a16.take(8 * 512, "l3xe")
    maskbc = a16.take(512, "l3mbc")
    mbs = [p.sem(f"l3mb{p.nsem()}"), p.sem(f"l3mb{p.nsem()}b")]
    emm_last = {}
    for e8 in range(8):
        we16 = we16b[e8 % 2]
        wdone = load_w16(c, "L3_expw", 8, D, we16, row_off=e8 * D,
                         dst_guard=(c.s_te, emm_last.get(e8 - 2, 0)))
        wlast = {}
        for s_, v_ in wdone:
            wlast[s_] = max(wlast.get(s_, 0), v_)
        wlast = list(wlast.items())
        for nt in range(4):
            mi = (e8 * 4 + nt) % 2
            mb_ = mbs[mi]
            mbp = p.val(mb_)

            def mload(eng, e8=e8, nt=nt, mb_=mb_, mbp=mbp, vprev=p.val(c.s_ve),
                      mdv=mdv, mds=mds):
                if e8 == 0 and nt == 0:
                    eng.wait_ge(mds, mdv)
                if mbp:
                    eng.wait_ge(mb_, mbp)
                eng.wait_ge(c.s_ve, vprev)
                eng.dma_start(out=maskbc,
                              in_=F16S[e8:e8 + 1, MOFF + nt * 512: MOFF + (nt + 1) * 512].broadcast_to((128, 512))).then_inc(mb_, 16)
            p.emit("sync", mload)
            mbv = p.bump(mb_, 16)

            def xeb(eng, nt=nt, mbv=mbv, mb_=mb_, teprev=p.val(c.s_te)):
                eng.wait_ge(mb_, mbv)
                eng.wait_ge(c.s_te, teprev)
                for k in range(NCH):
                    eng.tensor_tensor(out=xe16[:, k * 512:(k + 1) * 512],
                                      in0=t_sl(xn16, k, nt_sl(nt)), in1=maskbc,
                                      op=ALU.mult)
                eng.nop().then_inc(c.s_ve, 1)
            p.emit("vector", xeb)
            xv = p.bump(c.s_ve, 1)
            ve0 = p.val(c.s_ve)
            for oh in range(NCH):
                bank = oh % 4

                def emm(eng, oh=oh, bank=bank, xv=xv, ve0=ve0, we16=we16,
                        wl=(wlast if nt == 0 and oh == 0 else None)):
                    if oh == 0:
                        eng.wait_ge(c.s_ve, xv)
                    if wl:
                        for s_, v_ in wl:
                            eng.wait_ge(s_, v_)
                    if oh >= 4:
                        eng.wait_ge(c.s_ve, ve0 + oh - 3)
                    for k in range(NCH):
                        eng.matmul(out=c.ps1[:, bank * 512:(bank + 1) * 512],
                                   lhsT=we16[:, k * D + oh * 128: k * D + (oh + 1) * 128],
                                   rhs=xe16[:, k * 512:(k + 1) * 512],
                                   start=(k == 0), stop=(k == NCH - 1))
                    eng.nop().then_inc(c.s_te, 1)
                p.emit("tensor", emm)
                tv = p.bump(c.s_te, 1)
                emm_last[e8] = tv

                def gadd(eng, oh=oh, nt=nt, bank=bank, tv=tv):
                    eng.wait_ge(c.s_te, tv)
                    eng.tensor_tensor(out=t_sl(g32, oh, nt_sl(nt)),
                                      in0=t_sl(g32, oh, nt_sl(nt)),
                                      in1=c.ps1[:, bank * 512:(bank + 1) * 512],
                                      op=ALU.add).then_inc(c.s_ve, 1)
                p.emit("vector", gadd)
                p.bump(c.s_ve, 1)
    p.barrier()
    a32.drop(eb32)
    a16.drop(expb16)
    a16.drop(we16b[0])
    a16.drop(we16b[1])
    a16.drop(xe16)
    a16.drop(maskbc)
    p.barrier()
    _ckpt(c, "l3e")

    xg = a32.take(NCH * 512, "l3xg")
    sg32 = a32.take(512, "l3sg32")
    g16t = a16.take(512, "l3g16")
    sgrp = a16.take(NCH * 512, "l3sgrp")
    xgs = p.sem(f"l3xg{p.nsem()}")
    sgs = p.sem(f"l3sg{p.nsem()}")
    xos = p.sem(f"l3xo{p.nsem()}")
    for grp in range(4):
        def xload(eng, grp=grp, xop=p.val(xos), ssem=c.spill_sem, ssv=c.spill_done):
            if grp == 0:
                eng.wait_ge(ssem, ssv)
            if xop:
                eng.wait_ge(xos, xop)
            src = c.dram_spill.rearrange("(o pp) t -> pp o t", pp=128)
            eng.dma_start(out=xg.rearrange("p (o t) -> p o t", t=512),
                          in_=src[:, :, grp * 512:(grp + 1) * 512]).then_inc(xgs, 16)
        p.emit("sync", xload)
        xgv = p.bump(xgs, 16)

        def sload(eng, grp=grp, sgp=p.val(sgs), sv0=p.val(ssem[0]), sv1=p.val(ssem[1])):
            if grp == 0:
                eng.wait_ge(ssem[0], sv0)
                eng.wait_ge(ssem[1], sv1)
            if sgp:
                eng.wait_ge(sgs, sgp)
            src = F16S[:, SOFF:SOFF + 16384].rearrange("p (o t) -> p o t", t=2048)
            eng.dma_start(out=sgrp.rearrange("p (o t) -> p o t", t=512),
                          in_=src[:, :, grp * 512:(grp + 1) * 512]).then_inc(sgs, 16)
        p.emit("sync", sload)
        sgv = p.bump(sgs, 16)
        for och in range(NCH):
            def gsil(eng, grp=grp, och=och, vprev=p.val(c.s_ve)):
                eng.wait_ge(c.s_ve, vprev)
                eng.activation(out=g16t, in_=t_sl(g32, och, slice(grp * 512, (grp + 1) * 512)),
                               func=AF.Silu).then_inc(c.s_se, 1)
            p.emit("scalar", gsil)
            gv = p.bump(c.s_se, 1)

            def comb(eng, och=och, gv=gv, sgv=sgv, xgv=xgv):
                eng.wait_ge(c.s_se, gv)
                if och == 0:
                    eng.wait_ge(sgs, sgv)
                    eng.wait_ge(xgs, xgv)
                eng.tensor_tensor(out=sg32, in0=sgrp[:, och * 512:(och + 1) * 512],
                                  in1=g16t, op=ALU.mult)
                eng.tensor_tensor(out=xg[:, och * 512:(och + 1) * 512],
                                  in0=xg[:, och * 512:(och + 1) * 512], in1=sg32,
                                  op=ALU.add).then_inc(c.s_ve, 1)
            p.emit("vector", comb)
            p.bump(c.s_ve, 1)
        vev = p.val(c.s_ve)

        def xstore(eng, grp=grp, vev=vev):
            eng.wait_ge(c.s_ve, vev)
            dst = c.dram_spill.rearrange("(o pp) t -> pp o t", pp=128)
            eng.dma_start(out=dst[:, :, grp * 512:(grp + 1) * 512],
                          in_=xg.rearrange("p (o t) -> p o t", t=512)).then_inc(xos, 16)
        p.emit("sync", xstore)
        p.bump(xos, 16)
        p.barrier()
    a32.drop(xg)
    a32.drop(sg32)
    a32.drop(g32)
    a16.drop(g16t)
    a16.drop(sgrp)
    c.a16.drop(c.xn16)
    c.xn16 = None
    sem = p.sem(f"l3rf{p.nsem()}")
    c.x = c.a32.take(NCH * T, "x")
    x = c.x

    def fre(eng, sem=sem, xop=p.val(xos)):
        eng.wait_ge(xos, xop)
        for ch in range(NCH):
            eng.dma_start(out=t_sl(x, ch),
                          in_=c.dram_spill[ch * 128:(ch + 1) * 128, :]).then_inc(sem, 16)
    p.emit("sync", fre)
    v = p.bump(sem, 16 * NCH)
    for e in ("tensor", "vector", "scalar", "gpsimd"):
        p.emit(e, lambda eng, v=v, sem=sem: eng.wait_ge(sem, v))
    p.barrier()



def build_core_program(n_layers=4, upto=None):
    from contextlib import ExitStack
    nc = bass.Bass("TRN2", target_bir_lowering=False)
    es = ExitStack()
    p = Prog(nc, es)
    c = Ctx()
    c.nc, c.p, c.es = nc, p, es

    dram = {}

    def din(name, shape, dtype=F32):
        dram[name] = nc.dram_tensor(name, list(shape), dtype, kind="ExternalInput").ap()

    din("xT", (D, T))
    c.out = nc.dram_tensor("outT", [D, T], F32, kind="ExternalOutput").ap()
    c.dram_spill = nc.dram_tensor("xspill", [D, T], F32, kind="Internal").ap()
    c.dram_kv = nc.dram_tensor("kvscratch", [128, 8 * D], F16, kind="Internal").ap()
    c.dbg = nc.dram_tensor("dbg", [128, NCH * T], F16, kind="ExternalOutput").ap()
    c.dram_f16 = nc.dram_tensor("f16scratch", [128, 131072], F16, kind="Internal").ap()
    c.dram_w16 = nc.dram_tensor("w16scratch", [128, 98304], F16, kind="Internal").ap()
    din("consts32", (128, CW32))
    din("consts16", (128, CW16), F16)

    din("L0_Wq", (D, D)); din("L0_Wk", (D, D)); din("L0_Wv", (D, D)); din("L0_Wo", (D, D))
    din("L0_bq", (128, NCH)); din("L0_bo", (128, NCH))
    din("L0_bk_row", (1, D)); din("L0_bv_row", (1, D))
    din("L0_fw", (D, D))
    din("L1_wqkv", (D, 3 * D)); din("L1_outw", (D, D)); din("L1_outb", (128, NCH))
    din("L1_gate", (4096, D)); din("L1_up", (4096, D)); din("L1_down", (D, 4096))
    din("L1_sin", (1, 1)); din("L1_sind", (1, 1))
    din("L2_normw", (128, NCH)); din("L2_Wx", (D, 2048)); din("L2_bx", (128, 16))
    din("L2_Wz", (D, 2048)); din("L2_bz", (128, 16))
    din("L2_convw", (128, 64)); din("L2_convb", (128, 16))
    din("L2_Wxp", (2048, 96)); din("L2_Wdt", (64, 2048)); din("L2_bdt", (128, 16))
    din("L2_Alog", (128, 256)); din("L2_D", (128, 16))
    din("L2_Wout", (2048, D)); din("L2_bout", (128, NCH))
    din("L2_gate", (D, 4096)); din("L2_up", (D, 4096)); din("L2_down", (4096, D))
    din("L3_wqkv", (D, 3 * D)); din("L3_bqkv", (128, 24)); din("L3_bv_row", (1, D))
    din("L3_outw", (D, D)); din("L3_outb", (128, NCH))
    din("L3_W1", (D, 512)); din("L3_b1", (128, 4)); din("L3_W2", (512, 8)); din("L3_b2", (1, 8))
    din("L3_expw", (8 * D, D)); din("L3_expb", (8, D))
    c.dram = dram

    def sb(name, shape, dtype=F32):
        return es.enter_context(nc.sbuf_tensor(name, list(shape), dtype))

    pool32 = sb("pool32", (128, P32_COLS))
    pool16 = sb("pool16", (128, P16_COLS), F16)
    c.a32 = Alloc(pool32, P32_COLS, "a32")
    c.a16 = Alloc(pool16, P16_COLS, "a16")
    c.cons32 = sb("cons32", (128, CW32))
    c.cons16 = sb("cons16", (128, CW16), F16)
    c.cc = lambda v: c.cons32[:, OFF_SC + CONST_SCALARS.index(v): OFF_SC + CONST_SCALARS.index(v) + 1]
    c.alpha = sb("alpha", (128, 16))
    c.beta = sb("beta", (128, 16))
    c.stat32 = sb("stat32", (128, 32))
    c.bias_sb = sb("bias_sb", (128, 32))
    c.small16 = sb("small16", (128, 64), F16)
    c.pool8 = sb("pool8", (128, 4096), I8)
    c.convw_sb = sb("convw", (128, 64))
    c.alog_sb = sb("alog", (128, 256))
    c.a32_A = sb("a32A", (128, 256))
    c.bdt_sb = sb("bdt", (128, 16))
    c.l2conv32 = sb("l2cv", (128, 256))
    c.wstage = [sb("wst0", (128, 2048)), sb("wst1", (128, 2048))]

    c.ps1 = es.enter_context(nc.psum_tensor("ps1", [128, 2048], F32))
    c.ps2 = es.enter_context(nc.psum_tensor("ps2", [128, 1024], F32))
    c.psh = es.enter_context(nc.psum_tensor("psh", [128, 2048], F16))

    c.s_wl = [p.sem("wl0"), p.sem("wl1")]
    c.s_cv3 = {e: p.sem(f"cv_{e}") for e in ("vector", "scalar", "gpsimd")}
    for s in ("dma", "te", "ve", "se", "ge", "cv", "st", "gd", "ls", "bc"):
        setattr(c, "s_" + s, p.sem(s))

    c.x = c.a32.take(NCH * T, "x")
    c.xn16 = c.a16.take(NCH * T, "xn16")
    c.s16 = None
    x = c.x

    s_ld = p.sem("init_ld")

    def f(eng):
        for ch in range(NCH):
            eng.dma_start(out=t_sl(x, ch),
                          in_=dram["xT"][ch * 128:(ch + 1) * 128, :]).then_inc(s_ld, 16)
        eng.dma_start(out=c.cons32[:], in_=dram["consts32"]).then_inc(s_ld, 16)
        eng.dma_start(out=c.cons16[:], in_=dram["consts16"]).then_inc(s_ld, 16)
    p.emit("sync", f)
    iv = p.bump(s_ld, 16 * (NCH + 2))
    for e in ("tensor", "vector", "scalar", "gpsimd"):
        p.emit(e, lambda eng, v=iv: eng.wait_ge(s_ld, v))

    emitters = [emit_layer0, emit_layer1, emit_layer2, emit_layer3]
    c.upto = upto
    try:
        for li in range(n_layers):
            c.layer = li
            if li > 0 and c.xn16 is None:
                c.xn16 = c.a16.take(NCH * T, "xn16")
            emitters[li](c)
            p.barrier()
    except _StopEmit:
        pass

    xf = c.x

    def fs(eng):
        for ch in range(NCH):
            eng.dma_start(out=c.out[ch * 128:(ch + 1) * 128, :],
                          in_=t_sl(xf, ch)).then_inc(c.s_st, 16)
    p.emit("gpsimd", fs)
    p.bump(c.s_st, 16 * NCH)
    fin = p.val(c.s_st)
    p.emit("gpsimd", lambda eng, v=fin: eng.wait_ge(c.s_st, v))

    blk = es.enter_context(nc.Block())
    p.build(blk)
    es.close()
    return nc



def _prep_weights(inputs):
    g = {}
    f = np.ascontiguousarray

    def t(a):
        return f(np.asarray(a).T)

    for n in ("Wq", "Wk", "Wv", "Wo"):
        g[f"L0_{n}"] = t(inputs[f"L0_attn_{n}"])
    g["L0_bq"] = f(np.asarray(inputs["L0_attn_bq"]).reshape(NCH, 128).T)
    g["L0_bo"] = f(np.asarray(inputs["L0_attn_bo"]).reshape(NCH, 128).T)
    g["L0_bk_row"] = f(np.asarray(inputs["L0_attn_bk"]).reshape(1, D))
    g["L0_bv_row"] = f(np.asarray(inputs["L0_attn_bv"]).reshape(1, D))
    g["L0_fw"] = t(inputs["L0_ffn_weight"])
    g["L1_wqkv"] = f(np.asarray(inputs["L1_attn_weight_qkv"]))
    g["L1_outw"] = t(inputs["L1_attn_out_w"])
    g["L1_outb"] = f(np.asarray(inputs["L1_attn_out_b"]).reshape(NCH, 128).T)
    g["L1_gate"] = f(np.asarray(inputs["L1_ffn_gate_w"]))
    g["L1_up"] = f(np.asarray(inputs["L1_ffn_up_w"]))
    g["L1_down"] = f(np.asarray(inputs["L1_ffn_down_w"]))
    g["L1_sin"] = f(np.asarray(inputs["L1_ffn_s_in"]).reshape(1, 1))
    g["L1_sind"] = f(np.asarray(inputs["L1_ffn_s_in_down"]).reshape(1, 1))
    g["L2_normw"] = f(np.asarray(inputs["L2_attn_norm_w"]).reshape(NCH, 128).T)
    g["L2_Wx"] = t(inputs["L2_attn_Wx"])
    g["L2_bx"] = f(np.asarray(inputs["L2_attn_bx"]).reshape(16, 128).T)
    g["L2_Wz"] = t(inputs["L2_attn_Wz"])
    g["L2_bz"] = f(np.asarray(inputs["L2_attn_bz"]).reshape(16, 128).T)
    g["L2_convw"] = f(np.asarray(inputs["L2_attn_conv_w"]).reshape(16, 128, 4).transpose(1, 0, 2).reshape(128, 64))
    g["L2_convb"] = f(np.asarray(inputs["L2_attn_conv_b"]).reshape(16, 128).T)
    g["L2_Wxp"] = t(inputs["L2_attn_Wxp"])
    g["L2_Wdt"] = t(inputs["L2_attn_Wdt"])
    g["L2_bdt"] = f(np.asarray(inputs["L2_attn_bdt"]).reshape(16, 128).T)
    g["L2_Alog"] = f(np.asarray(inputs["L2_attn_A_log"]).reshape(16, 128, 16).transpose(1, 0, 2).reshape(128, 256))
    g["L2_D"] = f(np.asarray(inputs["L2_attn_D"]).reshape(16, 128).T)
    g["L2_Wout"] = t(inputs["L2_attn_Wout"])
    g["L2_bout"] = f(np.asarray(inputs["L2_attn_bout"]).reshape(NCH, 128).T)
    g["L2_gate"] = t(inputs["L2_ffn_w_gate"])
    g["L2_up"] = t(inputs["L2_ffn_w_up"])
    g["L2_down"] = t(inputs["L2_ffn_w_down"])
    g["L3_wqkv"] = t(inputs["L3_attn_w_qkv"])
    bq3 = np.asarray(inputs["L3_attn_b_qkv"], np.float32).copy()
    bq3[:1024] *= 0.125
    g["L3_bqkv"] = f(bq3.reshape(24, 128).T)
    g["L3_bv_row"] = f(np.asarray(inputs["L3_attn_b_qkv"], np.float32)[None, 2048:3072])
    g["L3_outw"] = t(inputs["L3_attn_w_out"])
    g["L3_outb"] = f(np.asarray(inputs["L3_attn_b_out"]).reshape(NCH, 128).T)
    g["L3_W1"] = f(np.asarray(inputs["L3_ffn_W1"]))
    g["L3_b1"] = f(np.asarray(inputs["L3_ffn_b1"]).reshape(4, 128).T)
    g["L3_W2"] = f(np.asarray(inputs["L3_ffn_W2"]))
    g["L3_b2"] = f(np.asarray(inputs["L3_ffn_b2"]).reshape(1, 8))
    g["L3_expw"] = f(np.asarray(inputs["L3_ffn_expert_W"]).reshape(8 * 1024, 1024))
    g["L3_expb"] = f(np.asarray(inputs["L3_ffn_expert_b"]))
    g["consts32"] = _consts32()
    g["consts16"] = _consts16()
    return g


def kernel(**inputs):
    g = _prep_weights(inputs)
    x = np.asarray(inputs["L0_attn_x"])
    nc = build_core_program(4)
    in_maps = []
    for b in range(B):
        m = dict(g)
        m["xT"] = np.ascontiguousarray(x[b].T)
        in_maps.append(m)
    res = run_bass_kernel_spmd(nc, in_maps, core_ids=list(range(B)))
    out = np.stack([np.ascontiguousarray(np.asarray(r["outT"]).T) for r in res.results], axis=0)
    return out.astype(np.float32)
